# version 15
# baseline (speedup 1.0000x reference)
import numpy as np
from contextlib import ExitStack

import concourse.bass as bass
import concourse.mybir as mybir
from concourse.bass_utils import run_bass_kernel_spmd
from concourse.bass_core_rust import IndirectOffsetOnAxis

F32 = mybir.dt.float32
BF16 = mybir.dt.bfloat16
U32 = mybir.dt.uint32
U8 = mybir.dt.uint8
OP = mybir.AluOpType
AF = mybir.ActivationFunctionType
AX = mybir.AxisListType.X

D = 2048
T = 2048
CH = 16
NTT = 4
TW = 512
HW = 256
HB = 16
SMS = float(1.0 / np.sqrt(128.0))
EPS = 1e-5
ENGINES = ("sync", "gpsimd", "scalar", "vector", "tensor")


class Prog:
    def __init__(self, nc):
        self.nc = nc
        self.cmd = {e: [] for e in ENGINES}
        self.cnt = {}
        self.sem = {}
        self.nbar = 0
        self.qsems = {}

    def mksem(self, stack, *names):
        self.stack = stack
        for nm in names:
            self.sem[nm] = stack.enter_context(self.nc.semaphore(nm))
            self.cnt[nm] = 0

    def sem_for(self, nm):
        if nm not in self.sem:
            self.sem[nm] = self.stack.enter_context(self.nc.semaphore(nm))
            self.cnt[nm] = 0
        return nm

    def w(self, eng, op=None, args=(), kwargs=None, wait=(), inc=()):
        for nm, n in inc:
            self.cnt[nm] += n
        self.cmd[eng].append((op, tuple(args), dict(kwargs or {}), tuple(wait), tuple(inc)))
        if eng == "vector" and op not in (None, "drain"):
            out_ap = (kwargs or {}).get("out", args[0] if args else None)
            try:
                small = out_ap is not None and out_ap.free_nbytes <= 512
            except Exception:
                small = True
            if small:
                self.cmd[eng].append(("drain", (), {}, (), ()))

    def tok(self, nm):
        return (nm, self.cnt[nm])

    def bar(self):
        self.nbar += 1
        k = self.nbar
        for e in ENGINES:
            self.cnt["b_" + e] += 1
            if e in ("sync",):
                self.cmd[e].append((None, (), {}, (), (("b_" + e, 1),)))
            else:
                self.cmd[e].append(("drain", (), {}, (), (("b_" + e, 1),)))
        for e in ENGINES:
            for o in ENGINES:
                if o != e:
                    self.w(e, wait=[("b_" + o, k)])

    def dma(self, q, out_ap, in_ap, wait=(), sem=None, **kw):
        semn = self.sem_for(sem) if sem else ("gd" if q == "gpsimd" else "qd")
        kwargs = {"out": out_ap, "in_": in_ap}
        kwargs.update(kw)
        self.qsems.setdefault(q, set()).add(semn)
        self.w(q, "dma_start", kwargs=kwargs, wait=wait, inc=[(semn, 16)])
        return self.tok(semn)

    def qfence(self, q):
        self.w(q, wait=[self.tok(s) for s in sorted(self.qsems.get(q, ()))])

    def replay(self, block):
        sem = self.sem

        def mk(e):
            def run(eng):
                for op, args, kwargs, wait, inc in self.cmd[e]:
                    for nm, v in wait:
                        eng.wait_ge(sem[nm], v)
                    if op is not None:
                        inst = getattr(eng, op)(*args, **kwargs)
                        for nm, n in inc:
                            inst.then_inc(sem[nm], n)
                    else:
                        for nm, n in inc:
                            eng.sem_inc(sem[nm], n)
            return run

        block.sync(mk("sync"))
        block.gpsimd(mk("gpsimd"))
        block.scalar(mk("scalar"))
        block.vector(mk("vector"))
        block.tensor(mk("tensor"))


def build_nc(stop_after=None):
    nc = bass.Bass("TRN2", target_bir_lowering=False)

    d_x = nc.dram_tensor("x", [NTT * 2, 128, CH * HW], F32, kind="ExternalInput")
    d_y = nc.dram_tensor("y", [NTT * 2, 128, CH * HW], F32, kind="ExternalOutput")
    d_wqk = nc.dram_tensor("wqk", [2, 128, CH * 128], F32, kind="ExternalInput")
    d_wv = nc.dram_tensor("wv", [4, CH, 128, 512], F32, kind="ExternalInput")
    d_wo = nc.dram_tensor("wo", [CH, 128, CH * 128], F32, kind="ExternalInput")
    d_wqkv = nc.dram_tensor("wqkv", [32, 128, CH * 128], F32, kind="ExternalInput")
    d_wqkvv = nc.dram_tensor("wqkvv", [CH, 128, 2048], F32, kind="ExternalInput")
    d_wout = nc.dram_tensor("wout", [CH, 128, CH * 128], F32, kind="ExternalInput")
    d_at = nc.dram_tensor("at", [4, 128, CH * 128], F32, kind="ExternalInput")
    d_bt = nc.dram_tensor("bt", [CH, 128, 4 * 128], F32, kind="ExternalInput")
    d_wr = nc.dram_tensor("wr", [128, CH * 8], F32, kind="ExternalInput")
    d_qkvbrow = nc.dram_tensor("qkvbrow", [1, 6144], F32, kind="ExternalInput")
    d_outb2 = nc.dram_tensor("outb2", [128, 16], F32, kind="ExternalInput")
    d_br = nc.dram_tensor("br", [8, 1], F32, kind="ExternalInput")
    d_dbg = nc.dram_tensor("dbg", [5, 128, 2048], F32, kind="ExternalOutput") \
        if stop_after in ("probe1", "probe0", "probe2", "probe3", "probe4",
                          "probe5") else None
    d_vtm = nc.dram_tensor("vtm_s", [T, D], BF16)
    d_keys = nc.dram_tensor("keys_s", [128, T], F32)
    d_ctx = nc.dram_tensor("ctx_s", [HB, 128, T], BF16)

    A = nc.alloc_sbuf_tensor
    XB = A("XB", [128, CH * T], BF16)
    XSTG = A("XSTG", [128, CH * HW], F32)
    SCR = A("SCR", [128, 32 * 1024], U8)
    WBUF = A("WBUF", [128, 25 * 1024], U8)
    P0 = A("P0", [128, 28 * 1024], U8)
    RSTD = A("RSTD", [128, T], F32)
    QA = A("QA", [128, T], F32)
    KAT = A("KAT", [128, T], F32)
    VGALL = A("VGALL", [128, 64 * 16], BF16)
    IDF = A("IDF", [128, 128], F32)
    ONE1 = A("ONE1", [1, 128], BF16)
    ONEC = A("ONEC", [128, 1], BF16)
    EPH = A("EPH", [2, 128], BF16)
    REP8 = A("REP8", [8, 128], BF16)
    G8A = A("G8A", [8, 512], BF16)
    SLT = A("SLT", [128, 256], BF16)
    SLB = A("SLB", [128, 256], BF16)
    VST = A("VST", [128, 448], BF16)
    VSB = A("VSB", [128, 448], BF16)
    TV = A("TV", [128, 64], F32)
    TIU = A("TIU", [128, 64], U32)
    TIF = A("TIF", [128, 64], F32)
    IXB = A("IXB", [128, 128], F32)
    TBALL = A("TBALL", [128, 128], F32)
    IXVU = A("IXVU", [128, 64], U32)
    IXKU = A("IXKU", [128, 64], U32)
    VALS = A("VALS", [128, 64], F32)
    VB = A("VB", [128, 64], BF16)
    SCL = A("SCL", [128, 64], F32)
    EXB = A("EXB", [128, 128], BF16)
    ZRO = A("ZRO", [128, 1], F32)
    TINY = A("TINY", [1, TW], BF16)
    PVEC = A("PVEC", [128, 1], F32)
    H2K = A("H2K", [128, 1], F32)
    NEGC = A("NEGC", [128, 1], F32)
    NEGCB = A("NEGCB", [128, 4], F32)
    CS1 = A("CS1", [128, 1], F32)
    CS2 = A("CS2", [128, 1], F32)
    RSTM = A("RSTM", [128, CH], F32)
    OUTB = A("OUTB", [128, 16], F32)
    GTM = A("GTM", [128, 8], F32)
    GSC = A("GSC", [128, 8], F32)
    PTB = A("PTB", [128, 4 * TW], BF16)
    VSTG = A("VSTG", [128, 2 * TW], BF16)
    RSB = A("RSB", [128, TW], BF16)
    PS = [nc.alloc_psum_tensor("ps%d" % i, [128, 512], F32) for i in range(8)]

    SCR_BF = SCR[:, :].bitcast(BF16)
    SCR_F32 = SCR[:, :].bitcast(F32)
    SCR_U32 = SCR[:, :].bitcast(U32)
    WB_BF = WBUF[:, :].bitcast(BF16)
    P0_F32 = P0[:, :].bitcast(F32)
    P0_BF = P0[:, :].bitcast(BF16)
    QA_BF = QA[:, :].bitcast(BF16)
    KAT_BF = KAT[:, :].bitcast(BF16)
    WTR = XSTG[:, :].bitcast(BF16)
    BRS = P0[:, :].bitcast(F32)[0:8, 7056:7057]
    LGT = SCR_F32[0:8, 1024:1536]
    GFM = SCR_F32[0:8, 1536:2048]
    G8H = SCR_BF[0:8, 4096:4608]
    G8L = SCR_BF[0:8, 4608:5120]

    def MQ():
        return QA_BF[:, 0:2048]

    def MQC():
        return QA_BF[:, 2048:4096]

    def QHI():
        return KAT_BF[:, 0:2048]

    def QLO():
        return KAT_BF[:, 2048:4096]

    def RH(tt):
        return P0_BF[0:1, tt * TW:(tt + 1) * TW]

    def RL(tt):
        return P0_BF[0:1, 2048 + tt * TW: 2048 + (tt + 1) * TW]

    def MH_(tt):
        return P0_BF[0:1, 4096 + tt * TW: 4096 + (tt + 1) * TW]

    def ML_(tt):
        return P0_BF[0:1, 6144 + tt * TW: 6144 + (tt + 1) * TW]

    def SQVB(tt):
        return P0_BF[0:1, 8192 + tt * TW: 8192 + (tt + 1) * TW]

    RHB = P0_BF[0:1, 10240:10752]
    RLB = P0_BF[0:1, 10752:11264]
    QKVBP = P0_BF[0:1, 13344:14112]
    SCRA = P0_F32[0:1, 5632:5888]
    SCRB = P0_F32[0:1, 5888:6144]
    SCRW = P0_F32[0:1, 6144:6400]
    SCRW2 = P0_F32[0:1, 6400:6656]
    CELLS = P0_F32[0:1, 6656:6672]
    EPSC = P0_F32[0:1, 6672:6673]

    p = Prog(nc)
    stack = ExitStack()
    p.mksem(stack, "qd", "gd", "tv", "vv", "sv",
             *["b_" + e for e in ENGINES])
    psg = [None] * 8
    st = {"xstg": None, "vstg": [None, None]}

    def ps_wait(i):
        return [psg[i]] if psg[i] else []

    def mm(out_ap, lhsT, rhs, start, stop, wait=(), inc=()):
        p.w("tensor", "matmul", (out_ap, lhsT, rhs), {"start": start, "stop": stop},
            wait=wait, inc=inc)

    IOTA = TBALL[:, :].bitcast(U32)
    IOTA5 = VGALL[:, :].bitcast(U32)[:, 0:512]
    p.w("gpsimd", "iota", (IOTA[:, 0:1], [[0, 1]]), {"channel_multiplier": 1})
    p.bar()
    p.w("vector", "tensor_copy", (PVEC[:, :], IOTA[:, 0:1]))
    p.bar()
    p.w("vector", "tensor_scalar", (H2K[:, :], PVEC[:, :], 2048.0, None, OP.mult))
    p.w("vector", "memset", (EPSC, float(EPS)))
    p.w("vector", "memset", (ONE1[:, :], 1.0))
    p.w("vector", "memset", (ONEC[:, :], 1.0))
    p.w("gpsimd", "iota", (IOTA[:, 0:128], [[1, 128]]), {"channel_multiplier": 0})
    p.bar()
    p.w("vector", "tensor_copy", (QA[:, 0:128], IOTA[:, 0:128]))
    p.bar()
    p.w("vector", "tensor_scalar", (IDF[:, :], QA[:, 0:128], PVEC[:, 0:1], None,
                                    OP.is_equal))
    p.w("gpsimd", "iota", (IOTA[:, 0:128], [[1, 2], [0, 64]]), {"channel_multiplier": 0})
    p.bar()
    p.w("vector", "tensor_copy", (QA[:, 0:128], IOTA[:, 0:128]))
    p.bar()
    p.w("vector", "tensor_scalar", (EPH[:, :], QA[0:2, 0:128], PVEC[0:2, 0:1], None,
                                    OP.is_equal))
    p.w("gpsimd", "iota", (IOTA[:, 0:128], [[1, 8], [0, 16]]), {"channel_multiplier": 0})
    p.bar()
    p.w("vector", "tensor_copy", (QA[:, 0:128], IOTA[:, 0:128]))
    p.bar()
    p.w("vector", "tensor_scalar", (REP8[:, :], QA[0:8, 0:128], PVEC[0:8, 0:1], None,
                                    OP.is_equal))
    p.w("gpsimd", "iota", (IOTA5[:, 0:512], [[1, 8], [0, 64]]), {"channel_multiplier": 0})
    p.bar()
    p.w("vector", "tensor_copy", (QA[:, 128:640], IOTA5[:, 0:512]))
    p.bar()
    p.w("vector", "tensor_scalar", (G8A[:, :], QA[0:8, 128:640], PVEC[0:8, 0:1], None,
                                    OP.is_equal))
    p.w("gpsimd", "iota", (IOTA[:, 0:128], [[32, 4], [0, 32]]), {"channel_multiplier": 0})
    p.bar()
    p.w("vector", "tensor_copy", (QA[:, 0:128], IOTA[:, 0:128]))
    p.bar()
    p.w("vector", "tensor_scalar", (EXB[:, :], QA[:, 0:128], PVEC[:, 0:1], None,
                                    OP.is_equal))
    p.w("vector", "memset", (ZRO[:, :], 0.0))
    p.w("vector", "memset", (TINY[:, :], 1e-20))
    p.w("vector", "memset", (SLT[:, :], 0.0))
    p.w("vector", "memset", (SLT[0:32, 128:129], 1.0))
    p.w("vector", "memset", (SLT[64:96, 129:130], 1.0))
    p.w("vector", "memset", (SLB[:, :], 0.0))
    p.w("vector", "memset", (SLB[32:64, 128:129], 1.0))
    p.w("vector", "memset", (SLB[96:128, 129:130], 1.0))
    p.w("vector", "memset", (VST[:, :], 0.0))
    p.w("vector", "memset", (VSB[:, :], 0.0))
    p.dma("sync", OUTB[:, :], d_outb2.ap()[:, :])
    p.dma("sync", BRS[:, :], d_br.ap()[:, :])
    p.qfence("sync")
    p.bar()

    XC = SCR_BF[:, 8192:8192 + CH * HW]
    SQH = SCR_BF[:, 12288:12288 + CH * HW]

    def phase_A(tt, hh, full_mode, wait_x=()):
        h0 = hh * HW
        p.w("scalar", "activation", (SQH, XSTG[:, :], AF.Square), wait=wait_x,
            inc=[("sv", 1)])
        sq_t = p.tok("sv")
        p.w("vector", "tensor_copy", (XC, XSTG[:, :]), wait=wait_x, inc=[("vv", 1)])
        cast_t = p.tok("vv")
        for dch in range(CH):
            mm(PS[0][0:1, 0:HW], ONEC[:, :], XC[:, dch * HW:(dch + 1) * HW],
               start=(dch == 0), stop=(dch == CH - 1),
               wait=([cast_t] + ps_wait(0)) if dch == 0 else ())
        for dch in range(CH):
            mm(PS[1][0:1, 0:HW], ONEC[:, :], SQH[:, dch * HW:(dch + 1) * HW],
               start=(dch == 0), stop=(dch == CH - 1),
               wait=([sq_t] + ps_wait(1)) if dch == 0 else (),
               inc=[("tv", 1)] if dch == CH - 1 else ())
        mm_t = p.tok("tv")
        p.w("vector", "tensor_scalar", (SCRA, PS[0][0:1, 0:HW], 1.0 / D, None, OP.mult),
            wait=[mm_t])
        p.w("vector", "tensor_tensor", (SCRB, SCRA, SCRA, OP.mult))
        p.w("vector", "scalar_tensor_tensor",
            (SCRB, PS[1][0:1, 0:HW], 1.0 / D, SCRB, OP.mult, OP.subtract), inc=[("vv", 1)])
        psg[0] = psg[1] = p.tok("vv")
        p.w("scalar", "activation", (SCRB, SCRB, AF.Sqrt), {"bias": EPSC, "scale": 1.0},
            wait=[p.tok("vv")], inc=[("sv", 1)])
        p.w("vector", "tensor_copy", (SQVB(tt)[0:1, h0:h0 + HW], SCRB), wait=[p.tok("sv")])
        p.w("vector", "reciprocal", (SCRB, SCRB))
        p.w("vector", "tensor_copy", (RH(tt)[0:1, h0:h0 + HW], SCRB))
        p.w("vector", "scalar_tensor_tensor",
            (RL(tt)[0:1, h0:h0 + HW], SCRB, 1.0, RH(tt)[0:1, h0:h0 + HW],
             OP.mult, OP.subtract))
        p.w("vector", "tensor_copy", (MH_(tt)[0:1, h0:h0 + HW], SCRA))
        p.w("vector", "scalar_tensor_tensor",
            (ML_(tt)[0:1, h0:h0 + HW], SCRA, 1.0, MH_(tt)[0:1, h0:h0 + HW],
             OP.mult, OP.subtract), inc=[("vv", 1)])
        rows_t = p.tok("vv")
        mm(PS[2][:, 0:HW], ONE1[:, :], RH(tt)[0:1, h0:h0 + HW], start=True, stop=False,
           wait=[rows_t] + ps_wait(2))
        mm(PS[2][:, 0:HW], ONE1[:, :], RL(tt)[0:1, h0:h0 + HW], start=False, stop=True,
           inc=[("tv", 1)])
        p.w("vector", "tensor_copy",
            (RSTD[:, tt * TW + h0: tt * TW + h0 + HW], PS[2][:, 0:HW]),
            wait=[p.tok("tv")], inc=[("vv", 1)])
        rstd_t = p.tok("vv")
        mm(PS[2][:, 0:HW], ONE1[:, :], MH_(tt)[0:1, h0:h0 + HW], start=True, stop=False,
           wait=[rstd_t])
        mm(PS[2][:, 0:HW], ONE1[:, :], ML_(tt)[0:1, h0:h0 + HW], start=False, stop=True,
           inc=[("tv", 1)])
        mrep_t = p.tok("tv")
        if full_mode:
            xb_out = XB[:, :].rearrange("q (c t) -> q c t", c=CH)[
                :, :, tt * TW + h0: tt * TW + h0 + HW]
        else:
            xb_out = XB[:, 0:CH * TW].rearrange("q (c t) -> q c t", c=CH)[:, :, h0:h0 + HW]
        xin = XSTG[:, :].rearrange("q (c t) -> q c t", c=CH)
        mrep_b = PS[2][:, 0:HW].unsqueeze(1).to_broadcast([128, CH, HW])
        p.w("vector", "tensor_tensor", (xb_out, xin, mrep_b, OP.subtract),
            wait=[mrep_t], inc=[("vv", 1)])
        psg[2] = p.tok("vv")
        return p.tok("vv")

    def rstd_tm(tt):
        for j in range(4):
            mm(PS[3][:, j:j + 1], RH(tt)[0:1, j * 128:(j + 1) * 128], ONE1[0:1, 0:1],
               start=True, stop=False, wait=(ps_wait(3) if j == 0 else ()))
            mm(PS[3][:, j:j + 1], RL(tt)[0:1, j * 128:(j + 1) * 128], ONE1[0:1, 0:1],
               start=False, stop=True, inc=([("tv", 1)] if j == 3 else ()))
        p.w("vector", "tensor_copy", (RSTM[:, tt * 4:tt * 4 + 4], PS[3][:, 0:4]),
            wait=[p.tok("tv")], inc=[("vv", 1)])
        psg[3] = p.tok("vv")

    class Ring:
        def __init__(self, n, cols, off, pfx):
            self.n, self.cols, self.off, self.pfx = n, cols, off, pfx
            self.i = 0
            self.free = [None] * n

        def load(self, src):
            s = self.i % self.n
            self.i += 1
            dst = WB_BF[:, self.off + s * self.cols: self.off + (s + 1) * self.cols]
            wl = [self.free[s]] if self.free[s] is not None else []
            tok = p.dma("gpsimd", dst, src, wait=wl, sem="%s%d" % (self.pfx, s))
            return dst, tok, s

        def done(self, s, tok):
            self.free[s] = tok

    def xin_ap():
        return XSTG[:, :]

    r_qk = Ring(2, 2048, 0, "rqk")
    r_wv = Ring(16, 512, 4096, "rwv")
    qk_tiles = [r_qk.load(d_wqk.ap()[e, :, :]) for e in range(2)]

    for tt in range(NTT):
        for hh in range(2):
            xt = p.dma("sync", xin_ap(), d_x.ap()[2 * tt + hh, :, :],
                       wait=([st["xstg"]] if st["xstg"] else []), sem="sx")
            phase_A(tt, hh, full_mode=False, wait_x=[xt])
            p.bar()
        rstd_tm(tt)
        if stop_after == "probe3" and tt == 0:
            p.bar()
            wtile0d, _, _ = qk_tiles[0]
            for sl in range(4):
                p.dma("gpsimd", d_dbg.ap()[sl, :, :],
                      XB[:, sl * 2048:(sl + 1) * 2048])
            p.dma("gpsimd", d_dbg.ap()[4, :, :], wtile0d[:, 0:2048])
            p.qfence("sync")
            p.qfence("gpsimd")
            p.bar()
            with nc.Block() as block:
                p.replay(block)
            stack.close()
            return nc
        if stop_after == "probe2" and tt == 0:
            p.bar()
            wtile0c, _, _ = qk_tiles[0]
            for li, ln_ in enumerate([2, 4, 8, 16]):
                for dch in range(ln_):
                    mm(PS[7][:, :], wtile0c[:, dch * 128:(dch + 1) * 128],
                       XB[:, dch * TW:(dch + 1) * TW],
                       start=(dch == 0), stop=(dch == ln_ - 1),
                       wait=(ps_wait(7) if dch == 0 else ()),
                       inc=([("tv", 1)] if dch == ln_ - 1 else ()))
                p.w("vector", "tensor_copy", (KAT[:, 1024:1536], PS[7][:, :]),
                    wait=[p.tok("tv")], inc=[("vv", 1)])
                psg[7] = p.tok("vv")
                p.dma("sync", d_dbg.ap()[4, :, 512 * li:512 * (li + 1)],
                      KAT[:, 1024:1536], wait=[p.tok("vv")])
            p.dma("gpsimd", d_dbg.ap()[0, :, :], XB[:, 0:2048])
            p.qfence("sync")
            p.qfence("gpsimd")
            p.bar()
            with nc.Block() as block:
                p.replay(block)
            stack.close()
            return nc
        if stop_after == "probe0" and tt == 0:
            p.dma("gpsimd", d_dbg.ap()[3, :, 0:1024], XB[:, 0:1024])
            wtile0, wtok0, _ = qk_tiles[0]
            for dch in range(CH):
                mm(PS[6][:, :], wtile0[:, dch * 128:(dch + 1) * 128],
                   XB[:, dch * TW:(dch + 1) * TW],
                   start=(dch == 0), stop=(dch == CH - 1),
                   inc=([("tv", 1)] if dch == CH - 1 else ()))
            p.w("vector", "tensor_mul",
                (QLO().bitcast(F32)[:, 0:512] if False else KAT[:, 1536:2048],
                 PS[6][:, :], RSTD[:, 0:512]), wait=[p.tok("tv")], inc=[("vv", 1)])
            psg[6] = p.tok("vv")
            p.dma("sync", d_dbg.ap()[3, :, 1024:1536], KAT[:, 1536:2048],
                  wait=[p.tok("vv")])
            wtile0b, _, _ = qk_tiles[0]
            p.dma("gpsimd", d_dbg.ap()[3, :, 1536:2048], wtile0b[:, 0:512])
            for li, ln_ in enumerate([2, 4, 8, 16]):
                for dch in range(ln_):
                    mm(PS[7][:, :], wtile0b[:, dch * 128:(dch + 1) * 128],
                       XB[:, dch * TW:(dch + 1) * TW],
                       start=(dch == 0), stop=(dch == ln_ - 1),
                       wait=(ps_wait(7) if dch == 0 else ()),
                       inc=([("tv", 1)] if dch == ln_ - 1 else ()))
                p.w("vector", "tensor_copy", (KAT[:, 1024:1536], PS[7][:, :]),
                    wait=[p.tok("tv")], inc=[("vv", 1)])
                psg[7] = p.tok("vv")
                p.dma("sync", d_dbg.ap()[4, :, 512 * li:512 * (li + 1)],
                      KAT[:, 1024:1536], wait=[p.tok("vv")])
        for e in range(2):
            wtile, wtok, _ = qk_tiles[e]
            for dch in range(CH):
                mm(PS[4][:, :], wtile[:, dch * 128:(dch + 1) * 128],
                   XB[:, dch * TW:(dch + 1) * TW],
                   start=(dch == 0), stop=(dch == CH - 1),
                   wait=([wtok] + ps_wait(4)) if dch == 0 else (),
                   inc=([("tv", 1)] if dch == CH - 1 else ()))
            dst = QA if e == 0 else KAT
            p.w("vector", "tensor_mul",
                (dst[:, tt * TW:(tt + 1) * TW], PS[4][:, :], RSTD[:, tt * TW:(tt + 1) * TW]),
                wait=[p.tok("tv")], inc=[("vv", 1)])
            psg[4] = p.tok("vv")
        for ecg in range(4):
            tiles = [r_wv.load(d_wv.ap()[ecg, dch, :, :]) for dch in range(CH)]
            for j in range(4):
                vb_ = (5, 6, 7)[j % 3]
                tc = tt * 4 + j
                for dch in range(CH):
                    wt, wtk, _ = tiles[dch]
                    mm(PS[vb_][:, :], XB[:, dch * TW + j * 128: dch * TW + (j + 1) * 128], wt,
                       start=(dch == 0), stop=(dch == CH - 1),
                       wait=(([wtk] if j == 0 else []) + (ps_wait(vb_) if dch == 0 else [])),
                       inc=([("tv", 1)] if dch == CH - 1 else ()))
                vs = VSTG[:, (j % 2) * TW:(j % 2 + 1) * TW]
                vw = [p.tok("tv")] + ([st["vstg"][j % 2]] if st["vstg"][j % 2] else [])
                p.w("vector", "tensor_scalar",
                    (vs, PS[vb_][:, :], RSTM[:, tc:tc + 1], None, OP.mult),
                    wait=vw, inc=[("vv", 1)])
                psg[vb_] = p.tok("vv")
                st["vstg"][j % 2] = p.dma(
                    "sync", d_vtm.ap()[tc * 128:(tc + 1) * 128, ecg * 512:(ecg + 1) * 512],
                    vs, wait=[p.tok("vv")], sem="vs%d" % (j % 2))
            lasttok = p.tok("tv")
            for dch in range(CH):
                r_wv.done(tiles[dch][2], lasttok)
        p.bar()
        if stop_after == "probe0" and tt == 0:
            p.w("gpsimd", wait=[p.tok("vs0"), p.tok("vs1")])
            p.dma("gpsimd", d_dbg.ap()[0, :, :], XB[:, 0:2048])
            p.dma("sync", d_dbg.ap()[1, :, 0:512], RSTD[:, 0:512])
            p.dma("sync", d_dbg.ap()[1, :, 512:516], RSTM[:, 0:4])
            p.dma("sync", d_dbg.ap()[2, :, 0:512], QA[:, 0:512])
            p.dma("sync", d_dbg.ap()[2, :, 512:1024], KAT[:, 0:512])
            p.dma("gpsimd", d_dbg.ap()[4, :, 0:1024], d_vtm.ap()[0:128, 0:1024])
            p.qfence("sync")
            p.qfence("gpsimd")
            p.bar()
            with nc.Block() as block:
                p.replay(block)
            stack.close()
            return nc

    p.dma("sync", d_keys.ap()[:, :], KAT[:, :])
    p.qfence("sync")
    p.bar()

    TOKU = SCR_U32[:, 0:2048]
    PKT = SCR_F32[:, 2048:4096]
    PKB = SCR_F32[:, 4096:6144]
    MSK = SCR[:, 24576:26624]
    p.w("gpsimd", "iota", (TOKU, [[1, T]]), {"channel_multiplier": 0})
    p.bar()
    p.w("vector", "tensor_scalar", (MSK, KAT[:, :], 0.0, None, OP.is_le))
    p.w("vector", "tensor_scalar",
        (PKT.bitcast(U32), KAT[:, :].bitcast(U32), 0xFFFFF800, None, OP.bitwise_and))
    p.w("vector", "tensor_tensor", (PKT.bitcast(U32), PKT.bitcast(U32), TOKU, OP.bitwise_or))
    p.w("vector", "tensor_scalar",
        (PKB.bitcast(U32), KAT[:, :].bitcast(U32), 0x80000000, None, OP.bitwise_xor))
    p.w("vector", "tensor_scalar",
        (PKB.bitcast(U32), PKB.bitcast(U32), 0xFFFFF800, None, OP.bitwise_and))
    p.w("vector", "tensor_tensor", (PKB.bitcast(U32), PKB.bitcast(U32), TOKU, OP.bitwise_or))
    p.w("vector", "memset", (TOKU, 0))
    p.w("vector", "copy_predicated", (PKT.bitcast(U32), MSK, TOKU))
    p.w("vector", "tensor_scalar", (MSK, KAT[:, :], 0.0, None, OP.is_ge))
    p.w("vector", "copy_predicated", (PKB.bitcast(U32), MSK, TOKU))
    for i in range(4):
        p.w("vector", "max", kwargs={"out": TV[:, 8 * i:8 * i + 8], "in_": PKT})
        p.w("vector", "drain")
        p.w("vector", "match_replace",
            kwargs={"out": PKT, "in_to_replace": TV[:, 8 * i:8 * i + 8],
                    "in_values": PKT, "imm_value": -1.0})
        p.w("vector", "drain")
    for i in range(4, 8):
        p.w("vector", "max", kwargs={"out": TV[:, 8 * i:8 * i + 8], "in_": PKB})
        p.w("vector", "drain")
        p.w("vector", "match_replace",
            kwargs={"out": PKB, "in_to_replace": TV[:, 8 * i:8 * i + 8],
                    "in_values": PKB, "imm_value": -1.0})
        p.w("vector", "drain")
    p.w("vector", "tensor_scalar",
        (TIU[:, :], TV[:, :].bitcast(U32), 2047, None, OP.bitwise_and))
    p.w("vector", "drain")
    p.w("vector", "tensor_copy", (TIF[:, :], TIU[:, :]))
    p.w("vector", "drain")
    p.w("vector", "scalar_tensor_tensor",
        (IXB[:, 0:64], TIF[:, :], 128.0, PVEC[:, 0:1].to_broadcast([128, 64]),
         OP.mult, OP.add))
    p.w("vector", "drain")
    p.w("vector", "scalar_tensor_tensor",
        (IXB[:, 64:128], TIF[:, :], 1.0, H2K[:, 0:1].to_broadcast([128, 64]),
         OP.mult, OP.add))
    p.w("vector", "drain", inc=[("vv", 1)])
    ix_t = p.tok("vv")
    p.w("tensor", "transpose", (PS[1][:, 0:128], IXB[:, :], IDF[:, :]),
        wait=[ix_t] + ps_wait(1), inc=[("tv", 1)])
    p.w("vector", "tensor_copy", (TBALL[:, :], PS[1][:, 0:128]), wait=[p.tok("tv")],
        inc=[("vv", 1)])
    psg[1] = p.tok("vv")
    tb_t = p.tok("vv")
    p.dma("sync", IXB[0:64, 0:64], TBALL[0:64, 0:128:2], wait=[tb_t],
          allow_slow_non_contiguous=True, sem="ix")
    p.dma("sync", IXB[64:128, 0:64], TBALL[0:64, 1:128:2], wait=[tb_t],
          allow_slow_non_contiguous=True, sem="ix")
    p.dma("sync", IXB[0:64, 64:128], TBALL[64:128, 0:128:2], wait=[tb_t],
          allow_slow_non_contiguous=True, sem="ix")
    tix = p.dma("sync", IXB[64:128, 64:128], TBALL[64:128, 1:128:2], wait=[tb_t],
                allow_slow_non_contiguous=True, sem="ix")
    p.bar()
    p.w("vector", "tensor_copy", (IXVU[:, :], IXB[:, 0:64]), wait=[tix])
    p.w("vector", "tensor_copy", (IXKU[:, :], IXB[:, 64:128]), inc=[("vv", 1)])
    ixu_t = p.tok("vv")
    if stop_after == "probe1":
        p.dma("sync", d_dbg.ap()[0, :, :], QA[:, :])
        p.dma("sync", d_dbg.ap()[4, :, :], KAT[:, :])
    p.w("vector", "tensor_copy", (SCR_BF[:, 12288:14336], QA[:, :]))
    p.w("vector", "scalar_tensor_tensor",
        (SCR_BF[:, 14336:16384], QA[:, :], 1.0, SCR_BF[:, 12288:14336],
         OP.mult, OP.subtract))
    p.w("vector", "tensor_scalar", (PKT, QA[:, :], 0.0, None, OP.is_gt))
    p.w("vector", "tensor_scalar", (PKB, QA[:, :], 0.0, None, OP.is_le), inc=[("vv", 1)])
    stg_t = p.tok("vv")
    p.w("vector", "tensor_copy", (QHI(), SCR_BF[:, 12288:14336]), wait=[stg_t])
    p.w("vector", "tensor_copy", (QLO(), SCR_BF[:, 14336:16384]))
    p.w("vector", "tensor_copy", (MQ(), PKT))
    p.w("vector", "tensor_copy", (MQC(), PKB), inc=[("vv", 1)])
    p.bar()
    for pr in range(64):
        p.w("gpsimd", "indirect_dma_start",
            (VALS[:, pr:pr + 1], None,
             bass.AP(d_keys, 0, [[1, T * 128], [1, 1]]),
             IndirectOffsetOnAxis(IXKU[:, pr:pr + 1], 0)),
            inc=[("gd", 16)])
    for pr in range(64):
        p.w("gpsimd", "indirect_dma_start",
            (VGALL[:, pr * 16:(pr + 1) * 16], None,
             bass.AP(d_vtm, 0, [[16, T * D // 16], [1, 16]]),
             IndirectOffsetOnAxis(IXVU[:, pr:pr + 1], 0)),
            inc=[("gd", 16)])
    p.qfence("gpsimd")
    p.bar()
    p.w("vector", "tensor_copy", (VB[:, :], VALS[:, :]), inc=[("vv", 1)])
    mm(PS[1][:, 0:64], EXB[:, :], VB[:, :], start=True, stop=True,
       wait=[p.tok("vv")] + ps_wait(1), inc=[("tv", 1)])
    p.w("vector", "tensor_tensor", (SCL[:, :], VALS[:, :], PS[1][:, 0:64],
                                    OP.subtract),
        wait=[p.tok("tv")], inc=[("vv", 1)])
    psg[1] = p.tok("vv")
    if stop_after == "probe1":
        p.dma("sync", d_dbg.ap()[1, :, 0:64], TIF[:, :])
        p.dma("sync", d_dbg.ap()[1, :, 64:128], VALS[:, :])
        p.dma("sync", d_dbg.ap()[1, :, 128:129], NEGC[:, :], allow_slow_non_contiguous=True)
        p.dma("sync", d_dbg.ap()[1, :, 129:193], TV[:, :])
        p.dma("gpsimd", d_dbg.ap()[2, :, 0:1024], VGALL[:, 0:1024])
        p.dma("gpsimd", d_dbg.ap()[3, :, 0:1024], d_vtm.ap()[0:128, 0:1024])
        p.qfence("sync")
        p.qfence("gpsimd")
        p.bar()
        with nc.Block() as block:
            p.replay(block)
        stack.close()
        return nc

    r_at = Ring(2, 2048, 4096, "rat")
    r_bt = Ring(8, 512, 8192, "rbt")
    r_wr = Ring(1, 128, 12288, "rwr")
    wr_tile, wr_tok, _ = r_wr.load(d_wr.ap()[:, :])
    HSB = SCR_BF[:, 0:4 * TW]

    def ffn_tt(tt):
        for dch in range(CH):
            mm(PS[0][0:8, :], wr_tile[:, dch * 8:(dch + 1) * 8],
               XB[:, dch * TW:(dch + 1) * TW],
               start=(dch == 0), stop=(dch == CH - 1),
               wait=([wr_tok] + ps_wait(0)) if dch == 0 else (),
               inc=([("tv", 1)] if dch == CH - 1 else ()))
        p.w("vector", "tensor_mul",
            (LGT[0:8, :], PS[0][0:8, :], RSTD[0:8, tt * TW:(tt + 1) * TW]),
            wait=[p.tok("tv")])
        p.w("vector", "tensor_scalar", (LGT[0:8, :], LGT[0:8, :], BRS[0:8, 0:1], None, OP.add),
            inc=[("vv", 1)])
        psg[0] = p.tok("vv")
        lg_t = p.tok("vv")
        for j in range(4):
            p.w("tensor", "transpose", (PS[1][0:128, 0:8], LGT[0:8, j * 128:(j + 1) * 128],
                                        IDF[0:8, 0:8]),
                wait=[lg_t] + ps_wait(1), inc=[("tv", 1)])
            p.w("vector", "tensor_copy", (GTM[:, :], PS[1][0:128, 0:8]), wait=[p.tok("tv")],
                inc=[("vv", 1)])
            psg[1] = p.tok("vv")
            p.w("vector", "reduce_max", (CS1[:, :], GTM[:, :], AX))
            p.w("vector", "tensor_scalar", (CS2[:, :], CS1[:, :], -1.0, None, OP.mult),
                inc=[("vv", 1)])
            p.w("scalar", "activation", (GSC[:, :], GTM[:, :], AF.Exp),
                {"bias": CS2[:, 0:1], "scale": 1.0, "accum_out": CS1[:, 0:1]},
                wait=[p.tok("vv")], inc=[("sv", 1)])
            p.w("vector", "reciprocal", (CS2[:, :], CS1[:, :]), wait=[p.tok("sv")])
            p.w("vector", "reduce_max", (CS1[:, :], GTM[:, :], AX))
            p.w("vector", "tensor_scalar", (GSC[:, :], GTM[:, :], CS1[:, 0:1], None,
                                            OP.is_equal))
            p.w("vector", "tensor_scalar", (GSC[:, :], GSC[:, :], CS2[:, 0:1], 1.0 / 64.0,
                                            OP.mult, OP.mult), inc=[("vv", 1)])
            g_t = p.tok("vv")
            p.w("tensor", "transpose", (PS[1][0:8, 0:128], GSC[:, :], IDF[:, :]),
                wait=[g_t] + ps_wait(1), inc=[("tv", 1)])
            p.w("vector", "tensor_copy", (GFM[0:8, j * 128:(j + 1) * 128], PS[1][0:8, 0:128]),
                wait=[p.tok("tv")], inc=[("vv", 1)])
            psg[1] = p.tok("vv")
        p.w("vector", "tensor_copy", (G8H[0:8, :], GFM[0:8, :]))
        p.w("vector", "scalar_tensor_tensor",
            (G8L[0:8, :], GFM[0:8, :], 1.0, G8H[0:8, :], OP.mult, OP.subtract),
            inc=[("vv", 1)])
        g8_t = p.tok("vv")
        at_tiles = [r_at.load(d_at.ap()[er, :, :]) for er in range(2)]
        for er in range(4):
            att, atk, aslot = at_tiles[er]
            for dch in range(CH):
                mm(PS[2][:, :], att[:, dch * 128:(dch + 1) * 128],
                   XB[:, dch * TW:(dch + 1) * TW],
                   start=(dch == 0), stop=(dch == CH - 1),
                   wait=([atk] + ps_wait(2)) if dch == 0 else (),
                   inc=([("tv", 1)] if dch == CH - 1 else ()))
            h_t = p.tok("tv")
            r_at.done(aslot, h_t)
            if er + 2 < 4:
                at_tiles.append(r_at.load(d_at.ap()[er + 2, :, :]))
            else:
                at_tiles.append(None)
            mm(PS[3][:, :], G8A[:, er * 128:(er + 1) * 128], G8H[0:8, :],
               start=True, stop=False, wait=[g8_t] + ps_wait(3))
            mm(PS[3][:, :], G8A[:, er * 128:(er + 1) * 128], G8L[0:8, :],
               start=False, stop=True, inc=[("tv", 1)])
            grep_t = p.tok("tv")
            RGF = VSTG[:, :].bitcast(F32)
            p.w("vector", "tensor_mul",
                (RGF[:, 0:512], PS[3][:, :], RSTD[:, tt * TW:(tt + 1) * TW]),
                wait=[grep_t], inc=[("vv", 1)])
            psg[3] = p.tok("vv")
            p.w("vector", "tensor_mul",
                (HSB[:, er * TW:(er + 1) * TW], PS[2][:, :], RGF[:, 0:512]),
                wait=[h_t], inc=[("vv", 1)])
            psg[2] = p.tok("vv")
        h_all = p.tok("vv")
        for hh in range(2):
            xt = p.dma("sync", xin_ap(), d_y.ap()[2 * tt + hh, :, :],
                       wait=([st["xstg"]] if st["xstg"] else []), sem="sx")
            p.w("vector", wait=[xt])
            for oc in range(CH):
                ob = (4, 2)[oc % 2]
                btt, btk, bslot = r_bt.load(d_bt.ap()[oc, :, :])
                for er in range(4):
                    mm(PS[ob][:, 0:HW], btt[:, er * 128:(er + 1) * 128],
                       HSB[:, er * TW + hh * HW: er * TW + hh * HW + HW],
                       start=(er == 0), stop=(er == 3),
                       wait=([btk, h_all] + ps_wait(ob)) if er == 0 else (),
                       inc=([("tv", 1)] if er == 3 else ()))
                y_t = p.tok("tv")
                r_bt.done(bslot, y_t)
                xs = XSTG[:, oc * HW:(oc + 1) * HW]
                p.w("vector", "scalar_tensor_tensor",
                    (xs, PS[ob][:, 0:HW], 0.0, xs, OP.bypass, OP.add),
                    wait=[y_t], inc=[("vv", 1)])
                psg[ob] = p.tok("vv")
            st["xstg"] = p.dma("sync", d_y.ap()[2 * tt + hh, :, :], xin_ap(),
                               wait=[p.tok("vv")], sem="sy")
        p.bar()

    r_wo = Ring(2, 2048, 0, "rwo")
    TOPT = SCR_BF[:, 0:CH * TW]
    BOTT = SCR_BF[:, CH * TW:2 * CH * TW]
    QPH_ = XB[0:2, 0:32 * TW]
    QPL_ = XB[0:2, 32 * TW:64 * TW]

    def build_qpair(tt, half, wait=()):
        tks = []
        h0 = 64 * half
        tks.append(p.dma("sync", QPH_[0:1, :].rearrange("q (r t) -> q r t", r=32),
                         QHI()[h0:h0 + 64:2, tt * TW:(tt + 1) * TW], wait=wait, sem="qp"))
        tks.append(p.dma("sync", QPH_[1:2, :].rearrange("q (r t) -> q r t", r=32),
                         QHI()[h0 + 1:h0 + 64:2, tt * TW:(tt + 1) * TW], wait=wait, sem="qp"))
        return tks

    def build_vstack(pr, wait=()):
        sl = pr % 2
        c0 = sl * 224 + 96
        g0 = pr * 16
        p.w("vector", "tensor_copy", (VST[0:32, c0:c0 + 16], VGALL[0:32, g0:g0 + 16]),
            wait=wait)
        p.w("vector", "tensor_copy", (VST[64:96, c0 + 16:c0 + 32], VGALL[64:96, g0:g0 + 16]))
        p.w("vector", "tensor_copy", (VSB[32:64, c0:c0 + 16], VGALL[32:64, g0:g0 + 16]))
        p.w("vector", "tensor_copy", (VSB[96:128, c0 + 16:c0 + 32], VGALL[96:128, g0:g0 + 16]),
            inc=[("vv", 1)])
        return p.tok("vv")

    def attn_a_sparse(tt):
        vs_tok = {}
        pair_end = {}
        for half in range(2):
            qp = build_qpair(tt, half,
                             wait=([pair_end[31]] if half == 1 else ()))
            for tk in qp:
                p.w("tensor", wait=[tk])
            if half == 0:
                vs_tok[0] = build_vstack(0)
                vs_tok[1] = build_vstack(1)
            for k in range(32):
                pr = 32 * half + k
                ch = pr // 4
                sl = pr % 2
                mm(PS[sl][:, :], EPH[:, :], QPH_[0:2, k * TW:(k + 1) * TW],
                   start=True, stop=True, wait=ps_wait(sl), inc=[("tv", 1)])
                arg_t = p.tok("tv")
                pb = PTB[:, (pr % 4) * TW:(pr % 4 + 1) * TW]
                ew = [arg_t] + ([pair_end[pr - 4]] if pr >= 4 else [])
                p.w("scalar", "activation", (pb, PS[sl][:, :], AF.Exp),
                    {"bias": ZRO[:, 0:1], "scale": SCL[:, pr:pr + 1]},
                    wait=ew, inc=[("sv", 1)])
                p.w("vector", "tensor_scalar", (pb, pb, 1e30, None, OP.min),
                    wait=[p.tok("sv")], inc=[("vv", 1)])
                psg[sl] = p.tok("vv")
                exp_t = p.tok("vv")
                mm(PS[2][:, :], SLT[:, 128 - 2 * pr:256 - 2 * pr], pb,
                   start=(pr == 0), stop=(pr == 63),
                   wait=[exp_t] + (ps_wait(2) if pr == 0 else []))
                mm(PS[3][:, :], SLB[:, 128 - 2 * pr:256 - 2 * pr], pb,
                   start=(pr == 0), stop=(pr == 63),
                   wait=(ps_wait(3) if pr == 0 else []),
                   inc=([("tv", 1)] if pr == 63 else ()))
                it, ib = 4 + (ch % 2), 6 + (ch % 2)
                w0 = (pr % 2) * 224 + 96 - (pr % 4) * 32
                mm(PS[it][:, :], VST[:, w0:w0 + 128], pb,
                   start=(pr % 4 == 0), stop=(pr % 4 == 3),
                   wait=[vs_tok[pr]] + (ps_wait(it) if pr % 4 == 0 else []))
                mm(PS[ib][:, :], VSB[:, w0:w0 + 128], pb,
                   start=(pr % 4 == 0), stop=(pr % 4 == 3),
                   wait=(ps_wait(ib) if pr % 4 == 0 else []),
                   inc=[("tv", 1)])
                pair_end[pr] = p.tok("tv")
                if pr % 4 == 3:
                    pv_t = pair_end[pr]
                    p.w("vector", "tensor_copy",
                        (TOPT[:, ch * TW:(ch + 1) * TW], PS[it][:, :]), wait=[pv_t])
                    p.w("vector", "tensor_copy",
                        (BOTT[:, ch * TW:(ch + 1) * TW], PS[ib][:, :]), inc=[("vv", 1)])
                    psg[it] = psg[ib] = p.tok("vv")
                if pr + 2 <= 63:
                    vs_tok[pr + 2] = build_vstack(pr + 2, wait=[pair_end[pr]])
        s_done = p.tok("tv")
        WF = VSTG[:, :].bitcast(F32)
        p.w("vector", "reciprocal", (WF[:, 0:512], PS[2][:, :]), wait=[s_done],
            inc=[("vv", 1)])
        psg[2] = p.tok("vv")
        p.w("vector", "tensor_mul",
            (RSB[:, 0:512], WF[:, 0:512], MQ()[:, tt * TW:(tt + 1) * TW]), inc=[("vv", 1)])
        wtop_t = p.tok("vv")
        p.w("vector", "reciprocal", (WF[:, 0:512], PS[3][:, :]), wait=[wtop_t],
            inc=[("vv", 1)])
        psg[3] = p.tok("vv")
        p.w("vector", "tensor_mul",
            (PTB[:, 0:512], WF[:, 0:512], MQC()[:, tt * TW:(tt + 1) * TW]), inc=[("vv", 1)])
        w_t = p.tok("vv")
        wtr_free = None
        for hh in range(2):
            h0 = hh * HW
            dwait = [w_t] + ([wtr_free] if wtr_free else []) + \
                ([p.tok("sy")] if "sy" in p.cnt else [])
            for c in range(CH):
                p.dma("sync", WTR[0:8, c * HW:(c + 1) * HW],
                      RSB[8 * c:8 * c + 8, h0:h0 + HW], wait=dwait, sem="wt")
                t2 = p.dma("sync", WTR[0:8, 4096 + c * HW: 4096 + (c + 1) * HW],
                           PTB[8 * c:8 * c + 8, h0:h0 + HW], wait=dwait, sem="wt")
            t1 = t2
            for ch in range(CH):
                i = ch % 2
                mm(PS[0 + i][:, 0:HW], REP8[:, :], WTR[0:8, ch * HW:(ch + 1) * HW],
                   start=True, stop=True, wait=[t1, t2] + ps_wait(0 + i))
                mm(PS[2 + i][:, 0:HW], REP8[:, :],
                   WTR[0:8, 4096 + ch * HW: 4096 + (ch + 1) * HW],
                   start=True, stop=True, wait=ps_wait(2 + i), inc=[("tv", 1)])
                rep_t = p.tok("tv")
                tch = TOPT[:, ch * TW + h0: ch * TW + h0 + HW]
                bch = BOTT[:, ch * TW + h0: ch * TW + h0 + HW]
                p.w("vector", "tensor_mul", (tch, tch, PS[0 + i][:, 0:HW]), wait=[rep_t])
                p.w("vector", "tensor_mul", (bch, bch, PS[2 + i][:, 0:HW]))
                p.w("vector", "tensor_add", (tch, tch, bch), inc=[("vv", 1)])
                psg[0 + i] = psg[2 + i] = p.tok("vv")
            wtr_free = p.tok("tv")
        return p.tok("vv")

    for tt in range(NTT):
        attn_a_sparse(tt)
        p.bar()
        if stop_after == "probe5" and tt == 0:
            p.dma("gpsimd", d_dbg.ap()[0, :, :], TOPT[:, 0:2048])
            p.dma("gpsimd", d_dbg.ap()[1, :, :], TOPT[:, 2048:4096])
            p.dma("gpsimd", d_dbg.ap()[2, :, 0:64], VALS[:, :])
            p.dma("gpsimd", d_dbg.ap()[2, :, 64:128], TIF[:, :])
            p.dma("gpsimd", d_dbg.ap()[2, :, 128:192], IXVU[:, :])
            p.dma("gpsimd", d_dbg.ap()[2, :, 192:256], IXKU[:, :])
            p.dma("gpsimd", d_dbg.ap()[3, :, 0:2048], QHI())
            p.dma("gpsimd", d_dbg.ap()[4, :, 0:2048], QLO())
            p.qfence("sync")
            p.qfence("gpsimd")
            p.bar()
            with nc.Block() as block:
                p.replay(block)
            stack.close()
            return nc
        for hh in range(2):
            xt = p.dma("sync", xin_ap(), d_x.ap()[2 * tt + hh, :, :],
                       wait=([st["xstg"]] if st["xstg"] else []), sem="sx")
            p.w("vector", wait=[xt])
            for oc in range(CH):
                ob = (5, 2, 3)[oc % 3]
                wtile, wtok, slot = r_wo.load(d_wo.ap()[oc, :, :])
                for dch in range(CH):
                    mm(PS[ob][:, 0:HW], wtile[:, dch * 128:(dch + 1) * 128],
                       TOPT[:, dch * TW + hh * HW: dch * TW + hh * HW + HW],
                       start=(dch == 0), stop=(dch == CH - 1),
                       wait=([wtok] + ps_wait(ob)) if dch == 0 else (),
                       inc=([("tv", 1)] if dch == CH - 1 else ()))
                wo_t = p.tok("tv")
                r_wo.done(slot, wo_t)
                xs = XSTG[:, oc * HW:(oc + 1) * HW]
                p.w("vector", "scalar_tensor_tensor",
                    (xs, PS[ob][:, 0:HW], 0.0, xs, OP.bypass, OP.add),
                    wait=[wo_t], inc=[("vv", 1)])
                psg[ob] = p.tok("vv")
            p.bar()
            if stop_after == "sub1":
                st["xstg"] = p.dma("sync", d_y.ap()[2 * tt + hh, :, :], xin_ap(),
                                   wait=[p.tok("vv")], sem="sy")
                continue
            phase_A(tt, hh, full_mode=False)
            st["xstg"] = p.dma("sync", d_y.ap()[2 * tt + hh, :, :], xin_ap(),
                               wait=[p.tok("vv")], sem="sy")
            p.bar()
        if stop_after == "sub1":
            continue
        rstd_tm(tt)
        ffn_tt(tt)
        p.bar()

    if stop_after in ("sub1", "sub2"):
        p.qfence("sync")
        p.qfence("gpsimd")
        p.bar()
        with nc.Block() as block:
            p.replay(block)
        stack.close()
        return nc

    for tt in range(NTT):
        for hh in range(2):
            xt = p.dma("sync", xin_ap(), d_y.ap()[2 * tt + hh, :, :],
                       wait=([st["xstg"]] if st["xstg"] else []), sem="sx")
            phase_A(tt, hh, full_mode=True, wait_x=[xt])
            p.bar()
        rstd_tm(tt)
    p.bar()

    if stop_after == "sub3a":
        p.qfence("sync")
        p.qfence("gpsimd")
        p.bar()
        with nc.Block() as block:
            p.replay(block)
        stack.close()
        return nc

    QH2 = SCR_BF[:, 0:4096]
    KH2 = SCR_BF[:, 4096:8192]
    VTM = SCR_BF[:, 8192:12288]
    QSQ = SCR_BF[:, 12288:14336]
    SEXP = SCR_F32[:, 6144:6656]
    SEH = SCR_BF[:, 13312:13824]
    SEL = SCR_BF[:, 13824:14336]
    r_qkv = Ring(4, 2048, 0, "rqv")
    r_vv = Ring(16, 256, 8192, "rvv")

    def xbf(dch, c0, cw):
        return XB[:, dch * T + c0: dch * T + c0 + cw]

    hp_prev = None
    for hp in range(8):
        if hp_prev is not None:
            p.w("vector", wait=[hp_prev])
            p.w("gpsimd", wait=[hp_prev])
        echs = [2 * hp, 2 * hp + 1, 16 + 2 * hp, 16 + 2 * hp + 1]
        tiles = [r_qkv.load(d_wqkv.ap()[e, :, :]) for e in echs]
        vtiles = [r_vv.load(d_wqkvv.ap()[dch, :, 256 * hp:256 * (hp + 1)]) for dch in range(CH)]
        p.dma("gpsimd", QKVBP[0:1, 0:256], d_qkvbrow.ap()[0:1, 256 * hp:256 * (hp + 1)], sem="qb")
        p.dma("gpsimd", QKVBP[0:1, 256:512],
              d_qkvbrow.ap()[0:1, 2048 + 256 * hp:2048 + 256 * (hp + 1)], sem="qb")
        bq_t = p.dma("gpsimd", QKVBP[0:1, 512:768],
                     d_qkvbrow.ap()[0:1, 4096 + 256 * hp:4096 + 256 * (hp + 1)], sem="qb")
        for i, e in enumerate(echs):
            wt, wtk, _ = tiles[i]
            for tt in range(NTT):
                for dch in range(CH):
                    mm(PS[0][:, :], wt[:, dch * 128:(dch + 1) * 128],
                       xbf(dch, tt * TW, TW), start=(dch == 0), stop=False,
                       wait=([wtk] + ps_wait(0)) if dch == 0 else ())
                boff = (i % 2) * 128 + (0 if i < 2 else 256)
                mm(PS[0][:, :], QKVBP[0:1, boff:boff + 128], SQVB(tt),
                   start=False, stop=True, wait=[bq_t], inc=[("tv", 1)])
                qk_t = p.tok("tv")
                dst = QH2 if i < 2 else KH2
                dsl = dst[:, (i % 2) * 2048 + tt * TW: (i % 2) * 2048 + (tt + 1) * TW]
                if i < 2:
                    p.w("vector", "tensor_mul",
                        (dsl, PS[0][:, :], RSTD[:, tt * TW:(tt + 1) * TW]),
                        wait=[qk_t], inc=[("vv", 1)])
                else:
                    p.w("vector", "scalar_tensor_tensor",
                        (dsl, PS[0][:, :], SMS, RSTD[:, tt * TW:(tt + 1) * TW],
                         OP.mult, OP.mult), wait=[qk_t], inc=[("vv", 1)])
                psg[0] = p.tok("vv")
        for tc in range(CH):
            tt, j = tc // 4, tc % 4
            for dch in range(CH):
                vt, vtk, _ = vtiles[dch]
                mm(PS[1][:, 0:256], xbf(dch, tc * 128, 128), vt,
                   start=(dch == 0), stop=False,
                   wait=(([vtk] if tc == 0 else []) + (ps_wait(1) if dch == 0 else [])))
            mm(PS[1][:, 0:256], SQVB(tt)[0:1, j * 128:(j + 1) * 128],
               QKVBP[0:1, 512:768],
               start=False, stop=True, wait=[bq_t], inc=[("tv", 1)])
            v_t = p.tok("tv")
            p.w("vector", "tensor_scalar",
                (VTM[:, tc * 256:(tc + 1) * 256], PS[1][:, 0:256], RSTM[:, tc:tc + 1],
                 None, OP.mult), wait=[v_t], inc=[("vv", 1)])
            psg[1] = p.tok("vv")
        p.bar()
        for a in range(2):
            h = 2 * hp + a
            negcb_toks = []
            for tt in range(NTT):
                sb2 = (2, 7)[tt % 2]
                q0 = a * 2048 + tt * TW
                mm(PS[sb2][:, :], QH2[:, q0:q0 + 128],
                   KH2[:, a * 2048:a * 2048 + TW],
                   start=True, stop=True, wait=ps_wait(sb2), inc=[("tv", 1)])
                p.w("vector", "reduce_max", (TV[:, tt:tt + 1], PS[sb2][:, :], AX),
                    wait=[p.tok("tv")], inc=[("vv", 1)])
                psg[sb2] = p.tok("vv")
                p.w("tensor", "transpose", (PS[sb2][0:1, 0:128], TV[:, tt:tt + 1],
                                            IDF[:, :]),
                    wait=[p.tok("vv")] + ps_wait(sb2), inc=[("tv", 1)])
                p.w("vector", "reduce_max",
                    (CELLS[0:1, 4 + tt:5 + tt], PS[sb2][0:1, 0:128], AX),
                    wait=[p.tok("tv")], inc=[("vv", 1)])
                psg[sb2] = p.tok("vv")
                p.w("vector", "tensor_scalar",
                    (CELLS[0:1, 8 + tt:9 + tt].bitcast(BF16)[0:1, 0:1],
                     CELLS[0:1, 4 + tt:5 + tt], -1.0,
                     -30.0, OP.mult, OP.add), inc=[("vv", 1)])
                mm(PS[sb2][:, 0:1], ONE1[:, :],
                   CELLS[0:1, 8 + tt:9 + tt].bitcast(BF16)[0:1, 0:1],
                   start=True, stop=True, wait=[p.tok("vv")] + ps_wait(sb2),
                   inc=[("tv", 1)])
                p.w("vector", "tensor_copy", (NEGCB[:, tt:tt + 1], PS[sb2][:, 0:1]),
                    wait=[p.tok("tv")], inc=[("vv", 1)])
                psg[sb2] = p.tok("vv")
                negcb_toks.append(p.tok("vv"))
            for tt in range(NTT):
                negcb_t = negcb_toks[tt]
                accs = {}
                for kch in range(CH):
                    sb = (2, 3, 4)[kch % 3]
                    sl = kch % 4
                    mm(PS[sb][:, :],
                       KH2[:, a * 2048 + kch * 128: a * 2048 + (kch + 1) * 128],
                       QH2[:, a * 2048 + tt * TW: a * 2048 + (tt + 1) * TW],
                       start=True, stop=True,
                       wait=ps_wait(sb) + ([negcb_t] if kch == 0 else []),
                       inc=[("tv", 1)])
                    arg_t = p.tok("tv")
                    pb = PTB[:, sl * TW:(sl + 1) * TW]
                    p.w("scalar", "activation", (pb, PS[sb][:, :], AF.Exp),
                        {"bias": NEGCB[:, tt:tt + 1], "scale": 1.0},
                        wait=[arg_t] + ([accs[kch - 4]] if kch >= 4 else []),
                        inc=[("sv", 1)])
                    psg[sb] = p.tok("sv")
                    exp_t = p.tok("sv")
                    if kch == 0:
                        p.w("vector", "tensor_copy", (SEXP, pb), wait=[exp_t],
                            inc=[("vv", 1)])
                    else:
                        p.w("vector", "scalar_tensor_tensor",
                            (SEXP, pb, 0.0, SEXP, OP.bypass, OP.add),
                            wait=[exp_t], inc=[("vv", 1)])
                    accs[kch] = p.tok("vv")
                    mm(PS[6][:, :], VTM[:, kch * 256 + a * 128: kch * 256 + (a + 1) * 128],
                       pb, start=(kch == 0), stop=(kch == CH - 1),
                       wait=[exp_t] + (ps_wait(6) if kch == 0 else []),
                       inc=([("tv", 1)] if kch == CH - 1 else ()))
                ctx_t = p.tok("tv")
                p.w("vector", "tensor_copy", (SEH, SEXP),
                    wait=([st["sum_t"]] if st.get("sum_t") else ()))
                p.w("vector", "scalar_tensor_tensor",
                    (SEL, SEXP, 1.0, SEH, OP.mult, OP.subtract), inc=[("vv", 1)])
                mm(PS[5][0:1, :], ONEC[:, :], SEH, start=True, stop=False,
                   wait=[p.tok("vv")] + ps_wait(5))
                mm(PS[5][0:1, :], ONEC[:, :], SEL, start=False, stop=False)
                mm(PS[5][0:1, :], ONE1[0:1, 0:1], TINY[0:1, :],
                   start=False, stop=True, inc=[("tv", 1)])
                sum_t = p.tok("tv")
                st["sum_t"] = sum_t
                p.w("vector", "reciprocal", (SCRW, PS[5][0:1, 0:256]),
                    wait=[ctx_t, sum_t])
                p.w("vector", "reciprocal", (SCRW2, PS[5][0:1, 256:512]), inc=[("vv", 1)])
                psg[5] = p.tok("vv")
                p.w("vector", "tensor_copy", (RHB[0:1, 0:256], SCRW))
                p.w("vector", "tensor_copy", (RHB[0:1, 256:512], SCRW2))
                p.w("vector", "scalar_tensor_tensor",
                    (RLB[0:1, 0:256], SCRW, 1.0, RHB[0:1, 0:256], OP.mult, OP.subtract))
                p.w("vector", "scalar_tensor_tensor",
                    (RLB[0:1, 256:512], SCRW2, 1.0, RHB[0:1, 256:512], OP.mult, OP.subtract),
                    inc=[("vv", 1)])
                mm(PS[7][:, :], ONE1[:, :], RHB, start=True, stop=False,
                   wait=[p.tok("vv")] + ps_wait(7))
                mm(PS[7][:, :], ONE1[:, :], RLB, start=False, stop=True, inc=[("tv", 1)])
                p.w("vector", "tensor_copy", (RSB[:, 0:TW], PS[7][:, :]), wait=[p.tok("tv")],
                    inc=[("vv", 1)])
                psg[7] = p.tok("vv")
                vs = VSTG[:, (tt % 2) * TW:(tt % 2 + 1) * TW]
                vw = [st["vstg"][tt % 2]] if st["vstg"][tt % 2] else []
                p.w("vector", "tensor_mul", (vs, PS[6][:, :], RSB[:, 0:TW]), wait=vw,
                    inc=[("vv", 1)])
                psg[6] = p.tok("vv")
                st["vstg"][tt % 2] = p.dma(
                    "sync", d_ctx.ap()[h, :, tt * TW:(tt + 1) * TW], vs,
                    wait=[p.tok("vv")], sem="vs%d" % (tt % 2))
        qk_last = p.tok("tv")
        hp_prev = qk_last
        for i in range(4):
            r_qkv.done(tiles[i][2], qk_last)
        for dch in range(CH):
            r_vv.done(vtiles[dch][2], qk_last)
        if stop_after == "probe4" and hp == 0:
            p.qfence("sync")
            p.bar()
            for h in range(2):
                p.dma("gpsimd", d_dbg.ap()[h, :, :], d_ctx.ap()[h, :, :])
            p.dma("gpsimd", d_dbg.ap()[2, :, :], QH2[:, 0:2048])
            p.dma("gpsimd", d_dbg.ap()[3, :, :], KH2[:, 0:2048])
            p.dma("gpsimd", d_dbg.ap()[4, :, 0:512], RSB[:, 0:512])
            p.dma("gpsimd", d_dbg.ap()[4, :, 512:513], NEGCB[:, 0:1],
                  allow_slow_non_contiguous=True)
            p.dma("gpsimd", d_dbg.ap()[4, 0:1, 520:536],
                  CELLS[0:1, 0:16], allow_slow_non_contiguous=True)
            p.dma("gpsimd", d_dbg.ap()[4, 0:1, 536:1048], RHB[0:1, 0:512])
            p.dma("gpsimd", d_dbg.ap()[4, :, 1048:2048], VTM[:, 0:1000])
            p.qfence("sync")
            p.qfence("gpsimd")
            p.bar()
            with nc.Block() as block:
                p.replay(block)
            stack.close()
            return nc
        if stop_after and stop_after.startswith("sub3b") and \
                hp == int(stop_after[5:] or 0):
            break
    p.bar()

    if stop_after and stop_after.startswith("sub3b"):
        p.qfence("sync")
        p.qfence("gpsimd")
        p.bar()
        with nc.Block() as block:
            p.replay(block)
        stack.close()
        return nc

    r_wout = Ring(2, 2048, 0, "rwt")
    CTXT = SCR_BF[:, 0:CH * TW]
    ctx_done = [p.tok("vs0"), p.tok("vs1")]
    for tt in range(NTT):
        ctk = []
        for g in range(4):
            ctk.append(p.dma("sync",
                             CTXT.rearrange("q (c t) -> q c t", c=CH)[:, 4 * g:4 * g + 4, :],
                             d_ctx.ap()[4 * g:4 * g + 4, :, tt * TW:(tt + 1) * TW]
                             .rearrange("c q t -> q c t"),
                             wait=(ctx_done if (tt == 0 and g == 0) else ()),
                             sem="ct"))
        for tk in ctk:
            p.w("tensor", wait=[tk])
        if stop_after == "sub3c":
            p.qfence("sync")
            p.qfence("gpsimd")
            p.bar()
            with nc.Block() as block:
                p.replay(block)
            stack.close()
            return nc
        for hh in range(2):
            xt = p.dma("sync", xin_ap(), d_y.ap()[2 * tt + hh, :, :],
                       wait=([st["xstg"]] if st["xstg"] else []), sem="sx")
            p.w("vector", wait=[xt])
            for oc in range(CH):
                if stop_after == "sub3d" and oc == 4:
                    break
                ob = (5, 2, 3)[oc % 3]
                wtile, wtok, slot = r_wout.load(d_wout.ap()[oc, :, :])
                for dch in range(CH):
                    mm(PS[ob][:, 0:HW], wtile[:, dch * 128:(dch + 1) * 128],
                       CTXT[:, dch * TW + hh * HW: dch * TW + hh * HW + HW],
                       start=(dch == 0), stop=(dch == CH - 1),
                       wait=([wtok] + ps_wait(ob)) if dch == 0 else (),
                       inc=([("tv", 1)] if dch == CH - 1 else ()))
                wo_t = p.tok("tv")
                r_wout.done(slot, wo_t)
                xs = XSTG[:, oc * HW:(oc + 1) * HW]
                p.w("vector", "scalar_tensor_tensor",
                    (xs, PS[ob][:, 0:HW], OUTB[:, oc:oc + 1], xs, OP.add, OP.add),
                    wait=[wo_t], inc=[("vv", 1)])
                psg[ob] = p.tok("vv")
            if stop_after == "sub3d":
                p.qfence("sync")
                p.qfence("gpsimd")
                p.bar()
                with nc.Block() as block:
                    p.replay(block)
                stack.close()
                return nc
            p.bar()
            if stop_after == "sub3":
                st["xstg"] = p.dma("sync", d_y.ap()[2 * tt + hh, :, :], xin_ap(),
                                   wait=[p.tok("vv")], sem="sy")
                continue
            phase_A(tt, hh, full_mode=False)
            st["xstg"] = p.dma("sync", d_y.ap()[2 * tt + hh, :, :], xin_ap(),
                               wait=[p.tok("vv")], sem="sy")
            p.bar()
        if stop_after == "sub3":
            continue
        rstd_tm(tt)
        ffn_tt(tt)
        p.bar()

    p.qfence("sync")
    p.qfence("gpsimd")
    p.bar()
    with nc.Block() as block:
        p.replay(block)
    stack.close()
    return nc


def _layout_inputs(inputs):
    f = np.float32
    x = np.asarray(inputs["L0_attn_a_x"], dtype=f)
    Wq = np.asarray(inputs["L0_attn_a_Wq"], dtype=f)
    Wk = np.asarray(inputs["L0_attn_a_Wk"], dtype=f)
    Wv = np.asarray(inputs["L0_attn_a_Wv"], dtype=f)
    Wo = np.asarray(inputs["L0_attn_a_Wo"], dtype=f)
    Wqkv = np.asarray(inputs["L0_attn_b_Wqkv_w"], dtype=f)
    qkvb = np.asarray(inputs["L0_attn_b_Wqkv_b"], dtype=f)
    outw = np.asarray(inputs["L0_attn_b_out_w"], dtype=f)
    outb = np.asarray(inputs["L0_attn_b_out_b"], dtype=f)
    Am = np.asarray(inputs["L0_ffn_A"], dtype=f)
    Bm = np.asarray(inputs["L0_ffn_B"], dtype=f)
    Wr = np.asarray(inputs["L0_ffn_Wr"], dtype=f)
    br = np.asarray(inputs["L0_ffn_br"], dtype=f)

    def lhs_blk(WT, nech):
        E = WT.shape[1]
        return np.ascontiguousarray(
            WT.reshape(CH, 128, nech, E // nech).transpose(2, 1, 0, 3)
            .reshape(nech, 128, CH * (E // nech)))

    wqkT = lhs_blk(np.ascontiguousarray(np.concatenate([Wq.T, Wk.T], axis=1)), 2)
    wvT = np.ascontiguousarray(Wv.T.reshape(CH, 128, 4, 512).transpose(2, 0, 1, 3))
    woT = lhs_blk(np.ascontiguousarray(Wo.T), CH)
    WqkvT = np.ascontiguousarray(Wqkv.T)
    wqkvT = lhs_blk(np.ascontiguousarray(WqkvT[:, 0:4096]), 32)
    wqkvv = np.ascontiguousarray(WqkvT[:, 4096:6144].reshape(CH, 128, 2048))
    woutT = lhs_blk(np.ascontiguousarray(outw.T), CH)
    AT = np.ascontiguousarray(Am.transpose(2, 0, 1).reshape(D, 512))
    atT = lhs_blk(AT, 4)
    BT = np.ascontiguousarray(Bm.transpose(0, 2, 1).reshape(512, D))
    btT = np.ascontiguousarray(
        BT.reshape(4, 128, CH, 128).transpose(2, 1, 0, 3).reshape(CH, 128, 512))
    wrT = np.ascontiguousarray(Wr.T.reshape(CH, 128, 8).transpose(1, 0, 2).reshape(128, CH * 8))
    qkvbrow = np.ascontiguousarray(qkvb.reshape(1, 6144))
    outb2 = np.ascontiguousarray(outb.reshape(16, 128).T)
    br2 = np.ascontiguousarray(br.reshape(8, 1))

    shared = {
        "wqk": wqkT, "wv": wvT, "wo": woT, "wqkv": wqkvT, "wqkvv": wqkvv,
        "wout": woutT, "at": atT, "bt": btT, "wr": wrT,
        "qkvbrow": qkvbrow, "outb2": outb2, "br": br2,
    }
    in_maps = []
    for c in range(8):
        xf = np.ascontiguousarray(x[c].T)
        xblk = np.ascontiguousarray(
            xf.reshape(CH, 128, NTT * 2, HW).transpose(2, 1, 0, 3)
            .reshape(NTT * 2, 128, CH * HW))
        m = dict(shared)
        m["x"] = xblk
        in_maps.append(m)
    return in_maps


_NC_CACHE = {}


def kernel(**inputs):
    in_maps = _layout_inputs(inputs)
    key = "full"
    if key not in _NC_CACHE:
        _NC_CACHE[key] = build_nc()
    nc = _NC_CACHE[key]
    res = run_bass_kernel_spmd(nc, in_maps, core_ids=list(range(8)))
    outs = []
    for c in range(8):
        yblk = res.results[c]["y"].reshape(8, 128, CH, HW)
        xf = yblk.transpose(2, 1, 0, 3).reshape(D, T)
        outs.append(np.ascontiguousarray(xf.T))
    return np.stack(outs, axis=0).astype(np.float32)


# version 16
# speedup vs baseline: 1.0019x; 1.0019x over previous
import numpy as np
from contextlib import ExitStack

import concourse.bass as bass
import concourse.mybir as mybir
from concourse.bass_utils import run_bass_kernel_spmd
from concourse.bass_core_rust import IndirectOffsetOnAxis

F32 = mybir.dt.float32
BF16 = mybir.dt.bfloat16
U32 = mybir.dt.uint32
U8 = mybir.dt.uint8
OP = mybir.AluOpType
AF = mybir.ActivationFunctionType
AX = mybir.AxisListType.X

D = 2048
T = 2048
CH = 16
NTT = 4
TW = 512
HW = 256
HB = 16
SMS = float(1.0 / np.sqrt(128.0))
EPS = 1e-5
ENGINES = ("sync", "gpsimd", "scalar", "vector", "tensor")


class Prog:
    def __init__(self, nc):
        self.nc = nc
        self.cmd = {e: [] for e in ENGINES}
        self.cnt = {}
        self.sem = {}
        self.nbar = 0
        self.qsems = {}

    def mksem(self, stack, *names):
        self.stack = stack
        for nm in names:
            self.sem[nm] = stack.enter_context(self.nc.semaphore(nm))
            self.cnt[nm] = 0

    def sem_for(self, nm):
        if nm not in self.sem:
            self.sem[nm] = self.stack.enter_context(self.nc.semaphore(nm))
            self.cnt[nm] = 0
        return nm

    def w(self, eng, op=None, args=(), kwargs=None, wait=(), inc=()):
        for nm, n in inc:
            self.cnt[nm] += n
        self.cmd[eng].append((op, tuple(args), dict(kwargs or {}), tuple(wait), tuple(inc)))
        if eng == "vector" and op not in (None, "drain"):
            out_ap = (kwargs or {}).get("out", args[0] if args else None)
            try:
                small = out_ap is not None and out_ap.free_nbytes <= 512
            except Exception:
                small = True
            if small:
                self.cmd[eng].append(("drain", (), {}, (), ()))

    def tok(self, nm):
        return (nm, self.cnt[nm])

    def bar(self):
        self.nbar += 1
        k = self.nbar
        for e in ENGINES:
            self.cnt["b_" + e] += 1
            if e in ("sync",):
                self.cmd[e].append((None, (), {}, (), (("b_" + e, 1),)))
            else:
                self.cmd[e].append(("drain", (), {}, (), (("b_" + e, 1),)))
        for e in ENGINES:
            for o in ENGINES:
                if o != e:
                    self.w(e, wait=[("b_" + o, k)])

    def dma(self, q, out_ap, in_ap, wait=(), sem=None, **kw):
        semn = self.sem_for(sem) if sem else ("gd" if q == "gpsimd" else "qd")
        kwargs = {"out": out_ap, "in_": in_ap}
        kwargs.update(kw)
        self.qsems.setdefault(q, set()).add(semn)
        self.w(q, "dma_start", kwargs=kwargs, wait=wait, inc=[(semn, 16)])
        return self.tok(semn)

    def qfence(self, q):
        base = "gd" if q == "gpsimd" else "qd"
        sems = set(self.qsems.get(q, ())) | {base}
        self.w(q, wait=[self.tok(s) for s in sorted(sems)])

    def replay(self, block):
        sem = self.sem

        def mk(e):
            def run(eng):
                for op, args, kwargs, wait, inc in self.cmd[e]:
                    for nm, v in wait:
                        eng.wait_ge(sem[nm], v)
                    if op is not None:
                        inst = getattr(eng, op)(*args, **kwargs)
                        for nm, n in inc:
                            inst.then_inc(sem[nm], n)
                    else:
                        for nm, n in inc:
                            eng.sem_inc(sem[nm], n)
            return run

        block.sync(mk("sync"))
        block.gpsimd(mk("gpsimd"))
        block.scalar(mk("scalar"))
        block.vector(mk("vector"))
        block.tensor(mk("tensor"))


def build_nc(stop_after=None):
    nc = bass.Bass("TRN2", target_bir_lowering=False)

    d_x = nc.dram_tensor("x", [NTT * 2, 128, CH * HW], F32, kind="ExternalInput")
    d_y = nc.dram_tensor("y", [NTT * 2, 128, CH * HW], F32, kind="ExternalOutput")
    d_wqk = nc.dram_tensor("wqk", [2, 128, CH * 128], F32, kind="ExternalInput")
    d_wv = nc.dram_tensor("wv", [4, CH, 128, 512], F32, kind="ExternalInput")
    d_wo = nc.dram_tensor("wo", [CH, 128, CH * 128], F32, kind="ExternalInput")
    d_wqkv = nc.dram_tensor("wqkv", [32, 128, CH * 128], F32, kind="ExternalInput")
    d_wqkvv = nc.dram_tensor("wqkvv", [CH, 128, 2048], F32, kind="ExternalInput")
    d_wout = nc.dram_tensor("wout", [CH, 128, CH * 128], F32, kind="ExternalInput")
    d_at = nc.dram_tensor("at", [4, 128, CH * 128], F32, kind="ExternalInput")
    d_bt = nc.dram_tensor("bt", [CH, 128, 4 * 128], F32, kind="ExternalInput")
    d_wr = nc.dram_tensor("wr", [128, CH * 8], F32, kind="ExternalInput")
    d_qkvbrow = nc.dram_tensor("qkvbrow", [1, 6144], F32, kind="ExternalInput")
    d_outb2 = nc.dram_tensor("outb2", [128, 16], F32, kind="ExternalInput")
    d_br = nc.dram_tensor("br", [8, 1], F32, kind="ExternalInput")
    d_dbg = nc.dram_tensor("dbg", [5, 128, 2048], F32, kind="ExternalOutput") \
        if stop_after in ("probe1", "probe0", "probe2", "probe3", "probe4",
                          "probe5") else None
    d_vtm = nc.dram_tensor("vtm_s", [T, D], BF16)
    d_keys = nc.dram_tensor("keys_s", [128, T], F32)
    d_ctx = nc.dram_tensor("ctx_s", [HB, 128, T], BF16)

    A = nc.alloc_sbuf_tensor
    XB = A("XB", [128, CH * T], BF16)
    XSTG = A("XSTG", [128, CH * HW], F32)
    SCR = A("SCR", [128, 32 * 1024], U8)
    WBUF = A("WBUF", [128, 25 * 1024], U8)
    P0 = A("P0", [128, 28 * 1024], U8)
    RSTD = A("RSTD", [128, T], F32)
    QA = A("QA", [128, T], F32)
    KAT = A("KAT", [128, T], F32)
    VGALL = A("VGALL", [128, 64 * 16], BF16)
    IDF = A("IDF", [128, 128], F32)
    ONE1 = A("ONE1", [1, 128], BF16)
    ONEC = A("ONEC", [128, 1], BF16)
    EPH = A("EPH", [2, 128], BF16)
    REP8 = A("REP8", [8, 128], BF16)
    G8A = A("G8A", [8, 512], BF16)
    SLT = A("SLT", [128, 256], BF16)
    SLB = A("SLB", [128, 256], BF16)
    VST = A("VST", [128, 448], BF16)
    VSB = A("VSB", [128, 448], BF16)
    TV = A("TV", [128, 64], F32)
    TIU = A("TIU", [128, 64], U32)
    TIF = A("TIF", [128, 64], F32)
    IXB = A("IXB", [128, 128], F32)
    TBALL = A("TBALL", [128, 128], F32)
    IXVU = A("IXVU", [128, 64], U32)
    IXKU = A("IXKU", [128, 64], U32)
    VALS = A("VALS", [128, 64], F32)
    VB = A("VB", [128, 64], BF16)
    SCL = A("SCL", [128, 64], F32)
    EXB = A("EXB", [128, 128], BF16)
    ZRO = A("ZRO", [128, 1], F32)
    TINY = A("TINY", [1, TW], BF16)
    PVEC = A("PVEC", [128, 1], F32)
    H2K = A("H2K", [128, 1], F32)
    NEGC = A("NEGC", [128, 1], F32)
    NEGCB = A("NEGCB", [128, 4], F32)
    CS1 = A("CS1", [128, 1], F32)
    CS2 = A("CS2", [128, 1], F32)
    RSTM = A("RSTM", [128, CH], F32)
    OUTB = A("OUTB", [128, 16], F32)
    GTM = A("GTM", [128, 8], F32)
    GSC = A("GSC", [128, 8], F32)
    PTB = A("PTB", [128, 4 * TW], BF16)
    VSTG = A("VSTG", [128, 2 * TW], BF16)
    RSB = A("RSB", [128, TW], BF16)
    PS = [nc.alloc_psum_tensor("ps%d" % i, [128, 512], F32) for i in range(8)]

    SCR_BF = SCR[:, :].bitcast(BF16)
    SCR_F32 = SCR[:, :].bitcast(F32)
    SCR_U32 = SCR[:, :].bitcast(U32)
    WB_BF = WBUF[:, :].bitcast(BF16)
    P0_F32 = P0[:, :].bitcast(F32)
    P0_BF = P0[:, :].bitcast(BF16)
    QA_BF = QA[:, :].bitcast(BF16)
    KAT_BF = KAT[:, :].bitcast(BF16)
    WTR = XSTG[:, :].bitcast(BF16)
    BRS = P0[:, :].bitcast(F32)[0:8, 7056:7057]
    LGT = SCR_F32[0:8, 1024:1536]
    GFM = SCR_F32[0:8, 1536:2048]
    G8H = SCR_BF[0:8, 4096:4608]
    G8L = SCR_BF[0:8, 4608:5120]

    def MQ():
        return QA_BF[:, 0:2048]

    def MQC():
        return QA_BF[:, 2048:4096]

    def QHI():
        return KAT_BF[:, 0:2048]

    def QLO():
        return KAT_BF[:, 2048:4096]

    def RH(tt):
        return P0_BF[0:1, tt * TW:(tt + 1) * TW]

    def RL(tt):
        return P0_BF[0:1, 2048 + tt * TW: 2048 + (tt + 1) * TW]

    def MH_(tt):
        return P0_BF[0:1, 4096 + tt * TW: 4096 + (tt + 1) * TW]

    def ML_(tt):
        return P0_BF[0:1, 6144 + tt * TW: 6144 + (tt + 1) * TW]

    def SQVB(tt):
        return P0_BF[0:1, 8192 + tt * TW: 8192 + (tt + 1) * TW]

    RHB = P0_BF[0:1, 10240:10752]
    RLB = P0_BF[0:1, 10752:11264]
    QKVBP = P0_BF[0:1, 13344:14112]
    SCRA = P0_F32[0:1, 5632:5888]
    SCRB = P0_F32[0:1, 5888:6144]
    SCRW = P0_F32[0:1, 6144:6400]
    SCRW2 = P0_F32[0:1, 6400:6656]
    CELLS = P0_F32[0:1, 6656:6672]
    EPSC = P0_F32[0:1, 6672:6673]

    p = Prog(nc)
    stack = ExitStack()
    p.mksem(stack, "qd", "gd", "tv", "vv", "sv",
             *["b_" + e for e in ENGINES])
    psg = [None] * 8
    st = {"xstg": None, "vstg": [None, None]}

    def ps_wait(i):
        return [psg[i]] if psg[i] else []

    def mm(out_ap, lhsT, rhs, start, stop, wait=(), inc=()):
        p.w("tensor", "matmul", (out_ap, lhsT, rhs), {"start": start, "stop": stop},
            wait=wait, inc=inc)

    IOTA = TBALL[:, :].bitcast(U32)
    IOTA5 = VGALL[:, :].bitcast(U32)[:, 0:512]
    p.w("gpsimd", "iota", (IOTA[:, 0:1], [[0, 1]]), {"channel_multiplier": 1})
    p.bar()
    p.w("vector", "tensor_copy", (PVEC[:, :], IOTA[:, 0:1]))
    p.bar()
    p.w("vector", "tensor_scalar", (H2K[:, :], PVEC[:, :], 2048.0, None, OP.mult))
    p.w("vector", "memset", (EPSC, float(EPS)))
    p.w("vector", "memset", (ONE1[:, :], 1.0))
    p.w("vector", "memset", (ONEC[:, :], 1.0))
    p.w("gpsimd", "iota", (IOTA[:, 0:128], [[1, 128]]), {"channel_multiplier": 0})
    p.bar()
    p.w("vector", "tensor_copy", (QA[:, 0:128], IOTA[:, 0:128]))
    p.bar()
    p.w("vector", "tensor_scalar", (IDF[:, :], QA[:, 0:128], PVEC[:, 0:1], None,
                                    OP.is_equal))
    p.w("gpsimd", "iota", (IOTA[:, 0:128], [[1, 2], [0, 64]]), {"channel_multiplier": 0})
    p.bar()
    p.w("vector", "tensor_copy", (QA[:, 0:128], IOTA[:, 0:128]))
    p.bar()
    p.w("vector", "tensor_scalar", (EPH[:, :], QA[0:2, 0:128], PVEC[0:2, 0:1], None,
                                    OP.is_equal))
    p.w("gpsimd", "iota", (IOTA[:, 0:128], [[1, 8], [0, 16]]), {"channel_multiplier": 0})
    p.bar()
    p.w("vector", "tensor_copy", (QA[:, 0:128], IOTA[:, 0:128]))
    p.bar()
    p.w("vector", "tensor_scalar", (REP8[:, :], QA[0:8, 0:128], PVEC[0:8, 0:1], None,
                                    OP.is_equal))
    p.w("gpsimd", "iota", (IOTA5[:, 0:512], [[1, 8], [0, 64]]), {"channel_multiplier": 0})
    p.bar()
    p.w("vector", "tensor_copy", (QA[:, 128:640], IOTA5[:, 0:512]))
    p.bar()
    p.w("vector", "tensor_scalar", (G8A[:, :], QA[0:8, 128:640], PVEC[0:8, 0:1], None,
                                    OP.is_equal))
    p.w("gpsimd", "iota", (IOTA[:, 0:128], [[32, 4], [0, 32]]), {"channel_multiplier": 0})
    p.bar()
    p.w("vector", "tensor_copy", (QA[:, 0:128], IOTA[:, 0:128]))
    p.bar()
    p.w("vector", "tensor_scalar", (EXB[:, :], QA[:, 0:128], PVEC[:, 0:1], None,
                                    OP.is_equal))
    p.w("vector", "memset", (ZRO[:, :], 0.0))
    p.w("vector", "memset", (TINY[:, :], 1e-20))
    p.w("vector", "memset", (SLT[:, :], 0.0))
    p.w("vector", "memset", (SLT[0:32, 128:129], 1.0))
    p.w("vector", "memset", (SLT[64:96, 129:130], 1.0))
    p.w("vector", "memset", (SLB[:, :], 0.0))
    p.w("vector", "memset", (SLB[32:64, 128:129], 1.0))
    p.w("vector", "memset", (SLB[96:128, 129:130], 1.0))
    p.w("vector", "memset", (VST[:, :], 0.0))
    p.w("vector", "memset", (VSB[:, :], 0.0))
    p.dma("sync", OUTB[:, :], d_outb2.ap()[:, :])
    p.dma("sync", BRS[:, :], d_br.ap()[:, :])
    p.qfence("sync")
    p.bar()

    XC = SCR_BF[:, 8192:8192 + CH * HW]
    SQH = SCR_BF[:, 12288:12288 + CH * HW]

    def phase_A(tt, hh, full_mode, wait_x=()):
        h0 = hh * HW
        p.w("scalar", "activation", (SQH, XSTG[:, :], AF.Square), wait=wait_x,
            inc=[("sv", 1)])
        sq_t = p.tok("sv")
        p.w("vector", "tensor_copy", (XC, XSTG[:, :]), wait=wait_x, inc=[("vv", 1)])
        cast_t = p.tok("vv")
        for dch in range(CH):
            mm(PS[0][0:1, 0:HW], ONEC[:, :], XC[:, dch * HW:(dch + 1) * HW],
               start=(dch == 0), stop=(dch == CH - 1),
               wait=([cast_t] + ps_wait(0)) if dch == 0 else ())
        for dch in range(CH):
            mm(PS[1][0:1, 0:HW], ONEC[:, :], SQH[:, dch * HW:(dch + 1) * HW],
               start=(dch == 0), stop=(dch == CH - 1),
               wait=([sq_t] + ps_wait(1)) if dch == 0 else (),
               inc=[("tv", 1)] if dch == CH - 1 else ())
        mm_t = p.tok("tv")
        p.w("vector", "tensor_scalar", (SCRA, PS[0][0:1, 0:HW], 1.0 / D, None, OP.mult),
            wait=[mm_t])
        p.w("vector", "tensor_tensor", (SCRB, SCRA, SCRA, OP.mult))
        p.w("vector", "scalar_tensor_tensor",
            (SCRB, PS[1][0:1, 0:HW], 1.0 / D, SCRB, OP.mult, OP.subtract), inc=[("vv", 1)])
        psg[0] = psg[1] = p.tok("vv")
        p.w("scalar", "activation", (SCRB, SCRB, AF.Sqrt), {"bias": EPSC, "scale": 1.0},
            wait=[p.tok("vv")], inc=[("sv", 1)])
        p.w("vector", "tensor_copy", (SQVB(tt)[0:1, h0:h0 + HW], SCRB), wait=[p.tok("sv")])
        p.w("vector", "reciprocal", (SCRB, SCRB))
        p.w("vector", "tensor_copy", (RH(tt)[0:1, h0:h0 + HW], SCRB))
        p.w("vector", "scalar_tensor_tensor",
            (RL(tt)[0:1, h0:h0 + HW], SCRB, 1.0, RH(tt)[0:1, h0:h0 + HW],
             OP.mult, OP.subtract))
        p.w("vector", "tensor_copy", (MH_(tt)[0:1, h0:h0 + HW], SCRA))
        p.w("vector", "scalar_tensor_tensor",
            (ML_(tt)[0:1, h0:h0 + HW], SCRA, 1.0, MH_(tt)[0:1, h0:h0 + HW],
             OP.mult, OP.subtract), inc=[("vv", 1)])
        rows_t = p.tok("vv")
        mm(PS[2][:, 0:HW], ONE1[:, :], RH(tt)[0:1, h0:h0 + HW], start=True, stop=False,
           wait=[rows_t] + ps_wait(2))
        mm(PS[2][:, 0:HW], ONE1[:, :], RL(tt)[0:1, h0:h0 + HW], start=False, stop=True,
           inc=[("tv", 1)])
        p.w("vector", "tensor_copy",
            (RSTD[:, tt * TW + h0: tt * TW + h0 + HW], PS[2][:, 0:HW]),
            wait=[p.tok("tv")], inc=[("vv", 1)])
        rstd_t = p.tok("vv")
        mm(PS[2][:, 0:HW], ONE1[:, :], MH_(tt)[0:1, h0:h0 + HW], start=True, stop=False,
           wait=[rstd_t])
        mm(PS[2][:, 0:HW], ONE1[:, :], ML_(tt)[0:1, h0:h0 + HW], start=False, stop=True,
           inc=[("tv", 1)])
        mrep_t = p.tok("tv")
        if full_mode:
            xb_out = XB[:, :].rearrange("q (c t) -> q c t", c=CH)[
                :, :, tt * TW + h0: tt * TW + h0 + HW]
        else:
            xb_out = XB[:, 0:CH * TW].rearrange("q (c t) -> q c t", c=CH)[:, :, h0:h0 + HW]
        xin = XSTG[:, :].rearrange("q (c t) -> q c t", c=CH)
        mrep_b = PS[2][:, 0:HW].unsqueeze(1).to_broadcast([128, CH, HW])
        p.w("vector", "tensor_tensor", (xb_out, xin, mrep_b, OP.subtract),
            wait=[mrep_t], inc=[("vv", 1)])
        psg[2] = p.tok("vv")
        return p.tok("vv")

    def rstd_tm(tt):
        for j in range(4):
            mm(PS[3][:, j:j + 1], RH(tt)[0:1, j * 128:(j + 1) * 128], ONE1[0:1, 0:1],
               start=True, stop=False, wait=(ps_wait(3) if j == 0 else ()))
            mm(PS[3][:, j:j + 1], RL(tt)[0:1, j * 128:(j + 1) * 128], ONE1[0:1, 0:1],
               start=False, stop=True, inc=([("tv", 1)] if j == 3 else ()))
        p.w("vector", "tensor_copy", (RSTM[:, tt * 4:tt * 4 + 4], PS[3][:, 0:4]),
            wait=[p.tok("tv")], inc=[("vv", 1)])
        psg[3] = p.tok("vv")

    class Ring:
        def __init__(self, n, cols, off, pfx):
            self.n, self.cols, self.off, self.pfx = n, cols, off, pfx
            self.i = 0
            self.free = [None] * n

        def load(self, src):
            s = self.i % self.n
            self.i += 1
            dst = WB_BF[:, self.off + s * self.cols: self.off + (s + 1) * self.cols]
            wl = [self.free[s]] if self.free[s] is not None else []
            tok = p.dma("gpsimd", dst, src, wait=wl, sem="%s%d" % (self.pfx, s))
            return dst, tok, s

        def done(self, s, tok):
            self.free[s] = tok

    def xin_ap():
        return XSTG[:, :]

    r_qk = Ring(2, 2048, 0, "rqk")
    r_wv = Ring(16, 512, 4096, "rwv")
    qk_tiles = [r_qk.load(d_wqk.ap()[e, :, :]) for e in range(2)]

    for tt in range(NTT):
        for hh in range(2):
            xt = p.dma("sync", xin_ap(), d_x.ap()[2 * tt + hh, :, :],
                       wait=([st["xstg"]] if st["xstg"] else []), sem="sx")
            phase_A(tt, hh, full_mode=False, wait_x=[xt])
            p.bar()
        rstd_tm(tt)
        if stop_after == "probe3" and tt == 0:
            p.bar()
            wtile0d, _, _ = qk_tiles[0]
            for sl in range(4):
                p.dma("gpsimd", d_dbg.ap()[sl, :, :],
                      XB[:, sl * 2048:(sl + 1) * 2048])
            p.dma("gpsimd", d_dbg.ap()[4, :, :], wtile0d[:, 0:2048])
            p.qfence("sync")
            p.qfence("gpsimd")
            p.bar()
            with nc.Block() as block:
                p.replay(block)
            stack.close()
            return nc
        if stop_after == "probe2" and tt == 0:
            p.bar()
            wtile0c, _, _ = qk_tiles[0]
            for li, ln_ in enumerate([2, 4, 8, 16]):
                for dch in range(ln_):
                    mm(PS[7][:, :], wtile0c[:, dch * 128:(dch + 1) * 128],
                       XB[:, dch * TW:(dch + 1) * TW],
                       start=(dch == 0), stop=(dch == ln_ - 1),
                       wait=(ps_wait(7) if dch == 0 else ()),
                       inc=([("tv", 1)] if dch == ln_ - 1 else ()))
                p.w("vector", "tensor_copy", (KAT[:, 1024:1536], PS[7][:, :]),
                    wait=[p.tok("tv")], inc=[("vv", 1)])
                psg[7] = p.tok("vv")
                p.dma("sync", d_dbg.ap()[4, :, 512 * li:512 * (li + 1)],
                      KAT[:, 1024:1536], wait=[p.tok("vv")])
            p.dma("gpsimd", d_dbg.ap()[0, :, :], XB[:, 0:2048])
            p.qfence("sync")
            p.qfence("gpsimd")
            p.bar()
            with nc.Block() as block:
                p.replay(block)
            stack.close()
            return nc
        if stop_after == "probe0" and tt == 0:
            p.dma("gpsimd", d_dbg.ap()[3, :, 0:1024], XB[:, 0:1024])
            wtile0, wtok0, _ = qk_tiles[0]
            for dch in range(CH):
                mm(PS[6][:, :], wtile0[:, dch * 128:(dch + 1) * 128],
                   XB[:, dch * TW:(dch + 1) * TW],
                   start=(dch == 0), stop=(dch == CH - 1),
                   inc=([("tv", 1)] if dch == CH - 1 else ()))
            p.w("vector", "tensor_mul",
                (QLO().bitcast(F32)[:, 0:512] if False else KAT[:, 1536:2048],
                 PS[6][:, :], RSTD[:, 0:512]), wait=[p.tok("tv")], inc=[("vv", 1)])
            psg[6] = p.tok("vv")
            p.dma("sync", d_dbg.ap()[3, :, 1024:1536], KAT[:, 1536:2048],
                  wait=[p.tok("vv")])
            wtile0b, _, _ = qk_tiles[0]
            p.dma("gpsimd", d_dbg.ap()[3, :, 1536:2048], wtile0b[:, 0:512])
            for li, ln_ in enumerate([2, 4, 8, 16]):
                for dch in range(ln_):
                    mm(PS[7][:, :], wtile0b[:, dch * 128:(dch + 1) * 128],
                       XB[:, dch * TW:(dch + 1) * TW],
                       start=(dch == 0), stop=(dch == ln_ - 1),
                       wait=(ps_wait(7) if dch == 0 else ()),
                       inc=([("tv", 1)] if dch == ln_ - 1 else ()))
                p.w("vector", "tensor_copy", (KAT[:, 1024:1536], PS[7][:, :]),
                    wait=[p.tok("tv")], inc=[("vv", 1)])
                psg[7] = p.tok("vv")
                p.dma("sync", d_dbg.ap()[4, :, 512 * li:512 * (li + 1)],
                      KAT[:, 1024:1536], wait=[p.tok("vv")])
        for e in range(2):
            wtile, wtok, _ = qk_tiles[e]
            for dch in range(CH):
                mm(PS[4][:, :], wtile[:, dch * 128:(dch + 1) * 128],
                   XB[:, dch * TW:(dch + 1) * TW],
                   start=(dch == 0), stop=(dch == CH - 1),
                   wait=([wtok] + ps_wait(4)) if dch == 0 else (),
                   inc=([("tv", 1)] if dch == CH - 1 else ()))
            dst = QA if e == 0 else KAT
            p.w("vector", "tensor_mul",
                (dst[:, tt * TW:(tt + 1) * TW], PS[4][:, :], RSTD[:, tt * TW:(tt + 1) * TW]),
                wait=[p.tok("tv")], inc=[("vv", 1)])
            psg[4] = p.tok("vv")
        for ecg in range(4):
            tiles = [r_wv.load(d_wv.ap()[ecg, dch, :, :]) for dch in range(CH)]
            for j in range(4):
                vb_ = (5, 6, 7)[j % 3]
                tc = tt * 4 + j
                for dch in range(CH):
                    wt, wtk, _ = tiles[dch]
                    mm(PS[vb_][:, :], XB[:, dch * TW + j * 128: dch * TW + (j + 1) * 128], wt,
                       start=(dch == 0), stop=(dch == CH - 1),
                       wait=(([wtk] if j == 0 else []) + (ps_wait(vb_) if dch == 0 else [])),
                       inc=([("tv", 1)] if dch == CH - 1 else ()))
                vs = VSTG[:, (j % 2) * TW:(j % 2 + 1) * TW]
                vw = [p.tok("tv")] + ([st["vstg"][j % 2]] if st["vstg"][j % 2] else [])
                p.w("vector", "tensor_scalar",
                    (vs, PS[vb_][:, :], RSTM[:, tc:tc + 1], None, OP.mult),
                    wait=vw, inc=[("vv", 1)])
                psg[vb_] = p.tok("vv")
                st["vstg"][j % 2] = p.dma(
                    "sync", d_vtm.ap()[tc * 128:(tc + 1) * 128, ecg * 512:(ecg + 1) * 512],
                    vs, wait=[p.tok("vv")], sem="vs%d" % (j % 2))
            lasttok = p.tok("tv")
            for dch in range(CH):
                r_wv.done(tiles[dch][2], lasttok)
        p.bar()
        if stop_after == "probe0" and tt == 0:
            p.w("gpsimd", wait=[p.tok("vs0"), p.tok("vs1")])
            p.dma("gpsimd", d_dbg.ap()[0, :, :], XB[:, 0:2048])
            p.dma("sync", d_dbg.ap()[1, :, 0:512], RSTD[:, 0:512])
            p.dma("sync", d_dbg.ap()[1, :, 512:516], RSTM[:, 0:4])
            p.dma("sync", d_dbg.ap()[2, :, 0:512], QA[:, 0:512])
            p.dma("sync", d_dbg.ap()[2, :, 512:1024], KAT[:, 0:512])
            p.dma("gpsimd", d_dbg.ap()[4, :, 0:1024], d_vtm.ap()[0:128, 0:1024])
            p.qfence("sync")
            p.qfence("gpsimd")
            p.bar()
            with nc.Block() as block:
                p.replay(block)
            stack.close()
            return nc

    p.dma("sync", d_keys.ap()[:, :], KAT[:, :])
    p.qfence("sync")
    p.bar()

    TOKU = SCR_U32[:, 0:2048]
    PKT = SCR_F32[:, 2048:4096]
    PKB = SCR_F32[:, 4096:6144]
    MSK = SCR[:, 24576:26624]
    p.w("gpsimd", "iota", (TOKU, [[1, T]]), {"channel_multiplier": 0})
    p.bar()
    p.w("vector", "tensor_scalar", (MSK, KAT[:, :], 0.0, None, OP.is_le))
    p.w("vector", "tensor_scalar",
        (PKT.bitcast(U32), KAT[:, :].bitcast(U32), 0xFFFFF800, None, OP.bitwise_and))
    p.w("vector", "tensor_tensor", (PKT.bitcast(U32), PKT.bitcast(U32), TOKU, OP.bitwise_or))
    p.w("vector", "tensor_scalar",
        (PKB.bitcast(U32), KAT[:, :].bitcast(U32), 0x80000000, None, OP.bitwise_xor))
    p.w("vector", "tensor_scalar",
        (PKB.bitcast(U32), PKB.bitcast(U32), 0xFFFFF800, None, OP.bitwise_and))
    p.w("vector", "tensor_tensor", (PKB.bitcast(U32), PKB.bitcast(U32), TOKU, OP.bitwise_or))
    p.w("vector", "memset", (TOKU, 0))
    p.w("vector", "copy_predicated", (PKT.bitcast(U32), MSK, TOKU))
    p.w("vector", "tensor_scalar", (MSK, KAT[:, :], 0.0, None, OP.is_ge))
    p.w("vector", "copy_predicated", (PKB.bitcast(U32), MSK, TOKU))
    for i in range(4):
        p.w("vector", "max", kwargs={"out": TV[:, 8 * i:8 * i + 8], "in_": PKT})
        p.w("vector", "drain")
        p.w("vector", "match_replace",
            kwargs={"out": PKT, "in_to_replace": TV[:, 8 * i:8 * i + 8],
                    "in_values": PKT, "imm_value": -1.0})
        p.w("vector", "drain")
    for i in range(4, 8):
        p.w("vector", "max", kwargs={"out": TV[:, 8 * i:8 * i + 8], "in_": PKB})
        p.w("vector", "drain")
        p.w("vector", "match_replace",
            kwargs={"out": PKB, "in_to_replace": TV[:, 8 * i:8 * i + 8],
                    "in_values": PKB, "imm_value": -1.0})
        p.w("vector", "drain")
    p.w("vector", "tensor_scalar",
        (TIU[:, :], TV[:, :].bitcast(U32), 2047, None, OP.bitwise_and))
    p.w("vector", "drain")
    p.w("vector", "tensor_copy", (TIF[:, :], TIU[:, :]))
    p.w("vector", "drain")
    p.w("vector", "scalar_tensor_tensor",
        (IXB[:, 0:64], TIF[:, :], 128.0, PVEC[:, 0:1].to_broadcast([128, 64]),
         OP.mult, OP.add))
    p.w("vector", "drain")
    p.w("vector", "scalar_tensor_tensor",
        (IXB[:, 64:128], TIF[:, :], 1.0, H2K[:, 0:1].to_broadcast([128, 64]),
         OP.mult, OP.add))
    p.w("vector", "drain", inc=[("vv", 1)])
    ix_t = p.tok("vv")
    p.w("tensor", "transpose", (PS[1][:, 0:128], IXB[:, :], IDF[:, :]),
        wait=[ix_t] + ps_wait(1), inc=[("tv", 1)])
    p.w("vector", "tensor_copy", (TBALL[:, :], PS[1][:, 0:128]), wait=[p.tok("tv")],
        inc=[("vv", 1)])
    psg[1] = p.tok("vv")
    tb_t = p.tok("vv")
    p.dma("sync", IXB[0:64, 0:64], TBALL[0:64, 0:128:2], wait=[tb_t],
          allow_slow_non_contiguous=True, sem="ix")
    p.dma("sync", IXB[64:128, 0:64], TBALL[0:64, 1:128:2], wait=[tb_t],
          allow_slow_non_contiguous=True, sem="ix")
    p.dma("sync", IXB[0:64, 64:128], TBALL[64:128, 0:128:2], wait=[tb_t],
          allow_slow_non_contiguous=True, sem="ix")
    tix = p.dma("sync", IXB[64:128, 64:128], TBALL[64:128, 1:128:2], wait=[tb_t],
                allow_slow_non_contiguous=True, sem="ix")
    p.bar()
    p.w("vector", "tensor_copy", (IXVU[:, :], IXB[:, 0:64]), wait=[tix])
    p.w("vector", "tensor_copy", (IXKU[:, :], IXB[:, 64:128]), inc=[("vv", 1)])
    ixu_t = p.tok("vv")
    if stop_after == "probe1":
        p.dma("sync", d_dbg.ap()[0, :, :], QA[:, :])
        p.dma("sync", d_dbg.ap()[4, :, :], KAT[:, :])
    p.w("vector", "tensor_copy", (SCR_BF[:, 12288:14336], QA[:, :]))
    p.w("vector", "scalar_tensor_tensor",
        (SCR_BF[:, 14336:16384], QA[:, :], 1.0, SCR_BF[:, 12288:14336],
         OP.mult, OP.subtract))
    p.w("vector", "tensor_scalar", (PKT, QA[:, :], 0.0, None, OP.is_gt))
    p.w("vector", "tensor_scalar", (PKB, QA[:, :], 0.0, None, OP.is_le), inc=[("vv", 1)])
    stg_t = p.tok("vv")
    p.w("vector", "tensor_copy", (QHI(), SCR_BF[:, 12288:14336]), wait=[stg_t])
    p.w("vector", "tensor_copy", (QLO(), SCR_BF[:, 14336:16384]))
    p.w("vector", "tensor_copy", (MQ(), PKT))
    p.w("vector", "tensor_copy", (MQC(), PKB), inc=[("vv", 1)])
    p.bar()
    for pr in range(64):
        p.w("gpsimd", "indirect_dma_start",
            (VALS[:, pr:pr + 1], None,
             bass.AP(d_keys, 0, [[1, T * 128], [1, 1]]),
             IndirectOffsetOnAxis(IXKU[:, pr:pr + 1], 0)),
            inc=[("gd", 16)])
    for pr in range(64):
        p.w("gpsimd", "indirect_dma_start",
            (VGALL[:, pr * 16:(pr + 1) * 16], None,
             bass.AP(d_vtm, 0, [[16, T * D // 16], [1, 16]]),
             IndirectOffsetOnAxis(IXVU[:, pr:pr + 1], 0)),
            inc=[("gd", 16)])
    p.qfence("gpsimd")
    p.bar()
    p.w("vector", "tensor_copy", (VB[:, :], VALS[:, :]), inc=[("vv", 1)])
    mm(PS[1][:, 0:64], EXB[:, :], VB[:, :], start=True, stop=True,
       wait=[p.tok("vv")] + ps_wait(1), inc=[("tv", 1)])
    p.w("vector", "tensor_tensor", (SCL[:, :], VALS[:, :], PS[1][:, 0:64],
                                    OP.subtract),
        wait=[p.tok("tv")], inc=[("vv", 1)])
    psg[1] = p.tok("vv")
    if stop_after == "probe1":
        p.dma("sync", d_dbg.ap()[1, :, 0:64], TIF[:, :])
        p.dma("sync", d_dbg.ap()[1, :, 64:128], VALS[:, :])
        p.dma("sync", d_dbg.ap()[1, :, 128:129], NEGC[:, :], allow_slow_non_contiguous=True)
        p.dma("sync", d_dbg.ap()[1, :, 129:193], TV[:, :])
        p.dma("gpsimd", d_dbg.ap()[2, :, 0:1024], VGALL[:, 0:1024])
        p.dma("gpsimd", d_dbg.ap()[3, :, 0:1024], d_vtm.ap()[0:128, 0:1024])
        p.qfence("sync")
        p.qfence("gpsimd")
        p.bar()
        with nc.Block() as block:
            p.replay(block)
        stack.close()
        return nc

    r_at = Ring(2, 2048, 4096, "rat")
    r_bt = Ring(8, 512, 8192, "rbt")
    r_wr = Ring(1, 128, 12288, "rwr")
    wr_tile, wr_tok, _ = r_wr.load(d_wr.ap()[:, :])
    HSB = SCR_BF[:, 0:4 * TW]

    def ffn_tt(tt):
        for dch in range(CH):
            mm(PS[0][0:8, :], wr_tile[:, dch * 8:(dch + 1) * 8],
               XB[:, dch * TW:(dch + 1) * TW],
               start=(dch == 0), stop=(dch == CH - 1),
               wait=([wr_tok] + ps_wait(0)) if dch == 0 else (),
               inc=([("tv", 1)] if dch == CH - 1 else ()))
        p.w("vector", "tensor_mul",
            (LGT[0:8, :], PS[0][0:8, :], RSTD[0:8, tt * TW:(tt + 1) * TW]),
            wait=[p.tok("tv")])
        p.w("vector", "tensor_scalar", (LGT[0:8, :], LGT[0:8, :], BRS[0:8, 0:1], None, OP.add),
            inc=[("vv", 1)])
        psg[0] = p.tok("vv")
        lg_t = p.tok("vv")
        for j in range(4):
            p.w("tensor", "transpose", (PS[1][0:128, 0:8], LGT[0:8, j * 128:(j + 1) * 128],
                                        IDF[0:8, 0:8]),
                wait=[lg_t] + ps_wait(1), inc=[("tv", 1)])
            p.w("vector", "tensor_copy", (GTM[:, :], PS[1][0:128, 0:8]), wait=[p.tok("tv")],
                inc=[("vv", 1)])
            psg[1] = p.tok("vv")
            p.w("vector", "reduce_max", (CS1[:, :], GTM[:, :], AX))
            p.w("vector", "tensor_scalar", (CS2[:, :], CS1[:, :], -1.0, None, OP.mult),
                inc=[("vv", 1)])
            p.w("scalar", "activation", (GSC[:, :], GTM[:, :], AF.Exp),
                {"bias": CS2[:, 0:1], "scale": 1.0, "accum_out": CS1[:, 0:1]},
                wait=[p.tok("vv")], inc=[("sv", 1)])
            p.w("vector", "reciprocal", (CS2[:, :], CS1[:, :]), wait=[p.tok("sv")])
            p.w("vector", "reduce_max", (CS1[:, :], GTM[:, :], AX))
            p.w("vector", "tensor_scalar", (GSC[:, :], GTM[:, :], CS1[:, 0:1], None,
                                            OP.is_equal))
            p.w("vector", "tensor_scalar", (GSC[:, :], GSC[:, :], CS2[:, 0:1], 1.0 / 64.0,
                                            OP.mult, OP.mult), inc=[("vv", 1)])
            g_t = p.tok("vv")
            p.w("tensor", "transpose", (PS[1][0:8, 0:128], GSC[:, :], IDF[:, :]),
                wait=[g_t] + ps_wait(1), inc=[("tv", 1)])
            p.w("vector", "tensor_copy", (GFM[0:8, j * 128:(j + 1) * 128], PS[1][0:8, 0:128]),
                wait=[p.tok("tv")], inc=[("vv", 1)])
            psg[1] = p.tok("vv")
        p.w("vector", "tensor_copy", (G8H[0:8, :], GFM[0:8, :]))
        p.w("vector", "scalar_tensor_tensor",
            (G8L[0:8, :], GFM[0:8, :], 1.0, G8H[0:8, :], OP.mult, OP.subtract),
            inc=[("vv", 1)])
        g8_t = p.tok("vv")
        at_tiles = [r_at.load(d_at.ap()[er, :, :]) for er in range(2)]
        for er in range(4):
            att, atk, aslot = at_tiles[er]
            for dch in range(CH):
                mm(PS[2][:, :], att[:, dch * 128:(dch + 1) * 128],
                   XB[:, dch * TW:(dch + 1) * TW],
                   start=(dch == 0), stop=(dch == CH - 1),
                   wait=([atk] + ps_wait(2)) if dch == 0 else (),
                   inc=([("tv", 1)] if dch == CH - 1 else ()))
            h_t = p.tok("tv")
            r_at.done(aslot, h_t)
            if er + 2 < 4:
                at_tiles.append(r_at.load(d_at.ap()[er + 2, :, :]))
            else:
                at_tiles.append(None)
            mm(PS[3][:, :], G8A[:, er * 128:(er + 1) * 128], G8H[0:8, :],
               start=True, stop=False, wait=[g8_t] + ps_wait(3))
            mm(PS[3][:, :], G8A[:, er * 128:(er + 1) * 128], G8L[0:8, :],
               start=False, stop=True, inc=[("tv", 1)])
            grep_t = p.tok("tv")
            RGF = VSTG[:, :].bitcast(F32)
            p.w("vector", "tensor_mul",
                (RGF[:, 0:512], PS[3][:, :], RSTD[:, tt * TW:(tt + 1) * TW]),
                wait=[grep_t], inc=[("vv", 1)])
            psg[3] = p.tok("vv")
            p.w("vector", "tensor_mul",
                (HSB[:, er * TW:(er + 1) * TW], PS[2][:, :], RGF[:, 0:512]),
                wait=[h_t], inc=[("vv", 1)])
            psg[2] = p.tok("vv")
        h_all = p.tok("vv")
        for hh in range(2):
            xt = p.dma("sync", xin_ap(), d_y.ap()[2 * tt + hh, :, :],
                       wait=([st["xstg"]] if st["xstg"] else []), sem="sx")
            p.w("vector", wait=[xt])
            for oc in range(CH):
                ob = (4, 2)[oc % 2]
                btt, btk, bslot = r_bt.load(d_bt.ap()[oc, :, :])
                for er in range(4):
                    mm(PS[ob][:, 0:HW], btt[:, er * 128:(er + 1) * 128],
                       HSB[:, er * TW + hh * HW: er * TW + hh * HW + HW],
                       start=(er == 0), stop=(er == 3),
                       wait=([btk, h_all] + ps_wait(ob)) if er == 0 else (),
                       inc=([("tv", 1)] if er == 3 else ()))
                y_t = p.tok("tv")
                r_bt.done(bslot, y_t)
                xs = XSTG[:, oc * HW:(oc + 1) * HW]
                p.w("vector", "scalar_tensor_tensor",
                    (xs, PS[ob][:, 0:HW], 0.0, xs, OP.bypass, OP.add),
                    wait=[y_t], inc=[("vv", 1)])
                psg[ob] = p.tok("vv")
            st["xstg"] = p.dma("sync", d_y.ap()[2 * tt + hh, :, :], xin_ap(),
                               wait=[p.tok("vv")], sem="sy")
        p.bar()

    r_wo = Ring(2, 2048, 0, "rwo")
    TOPT = SCR_BF[:, 0:CH * TW]
    BOTT = SCR_BF[:, CH * TW:2 * CH * TW]
    QPH_ = XB[0:2, 0:32 * TW]
    QPL_ = XB[0:2, 32 * TW:64 * TW]

    def build_qpair(tt, half, wait=()):
        tks = []
        h0 = 64 * half
        tks.append(p.dma("sync", QPH_[0:1, :].rearrange("q (r t) -> q r t", r=32),
                         QHI()[h0:h0 + 64:2, tt * TW:(tt + 1) * TW], wait=wait, sem="qp"))
        tks.append(p.dma("sync", QPH_[1:2, :].rearrange("q (r t) -> q r t", r=32),
                         QHI()[h0 + 1:h0 + 64:2, tt * TW:(tt + 1) * TW], wait=wait, sem="qp"))
        return tks

    def build_vstack(pr, wait=()):
        sl = pr % 2
        c0 = sl * 224 + 96
        g0 = pr * 16
        p.w("vector", "tensor_copy", (VST[0:32, c0:c0 + 16], VGALL[0:32, g0:g0 + 16]),
            wait=wait)
        p.w("vector", "tensor_copy", (VST[64:96, c0 + 16:c0 + 32], VGALL[64:96, g0:g0 + 16]))
        p.w("vector", "tensor_copy", (VSB[32:64, c0:c0 + 16], VGALL[32:64, g0:g0 + 16]))
        p.w("vector", "tensor_copy", (VSB[96:128, c0 + 16:c0 + 32], VGALL[96:128, g0:g0 + 16]),
            inc=[("vv", 1)])
        return p.tok("vv")

    def attn_a_sparse(tt):
        vs_tok = {}
        pair_end = {}
        for half in range(2):
            qp = build_qpair(tt, half,
                             wait=([pair_end[31]] if half == 1 else ()))
            for tk in qp:
                p.w("tensor", wait=[tk])
            if half == 0:
                vs_tok[0] = build_vstack(0)
                vs_tok[1] = build_vstack(1)
            for k in range(32):
                pr = 32 * half + k
                ch = pr // 4
                sl = pr % 2
                mm(PS[sl][:, :], EPH[:, :], QPH_[0:2, k * TW:(k + 1) * TW],
                   start=True, stop=True, wait=ps_wait(sl), inc=[("tv", 1)])
                arg_t = p.tok("tv")
                pb = PTB[:, (pr % 4) * TW:(pr % 4 + 1) * TW]
                ew = [arg_t] + ([pair_end[pr - 4]] if pr >= 4 else [])
                p.w("scalar", "activation", (pb, PS[sl][:, :], AF.Exp),
                    {"bias": ZRO[:, 0:1], "scale": SCL[:, pr:pr + 1]},
                    wait=ew, inc=[("sv", 1)])
                p.w("vector", "tensor_scalar", (pb, pb, 1e30, None, OP.min),
                    wait=[p.tok("sv")], inc=[("vv", 1)])
                psg[sl] = p.tok("vv")
                exp_t = p.tok("vv")
                mm(PS[2][:, :], SLT[:, 128 - 2 * pr:256 - 2 * pr], pb,
                   start=(pr == 0), stop=(pr == 63),
                   wait=[exp_t] + (ps_wait(2) if pr == 0 else []))
                mm(PS[3][:, :], SLB[:, 128 - 2 * pr:256 - 2 * pr], pb,
                   start=(pr == 0), stop=(pr == 63),
                   wait=(ps_wait(3) if pr == 0 else []),
                   inc=([("tv", 1)] if pr == 63 else ()))
                it, ib = 4 + (ch % 2), 6 + (ch % 2)
                w0 = (pr % 2) * 224 + 96 - (pr % 4) * 32
                mm(PS[it][:, :], VST[:, w0:w0 + 128], pb,
                   start=(pr % 4 == 0), stop=(pr % 4 == 3),
                   wait=[vs_tok[pr]] + (ps_wait(it) if pr % 4 == 0 else []))
                mm(PS[ib][:, :], VSB[:, w0:w0 + 128], pb,
                   start=(pr % 4 == 0), stop=(pr % 4 == 3),
                   wait=(ps_wait(ib) if pr % 4 == 0 else []),
                   inc=[("tv", 1)])
                pair_end[pr] = p.tok("tv")
                if pr % 4 == 3:
                    pv_t = pair_end[pr]
                    p.w("vector", "tensor_copy",
                        (TOPT[:, ch * TW:(ch + 1) * TW], PS[it][:, :]), wait=[pv_t])
                    p.w("vector", "tensor_copy",
                        (BOTT[:, ch * TW:(ch + 1) * TW], PS[ib][:, :]), inc=[("vv", 1)])
                    psg[it] = psg[ib] = p.tok("vv")
                if pr + 2 <= 63:
                    vs_tok[pr + 2] = build_vstack(pr + 2, wait=[pair_end[pr]])
        s_done = p.tok("tv")
        WF = VSTG[:, :].bitcast(F32)
        p.w("vector", "reciprocal", (WF[:, 0:512], PS[2][:, :]), wait=[s_done],
            inc=[("vv", 1)])
        psg[2] = p.tok("vv")
        p.w("vector", "tensor_mul",
            (RSB[:, 0:512], WF[:, 0:512], MQ()[:, tt * TW:(tt + 1) * TW]), inc=[("vv", 1)])
        wtop_t = p.tok("vv")
        p.w("vector", "reciprocal", (WF[:, 0:512], PS[3][:, :]), wait=[wtop_t],
            inc=[("vv", 1)])
        psg[3] = p.tok("vv")
        p.w("vector", "tensor_mul",
            (PTB[:, 0:512], WF[:, 0:512], MQC()[:, tt * TW:(tt + 1) * TW]), inc=[("vv", 1)])
        w_t = p.tok("vv")
        wtr_free = None
        for hh in range(2):
            h0 = hh * HW
            dwait = [w_t] + ([wtr_free] if wtr_free else []) + \
                ([p.tok("sy")] if "sy" in p.cnt else [])
            for c in range(CH):
                p.dma("sync", WTR[0:8, c * HW:(c + 1) * HW],
                      RSB[8 * c:8 * c + 8, h0:h0 + HW], wait=dwait, sem="wt")
                t2 = p.dma("sync", WTR[0:8, 4096 + c * HW: 4096 + (c + 1) * HW],
                           PTB[8 * c:8 * c + 8, h0:h0 + HW], wait=dwait, sem="wt")
            t1 = t2
            for ch in range(CH):
                i = ch % 2
                mm(PS[0 + i][:, 0:HW], REP8[:, :], WTR[0:8, ch * HW:(ch + 1) * HW],
                   start=True, stop=True, wait=[t1, t2] + ps_wait(0 + i))
                mm(PS[2 + i][:, 0:HW], REP8[:, :],
                   WTR[0:8, 4096 + ch * HW: 4096 + (ch + 1) * HW],
                   start=True, stop=True, wait=ps_wait(2 + i), inc=[("tv", 1)])
                rep_t = p.tok("tv")
                tch = TOPT[:, ch * TW + h0: ch * TW + h0 + HW]
                bch = BOTT[:, ch * TW + h0: ch * TW + h0 + HW]
                p.w("vector", "tensor_mul", (tch, tch, PS[0 + i][:, 0:HW]), wait=[rep_t])
                p.w("vector", "tensor_mul", (bch, bch, PS[2 + i][:, 0:HW]))
                p.w("vector", "tensor_add", (tch, tch, bch), inc=[("vv", 1)])
                psg[0 + i] = psg[2 + i] = p.tok("vv")
            wtr_free = p.tok("tv")
        return p.tok("vv")

    for tt in range(NTT):
        attn_a_sparse(tt)
        p.bar()
        if stop_after == "probe5" and tt == 0:
            p.dma("gpsimd", d_dbg.ap()[0, :, :], TOPT[:, 0:2048])
            p.dma("gpsimd", d_dbg.ap()[1, :, :], TOPT[:, 2048:4096])
            p.dma("gpsimd", d_dbg.ap()[2, :, 0:64], VALS[:, :])
            p.dma("gpsimd", d_dbg.ap()[2, :, 64:128], TIF[:, :])
            p.dma("gpsimd", d_dbg.ap()[2, :, 128:192], IXVU[:, :])
            p.dma("gpsimd", d_dbg.ap()[2, :, 192:256], IXKU[:, :])
            p.dma("gpsimd", d_dbg.ap()[3, :, 0:2048], QHI())
            p.dma("gpsimd", d_dbg.ap()[4, :, 0:2048], QLO())
            p.qfence("sync")
            p.qfence("gpsimd")
            p.bar()
            with nc.Block() as block:
                p.replay(block)
            stack.close()
            return nc
        for hh in range(2):
            xt = p.dma("sync", xin_ap(), d_x.ap()[2 * tt + hh, :, :],
                       wait=([st["xstg"]] if st["xstg"] else []), sem="sx")
            p.w("vector", wait=[xt])
            for oc in range(CH):
                ob = (5, 2, 3)[oc % 3]
                wtile, wtok, slot = r_wo.load(d_wo.ap()[oc, :, :])
                for dch in range(CH):
                    mm(PS[ob][:, 0:HW], wtile[:, dch * 128:(dch + 1) * 128],
                       TOPT[:, dch * TW + hh * HW: dch * TW + hh * HW + HW],
                       start=(dch == 0), stop=(dch == CH - 1),
                       wait=([wtok] + ps_wait(ob)) if dch == 0 else (),
                       inc=([("tv", 1)] if dch == CH - 1 else ()))
                wo_t = p.tok("tv")
                r_wo.done(slot, wo_t)
                xs = XSTG[:, oc * HW:(oc + 1) * HW]
                p.w("vector", "scalar_tensor_tensor",
                    (xs, PS[ob][:, 0:HW], 0.0, xs, OP.bypass, OP.add),
                    wait=[wo_t], inc=[("vv", 1)])
                psg[ob] = p.tok("vv")
            p.bar()
            if stop_after == "sub1":
                st["xstg"] = p.dma("sync", d_y.ap()[2 * tt + hh, :, :], xin_ap(),
                                   wait=[p.tok("vv")], sem="sy")
                continue
            phase_A(tt, hh, full_mode=False)
            st["xstg"] = p.dma("sync", d_y.ap()[2 * tt + hh, :, :], xin_ap(),
                               wait=[p.tok("vv")], sem="sy")
            p.bar()
        if stop_after == "sub1":
            continue
        rstd_tm(tt)
        ffn_tt(tt)
        p.bar()

    if stop_after in ("sub1", "sub2"):
        p.qfence("sync")
        p.qfence("gpsimd")
        p.bar()
        with nc.Block() as block:
            p.replay(block)
        stack.close()
        return nc

    for tt in range(NTT):
        for hh in range(2):
            xt = p.dma("sync", xin_ap(), d_y.ap()[2 * tt + hh, :, :],
                       wait=([st["xstg"]] if st["xstg"] else []), sem="sx")
            phase_A(tt, hh, full_mode=True, wait_x=[xt])
            p.bar()
        rstd_tm(tt)
    p.bar()

    if stop_after == "sub3a":
        p.qfence("sync")
        p.qfence("gpsimd")
        p.bar()
        with nc.Block() as block:
            p.replay(block)
        stack.close()
        return nc

    QH2 = SCR_BF[:, 0:4096]
    KH2 = SCR_BF[:, 4096:8192]
    VTM = SCR_BF[:, 8192:12288]
    QSQ = SCR_BF[:, 12288:14336]
    SEXP = SCR_F32[:, 6144:6656]
    SEH = SCR_BF[:, 13312:13824]
    SEL = SCR_BF[:, 13824:14336]
    r_qkv = Ring(4, 2048, 0, "rqv")
    r_vv = Ring(16, 256, 8192, "rvv")

    def xbf(dch, c0, cw):
        return XB[:, dch * T + c0: dch * T + c0 + cw]

    hp_prev = None
    for hp in range(8):
        if hp_prev is not None:
            p.w("vector", wait=[hp_prev])
            p.w("gpsimd", wait=[hp_prev])
        echs = [2 * hp, 2 * hp + 1, 16 + 2 * hp, 16 + 2 * hp + 1]
        tiles = [r_qkv.load(d_wqkv.ap()[e, :, :]) for e in echs]
        vtiles = [r_vv.load(d_wqkvv.ap()[dch, :, 256 * hp:256 * (hp + 1)]) for dch in range(CH)]
        p.dma("gpsimd", QKVBP[0:1, 0:256], d_qkvbrow.ap()[0:1, 256 * hp:256 * (hp + 1)], sem="qb")
        p.dma("gpsimd", QKVBP[0:1, 256:512],
              d_qkvbrow.ap()[0:1, 2048 + 256 * hp:2048 + 256 * (hp + 1)], sem="qb")
        bq_t = p.dma("gpsimd", QKVBP[0:1, 512:768],
                     d_qkvbrow.ap()[0:1, 4096 + 256 * hp:4096 + 256 * (hp + 1)], sem="qb")
        for i, e in enumerate(echs):
            wt, wtk, _ = tiles[i]
            for tt in range(NTT):
                for dch in range(CH):
                    mm(PS[0][:, :], wt[:, dch * 128:(dch + 1) * 128],
                       xbf(dch, tt * TW, TW), start=(dch == 0), stop=False,
                       wait=([wtk] + ps_wait(0)) if dch == 0 else ())
                boff = (i % 2) * 128 + (0 if i < 2 else 256)
                mm(PS[0][:, :], QKVBP[0:1, boff:boff + 128], SQVB(tt),
                   start=False, stop=True, wait=[bq_t], inc=[("tv", 1)])
                qk_t = p.tok("tv")
                dst = QH2 if i < 2 else KH2
                dsl = dst[:, (i % 2) * 2048 + tt * TW: (i % 2) * 2048 + (tt + 1) * TW]
                if i < 2:
                    p.w("vector", "tensor_mul",
                        (dsl, PS[0][:, :], RSTD[:, tt * TW:(tt + 1) * TW]),
                        wait=[qk_t], inc=[("vv", 1)])
                else:
                    p.w("vector", "scalar_tensor_tensor",
                        (dsl, PS[0][:, :], SMS, RSTD[:, tt * TW:(tt + 1) * TW],
                         OP.mult, OP.mult), wait=[qk_t], inc=[("vv", 1)])
                psg[0] = p.tok("vv")
        for tc in range(CH):
            tt, j = tc // 4, tc % 4
            for dch in range(CH):
                vt, vtk, _ = vtiles[dch]
                mm(PS[1][:, 0:256], xbf(dch, tc * 128, 128), vt,
                   start=(dch == 0), stop=False,
                   wait=(([vtk] if tc == 0 else []) + (ps_wait(1) if dch == 0 else [])))
            mm(PS[1][:, 0:256], SQVB(tt)[0:1, j * 128:(j + 1) * 128],
               QKVBP[0:1, 512:768],
               start=False, stop=True, wait=[bq_t], inc=[("tv", 1)])
            v_t = p.tok("tv")
            p.w("vector", "tensor_scalar",
                (VTM[:, tc * 256:(tc + 1) * 256], PS[1][:, 0:256], RSTM[:, tc:tc + 1],
                 None, OP.mult), wait=[v_t], inc=[("vv", 1)])
            psg[1] = p.tok("vv")
        p.bar()
        for a in range(2):
            h = 2 * hp + a
            negcb_toks = []
            for tt in range(NTT):
                sb2 = (2, 7)[tt % 2]
                q0 = a * 2048 + tt * TW
                mm(PS[sb2][:, :], QH2[:, q0:q0 + 128],
                   KH2[:, a * 2048:a * 2048 + TW],
                   start=True, stop=True, wait=ps_wait(sb2), inc=[("tv", 1)])
                p.w("vector", "reduce_max", (TV[:, tt:tt + 1], PS[sb2][:, :], AX),
                    wait=[p.tok("tv")], inc=[("vv", 1)])
                psg[sb2] = p.tok("vv")
                p.w("tensor", "transpose", (PS[sb2][0:1, 0:128], TV[:, tt:tt + 1],
                                            IDF[:, :]),
                    wait=[p.tok("vv")] + ps_wait(sb2), inc=[("tv", 1)])
                p.w("vector", "reduce_max",
                    (CELLS[0:1, 4 + tt:5 + tt], PS[sb2][0:1, 0:128], AX),
                    wait=[p.tok("tv")], inc=[("vv", 1)])
                psg[sb2] = p.tok("vv")
                p.w("vector", "tensor_scalar",
                    (CELLS[0:1, 8 + tt:9 + tt].bitcast(BF16)[0:1, 0:1],
                     CELLS[0:1, 4 + tt:5 + tt], -1.0,
                     -30.0, OP.mult, OP.add), inc=[("vv", 1)])
                mm(PS[sb2][:, 0:1], ONE1[:, :],
                   CELLS[0:1, 8 + tt:9 + tt].bitcast(BF16)[0:1, 0:1],
                   start=True, stop=True, wait=[p.tok("vv")] + ps_wait(sb2),
                   inc=[("tv", 1)])
                p.w("vector", "tensor_copy", (NEGCB[:, tt:tt + 1], PS[sb2][:, 0:1]),
                    wait=[p.tok("tv")], inc=[("vv", 1)])
                psg[sb2] = p.tok("vv")
                negcb_toks.append(p.tok("vv"))
            for tt in range(NTT):
                negcb_t = negcb_toks[tt]
                accs = {}
                for kch in range(CH):
                    sb = (2, 3, 4)[kch % 3]
                    sl = kch % 4
                    mm(PS[sb][:, :],
                       KH2[:, a * 2048 + kch * 128: a * 2048 + (kch + 1) * 128],
                       QH2[:, a * 2048 + tt * TW: a * 2048 + (tt + 1) * TW],
                       start=True, stop=True,
                       wait=ps_wait(sb) + ([negcb_t] if kch == 0 else []),
                       inc=[("tv", 1)])
                    arg_t = p.tok("tv")
                    pb = PTB[:, sl * TW:(sl + 1) * TW]
                    p.w("scalar", "activation", (pb, PS[sb][:, :], AF.Exp),
                        {"bias": NEGCB[:, tt:tt + 1], "scale": 1.0},
                        wait=[arg_t] + ([accs[kch - 4]] if kch >= 4 else []),
                        inc=[("sv", 1)])
                    psg[sb] = p.tok("sv")
                    exp_t = p.tok("sv")
                    if kch == 0:
                        p.w("vector", "tensor_copy", (SEXP, pb), wait=[exp_t],
                            inc=[("vv", 1)])
                    else:
                        p.w("vector", "scalar_tensor_tensor",
                            (SEXP, pb, 0.0, SEXP, OP.bypass, OP.add),
                            wait=[exp_t], inc=[("vv", 1)])
                    accs[kch] = p.tok("vv")
                    mm(PS[6][:, :], VTM[:, kch * 256 + a * 128: kch * 256 + (a + 1) * 128],
                       pb, start=(kch == 0), stop=(kch == CH - 1),
                       wait=[exp_t] + (ps_wait(6) if kch == 0 else []),
                       inc=([("tv", 1)] if kch == CH - 1 else ()))
                ctx_t = p.tok("tv")
                p.w("vector", "tensor_copy", (SEH, SEXP),
                    wait=([st["sum_t"]] if st.get("sum_t") else ()))
                p.w("vector", "scalar_tensor_tensor",
                    (SEL, SEXP, 1.0, SEH, OP.mult, OP.subtract), inc=[("vv", 1)])
                mm(PS[5][0:1, :], ONEC[:, :], SEH, start=True, stop=False,
                   wait=[p.tok("vv")] + ps_wait(5))
                mm(PS[5][0:1, :], ONEC[:, :], SEL, start=False, stop=False)
                mm(PS[5][0:1, :], ONE1[0:1, 0:1], TINY[0:1, :],
                   start=False, stop=True, inc=[("tv", 1)])
                sum_t = p.tok("tv")
                st["sum_t"] = sum_t
                p.w("vector", "reciprocal", (SCRW, PS[5][0:1, 0:256]),
                    wait=[ctx_t, sum_t])
                p.w("vector", "reciprocal", (SCRW2, PS[5][0:1, 256:512]), inc=[("vv", 1)])
                psg[5] = p.tok("vv")
                p.w("vector", "tensor_copy", (RHB[0:1, 0:256], SCRW))
                p.w("vector", "tensor_copy", (RHB[0:1, 256:512], SCRW2))
                p.w("vector", "scalar_tensor_tensor",
                    (RLB[0:1, 0:256], SCRW, 1.0, RHB[0:1, 0:256], OP.mult, OP.subtract))
                p.w("vector", "scalar_tensor_tensor",
                    (RLB[0:1, 256:512], SCRW2, 1.0, RHB[0:1, 256:512], OP.mult, OP.subtract),
                    inc=[("vv", 1)])
                mm(PS[7][:, :], ONE1[:, :], RHB, start=True, stop=False,
                   wait=[p.tok("vv")] + ps_wait(7))
                mm(PS[7][:, :], ONE1[:, :], RLB, start=False, stop=True, inc=[("tv", 1)])
                p.w("vector", "tensor_copy", (RSB[:, 0:TW], PS[7][:, :]), wait=[p.tok("tv")],
                    inc=[("vv", 1)])
                psg[7] = p.tok("vv")
                vs = VSTG[:, (tt % 2) * TW:(tt % 2 + 1) * TW]
                vw = [st["vstg"][tt % 2]] if st["vstg"][tt % 2] else []
                p.w("vector", "tensor_mul", (vs, PS[6][:, :], RSB[:, 0:TW]), wait=vw,
                    inc=[("vv", 1)])
                psg[6] = p.tok("vv")
                st["vstg"][tt % 2] = p.dma(
                    "sync", d_ctx.ap()[h, :, tt * TW:(tt + 1) * TW], vs,
                    wait=[p.tok("vv")], sem="vs%d" % (tt % 2))
        qk_last = p.tok("tv")
        hp_prev = qk_last
        for i in range(4):
            r_qkv.done(tiles[i][2], qk_last)
        for dch in range(CH):
            r_vv.done(vtiles[dch][2], qk_last)
        if stop_after == "probe4" and hp == 0:
            p.qfence("sync")
            p.bar()
            for h in range(2):
                p.dma("gpsimd", d_dbg.ap()[h, :, :], d_ctx.ap()[h, :, :])
            p.dma("gpsimd", d_dbg.ap()[2, :, :], QH2[:, 0:2048])
            p.dma("gpsimd", d_dbg.ap()[3, :, :], KH2[:, 0:2048])
            p.dma("gpsimd", d_dbg.ap()[4, :, 0:512], RSB[:, 0:512])
            p.dma("gpsimd", d_dbg.ap()[4, :, 512:513], NEGCB[:, 0:1],
                  allow_slow_non_contiguous=True)
            p.dma("gpsimd", d_dbg.ap()[4, 0:1, 520:536],
                  CELLS[0:1, 0:16], allow_slow_non_contiguous=True)
            p.dma("gpsimd", d_dbg.ap()[4, 0:1, 536:1048], RHB[0:1, 0:512])
            p.dma("gpsimd", d_dbg.ap()[4, :, 1048:2048], VTM[:, 0:1000])
            p.qfence("sync")
            p.qfence("gpsimd")
            p.bar()
            with nc.Block() as block:
                p.replay(block)
            stack.close()
            return nc
        if stop_after and stop_after.startswith("sub3b") and \
                hp == int(stop_after[5:] or 0):
            break
    p.bar()

    if stop_after and stop_after.startswith("sub3b"):
        p.qfence("sync")
        p.qfence("gpsimd")
        p.bar()
        with nc.Block() as block:
            p.replay(block)
        stack.close()
        return nc

    r_wout = Ring(2, 2048, 0, "rwt")
    CTXT = SCR_BF[:, 0:CH * TW]
    ctx_done = [p.tok("vs0"), p.tok("vs1")]
    for tt in range(NTT):
        ctk = []
        for g in range(4):
            ctk.append(p.dma("sync",
                             CTXT.rearrange("q (c t) -> q c t", c=CH)[:, 4 * g:4 * g + 4, :],
                             d_ctx.ap()[4 * g:4 * g + 4, :, tt * TW:(tt + 1) * TW]
                             .rearrange("c q t -> q c t"),
                             wait=(ctx_done if (tt == 0 and g == 0) else ()),
                             sem="ct"))
        for tk in ctk:
            p.w("tensor", wait=[tk])
        if stop_after == "sub3c":
            p.qfence("sync")
            p.qfence("gpsimd")
            p.bar()
            with nc.Block() as block:
                p.replay(block)
            stack.close()
            return nc
        for hh in range(2):
            xt = p.dma("sync", xin_ap(), d_y.ap()[2 * tt + hh, :, :],
                       wait=([st["xstg"]] if st["xstg"] else []), sem="sx")
            p.w("vector", wait=[xt])
            for oc in range(CH):
                if stop_after == "sub3d" and oc == 4:
                    break
                ob = (5, 2, 3)[oc % 3]
                wtile, wtok, slot = r_wout.load(d_wout.ap()[oc, :, :])
                for dch in range(CH):
                    mm(PS[ob][:, 0:HW], wtile[:, dch * 128:(dch + 1) * 128],
                       CTXT[:, dch * TW + hh * HW: dch * TW + hh * HW + HW],
                       start=(dch == 0), stop=(dch == CH - 1),
                       wait=([wtok] + ps_wait(ob)) if dch == 0 else (),
                       inc=([("tv", 1)] if dch == CH - 1 else ()))
                wo_t = p.tok("tv")
                r_wout.done(slot, wo_t)
                xs = XSTG[:, oc * HW:(oc + 1) * HW]
                p.w("vector", "scalar_tensor_tensor",
                    (xs, PS[ob][:, 0:HW], OUTB[:, oc:oc + 1], xs, OP.add, OP.add),
                    wait=[wo_t], inc=[("vv", 1)])
                psg[ob] = p.tok("vv")
            if stop_after == "sub3d":
                p.qfence("sync")
                p.qfence("gpsimd")
                p.bar()
                with nc.Block() as block:
                    p.replay(block)
                stack.close()
                return nc
            p.bar()
            if stop_after == "sub3":
                st["xstg"] = p.dma("sync", d_y.ap()[2 * tt + hh, :, :], xin_ap(),
                                   wait=[p.tok("vv")], sem="sy")
                continue
            phase_A(tt, hh, full_mode=False)
            st["xstg"] = p.dma("sync", d_y.ap()[2 * tt + hh, :, :], xin_ap(),
                               wait=[p.tok("vv")], sem="sy")
            p.bar()
        if stop_after == "sub3":
            continue
        rstd_tm(tt)
        ffn_tt(tt)
        p.bar()

    p.qfence("sync")
    p.qfence("gpsimd")
    p.bar()
    with nc.Block() as block:
        p.replay(block)
    stack.close()
    return nc


def _layout_inputs(inputs):
    f = np.float32
    x = np.asarray(inputs["L0_attn_a_x"], dtype=f)
    Wq = np.asarray(inputs["L0_attn_a_Wq"], dtype=f)
    Wk = np.asarray(inputs["L0_attn_a_Wk"], dtype=f)
    Wv = np.asarray(inputs["L0_attn_a_Wv"], dtype=f)
    Wo = np.asarray(inputs["L0_attn_a_Wo"], dtype=f)
    Wqkv = np.asarray(inputs["L0_attn_b_Wqkv_w"], dtype=f)
    qkvb = np.asarray(inputs["L0_attn_b_Wqkv_b"], dtype=f)
    outw = np.asarray(inputs["L0_attn_b_out_w"], dtype=f)
    outb = np.asarray(inputs["L0_attn_b_out_b"], dtype=f)
    Am = np.asarray(inputs["L0_ffn_A"], dtype=f)
    Bm = np.asarray(inputs["L0_ffn_B"], dtype=f)
    Wr = np.asarray(inputs["L0_ffn_Wr"], dtype=f)
    br = np.asarray(inputs["L0_ffn_br"], dtype=f)

    def lhs_blk(WT, nech):
        E = WT.shape[1]
        return np.ascontiguousarray(
            WT.reshape(CH, 128, nech, E // nech).transpose(2, 1, 0, 3)
            .reshape(nech, 128, CH * (E // nech)))

    wqkT = lhs_blk(np.ascontiguousarray(np.concatenate([Wq.T, Wk.T], axis=1)), 2)
    wvT = np.ascontiguousarray(Wv.T.reshape(CH, 128, 4, 512).transpose(2, 0, 1, 3))
    woT = lhs_blk(np.ascontiguousarray(Wo.T), CH)
    WqkvT = np.ascontiguousarray(Wqkv.T)
    wqkvT = lhs_blk(np.ascontiguousarray(WqkvT[:, 0:4096]), 32)
    wqkvv = np.ascontiguousarray(WqkvT[:, 4096:6144].reshape(CH, 128, 2048))
    woutT = lhs_blk(np.ascontiguousarray(outw.T), CH)
    AT = np.ascontiguousarray(Am.transpose(2, 0, 1).reshape(D, 512))
    atT = lhs_blk(AT, 4)
    BT = np.ascontiguousarray(Bm.transpose(0, 2, 1).reshape(512, D))
    btT = np.ascontiguousarray(
        BT.reshape(4, 128, CH, 128).transpose(2, 1, 0, 3).reshape(CH, 128, 512))
    wrT = np.ascontiguousarray(Wr.T.reshape(CH, 128, 8).transpose(1, 0, 2).reshape(128, CH * 8))
    qkvbrow = np.ascontiguousarray(qkvb.reshape(1, 6144))
    outb2 = np.ascontiguousarray(outb.reshape(16, 128).T)
    br2 = np.ascontiguousarray(br.reshape(8, 1))

    shared = {
        "wqk": wqkT, "wv": wvT, "wo": woT, "wqkv": wqkvT, "wqkvv": wqkvv,
        "wout": woutT, "at": atT, "bt": btT, "wr": wrT,
        "qkvbrow": qkvbrow, "outb2": outb2, "br": br2,
    }
    in_maps = []
    for c in range(8):
        xf = np.ascontiguousarray(x[c].T)
        xblk = np.ascontiguousarray(
            xf.reshape(CH, 128, NTT * 2, HW).transpose(2, 1, 0, 3)
            .reshape(NTT * 2, 128, CH * HW))
        m = dict(shared)
        m["x"] = xblk
        in_maps.append(m)
    return in_maps


_NC_CACHE = {}


def kernel(**inputs):
    in_maps = _layout_inputs(inputs)
    key = "full"
    if key not in _NC_CACHE:
        _NC_CACHE[key] = build_nc()
    nc = _NC_CACHE[key]
    res = run_bass_kernel_spmd(nc, in_maps, core_ids=list(range(8)))
    outs = []
    for c in range(8):
        yblk = res.results[c]["y"].reshape(8, 128, CH, HW)
        xf = yblk.transpose(2, 1, 0, 3).reshape(D, T)
        outs.append(np.ascontiguousarray(xf.T))
    return np.stack(outs, axis=0).astype(np.float32)


# version 17
# speedup vs baseline: 1.0240x; 1.0221x over previous
import numpy as np
from contextlib import ExitStack

import concourse.bass as bass
import concourse.mybir as mybir
from concourse.bass_utils import run_bass_kernel_spmd
from concourse.bass_core_rust import IndirectOffsetOnAxis

F32 = mybir.dt.float32
BF16 = mybir.dt.bfloat16
U32 = mybir.dt.uint32
U8 = mybir.dt.uint8
OP = mybir.AluOpType
AF = mybir.ActivationFunctionType
AX = mybir.AxisListType.X

D = 2048
T = 2048
CH = 16
NTT = 4
TW = 512
HW = 256
HB = 16
SMS = float(1.0 / np.sqrt(128.0))
EPS = 1e-5
ENGINES = ("sync", "gpsimd", "scalar", "vector", "tensor")


class Prog:
    def __init__(self, nc):
        self.nc = nc
        self.cmd = {e: [] for e in ENGINES}
        self.cnt = {}
        self.sem = {}
        self.nbar = 0
        self.qsems = {}

    def mksem(self, stack, *names):
        self.stack = stack
        for nm in names:
            self.sem[nm] = stack.enter_context(self.nc.semaphore(nm))
            self.cnt[nm] = 0

    def sem_for(self, nm):
        if nm not in self.sem:
            self.sem[nm] = self.stack.enter_context(self.nc.semaphore(nm))
            self.cnt[nm] = 0
        return nm

    def w(self, eng, op=None, args=(), kwargs=None, wait=(), inc=()):
        for nm, n in inc:
            self.cnt[nm] += n
        self.cmd[eng].append((op, tuple(args), dict(kwargs or {}), tuple(wait), tuple(inc)))
        if eng == "vector" and op not in (None, "drain"):
            out_ap = (kwargs or {}).get("out", args[0] if args else None)
            try:
                small = out_ap is not None and out_ap.free_nbytes <= 512
            except Exception:
                small = True
            if small:
                self.cmd[eng].append(("drain", (), {}, (), ()))

    def tok(self, nm):
        return (nm, self.cnt[nm])

    def bar(self):
        self.nbar += 1
        k = self.nbar
        for e in ENGINES:
            self.cnt["b_" + e] += 1
            if e in ("sync",):
                self.cmd[e].append((None, (), {}, (), (("b_" + e, 1),)))
            else:
                self.cmd[e].append(("drain", (), {}, (), (("b_" + e, 1),)))
        for e in ENGINES:
            for o in ENGINES:
                if o != e:
                    self.w(e, wait=[("b_" + o, k)])

    def dma(self, q, out_ap, in_ap, wait=(), sem=None, **kw):
        semn = self.sem_for(sem) if sem else ("gd" if q == "gpsimd" else "qd")
        kwargs = {"out": out_ap, "in_": in_ap}
        kwargs.update(kw)
        self.qsems.setdefault(q, set()).add(semn)
        self.w(q, "dma_start", kwargs=kwargs, wait=wait, inc=[(semn, 16)])
        return self.tok(semn)

    def qfence(self, q):
        base = "gd" if q == "gpsimd" else "qd"
        sems = set(self.qsems.get(q, ())) | {base}
        self.w(q, wait=[self.tok(s) for s in sorted(sems)])

    def replay(self, block):
        sem = self.sem

        def mk(e):
            def run(eng):
                for op, args, kwargs, wait, inc in self.cmd[e]:
                    for nm, v in wait:
                        eng.wait_ge(sem[nm], v)
                    if op is not None:
                        inst = getattr(eng, op)(*args, **kwargs)
                        for nm, n in inc:
                            inst.then_inc(sem[nm], n)
                    else:
                        for nm, n in inc:
                            eng.sem_inc(sem[nm], n)
            return run

        block.sync(mk("sync"))
        block.gpsimd(mk("gpsimd"))
        block.scalar(mk("scalar"))
        block.vector(mk("vector"))
        block.tensor(mk("tensor"))


def build_nc(stop_after=None):
    nc = bass.Bass("TRN2", target_bir_lowering=False)

    d_x = nc.dram_tensor("x", [NTT * 2, 128, CH * HW], F32, kind="ExternalInput")
    d_y = nc.dram_tensor("y", [NTT * 2, 128, CH * HW], F32, kind="ExternalOutput")
    d_wqk = nc.dram_tensor("wqk", [2, 128, CH * 128], F32, kind="ExternalInput")
    d_wv = nc.dram_tensor("wv", [4, CH, 128, 512], F32, kind="ExternalInput")
    d_wo = nc.dram_tensor("wo", [CH, 128, CH * 128], F32, kind="ExternalInput")
    d_wqkv = nc.dram_tensor("wqkv", [32, 128, CH * 128], F32, kind="ExternalInput")
    d_wqkvv = nc.dram_tensor("wqkvv", [CH, 128, 2048], F32, kind="ExternalInput")
    d_wout = nc.dram_tensor("wout", [CH, 128, CH * 128], F32, kind="ExternalInput")
    d_at = nc.dram_tensor("at", [4, 128, CH * 128], F32, kind="ExternalInput")
    d_bt = nc.dram_tensor("bt", [CH, 128, 4 * 128], F32, kind="ExternalInput")
    d_wr = nc.dram_tensor("wr", [128, CH * 8], F32, kind="ExternalInput")
    d_qkvbrow = nc.dram_tensor("qkvbrow", [1, 6144], F32, kind="ExternalInput")
    d_outb2 = nc.dram_tensor("outb2", [128, 16], F32, kind="ExternalInput")
    d_br = nc.dram_tensor("br", [8, 1], F32, kind="ExternalInput")
    d_dbg = nc.dram_tensor("dbg", [5, 128, 2048], F32, kind="ExternalOutput") \
        if stop_after in ("probe1", "probe0", "probe2", "probe3", "probe4",
                          "probe5") else None
    d_vtm = nc.dram_tensor("vtm_s", [T, D], BF16)
    d_keys = nc.dram_tensor("keys_s", [128, T], F32)
    d_ctx = nc.dram_tensor("ctx_s", [HB, 128, T], BF16)

    A = nc.alloc_sbuf_tensor
    XB = A("XB", [128, CH * T], BF16)
    XSTG = A("XSTG", [128, CH * HW], F32)
    SCR = A("SCR", [128, 32 * 1024], U8)
    WBUF = A("WBUF", [128, 25 * 1024], U8)
    P0 = A("P0", [128, 28 * 1024], U8)
    RSTD = A("RSTD", [128, T], F32)
    QA = A("QA", [128, T], F32)
    KAT = A("KAT", [128, T], F32)
    VGALL = A("VGALL", [128, 64 * 16], BF16)
    IDF = A("IDF", [128, 128], F32)
    ONE1 = A("ONE1", [1, 128], BF16)
    ONEC = A("ONEC", [128, 1], BF16)
    EPH = A("EPH", [2, 128], BF16)
    REP8 = A("REP8", [8, 128], BF16)
    G8A = A("G8A", [8, 512], BF16)
    SLT = A("SLT", [128, 256], BF16)
    SLB = A("SLB", [128, 256], BF16)
    VST = A("VST", [128, 448], BF16)
    VSB = A("VSB", [128, 448], BF16)
    TV = A("TV", [128, 64], F32)
    TIU = A("TIU", [128, 64], U32)
    TIF = A("TIF", [128, 64], F32)
    IXB = A("IXB", [128, 128], F32)
    TBALL = A("TBALL", [128, 128], F32)
    IXVU = A("IXVU", [128, 64], U32)
    IXKU = A("IXKU", [128, 64], U32)
    VALS = A("VALS", [128, 64], F32)
    VB = A("VB", [128, 64], BF16)
    SCL = A("SCL", [128, 64], F32)
    EXB = A("EXB", [128, 128], BF16)
    ZRO = A("ZRO", [128, 1], F32)
    TINY = A("TINY", [1, TW], BF16)
    PVEC = A("PVEC", [128, 1], F32)
    H2K = A("H2K", [128, 1], F32)
    NEGC = A("NEGC", [128, 1], F32)
    NEGCB = A("NEGCB", [128, 4], F32)
    CS1 = A("CS1", [128, 1], F32)
    CS2 = A("CS2", [128, 1], F32)
    RSTM = A("RSTM", [128, CH], F32)
    OUTB = A("OUTB", [128, 16], F32)
    GTM = A("GTM", [128, 8], F32)
    GSC = A("GSC", [128, 8], F32)
    PTB = A("PTB", [128, 4 * TW], BF16)
    VSTG = A("VSTG", [128, 2 * TW], BF16)
    RSB = A("RSB", [128, TW], BF16)
    PS = [nc.alloc_psum_tensor("ps%d" % i, [128, 512], F32) for i in range(8)]

    SCR_BF = SCR[:, :].bitcast(BF16)
    SCR_F32 = SCR[:, :].bitcast(F32)
    SCR_U32 = SCR[:, :].bitcast(U32)
    WB_BF = WBUF[:, :].bitcast(BF16)
    P0_F32 = P0[:, :].bitcast(F32)
    P0_BF = P0[:, :].bitcast(BF16)
    QA_BF = QA[:, :].bitcast(BF16)
    KAT_BF = KAT[:, :].bitcast(BF16)
    WTR = XSTG[:, :].bitcast(BF16)
    BRS = P0[:, :].bitcast(F32)[0:8, 7056:7057]
    LGT = SCR_F32[0:8, 1024:1536]
    GFM = SCR_F32[0:8, 1536:2048]
    G8H = SCR_BF[0:8, 4096:4608]
    G8L = SCR_BF[0:8, 4608:5120]

    def MQ():
        return QA_BF[:, 0:2048]

    def MQC():
        return QA_BF[:, 2048:4096]

    def QHI():
        return KAT_BF[:, 0:2048]

    def QLO():
        return KAT_BF[:, 2048:4096]

    def RH(tt):
        return P0_BF[0:1, tt * TW:(tt + 1) * TW]

    def RL(tt):
        return P0_BF[0:1, 2048 + tt * TW: 2048 + (tt + 1) * TW]

    def MH_(tt):
        return P0_BF[0:1, 4096 + tt * TW: 4096 + (tt + 1) * TW]

    def ML_(tt):
        return P0_BF[0:1, 6144 + tt * TW: 6144 + (tt + 1) * TW]

    def SQVB(tt):
        return P0_BF[0:1, 8192 + tt * TW: 8192 + (tt + 1) * TW]

    RHB = P0_BF[0:1, 10240:10752]
    RLB = P0_BF[0:1, 10752:11264]
    QKVBP = P0_BF[0:1, 13344:14112]
    SCRA = P0_F32[0:1, 5632:5888]
    SCRB = P0_F32[0:1, 5888:6144]
    SCRW = P0_F32[0:1, 6144:6400]
    SCRW2 = P0_F32[0:1, 6400:6656]
    CELLS = P0_F32[0:1, 6656:6672]
    EPSC = P0_F32[0:1, 6672:6673]

    p = Prog(nc)
    stack = ExitStack()
    p.mksem(stack, "qd", "gd", "tv", "vv", "sv",
             *["b_" + e for e in ENGINES])
    psg = [None] * 8
    st = {"xstg": None, "vstg": [None, None]}

    def ps_wait(i):
        return [psg[i]] if psg[i] else []

    def mm(out_ap, lhsT, rhs, start, stop, wait=(), inc=()):
        p.w("tensor", "matmul", (out_ap, lhsT, rhs), {"start": start, "stop": stop},
            wait=wait, inc=inc)

    IOTA = TBALL[:, :].bitcast(U32)
    IOTA5 = VGALL[:, :].bitcast(U32)[:, 0:512]
    p.w("gpsimd", "iota", (IOTA[:, 0:1], [[0, 1]]), {"channel_multiplier": 1})
    p.bar()
    p.w("vector", "tensor_copy", (PVEC[:, :], IOTA[:, 0:1]))
    p.bar()
    p.w("vector", "tensor_scalar", (H2K[:, :], PVEC[:, :], 2048.0, None, OP.mult))
    p.w("vector", "memset", (EPSC, float(EPS)))
    p.w("vector", "memset", (ONE1[:, :], 1.0))
    p.w("vector", "memset", (ONEC[:, :], 1.0))
    p.w("gpsimd", "iota", (IOTA[:, 0:128], [[1, 128]]), {"channel_multiplier": 0})
    p.bar()
    p.w("vector", "tensor_copy", (QA[:, 0:128], IOTA[:, 0:128]))
    p.bar()
    p.w("vector", "tensor_scalar", (IDF[:, :], QA[:, 0:128], PVEC[:, 0:1], None,
                                    OP.is_equal))
    p.w("gpsimd", "iota", (IOTA[:, 0:128], [[1, 2], [0, 64]]), {"channel_multiplier": 0})
    p.bar()
    p.w("vector", "tensor_copy", (QA[:, 0:128], IOTA[:, 0:128]))
    p.bar()
    p.w("vector", "tensor_scalar", (EPH[:, :], QA[0:2, 0:128], PVEC[0:2, 0:1], None,
                                    OP.is_equal))
    p.w("gpsimd", "iota", (IOTA[:, 0:128], [[1, 8], [0, 16]]), {"channel_multiplier": 0})
    p.bar()
    p.w("vector", "tensor_copy", (QA[:, 0:128], IOTA[:, 0:128]))
    p.bar()
    p.w("vector", "tensor_scalar", (REP8[:, :], QA[0:8, 0:128], PVEC[0:8, 0:1], None,
                                    OP.is_equal))
    p.w("gpsimd", "iota", (IOTA5[:, 0:512], [[1, 8], [0, 64]]), {"channel_multiplier": 0})
    p.bar()
    p.w("vector", "tensor_copy", (QA[:, 128:640], IOTA5[:, 0:512]))
    p.bar()
    p.w("vector", "tensor_scalar", (G8A[:, :], QA[0:8, 128:640], PVEC[0:8, 0:1], None,
                                    OP.is_equal))
    p.w("gpsimd", "iota", (IOTA[:, 0:128], [[32, 4], [0, 32]]), {"channel_multiplier": 0})
    p.bar()
    p.w("vector", "tensor_copy", (QA[:, 0:128], IOTA[:, 0:128]))
    p.bar()
    p.w("vector", "tensor_scalar", (EXB[:, :], QA[:, 0:128], PVEC[:, 0:1], None,
                                    OP.is_equal))
    p.w("vector", "memset", (ZRO[:, :], 0.0))
    p.w("vector", "memset", (TINY[:, :], 1e-20))
    p.w("vector", "memset", (SLT[:, :], 0.0))
    p.w("vector", "memset", (SLT[0:32, 128:129], 1.0))
    p.w("vector", "memset", (SLT[64:96, 129:130], 1.0))
    p.w("vector", "memset", (SLB[:, :], 0.0))
    p.w("vector", "memset", (SLB[32:64, 128:129], 1.0))
    p.w("vector", "memset", (SLB[96:128, 129:130], 1.0))
    p.w("vector", "memset", (VST[:, :], 0.0))
    p.w("vector", "memset", (VSB[:, :], 0.0))
    p.dma("sync", OUTB[:, :], d_outb2.ap()[:, :])
    p.dma("sync", BRS[:, :], d_br.ap()[:, :])
    p.qfence("sync")
    p.bar()

    XC = SCR_BF[:, 8192:8192 + CH * HW]
    SQH = SCR_BF[:, 12288:12288 + CH * HW]

    def phase_A(tt, hh, full_mode, wait_x=()):
        h0 = hh * HW
        p.w("scalar", "activation", (SQH, XSTG[:, :], AF.Square), wait=wait_x,
            inc=[("sv", 1)])
        sq_t = p.tok("sv")
        p.w("vector", "tensor_copy", (XC, XSTG[:, :]), wait=wait_x, inc=[("vv", 1)])
        cast_t = p.tok("vv")
        for dch in range(CH):
            mm(PS[0][0:1, 0:HW], ONEC[:, :], XC[:, dch * HW:(dch + 1) * HW],
               start=(dch == 0), stop=(dch == CH - 1),
               wait=([cast_t] + ps_wait(0)) if dch == 0 else ())
        for dch in range(CH):
            mm(PS[1][0:1, 0:HW], ONEC[:, :], SQH[:, dch * HW:(dch + 1) * HW],
               start=(dch == 0), stop=(dch == CH - 1),
               wait=([sq_t] + ps_wait(1)) if dch == 0 else (),
               inc=[("tv", 1)] if dch == CH - 1 else ())
        mm_t = p.tok("tv")
        p.w("vector", "tensor_scalar", (SCRA, PS[0][0:1, 0:HW], 1.0 / D, None, OP.mult),
            wait=[mm_t])
        p.w("vector", "tensor_tensor", (SCRB, SCRA, SCRA, OP.mult))
        p.w("vector", "scalar_tensor_tensor",
            (SCRB, PS[1][0:1, 0:HW], 1.0 / D, SCRB, OP.mult, OP.subtract), inc=[("vv", 1)])
        psg[0] = psg[1] = p.tok("vv")
        p.w("scalar", "activation", (SCRB, SCRB, AF.Sqrt), {"bias": EPSC, "scale": 1.0},
            wait=[p.tok("vv")], inc=[("sv", 1)])
        p.w("vector", "tensor_copy", (SQVB(tt)[0:1, h0:h0 + HW], SCRB), wait=[p.tok("sv")])
        p.w("vector", "reciprocal", (SCRB, SCRB))
        p.w("vector", "tensor_copy", (RH(tt)[0:1, h0:h0 + HW], SCRB))
        p.w("vector", "scalar_tensor_tensor",
            (RL(tt)[0:1, h0:h0 + HW], SCRB, 1.0, RH(tt)[0:1, h0:h0 + HW],
             OP.mult, OP.subtract))
        p.w("vector", "tensor_copy", (MH_(tt)[0:1, h0:h0 + HW], SCRA))
        p.w("vector", "scalar_tensor_tensor",
            (ML_(tt)[0:1, h0:h0 + HW], SCRA, 1.0, MH_(tt)[0:1, h0:h0 + HW],
             OP.mult, OP.subtract), inc=[("vv", 1)])
        rows_t = p.tok("vv")
        mm(PS[2][:, 0:HW], ONE1[:, :], RH(tt)[0:1, h0:h0 + HW], start=True, stop=False,
           wait=[rows_t] + ps_wait(2))
        mm(PS[2][:, 0:HW], ONE1[:, :], RL(tt)[0:1, h0:h0 + HW], start=False, stop=True,
           inc=[("tv", 1)])
        p.w("vector", "tensor_copy",
            (RSTD[:, tt * TW + h0: tt * TW + h0 + HW], PS[2][:, 0:HW]),
            wait=[p.tok("tv")], inc=[("vv", 1)])
        rstd_t = p.tok("vv")
        mm(PS[2][:, 0:HW], ONE1[:, :], MH_(tt)[0:1, h0:h0 + HW], start=True, stop=False,
           wait=[rstd_t])
        mm(PS[2][:, 0:HW], ONE1[:, :], ML_(tt)[0:1, h0:h0 + HW], start=False, stop=True,
           inc=[("tv", 1)])
        mrep_t = p.tok("tv")
        if full_mode:
            xb_out = XB[:, :].rearrange("q (c t) -> q c t", c=CH)[
                :, :, tt * TW + h0: tt * TW + h0 + HW]
        else:
            xb_out = XB[:, 0:CH * TW].rearrange("q (c t) -> q c t", c=CH)[:, :, h0:h0 + HW]
        xin = XSTG[:, :].rearrange("q (c t) -> q c t", c=CH)
        mrep_b = PS[2][:, 0:HW].unsqueeze(1).to_broadcast([128, CH, HW])
        p.w("vector", "tensor_tensor", (xb_out, xin, mrep_b, OP.subtract),
            wait=[mrep_t], inc=[("vv", 1)])
        psg[2] = p.tok("vv")
        return p.tok("vv")

    def rstd_tm(tt):
        for j in range(4):
            mm(PS[3][:, j:j + 1], RH(tt)[0:1, j * 128:(j + 1) * 128], ONE1[0:1, 0:1],
               start=True, stop=False, wait=(ps_wait(3) if j == 0 else ()))
            mm(PS[3][:, j:j + 1], RL(tt)[0:1, j * 128:(j + 1) * 128], ONE1[0:1, 0:1],
               start=False, stop=True, inc=([("tv", 1)] if j == 3 else ()))
        p.w("vector", "tensor_copy", (RSTM[:, tt * 4:tt * 4 + 4], PS[3][:, 0:4]),
            wait=[p.tok("tv")], inc=[("vv", 1)])
        psg[3] = p.tok("vv")

    class Ring:
        def __init__(self, n, cols, off, pfx):
            self.n, self.cols, self.off, self.pfx = n, cols, off, pfx
            self.i = 0
            self.free = [None] * n

        def load(self, src):
            s = self.i % self.n
            self.i += 1
            dst = WB_BF[:, self.off + s * self.cols: self.off + (s + 1) * self.cols]
            wl = [self.free[s]] if self.free[s] is not None else []
            tok = p.dma("gpsimd", dst, src, wait=wl, sem="%s%d" % (self.pfx, s))
            return dst, tok, s

        def done(self, s, tok):
            self.free[s] = tok

    def xin_ap():
        return XSTG[:, :]

    r_qk = Ring(2, 2048, 0, "rqk")
    r_wv = Ring(16, 512, 4096, "rwv")
    qk_tiles = [r_qk.load(d_wqk.ap()[e, :, :]) for e in range(2)]

    for tt in range(NTT):
        for hh in range(2):
            xt = p.dma("sync", xin_ap(), d_x.ap()[2 * tt + hh, :, :],
                       wait=([st["xstg"]] if st["xstg"] else []), sem="sx")
            phase_A(tt, hh, full_mode=False, wait_x=[xt])
            p.bar()
        rstd_tm(tt)
        if stop_after == "probe3" and tt == 0:
            p.bar()
            wtile0d, _, _ = qk_tiles[0]
            for sl in range(4):
                p.dma("gpsimd", d_dbg.ap()[sl, :, :],
                      XB[:, sl * 2048:(sl + 1) * 2048])
            p.dma("gpsimd", d_dbg.ap()[4, :, :], wtile0d[:, 0:2048])
            p.qfence("sync")
            p.qfence("gpsimd")
            p.bar()
            with nc.Block() as block:
                p.replay(block)
            stack.close()
            return nc
        if stop_after == "probe2" and tt == 0:
            p.bar()
            wtile0c, _, _ = qk_tiles[0]
            for li, ln_ in enumerate([2, 4, 8, 16]):
                for dch in range(ln_):
                    mm(PS[7][:, :], wtile0c[:, dch * 128:(dch + 1) * 128],
                       XB[:, dch * TW:(dch + 1) * TW],
                       start=(dch == 0), stop=(dch == ln_ - 1),
                       wait=(ps_wait(7) if dch == 0 else ()),
                       inc=([("tv", 1)] if dch == ln_ - 1 else ()))
                p.w("vector", "tensor_copy", (KAT[:, 1024:1536], PS[7][:, :]),
                    wait=[p.tok("tv")], inc=[("vv", 1)])
                psg[7] = p.tok("vv")
                p.dma("sync", d_dbg.ap()[4, :, 512 * li:512 * (li + 1)],
                      KAT[:, 1024:1536], wait=[p.tok("vv")])
            p.dma("gpsimd", d_dbg.ap()[0, :, :], XB[:, 0:2048])
            p.qfence("sync")
            p.qfence("gpsimd")
            p.bar()
            with nc.Block() as block:
                p.replay(block)
            stack.close()
            return nc
        if stop_after == "probe0" and tt == 0:
            p.dma("gpsimd", d_dbg.ap()[3, :, 0:1024], XB[:, 0:1024])
            wtile0, wtok0, _ = qk_tiles[0]
            for dch in range(CH):
                mm(PS[6][:, :], wtile0[:, dch * 128:(dch + 1) * 128],
                   XB[:, dch * TW:(dch + 1) * TW],
                   start=(dch == 0), stop=(dch == CH - 1),
                   inc=([("tv", 1)] if dch == CH - 1 else ()))
            p.w("vector", "tensor_mul",
                (QLO().bitcast(F32)[:, 0:512] if False else KAT[:, 1536:2048],
                 PS[6][:, :], RSTD[:, 0:512]), wait=[p.tok("tv")], inc=[("vv", 1)])
            psg[6] = p.tok("vv")
            p.dma("sync", d_dbg.ap()[3, :, 1024:1536], KAT[:, 1536:2048],
                  wait=[p.tok("vv")])
            wtile0b, _, _ = qk_tiles[0]
            p.dma("gpsimd", d_dbg.ap()[3, :, 1536:2048], wtile0b[:, 0:512])
            for li, ln_ in enumerate([2, 4, 8, 16]):
                for dch in range(ln_):
                    mm(PS[7][:, :], wtile0b[:, dch * 128:(dch + 1) * 128],
                       XB[:, dch * TW:(dch + 1) * TW],
                       start=(dch == 0), stop=(dch == ln_ - 1),
                       wait=(ps_wait(7) if dch == 0 else ()),
                       inc=([("tv", 1)] if dch == ln_ - 1 else ()))
                p.w("vector", "tensor_copy", (KAT[:, 1024:1536], PS[7][:, :]),
                    wait=[p.tok("tv")], inc=[("vv", 1)])
                psg[7] = p.tok("vv")
                p.dma("sync", d_dbg.ap()[4, :, 512 * li:512 * (li + 1)],
                      KAT[:, 1024:1536], wait=[p.tok("vv")])
        for e in range(2):
            wtile, wtok, _ = qk_tiles[e]
            for dch in range(CH):
                mm(PS[4][:, :], wtile[:, dch * 128:(dch + 1) * 128],
                   XB[:, dch * TW:(dch + 1) * TW],
                   start=(dch == 0), stop=(dch == CH - 1),
                   wait=([wtok] + ps_wait(4)) if dch == 0 else (),
                   inc=([("tv", 1)] if dch == CH - 1 else ()))
            dst = QA if e == 0 else KAT
            p.w("vector", "tensor_mul",
                (dst[:, tt * TW:(tt + 1) * TW], PS[4][:, :], RSTD[:, tt * TW:(tt + 1) * TW]),
                wait=[p.tok("tv")], inc=[("vv", 1)])
            psg[4] = p.tok("vv")
        for ecg in range(4):
            tiles = [r_wv.load(d_wv.ap()[ecg, dch, :, :]) for dch in range(CH)]
            for j in range(4):
                vb_ = (5, 6, 7)[j % 3]
                tc = tt * 4 + j
                for dch in range(CH):
                    wt, wtk, _ = tiles[dch]
                    mm(PS[vb_][:, :], XB[:, dch * TW + j * 128: dch * TW + (j + 1) * 128], wt,
                       start=(dch == 0), stop=(dch == CH - 1),
                       wait=(([wtk] if j == 0 else []) + (ps_wait(vb_) if dch == 0 else [])),
                       inc=([("tv", 1)] if dch == CH - 1 else ()))
                vs = VSTG[:, (j % 2) * TW:(j % 2 + 1) * TW]
                vw = [p.tok("tv")] + ([st["vstg"][j % 2]] if st["vstg"][j % 2] else [])
                p.w("vector", "tensor_scalar",
                    (vs, PS[vb_][:, :], RSTM[:, tc:tc + 1], None, OP.mult),
                    wait=vw, inc=[("vv", 1)])
                psg[vb_] = p.tok("vv")
                st["vstg"][j % 2] = p.dma(
                    "sync", d_vtm.ap()[tc * 128:(tc + 1) * 128, ecg * 512:(ecg + 1) * 512],
                    vs, wait=[p.tok("vv")], sem="vs%d" % (j % 2))
            lasttok = p.tok("tv")
            for dch in range(CH):
                r_wv.done(tiles[dch][2], lasttok)
        p.bar()
        if stop_after == "probe0" and tt == 0:
            p.w("gpsimd", wait=[p.tok("vs0"), p.tok("vs1")])
            p.dma("gpsimd", d_dbg.ap()[0, :, :], XB[:, 0:2048])
            p.dma("sync", d_dbg.ap()[1, :, 0:512], RSTD[:, 0:512])
            p.dma("sync", d_dbg.ap()[1, :, 512:516], RSTM[:, 0:4])
            p.dma("sync", d_dbg.ap()[2, :, 0:512], QA[:, 0:512])
            p.dma("sync", d_dbg.ap()[2, :, 512:1024], KAT[:, 0:512])
            p.dma("gpsimd", d_dbg.ap()[4, :, 0:1024], d_vtm.ap()[0:128, 0:1024])
            p.qfence("sync")
            p.qfence("gpsimd")
            p.bar()
            with nc.Block() as block:
                p.replay(block)
            stack.close()
            return nc

    p.dma("sync", d_keys.ap()[:, :], KAT[:, :])
    p.qfence("sync")
    p.bar()

    TOKU = SCR_U32[:, 0:2048]
    PKT = SCR_F32[:, 2048:4096]
    PKB = SCR_F32[:, 4096:6144]
    MSK = SCR[:, 24576:26624]
    p.w("gpsimd", "iota", (TOKU, [[1, T]]), {"channel_multiplier": 0})
    p.bar()
    p.w("vector", "tensor_scalar", (MSK, KAT[:, :], 0.0, None, OP.is_le))
    p.w("vector", "tensor_scalar",
        (PKT.bitcast(U32), KAT[:, :].bitcast(U32), 0xFFFFF800, None, OP.bitwise_and))
    p.w("vector", "tensor_tensor", (PKT.bitcast(U32), PKT.bitcast(U32), TOKU, OP.bitwise_or))
    p.w("vector", "tensor_scalar",
        (PKB.bitcast(U32), KAT[:, :].bitcast(U32), 0x80000000, None, OP.bitwise_xor))
    p.w("vector", "tensor_scalar",
        (PKB.bitcast(U32), PKB.bitcast(U32), 0xFFFFF800, None, OP.bitwise_and))
    p.w("vector", "tensor_tensor", (PKB.bitcast(U32), PKB.bitcast(U32), TOKU, OP.bitwise_or))
    p.w("vector", "memset", (TOKU, 0))
    p.w("vector", "copy_predicated", (PKT.bitcast(U32), MSK, TOKU))
    p.w("vector", "tensor_scalar", (MSK, KAT[:, :], 0.0, None, OP.is_ge))
    p.w("vector", "copy_predicated", (PKB.bitcast(U32), MSK, TOKU))
    for i in range(4):
        p.w("vector", "max", kwargs={"out": TV[:, 8 * i:8 * i + 8], "in_": PKT})
        p.w("vector", "drain")
        p.w("vector", "match_replace",
            kwargs={"out": PKT, "in_to_replace": TV[:, 8 * i:8 * i + 8],
                    "in_values": PKT, "imm_value": -1.0})
        p.w("vector", "drain")
    for i in range(4, 8):
        p.w("vector", "max", kwargs={"out": TV[:, 8 * i:8 * i + 8], "in_": PKB})
        p.w("vector", "drain")
        p.w("vector", "match_replace",
            kwargs={"out": PKB, "in_to_replace": TV[:, 8 * i:8 * i + 8],
                    "in_values": PKB, "imm_value": -1.0})
        p.w("vector", "drain")
    p.w("vector", "tensor_scalar",
        (TIU[:, :], TV[:, :].bitcast(U32), 2047, None, OP.bitwise_and))
    p.w("vector", "drain")
    p.w("vector", "tensor_copy", (TIF[:, :], TIU[:, :]))
    p.w("vector", "drain")
    p.w("vector", "scalar_tensor_tensor",
        (IXB[:, 0:64], TIF[:, :], 128.0, PVEC[:, 0:1].to_broadcast([128, 64]),
         OP.mult, OP.add))
    p.w("vector", "drain")
    p.w("vector", "scalar_tensor_tensor",
        (IXB[:, 64:128], TIF[:, :], 1.0, H2K[:, 0:1].to_broadcast([128, 64]),
         OP.mult, OP.add))
    p.w("vector", "drain", inc=[("vv", 1)])
    ix_t = p.tok("vv")
    p.w("tensor", "transpose", (PS[1][:, 0:128], IXB[:, :], IDF[:, :]),
        wait=[ix_t] + ps_wait(1), inc=[("tv", 1)])
    p.w("vector", "tensor_copy", (TBALL[:, :], PS[1][:, 0:128]), wait=[p.tok("tv")],
        inc=[("vv", 1)])
    psg[1] = p.tok("vv")
    tb_t = p.tok("vv")
    p.dma("sync", IXB[0:64, 0:64], TBALL[0:64, 0:128:2], wait=[tb_t],
          allow_slow_non_contiguous=True, sem="ix")
    p.dma("sync", IXB[64:128, 0:64], TBALL[0:64, 1:128:2], wait=[tb_t],
          allow_slow_non_contiguous=True, sem="ix")
    p.dma("sync", IXB[0:64, 64:128], TBALL[64:128, 0:128:2], wait=[tb_t],
          allow_slow_non_contiguous=True, sem="ix")
    tix = p.dma("sync", IXB[64:128, 64:128], TBALL[64:128, 1:128:2], wait=[tb_t],
                allow_slow_non_contiguous=True, sem="ix")
    p.bar()
    p.w("vector", "tensor_copy", (IXVU[:, :], IXB[:, 0:64]), wait=[tix])
    p.w("vector", "tensor_copy", (IXKU[:, :], IXB[:, 64:128]), inc=[("vv", 1)])
    ixu_t = p.tok("vv")
    if stop_after == "probe1":
        p.dma("sync", d_dbg.ap()[0, :, :], QA[:, :])
        p.dma("sync", d_dbg.ap()[4, :, :], KAT[:, :])
    p.w("vector", "tensor_copy", (SCR_BF[:, 12288:14336], QA[:, :]))
    p.w("vector", "scalar_tensor_tensor",
        (SCR_BF[:, 14336:16384], QA[:, :], 1.0, SCR_BF[:, 12288:14336],
         OP.mult, OP.subtract))
    p.w("vector", "tensor_scalar", (PKT, QA[:, :], 0.0, None, OP.is_gt))
    p.w("vector", "tensor_scalar", (PKB, QA[:, :], 0.0, None, OP.is_le), inc=[("vv", 1)])
    stg_t = p.tok("vv")
    p.w("vector", "tensor_copy", (QHI(), SCR_BF[:, 12288:14336]), wait=[stg_t])
    p.w("vector", "tensor_copy", (QLO(), SCR_BF[:, 14336:16384]))
    p.w("vector", "tensor_copy", (MQ(), PKT))
    p.w("vector", "tensor_copy", (MQC(), PKB), inc=[("vv", 1)])
    p.bar()
    for pr in range(64):
        p.w("gpsimd", "indirect_dma_start",
            (VALS[:, pr:pr + 1], None,
             bass.AP(d_keys, 0, [[1, T * 128], [1, 1]]),
             IndirectOffsetOnAxis(IXKU[:, pr:pr + 1], 0)),
            inc=[("gd", 16)])
    for pr in range(64):
        p.w("gpsimd", "indirect_dma_start",
            (VGALL[:, pr * 16:(pr + 1) * 16], None,
             bass.AP(d_vtm, 0, [[16, T * D // 16], [1, 16]]),
             IndirectOffsetOnAxis(IXVU[:, pr:pr + 1], 0)),
            inc=[("gd", 16)])
    p.qfence("gpsimd")
    p.bar()
    p.w("vector", "tensor_copy", (VB[:, :], VALS[:, :]), inc=[("vv", 1)])
    mm(PS[1][:, 0:64], EXB[:, :], VB[:, :], start=True, stop=True,
       wait=[p.tok("vv")] + ps_wait(1), inc=[("tv", 1)])
    p.w("vector", "tensor_tensor", (SCL[:, :], VALS[:, :], PS[1][:, 0:64],
                                    OP.subtract),
        wait=[p.tok("tv")], inc=[("vv", 1)])
    psg[1] = p.tok("vv")
    if stop_after == "probe1":
        p.dma("sync", d_dbg.ap()[1, :, 0:64], TIF[:, :])
        p.dma("sync", d_dbg.ap()[1, :, 64:128], VALS[:, :])
        p.dma("sync", d_dbg.ap()[1, :, 128:129], NEGC[:, :], allow_slow_non_contiguous=True)
        p.dma("sync", d_dbg.ap()[1, :, 129:193], TV[:, :])
        p.dma("gpsimd", d_dbg.ap()[2, :, 0:1024], VGALL[:, 0:1024])
        p.dma("gpsimd", d_dbg.ap()[3, :, 0:1024], d_vtm.ap()[0:128, 0:1024])
        p.qfence("sync")
        p.qfence("gpsimd")
        p.bar()
        with nc.Block() as block:
            p.replay(block)
        stack.close()
        return nc

    r_at = Ring(2, 2048, 4096, "rat")
    r_bt = Ring(8, 512, 8192, "rbt")
    r_wr = Ring(1, 128, 12288, "rwr")
    wr_tile, wr_tok, _ = r_wr.load(d_wr.ap()[:, :])
    HSB = SCR_BF[:, 0:4 * TW]

    def ffn_tt(tt):
        for dch in range(CH):
            mm(PS[0][0:8, :], wr_tile[:, dch * 8:(dch + 1) * 8],
               XB[:, dch * TW:(dch + 1) * TW],
               start=(dch == 0), stop=(dch == CH - 1),
               wait=([wr_tok] + ps_wait(0)) if dch == 0 else (),
               inc=([("tv", 1)] if dch == CH - 1 else ()))
        p.w("vector", "tensor_mul",
            (LGT[0:8, :], PS[0][0:8, :], RSTD[0:8, tt * TW:(tt + 1) * TW]),
            wait=[p.tok("tv")])
        p.w("vector", "tensor_scalar", (LGT[0:8, :], LGT[0:8, :], BRS[0:8, 0:1], None, OP.add),
            inc=[("vv", 1)])
        psg[0] = p.tok("vv")
        lg_t = p.tok("vv")
        for j in range(4):
            p.w("tensor", "transpose", (PS[1][0:128, 0:8], LGT[0:8, j * 128:(j + 1) * 128],
                                        IDF[0:8, 0:8]),
                wait=[lg_t] + ps_wait(1), inc=[("tv", 1)])
            p.w("vector", "tensor_copy", (GTM[:, :], PS[1][0:128, 0:8]), wait=[p.tok("tv")],
                inc=[("vv", 1)])
            psg[1] = p.tok("vv")
            p.w("vector", "reduce_max", (CS1[:, :], GTM[:, :], AX))
            p.w("vector", "tensor_scalar", (CS2[:, :], CS1[:, :], -1.0, None, OP.mult),
                inc=[("vv", 1)])
            p.w("scalar", "activation", (GSC[:, :], GTM[:, :], AF.Exp),
                {"bias": CS2[:, 0:1], "scale": 1.0, "accum_out": CS1[:, 0:1]},
                wait=[p.tok("vv")], inc=[("sv", 1)])
            p.w("vector", "reciprocal", (CS2[:, :], CS1[:, :]), wait=[p.tok("sv")])
            p.w("vector", "reduce_max", (CS1[:, :], GTM[:, :], AX))
            p.w("vector", "tensor_scalar", (GSC[:, :], GTM[:, :], CS1[:, 0:1], None,
                                            OP.is_equal))
            p.w("vector", "tensor_scalar", (GSC[:, :], GSC[:, :], CS2[:, 0:1], 1.0 / 64.0,
                                            OP.mult, OP.mult), inc=[("vv", 1)])
            g_t = p.tok("vv")
            p.w("tensor", "transpose", (PS[1][0:8, 0:128], GSC[:, :], IDF[:, :]),
                wait=[g_t] + ps_wait(1), inc=[("tv", 1)])
            p.w("vector", "tensor_copy", (GFM[0:8, j * 128:(j + 1) * 128], PS[1][0:8, 0:128]),
                wait=[p.tok("tv")], inc=[("vv", 1)])
            psg[1] = p.tok("vv")
        p.w("vector", "tensor_copy", (G8H[0:8, :], GFM[0:8, :]))
        p.w("vector", "scalar_tensor_tensor",
            (G8L[0:8, :], GFM[0:8, :], 1.0, G8H[0:8, :], OP.mult, OP.subtract),
            inc=[("vv", 1)])
        g8_t = p.tok("vv")
        at_tiles = [r_at.load(d_at.ap()[er, :, :]) for er in range(2)]
        for er in range(4):
            att, atk, aslot = at_tiles[er]
            for dch in range(CH):
                mm(PS[2][:, :], att[:, dch * 128:(dch + 1) * 128],
                   XB[:, dch * TW:(dch + 1) * TW],
                   start=(dch == 0), stop=(dch == CH - 1),
                   wait=([atk] + ps_wait(2)) if dch == 0 else (),
                   inc=([("tv", 1)] if dch == CH - 1 else ()))
            h_t = p.tok("tv")
            r_at.done(aslot, h_t)
            if er + 2 < 4:
                at_tiles.append(r_at.load(d_at.ap()[er + 2, :, :]))
            else:
                at_tiles.append(None)
            mm(PS[3][:, :], G8A[:, er * 128:(er + 1) * 128], G8H[0:8, :],
               start=True, stop=False, wait=[g8_t] + ps_wait(3))
            mm(PS[3][:, :], G8A[:, er * 128:(er + 1) * 128], G8L[0:8, :],
               start=False, stop=True, inc=[("tv", 1)])
            grep_t = p.tok("tv")
            RGF = VSTG[:, :].bitcast(F32)
            p.w("vector", "tensor_mul",
                (RGF[:, 0:512], PS[3][:, :], RSTD[:, tt * TW:(tt + 1) * TW]),
                wait=[grep_t], inc=[("vv", 1)])
            psg[3] = p.tok("vv")
            p.w("vector", "tensor_mul",
                (HSB[:, er * TW:(er + 1) * TW], PS[2][:, :], RGF[:, 0:512]),
                wait=[h_t], inc=[("vv", 1)])
            psg[2] = p.tok("vv")
        h_all = p.tok("vv")
        for hh in range(2):
            xt = p.dma("sync", xin_ap(), d_y.ap()[2 * tt + hh, :, :],
                       wait=([st["xstg"]] if st["xstg"] else []), sem="sx")
            p.w("vector", wait=[xt])
            for oc in range(CH):
                ob = (4, 2)[oc % 2]
                btt, btk, bslot = r_bt.load(d_bt.ap()[oc, :, :])
                for er in range(4):
                    mm(PS[ob][:, 0:HW], btt[:, er * 128:(er + 1) * 128],
                       HSB[:, er * TW + hh * HW: er * TW + hh * HW + HW],
                       start=(er == 0), stop=(er == 3),
                       wait=([btk, h_all] + ps_wait(ob)) if er == 0 else (),
                       inc=([("tv", 1)] if er == 3 else ()))
                y_t = p.tok("tv")
                r_bt.done(bslot, y_t)
                xs = XSTG[:, oc * HW:(oc + 1) * HW]
                p.w("vector", "scalar_tensor_tensor",
                    (xs, PS[ob][:, 0:HW], 0.0, xs, OP.bypass, OP.add),
                    wait=[y_t], inc=[("vv", 1)])
                psg[ob] = p.tok("vv")
            st["xstg"] = p.dma("sync", d_y.ap()[2 * tt + hh, :, :], xin_ap(),
                               wait=[p.tok("vv")], sem="sy")
        p.bar()

    r_wo = Ring(2, 2048, 0, "rwo")
    TOPT = SCR_BF[:, 0:CH * TW]
    BOTT = SCR_BF[:, CH * TW:2 * CH * TW]
    QPH_ = XB[0:2, 0:32 * TW]
    QPL_ = XB[0:2, 32 * TW:64 * TW]

    def build_qpair(tt, half, wait=()):
        tks = []
        h0 = 64 * half
        tks.append(p.dma("sync", QPH_[0:1, :].rearrange("q (r t) -> q r t", r=32),
                         QHI()[h0:h0 + 64:2, tt * TW:(tt + 1) * TW], wait=wait, sem="qp"))
        tks.append(p.dma("sync", QPH_[1:2, :].rearrange("q (r t) -> q r t", r=32),
                         QHI()[h0 + 1:h0 + 64:2, tt * TW:(tt + 1) * TW], wait=wait, sem="qp"))
        return tks

    def build_vstack(pr, wait=()):
        sl = pr % 2
        c0 = sl * 224 + 96
        g0 = pr * 16
        p.w("vector", "tensor_copy", (VST[0:32, c0:c0 + 16], VGALL[0:32, g0:g0 + 16]),
            wait=wait)
        p.w("vector", "tensor_copy", (VST[64:96, c0 + 16:c0 + 32], VGALL[64:96, g0:g0 + 16]))
        p.w("vector", "tensor_copy", (VSB[32:64, c0:c0 + 16], VGALL[32:64, g0:g0 + 16]))
        p.w("vector", "tensor_copy", (VSB[96:128, c0 + 16:c0 + 32], VGALL[96:128, g0:g0 + 16]),
            inc=[("vv", 1)])
        return p.tok("vv")

    def attn_a_sparse(tt):
        vs_tok = {}
        pair_end = {}
        for half in range(2):
            qp = build_qpair(tt, half,
                             wait=([pair_end[31]] if half == 1 else ()))
            for tk in qp:
                p.w("tensor", wait=[tk])
            if half == 0:
                vs_tok[0] = build_vstack(0)
                vs_tok[1] = build_vstack(1)
            for k in range(32):
                pr = 32 * half + k
                ch = pr // 4
                sl = pr % 2
                mm(PS[sl][:, :], EPH[:, :], QPH_[0:2, k * TW:(k + 1) * TW],
                   start=True, stop=True, wait=ps_wait(sl), inc=[("tv", 1)])
                arg_t = p.tok("tv")
                pb = PTB[:, (pr % 4) * TW:(pr % 4 + 1) * TW]
                ew = [arg_t] + ([pair_end[pr - 4]] if pr >= 4 else [])
                p.w("scalar", "activation", (pb, PS[sl][:, :], AF.Exp),
                    {"bias": ZRO[:, 0:1], "scale": SCL[:, pr:pr + 1]},
                    wait=ew, inc=[("sv", 1)])
                p.w("vector", "tensor_scalar", (pb, pb, 1e30, None, OP.min),
                    wait=[p.tok("sv")], inc=[("vv", 1)])
                psg[sl] = p.tok("vv")
                exp_t = p.tok("vv")
                mm(PS[2][:, :], SLT[:, 128 - 2 * pr:256 - 2 * pr], pb,
                   start=(pr == 0), stop=(pr == 63),
                   wait=[exp_t] + (ps_wait(2) if pr == 0 else []))
                mm(PS[3][:, :], SLB[:, 128 - 2 * pr:256 - 2 * pr], pb,
                   start=(pr == 0), stop=(pr == 63),
                   wait=(ps_wait(3) if pr == 0 else []),
                   inc=([("tv", 1)] if pr == 63 else ()))
                it, ib = 4 + (ch % 2), 6 + (ch % 2)
                w0 = (pr % 2) * 224 + 96 - (pr % 4) * 32
                mm(PS[it][:, :], VST[:, w0:w0 + 128], pb,
                   start=(pr % 4 == 0), stop=(pr % 4 == 3),
                   wait=[vs_tok[pr]] + (ps_wait(it) if pr % 4 == 0 else []))
                mm(PS[ib][:, :], VSB[:, w0:w0 + 128], pb,
                   start=(pr % 4 == 0), stop=(pr % 4 == 3),
                   wait=(ps_wait(ib) if pr % 4 == 0 else []),
                   inc=[("tv", 1)])
                pair_end[pr] = p.tok("tv")
                if pr % 4 == 3:
                    pv_t = pair_end[pr]
                    p.w("vector", "tensor_copy",
                        (TOPT[:, ch * TW:(ch + 1) * TW], PS[it][:, :]), wait=[pv_t])
                    p.w("vector", "tensor_copy",
                        (BOTT[:, ch * TW:(ch + 1) * TW], PS[ib][:, :]), inc=[("vv", 1)])
                    psg[it] = psg[ib] = p.tok("vv")
                if pr + 2 <= 63:
                    vs_tok[pr + 2] = build_vstack(pr + 2, wait=[pair_end[pr]])
        s_done = p.tok("tv")
        WF = VSTG[:, :].bitcast(F32)
        p.w("vector", "reciprocal", (WF[:, 0:512], PS[2][:, :]), wait=[s_done],
            inc=[("vv", 1)])
        psg[2] = p.tok("vv")
        p.w("vector", "tensor_mul",
            (RSB[:, 0:512], WF[:, 0:512], MQ()[:, tt * TW:(tt + 1) * TW]), inc=[("vv", 1)])
        wtop_t = p.tok("vv")
        p.w("vector", "reciprocal", (WF[:, 0:512], PS[3][:, :]), wait=[wtop_t],
            inc=[("vv", 1)])
        psg[3] = p.tok("vv")
        p.w("vector", "tensor_mul",
            (PTB[:, 0:512], WF[:, 0:512], MQC()[:, tt * TW:(tt + 1) * TW]), inc=[("vv", 1)])
        w_t = p.tok("vv")
        wtr_free = None
        for hh in range(2):
            h0 = hh * HW
            dwait = [w_t] + ([wtr_free] if wtr_free else []) + \
                ([p.tok("sy")] if "sy" in p.cnt else [])
            for c in range(CH):
                p.dma("sync", WTR[0:8, c * HW:(c + 1) * HW],
                      RSB[8 * c:8 * c + 8, h0:h0 + HW], wait=dwait, sem="wt")
                t2 = p.dma("sync", WTR[0:8, 4096 + c * HW: 4096 + (c + 1) * HW],
                           PTB[8 * c:8 * c + 8, h0:h0 + HW], wait=dwait, sem="wt")
            t1 = t2
            for ch in range(CH):
                i = ch % 2
                mm(PS[0 + i][:, 0:HW], REP8[:, :], WTR[0:8, ch * HW:(ch + 1) * HW],
                   start=True, stop=True, wait=[t1, t2] + ps_wait(0 + i))
                mm(PS[2 + i][:, 0:HW], REP8[:, :],
                   WTR[0:8, 4096 + ch * HW: 4096 + (ch + 1) * HW],
                   start=True, stop=True, wait=ps_wait(2 + i), inc=[("tv", 1)])
                rep_t = p.tok("tv")
                tch = TOPT[:, ch * TW + h0: ch * TW + h0 + HW]
                bch = BOTT[:, ch * TW + h0: ch * TW + h0 + HW]
                p.w("vector", "tensor_mul", (tch, tch, PS[0 + i][:, 0:HW]), wait=[rep_t])
                p.w("vector", "tensor_mul", (bch, bch, PS[2 + i][:, 0:HW]))
                p.w("vector", "tensor_add", (tch, tch, bch), inc=[("vv", 1)])
                psg[0 + i] = psg[2 + i] = p.tok("vv")
            wtr_free = p.tok("tv")
        return p.tok("vv")

    for tt in range(NTT):
        attn_a_sparse(tt)
        p.bar()
        if stop_after == "probe5" and tt == 0:
            p.dma("gpsimd", d_dbg.ap()[0, :, :], TOPT[:, 0:2048])
            p.dma("gpsimd", d_dbg.ap()[1, :, :], TOPT[:, 2048:4096])
            p.dma("gpsimd", d_dbg.ap()[2, :, 0:64], VALS[:, :])
            p.dma("gpsimd", d_dbg.ap()[2, :, 64:128], TIF[:, :])
            p.dma("gpsimd", d_dbg.ap()[2, :, 128:192], IXVU[:, :])
            p.dma("gpsimd", d_dbg.ap()[2, :, 192:256], IXKU[:, :])
            p.dma("gpsimd", d_dbg.ap()[3, :, 0:2048], QHI())
            p.dma("gpsimd", d_dbg.ap()[4, :, 0:2048], QLO())
            p.qfence("sync")
            p.qfence("gpsimd")
            p.bar()
            with nc.Block() as block:
                p.replay(block)
            stack.close()
            return nc
        for hh in range(2):
            xt = p.dma("sync", xin_ap(), d_x.ap()[2 * tt + hh, :, :],
                       wait=([st["xstg"]] if st["xstg"] else []), sem="sx")
            p.w("vector", wait=[xt])
            for oc in range(CH):
                ob = (5, 2, 3)[oc % 3]
                wtile, wtok, slot = r_wo.load(d_wo.ap()[oc, :, :])
                for dch in range(CH):
                    mm(PS[ob][:, 0:HW], wtile[:, dch * 128:(dch + 1) * 128],
                       TOPT[:, dch * TW + hh * HW: dch * TW + hh * HW + HW],
                       start=(dch == 0), stop=(dch == CH - 1),
                       wait=([wtok] + ps_wait(ob)) if dch == 0 else (),
                       inc=([("tv", 1)] if dch == CH - 1 else ()))
                wo_t = p.tok("tv")
                r_wo.done(slot, wo_t)
                xs = XSTG[:, oc * HW:(oc + 1) * HW]
                p.w("vector", "scalar_tensor_tensor",
                    (xs, PS[ob][:, 0:HW], 0.0, xs, OP.bypass, OP.add),
                    wait=[wo_t], inc=[("vv", 1)])
                psg[ob] = p.tok("vv")
            p.bar()
            if stop_after == "sub1":
                st["xstg"] = p.dma("sync", d_y.ap()[2 * tt + hh, :, :], xin_ap(),
                                   wait=[p.tok("vv")], sem="sy")
                continue
            phase_A(tt, hh, full_mode=False)
            st["xstg"] = p.dma("sync", d_y.ap()[2 * tt + hh, :, :], xin_ap(),
                               wait=[p.tok("vv")], sem="sy")
            p.bar()
        if stop_after == "sub1":
            continue
        rstd_tm(tt)
        ffn_tt(tt)
        p.bar()

    if stop_after in ("sub1", "sub2"):
        p.qfence("sync")
        p.qfence("gpsimd")
        p.bar()
        with nc.Block() as block:
            p.replay(block)
        stack.close()
        return nc

    for tt in range(NTT):
        for hh in range(2):
            xt = p.dma("sync", xin_ap(), d_y.ap()[2 * tt + hh, :, :],
                       wait=([st["xstg"]] if st["xstg"] else []), sem="sx")
            phase_A(tt, hh, full_mode=True, wait_x=[xt])
            p.bar()
        rstd_tm(tt)
    p.bar()

    if stop_after == "sub3a":
        p.qfence("sync")
        p.qfence("gpsimd")
        p.bar()
        with nc.Block() as block:
            p.replay(block)
        stack.close()
        return nc

    QH2 = SCR_BF[:, 0:4096]
    KH2 = SCR_BF[:, 4096:8192]
    VTM = SCR_BF[:, 8192:12288]
    QSQ = SCR_BF[:, 12288:14336]
    SEXP = SCR_F32[:, 6144:6656]
    SEH = SCR_BF[:, 13312:13824]
    SEL = SCR_BF[:, 13824:14336]
    r_qkv = Ring(4, 2048, 0, "rqv")
    r_vv = Ring(16, 256, 8192, "rvv")

    def xbf(dch, c0, cw):
        return XB[:, dch * T + c0: dch * T + c0 + cw]

    hp_prev = None
    for hp in range(8):
        if hp_prev is not None:
            p.w("vector", wait=[hp_prev])
            p.w("gpsimd", wait=[hp_prev])
        echs = [2 * hp, 2 * hp + 1, 16 + 2 * hp, 16 + 2 * hp + 1]
        tiles = [r_qkv.load(d_wqkv.ap()[e, :, :]) for e in echs]
        vtiles = [r_vv.load(d_wqkvv.ap()[dch, :, 256 * hp:256 * (hp + 1)]) for dch in range(CH)]
        p.dma("gpsimd", QKVBP[0:1, 0:256], d_qkvbrow.ap()[0:1, 256 * hp:256 * (hp + 1)], sem="qb")
        p.dma("gpsimd", QKVBP[0:1, 256:512],
              d_qkvbrow.ap()[0:1, 2048 + 256 * hp:2048 + 256 * (hp + 1)], sem="qb")
        bq_t = p.dma("gpsimd", QKVBP[0:1, 512:768],
                     d_qkvbrow.ap()[0:1, 4096 + 256 * hp:4096 + 256 * (hp + 1)], sem="qb")
        for i, e in enumerate(echs):
            wt, wtk, _ = tiles[i]
            for tt in range(NTT):
                qb_ = (0, 5)[tt % 2]
                for dch in range(CH):
                    mm(PS[qb_][:, :], wt[:, dch * 128:(dch + 1) * 128],
                       xbf(dch, tt * TW, TW), start=(dch == 0), stop=False,
                       wait=([wtk] + ps_wait(qb_)) if dch == 0 else ())
                boff = (i % 2) * 128 + (0 if i < 2 else 256)
                mm(PS[qb_][:, :], QKVBP[0:1, boff:boff + 128], SQVB(tt),
                   start=False, stop=True, wait=[bq_t], inc=[("tv", 1)])
                qk_t = p.tok("tv")
                dst = QH2 if i < 2 else KH2
                dsl = dst[:, (i % 2) * 2048 + tt * TW: (i % 2) * 2048 + (tt + 1) * TW]
                if i < 2:
                    p.w("vector", "tensor_mul",
                        (dsl, PS[qb_][:, :], RSTD[:, tt * TW:(tt + 1) * TW]),
                        wait=[qk_t], inc=[("vv", 1)])
                else:
                    p.w("vector", "scalar_tensor_tensor",
                        (dsl, PS[qb_][:, :], SMS, RSTD[:, tt * TW:(tt + 1) * TW],
                         OP.mult, OP.mult), wait=[qk_t], inc=[("vv", 1)])
                psg[qb_] = p.tok("vv")
        for tc in range(CH):
            tt, j = tc // 4, tc % 4
            vb2 = (1, 6)[tc % 2]
            for dch in range(CH):
                vt, vtk, _ = vtiles[dch]
                mm(PS[vb2][:, 0:256], xbf(dch, tc * 128, 128), vt,
                   start=(dch == 0), stop=False,
                   wait=(([vtk] if tc == 0 else []) + (ps_wait(vb2) if dch == 0 else [])))
            mm(PS[vb2][:, 0:256], SQVB(tt)[0:1, j * 128:(j + 1) * 128],
               QKVBP[0:1, 512:768],
               start=False, stop=True, wait=[bq_t], inc=[("tv", 1)])
            v_t = p.tok("tv")
            p.w("vector", "tensor_scalar",
                (VTM[:, tc * 256:(tc + 1) * 256], PS[vb2][:, 0:256], RSTM[:, tc:tc + 1],
                 None, OP.mult), wait=[v_t], inc=[("vv", 1)])
            psg[vb2] = p.tok("vv")
        p.bar()
        for a in range(2):
            h = 2 * hp + a
            negcb_toks = []
            for tt in range(NTT):
                sb2 = (2, 7)[tt % 2]
                q0 = a * 2048 + tt * TW
                mm(PS[sb2][:, :], QH2[:, q0:q0 + 128],
                   KH2[:, a * 2048:a * 2048 + TW],
                   start=True, stop=True, wait=ps_wait(sb2), inc=[("tv", 1)])
                p.w("vector", "reduce_max", (TV[:, tt:tt + 1], PS[sb2][:, :], AX),
                    wait=[p.tok("tv")], inc=[("vv", 1)])
                psg[sb2] = p.tok("vv")
                p.w("tensor", "transpose", (PS[sb2][0:1, 0:128], TV[:, tt:tt + 1],
                                            IDF[:, :]),
                    wait=[p.tok("vv")] + ps_wait(sb2), inc=[("tv", 1)])
                p.w("vector", "reduce_max",
                    (CELLS[0:1, 4 + tt:5 + tt], PS[sb2][0:1, 0:128], AX),
                    wait=[p.tok("tv")], inc=[("vv", 1)])
                psg[sb2] = p.tok("vv")
                p.w("vector", "tensor_scalar",
                    (CELLS[0:1, 8 + tt:9 + tt].bitcast(BF16)[0:1, 0:1],
                     CELLS[0:1, 4 + tt:5 + tt], -1.0,
                     -30.0, OP.mult, OP.add), inc=[("vv", 1)])
                mm(PS[sb2][:, 0:1], ONE1[:, :],
                   CELLS[0:1, 8 + tt:9 + tt].bitcast(BF16)[0:1, 0:1],
                   start=True, stop=True, wait=[p.tok("vv")] + ps_wait(sb2),
                   inc=[("tv", 1)])
                p.w("vector", "tensor_copy", (NEGCB[:, tt:tt + 1], PS[sb2][:, 0:1]),
                    wait=[p.tok("tv")], inc=[("vv", 1)])
                psg[sb2] = p.tok("vv")
                negcb_toks.append(p.tok("vv"))
            for tt in range(NTT):
                negcb_t = negcb_toks[tt]
                accs = {}
                for kch in range(CH):
                    sb = (2, 3, 4)[kch % 3]
                    sl = kch % 4
                    mm(PS[sb][:, :],
                       KH2[:, a * 2048 + kch * 128: a * 2048 + (kch + 1) * 128],
                       QH2[:, a * 2048 + tt * TW: a * 2048 + (tt + 1) * TW],
                       start=True, stop=True,
                       wait=ps_wait(sb) + ([negcb_t] if kch == 0 else []),
                       inc=[("tv", 1)])
                    arg_t = p.tok("tv")
                    pb = PTB[:, sl * TW:(sl + 1) * TW]
                    p.w("scalar", "activation", (pb, PS[sb][:, :], AF.Exp),
                        {"bias": NEGCB[:, tt:tt + 1], "scale": 1.0},
                        wait=[arg_t] + ([accs[kch - 4]] if kch >= 4 else []),
                        inc=[("sv", 1)])
                    psg[sb] = p.tok("sv")
                    exp_t = p.tok("sv")
                    if kch == 0:
                        p.w("vector", "tensor_copy", (SEXP, pb), wait=[exp_t],
                            inc=[("vv", 1)])
                    else:
                        p.w("vector", "scalar_tensor_tensor",
                            (SEXP, pb, 0.0, SEXP, OP.bypass, OP.add),
                            wait=[exp_t], inc=[("vv", 1)])
                    accs[kch] = p.tok("vv")
                    mm(PS[6][:, :], VTM[:, kch * 256 + a * 128: kch * 256 + (a + 1) * 128],
                       pb, start=(kch == 0), stop=(kch == CH - 1),
                       wait=[exp_t] + (ps_wait(6) if kch == 0 else []),
                       inc=([("tv", 1)] if kch == CH - 1 else ()))
                ctx_t = p.tok("tv")
                p.w("vector", "tensor_copy", (SEH, SEXP),
                    wait=([st["sum_t"]] if st.get("sum_t") else ()))
                p.w("vector", "scalar_tensor_tensor",
                    (SEL, SEXP, 1.0, SEH, OP.mult, OP.subtract), inc=[("vv", 1)])
                mm(PS[5][0:1, :], ONEC[:, :], SEH, start=True, stop=False,
                   wait=[p.tok("vv")] + ps_wait(5))
                mm(PS[5][0:1, :], ONEC[:, :], SEL, start=False, stop=False)
                mm(PS[5][0:1, :], ONE1[0:1, 0:1], TINY[0:1, :],
                   start=False, stop=True, inc=[("tv", 1)])
                sum_t = p.tok("tv")
                st["sum_t"] = sum_t
                p.w("vector", "reciprocal", (SCRW, PS[5][0:1, 0:256]),
                    wait=[ctx_t, sum_t])
                p.w("vector", "reciprocal", (SCRW2, PS[5][0:1, 256:512]), inc=[("vv", 1)])
                psg[5] = p.tok("vv")
                p.w("vector", "tensor_copy", (RHB[0:1, 0:256], SCRW))
                p.w("vector", "tensor_copy", (RHB[0:1, 256:512], SCRW2))
                p.w("vector", "scalar_tensor_tensor",
                    (RLB[0:1, 0:256], SCRW, 1.0, RHB[0:1, 0:256], OP.mult, OP.subtract))
                p.w("vector", "scalar_tensor_tensor",
                    (RLB[0:1, 256:512], SCRW2, 1.0, RHB[0:1, 256:512], OP.mult, OP.subtract),
                    inc=[("vv", 1)])
                mm(PS[7][:, :], ONE1[:, :], RHB, start=True, stop=False,
                   wait=[p.tok("vv")] + ps_wait(7))
                mm(PS[7][:, :], ONE1[:, :], RLB, start=False, stop=True, inc=[("tv", 1)])
                p.w("vector", "tensor_copy", (RSB[:, 0:TW], PS[7][:, :]), wait=[p.tok("tv")],
                    inc=[("vv", 1)])
                psg[7] = p.tok("vv")
                vs = VSTG[:, (tt % 2) * TW:(tt % 2 + 1) * TW]
                vw = [st["vstg"][tt % 2]] if st["vstg"][tt % 2] else []
                p.w("vector", "tensor_mul", (vs, PS[6][:, :], RSB[:, 0:TW]), wait=vw,
                    inc=[("vv", 1)])
                psg[6] = p.tok("vv")
                st["vstg"][tt % 2] = p.dma(
                    "sync", d_ctx.ap()[h, :, tt * TW:(tt + 1) * TW], vs,
                    wait=[p.tok("vv")], sem="vs%d" % (tt % 2))
        qk_last = p.tok("tv")
        hp_prev = qk_last
        for i in range(4):
            r_qkv.done(tiles[i][2], qk_last)
        for dch in range(CH):
            r_vv.done(vtiles[dch][2], qk_last)
        if stop_after == "probe4" and hp == 0:
            p.qfence("sync")
            p.bar()
            for h in range(2):
                p.dma("gpsimd", d_dbg.ap()[h, :, :], d_ctx.ap()[h, :, :])
            p.dma("gpsimd", d_dbg.ap()[2, :, :], QH2[:, 0:2048])
            p.dma("gpsimd", d_dbg.ap()[3, :, :], KH2[:, 0:2048])
            p.dma("gpsimd", d_dbg.ap()[4, :, 0:512], RSB[:, 0:512])
            p.dma("gpsimd", d_dbg.ap()[4, :, 512:513], NEGCB[:, 0:1],
                  allow_slow_non_contiguous=True)
            p.dma("gpsimd", d_dbg.ap()[4, 0:1, 520:536],
                  CELLS[0:1, 0:16], allow_slow_non_contiguous=True)
            p.dma("gpsimd", d_dbg.ap()[4, 0:1, 536:1048], RHB[0:1, 0:512])
            p.dma("gpsimd", d_dbg.ap()[4, :, 1048:2048], VTM[:, 0:1000])
            p.qfence("sync")
            p.qfence("gpsimd")
            p.bar()
            with nc.Block() as block:
                p.replay(block)
            stack.close()
            return nc
        if stop_after and stop_after.startswith("sub3b") and \
                hp == int(stop_after[5:] or 0):
            break
    p.bar()

    if stop_after and stop_after.startswith("sub3b"):
        p.qfence("sync")
        p.qfence("gpsimd")
        p.bar()
        with nc.Block() as block:
            p.replay(block)
        stack.close()
        return nc

    r_wout = Ring(2, 2048, 0, "rwt")
    CTXT = SCR_BF[:, 0:CH * TW]
    ctx_done = [p.tok("vs0"), p.tok("vs1")]
    for tt in range(NTT):
        ctk = []
        for g in range(4):
            ctk.append(p.dma("sync",
                             CTXT.rearrange("q (c t) -> q c t", c=CH)[:, 4 * g:4 * g + 4, :],
                             d_ctx.ap()[4 * g:4 * g + 4, :, tt * TW:(tt + 1) * TW]
                             .rearrange("c q t -> q c t"),
                             wait=(ctx_done if (tt == 0 and g == 0) else ()),
                             sem="ct"))
        for tk in ctk:
            p.w("tensor", wait=[tk])
        if stop_after == "sub3c":
            p.qfence("sync")
            p.qfence("gpsimd")
            p.bar()
            with nc.Block() as block:
                p.replay(block)
            stack.close()
            return nc
        for hh in range(2):
            xt = p.dma("sync", xin_ap(), d_y.ap()[2 * tt + hh, :, :],
                       wait=([st["xstg"]] if st["xstg"] else []), sem="sx")
            p.w("vector", wait=[xt])
            for oc in range(CH):
                if stop_after == "sub3d" and oc == 4:
                    break
                ob = (5, 2, 3)[oc % 3]
                wtile, wtok, slot = r_wout.load(d_wout.ap()[oc, :, :])
                for dch in range(CH):
                    mm(PS[ob][:, 0:HW], wtile[:, dch * 128:(dch + 1) * 128],
                       CTXT[:, dch * TW + hh * HW: dch * TW + hh * HW + HW],
                       start=(dch == 0), stop=(dch == CH - 1),
                       wait=([wtok] + ps_wait(ob)) if dch == 0 else (),
                       inc=([("tv", 1)] if dch == CH - 1 else ()))
                wo_t = p.tok("tv")
                r_wout.done(slot, wo_t)
                xs = XSTG[:, oc * HW:(oc + 1) * HW]
                p.w("vector", "scalar_tensor_tensor",
                    (xs, PS[ob][:, 0:HW], OUTB[:, oc:oc + 1], xs, OP.add, OP.add),
                    wait=[wo_t], inc=[("vv", 1)])
                psg[ob] = p.tok("vv")
            if stop_after == "sub3d":
                p.qfence("sync")
                p.qfence("gpsimd")
                p.bar()
                with nc.Block() as block:
                    p.replay(block)
                stack.close()
                return nc
            p.bar()
            if stop_after == "sub3":
                st["xstg"] = p.dma("sync", d_y.ap()[2 * tt + hh, :, :], xin_ap(),
                                   wait=[p.tok("vv")], sem="sy")
                continue
            phase_A(tt, hh, full_mode=False)
            st["xstg"] = p.dma("sync", d_y.ap()[2 * tt + hh, :, :], xin_ap(),
                               wait=[p.tok("vv")], sem="sy")
            p.bar()
        if stop_after == "sub3":
            continue
        rstd_tm(tt)
        ffn_tt(tt)
        p.bar()

    p.qfence("sync")
    p.qfence("gpsimd")
    p.bar()
    with nc.Block() as block:
        p.replay(block)
    stack.close()
    return nc


def _layout_inputs(inputs):
    f = np.float32
    x = np.asarray(inputs["L0_attn_a_x"], dtype=f)
    Wq = np.asarray(inputs["L0_attn_a_Wq"], dtype=f)
    Wk = np.asarray(inputs["L0_attn_a_Wk"], dtype=f)
    Wv = np.asarray(inputs["L0_attn_a_Wv"], dtype=f)
    Wo = np.asarray(inputs["L0_attn_a_Wo"], dtype=f)
    Wqkv = np.asarray(inputs["L0_attn_b_Wqkv_w"], dtype=f)
    qkvb = np.asarray(inputs["L0_attn_b_Wqkv_b"], dtype=f)
    outw = np.asarray(inputs["L0_attn_b_out_w"], dtype=f)
    outb = np.asarray(inputs["L0_attn_b_out_b"], dtype=f)
    Am = np.asarray(inputs["L0_ffn_A"], dtype=f)
    Bm = np.asarray(inputs["L0_ffn_B"], dtype=f)
    Wr = np.asarray(inputs["L0_ffn_Wr"], dtype=f)
    br = np.asarray(inputs["L0_ffn_br"], dtype=f)

    def lhs_blk(WT, nech):
        E = WT.shape[1]
        return np.ascontiguousarray(
            WT.reshape(CH, 128, nech, E // nech).transpose(2, 1, 0, 3)
            .reshape(nech, 128, CH * (E // nech)))

    wqkT = lhs_blk(np.ascontiguousarray(np.concatenate([Wq.T, Wk.T], axis=1)), 2)
    wvT = np.ascontiguousarray(Wv.T.reshape(CH, 128, 4, 512).transpose(2, 0, 1, 3))
    woT = lhs_blk(np.ascontiguousarray(Wo.T), CH)
    WqkvT = np.ascontiguousarray(Wqkv.T)
    wqkvT = lhs_blk(np.ascontiguousarray(WqkvT[:, 0:4096]), 32)
    wqkvv = np.ascontiguousarray(WqkvT[:, 4096:6144].reshape(CH, 128, 2048))
    woutT = lhs_blk(np.ascontiguousarray(outw.T), CH)
    AT = np.ascontiguousarray(Am.transpose(2, 0, 1).reshape(D, 512))
    atT = lhs_blk(AT, 4)
    BT = np.ascontiguousarray(Bm.transpose(0, 2, 1).reshape(512, D))
    btT = np.ascontiguousarray(
        BT.reshape(4, 128, CH, 128).transpose(2, 1, 0, 3).reshape(CH, 128, 512))
    wrT = np.ascontiguousarray(Wr.T.reshape(CH, 128, 8).transpose(1, 0, 2).reshape(128, CH * 8))
    qkvbrow = np.ascontiguousarray(qkvb.reshape(1, 6144))
    outb2 = np.ascontiguousarray(outb.reshape(16, 128).T)
    br2 = np.ascontiguousarray(br.reshape(8, 1))

    shared = {
        "wqk": wqkT, "wv": wvT, "wo": woT, "wqkv": wqkvT, "wqkvv": wqkvv,
        "wout": woutT, "at": atT, "bt": btT, "wr": wrT,
        "qkvbrow": qkvbrow, "outb2": outb2, "br": br2,
    }
    in_maps = []
    for c in range(8):
        xf = np.ascontiguousarray(x[c].T)
        xblk = np.ascontiguousarray(
            xf.reshape(CH, 128, NTT * 2, HW).transpose(2, 1, 0, 3)
            .reshape(NTT * 2, 128, CH * HW))
        m = dict(shared)
        m["x"] = xblk
        in_maps.append(m)
    return in_maps


_NC_CACHE = {}


def kernel(**inputs):
    in_maps = _layout_inputs(inputs)
    key = "full"
    if key not in _NC_CACHE:
        _NC_CACHE[key] = build_nc()
    nc = _NC_CACHE[key]
    res = run_bass_kernel_spmd(nc, in_maps, core_ids=list(range(8)))
    outs = []
    for c in range(8):
        yblk = res.results[c]["y"].reshape(8, 128, CH, HW)
        xf = yblk.transpose(2, 1, 0, 3).reshape(D, T)
        outs.append(np.ascontiguousarray(xf.T))
    return np.stack(outs, axis=0).astype(np.float32)


# version 18
# speedup vs baseline: 1.0245x; 1.0005x over previous
import numpy as np
from contextlib import ExitStack

import concourse.bass as bass
import concourse.mybir as mybir
from concourse.bass_utils import run_bass_kernel_spmd
from concourse.bass_core_rust import IndirectOffsetOnAxis

F32 = mybir.dt.float32
BF16 = mybir.dt.bfloat16
U32 = mybir.dt.uint32
U8 = mybir.dt.uint8
OP = mybir.AluOpType
AF = mybir.ActivationFunctionType
AX = mybir.AxisListType.X

D = 2048
T = 2048
CH = 16
NTT = 4
TW = 512
HW = 256
HB = 16
SMS = float(1.0 / np.sqrt(128.0))
EPS = 1e-5
ENGINES = ("sync", "gpsimd", "scalar", "vector", "tensor")


class Prog:
    def __init__(self, nc):
        self.nc = nc
        self.cmd = {e: [] for e in ENGINES}
        self.cnt = {}
        self.sem = {}
        self.nbar = 0
        self.qsems = {}

    def mksem(self, stack, *names):
        self.stack = stack
        for nm in names:
            self.sem[nm] = stack.enter_context(self.nc.semaphore(nm))
            self.cnt[nm] = 0

    def sem_for(self, nm):
        if nm not in self.sem:
            self.sem[nm] = self.stack.enter_context(self.nc.semaphore(nm))
            self.cnt[nm] = 0
        return nm

    def w(self, eng, op=None, args=(), kwargs=None, wait=(), inc=()):
        for nm, n in inc:
            self.cnt[nm] += n
        self.cmd[eng].append((op, tuple(args), dict(kwargs or {}), tuple(wait), tuple(inc)))
        if eng == "vector" and op not in (None, "drain"):
            out_ap = (kwargs or {}).get("out", args[0] if args else None)
            try:
                small = out_ap is not None and out_ap.free_nbytes <= 512
            except Exception:
                small = True
            if small:
                self.cmd[eng].append(("drain", (), {}, (), ()))

    def tok(self, nm):
        return (nm, self.cnt[nm])

    def bar(self):
        self.nbar += 1
        k = self.nbar
        for e in ENGINES:
            self.cnt["b_" + e] += 1
            if e in ("sync",):
                self.cmd[e].append((None, (), {}, (), (("b_" + e, 1),)))
            else:
                self.cmd[e].append(("drain", (), {}, (), (("b_" + e, 1),)))
        for e in ENGINES:
            for o in ENGINES:
                if o != e:
                    self.w(e, wait=[("b_" + o, k)])

    def dma(self, q, out_ap, in_ap, wait=(), sem=None, **kw):
        semn = self.sem_for(sem) if sem else ("gd" if q == "gpsimd" else "qd")
        kwargs = {"out": out_ap, "in_": in_ap}
        kwargs.update(kw)
        self.qsems.setdefault(q, set()).add(semn)
        self.w(q, "dma_start", kwargs=kwargs, wait=wait, inc=[(semn, 16)])
        return self.tok(semn)

    def qfence(self, q):
        base = "gd" if q == "gpsimd" else "qd"
        sems = set(self.qsems.get(q, ())) | {base}
        self.w(q, wait=[self.tok(s) for s in sorted(sems)])

    def replay(self, block):
        sem = self.sem

        def mk(e):
            def run(eng):
                for op, args, kwargs, wait, inc in self.cmd[e]:
                    for nm, v in wait:
                        eng.wait_ge(sem[nm], v)
                    if op is not None:
                        inst = getattr(eng, op)(*args, **kwargs)
                        for nm, n in inc:
                            inst.then_inc(sem[nm], n)
                    else:
                        for nm, n in inc:
                            eng.sem_inc(sem[nm], n)
            return run

        block.sync(mk("sync"))
        block.gpsimd(mk("gpsimd"))
        block.scalar(mk("scalar"))
        block.vector(mk("vector"))
        block.tensor(mk("tensor"))


def build_nc(stop_after=None):
    nc = bass.Bass("TRN2", target_bir_lowering=False)

    d_x = nc.dram_tensor("x", [NTT * 2, 128, CH * HW], F32, kind="ExternalInput")
    d_y = nc.dram_tensor("y", [NTT * 2, 128, CH * HW], F32, kind="ExternalOutput")
    d_wqk = nc.dram_tensor("wqk", [2, 128, CH * 128], F32, kind="ExternalInput")
    d_wv = nc.dram_tensor("wv", [4, CH, 128, 512], F32, kind="ExternalInput")
    d_wo = nc.dram_tensor("wo", [CH, 128, CH * 128], F32, kind="ExternalInput")
    d_wqkv = nc.dram_tensor("wqkv", [32, 128, CH * 128], F32, kind="ExternalInput")
    d_wqkvv = nc.dram_tensor("wqkvv", [CH, 128, 2048], F32, kind="ExternalInput")
    d_wout = nc.dram_tensor("wout", [CH, 128, CH * 128], F32, kind="ExternalInput")
    d_at = nc.dram_tensor("at", [4, 128, CH * 128], F32, kind="ExternalInput")
    d_bt = nc.dram_tensor("bt", [CH, 128, 4 * 128], F32, kind="ExternalInput")
    d_wr = nc.dram_tensor("wr", [128, CH * 8], F32, kind="ExternalInput")
    d_qkvbrow = nc.dram_tensor("qkvbrow", [1, 6144], F32, kind="ExternalInput")
    d_outb2 = nc.dram_tensor("outb2", [128, 16], F32, kind="ExternalInput")
    d_br = nc.dram_tensor("br", [8, 1], F32, kind="ExternalInput")
    d_dbg = nc.dram_tensor("dbg", [5, 128, 2048], F32, kind="ExternalOutput") \
        if stop_after in ("probe1", "probe0", "probe2", "probe3", "probe4",
                          "probe5") else None
    d_vtm = nc.dram_tensor("vtm_s", [T, D], BF16)
    d_keys = nc.dram_tensor("keys_s", [128, T], F32)
    d_ctx = nc.dram_tensor("ctx_s", [HB, 128, T], BF16)

    A = nc.alloc_sbuf_tensor
    XB = A("XB", [128, CH * T], BF16)
    XSTG = A("XSTG", [128, CH * HW], F32)
    SCR = A("SCR", [128, 32 * 1024], U8)
    WBUF = A("WBUF", [128, 25 * 1024], U8)
    P0 = A("P0", [128, 28 * 1024], U8)
    RSTD = A("RSTD", [128, T], F32)
    QA = A("QA", [128, T], F32)
    KAT = A("KAT", [128, T], F32)
    VGALL = A("VGALL", [128, 64 * 16], BF16)
    IDF = A("IDF", [128, 128], F32)
    ONE1 = A("ONE1", [1, 128], BF16)
    ONEC = A("ONEC", [128, 1], BF16)
    EPH = A("EPH", [2, 128], BF16)
    REP8 = A("REP8", [8, 128], BF16)
    G8A = A("G8A", [8, 512], BF16)
    SLT = A("SLT", [128, 256], BF16)
    SLB = A("SLB", [128, 256], BF16)
    VST = A("VST", [128, 448], BF16)
    VSB = A("VSB", [128, 448], BF16)
    TV = A("TV", [128, 64], F32)
    TIU = A("TIU", [128, 64], U32)
    TIF = A("TIF", [128, 64], F32)
    IXB = A("IXB", [128, 128], F32)
    TBALL = A("TBALL", [128, 128], F32)
    IXVU = A("IXVU", [128, 64], U32)
    IXKU = A("IXKU", [128, 64], U32)
    VALS = A("VALS", [128, 64], F32)
    VB = A("VB", [128, 64], BF16)
    SCL = A("SCL", [128, 64], F32)
    EXB = A("EXB", [128, 128], BF16)
    ZRO = A("ZRO", [128, 1], F32)
    TINY = A("TINY", [1, TW], BF16)
    PVEC = A("PVEC", [128, 1], F32)
    H2K = A("H2K", [128, 1], F32)
    NEGC = A("NEGC", [128, 1], F32)
    NEGCB = A("NEGCB", [128, 4], F32)
    CS1 = A("CS1", [128, 1], F32)
    CS2 = A("CS2", [128, 1], F32)
    RSTM = A("RSTM", [128, CH], F32)
    OUTB = A("OUTB", [128, 16], F32)
    GTM = A("GTM", [128, 8], F32)
    GSC = A("GSC", [128, 8], F32)
    PTB = A("PTB", [128, 4 * TW], BF16)
    VSTG = A("VSTG", [128, 2 * TW], BF16)
    RSB = A("RSB", [128, TW], BF16)
    PS = [nc.alloc_psum_tensor("ps%d" % i, [128, 512], F32) for i in range(8)]

    SCR_BF = SCR[:, :].bitcast(BF16)
    SCR_F32 = SCR[:, :].bitcast(F32)
    SCR_U32 = SCR[:, :].bitcast(U32)
    WB_BF = WBUF[:, :].bitcast(BF16)
    P0_F32 = P0[:, :].bitcast(F32)
    P0_BF = P0[:, :].bitcast(BF16)
    QA_BF = QA[:, :].bitcast(BF16)
    KAT_BF = KAT[:, :].bitcast(BF16)
    WTR = XSTG[:, :].bitcast(BF16)
    BRS = P0[:, :].bitcast(F32)[0:8, 7056:7057]
    LGT = SCR_F32[0:8, 1024:1536]
    GFM = SCR_F32[0:8, 1536:2048]
    G8H = SCR_BF[0:8, 4096:4608]
    G8L = SCR_BF[0:8, 4608:5120]

    def MQ():
        return QA_BF[:, 0:2048]

    def MQC():
        return QA_BF[:, 2048:4096]

    def QHI():
        return KAT_BF[:, 0:2048]

    def QLO():
        return KAT_BF[:, 2048:4096]

    def RH(tt):
        return P0_BF[0:1, tt * TW:(tt + 1) * TW]

    def RL(tt):
        return P0_BF[0:1, 2048 + tt * TW: 2048 + (tt + 1) * TW]

    def MH_(tt):
        return P0_BF[0:1, 4096 + tt * TW: 4096 + (tt + 1) * TW]

    def ML_(tt):
        return P0_BF[0:1, 6144 + tt * TW: 6144 + (tt + 1) * TW]

    def SQVB(tt):
        return P0_BF[0:1, 8192 + tt * TW: 8192 + (tt + 1) * TW]

    RHB = P0_BF[0:1, 10240:10752]
    RLB = P0_BF[0:1, 10752:11264]
    QKVBP = P0_BF[0:1, 13344:14112]
    SCRA = P0_F32[0:1, 5632:5888]
    SCRB = P0_F32[0:1, 5888:6144]
    SCRW = P0_F32[0:1, 6144:6400]
    SCRW2 = P0_F32[0:1, 6400:6656]
    CELLS = P0_F32[0:1, 6656:6672]
    EPSC = P0_F32[0:1, 6672:6673]

    p = Prog(nc)
    stack = ExitStack()
    p.mksem(stack, "qd", "gd", "tv", "vv", "sv",
             *["b_" + e for e in ENGINES])
    psg = [None] * 8
    st = {"xstg": None, "vstg": [None, None]}

    def ps_wait(i):
        return [psg[i]] if psg[i] else []

    def mm(out_ap, lhsT, rhs, start, stop, wait=(), inc=()):
        p.w("tensor", "matmul", (out_ap, lhsT, rhs), {"start": start, "stop": stop},
            wait=wait, inc=inc)

    IOTA = TBALL[:, :].bitcast(U32)
    IOTA5 = VGALL[:, :].bitcast(U32)[:, 0:512]
    p.w("gpsimd", "iota", (IOTA[:, 0:1], [[0, 1]]), {"channel_multiplier": 1})
    p.bar()
    p.w("vector", "tensor_copy", (PVEC[:, :], IOTA[:, 0:1]))
    p.bar()
    p.w("vector", "tensor_scalar", (H2K[:, :], PVEC[:, :], 2048.0, None, OP.mult))
    p.w("vector", "memset", (EPSC, float(EPS)))
    p.w("vector", "memset", (ONE1[:, :], 1.0))
    p.w("vector", "memset", (ONEC[:, :], 1.0))
    p.w("gpsimd", "iota", (IOTA[:, 0:128], [[1, 128]]), {"channel_multiplier": 0})
    p.bar()
    p.w("vector", "tensor_copy", (QA[:, 0:128], IOTA[:, 0:128]))
    p.bar()
    p.w("vector", "tensor_scalar", (IDF[:, :], QA[:, 0:128], PVEC[:, 0:1], None,
                                    OP.is_equal))
    p.w("gpsimd", "iota", (IOTA[:, 0:128], [[1, 2], [0, 64]]), {"channel_multiplier": 0})
    p.bar()
    p.w("vector", "tensor_copy", (QA[:, 0:128], IOTA[:, 0:128]))
    p.bar()
    p.w("vector", "tensor_scalar", (EPH[:, :], QA[0:2, 0:128], PVEC[0:2, 0:1], None,
                                    OP.is_equal))
    p.w("gpsimd", "iota", (IOTA[:, 0:128], [[1, 8], [0, 16]]), {"channel_multiplier": 0})
    p.bar()
    p.w("vector", "tensor_copy", (QA[:, 0:128], IOTA[:, 0:128]))
    p.bar()
    p.w("vector", "tensor_scalar", (REP8[:, :], QA[0:8, 0:128], PVEC[0:8, 0:1], None,
                                    OP.is_equal))
    p.w("gpsimd", "iota", (IOTA5[:, 0:512], [[1, 8], [0, 64]]), {"channel_multiplier": 0})
    p.bar()
    p.w("vector", "tensor_copy", (QA[:, 128:640], IOTA5[:, 0:512]))
    p.bar()
    p.w("vector", "tensor_scalar", (G8A[:, :], QA[0:8, 128:640], PVEC[0:8, 0:1], None,
                                    OP.is_equal))
    p.w("gpsimd", "iota", (IOTA[:, 0:128], [[32, 4], [0, 32]]), {"channel_multiplier": 0})
    p.bar()
    p.w("vector", "tensor_copy", (QA[:, 0:128], IOTA[:, 0:128]))
    p.bar()
    p.w("vector", "tensor_scalar", (EXB[:, :], QA[:, 0:128], PVEC[:, 0:1], None,
                                    OP.is_equal))
    p.w("vector", "memset", (ZRO[:, :], 0.0))
    p.w("vector", "memset", (TINY[:, :], 1e-20))
    p.w("vector", "memset", (SLT[:, :], 0.0))
    p.w("vector", "memset", (SLT[0:32, 128:129], 1.0))
    p.w("vector", "memset", (SLT[64:96, 129:130], 1.0))
    p.w("vector", "memset", (SLB[:, :], 0.0))
    p.w("vector", "memset", (SLB[32:64, 128:129], 1.0))
    p.w("vector", "memset", (SLB[96:128, 129:130], 1.0))
    p.w("vector", "memset", (VST[:, :], 0.0))
    p.w("vector", "memset", (VSB[:, :], 0.0))
    p.dma("sync", OUTB[:, :], d_outb2.ap()[:, :])
    p.dma("sync", BRS[:, :], d_br.ap()[:, :])
    p.qfence("sync")
    p.bar()

    XC = SCR_BF[:, 8192:8192 + CH * HW]
    SQH = SCR_BF[:, 12288:12288 + CH * HW]

    def phase_A(tt, hh, full_mode, wait_x=()):
        h0 = hh * HW
        p.w("scalar", "activation", (SQH, XSTG[:, :], AF.Square), wait=wait_x,
            inc=[("sv", 1)])
        sq_t = p.tok("sv")
        p.w("vector", "tensor_copy", (XC, XSTG[:, :]), wait=wait_x, inc=[("vv", 1)])
        cast_t = p.tok("vv")
        for dch in range(CH):
            mm(PS[0][0:1, 0:HW], ONEC[:, :], XC[:, dch * HW:(dch + 1) * HW],
               start=(dch == 0), stop=(dch == CH - 1),
               wait=([cast_t] + ps_wait(0)) if dch == 0 else ())
        for dch in range(CH):
            mm(PS[1][0:1, 0:HW], ONEC[:, :], SQH[:, dch * HW:(dch + 1) * HW],
               start=(dch == 0), stop=(dch == CH - 1),
               wait=([sq_t] + ps_wait(1)) if dch == 0 else (),
               inc=[("tv", 1)] if dch == CH - 1 else ())
        mm_t = p.tok("tv")
        p.w("vector", "tensor_scalar", (SCRA, PS[0][0:1, 0:HW], 1.0 / D, None, OP.mult),
            wait=[mm_t])
        p.w("vector", "tensor_tensor", (SCRB, SCRA, SCRA, OP.mult))
        p.w("vector", "scalar_tensor_tensor",
            (SCRB, PS[1][0:1, 0:HW], 1.0 / D, SCRB, OP.mult, OP.subtract), inc=[("vv", 1)])
        psg[0] = psg[1] = p.tok("vv")
        p.w("scalar", "activation", (SCRB, SCRB, AF.Sqrt), {"bias": EPSC, "scale": 1.0},
            wait=[p.tok("vv")], inc=[("sv", 1)])
        p.w("vector", "tensor_copy", (SQVB(tt)[0:1, h0:h0 + HW], SCRB), wait=[p.tok("sv")])
        p.w("vector", "reciprocal", (SCRB, SCRB))
        p.w("vector", "tensor_copy", (RH(tt)[0:1, h0:h0 + HW], SCRB))
        p.w("vector", "scalar_tensor_tensor",
            (RL(tt)[0:1, h0:h0 + HW], SCRB, 1.0, RH(tt)[0:1, h0:h0 + HW],
             OP.mult, OP.subtract))
        p.w("vector", "tensor_copy", (MH_(tt)[0:1, h0:h0 + HW], SCRA))
        p.w("vector", "scalar_tensor_tensor",
            (ML_(tt)[0:1, h0:h0 + HW], SCRA, 1.0, MH_(tt)[0:1, h0:h0 + HW],
             OP.mult, OP.subtract), inc=[("vv", 1)])
        rows_t = p.tok("vv")
        mm(PS[2][:, 0:HW], ONE1[:, :], RH(tt)[0:1, h0:h0 + HW], start=True, stop=False,
           wait=[rows_t] + ps_wait(2))
        mm(PS[2][:, 0:HW], ONE1[:, :], RL(tt)[0:1, h0:h0 + HW], start=False, stop=True,
           inc=[("tv", 1)])
        p.w("vector", "tensor_copy",
            (RSTD[:, tt * TW + h0: tt * TW + h0 + HW], PS[2][:, 0:HW]),
            wait=[p.tok("tv")], inc=[("vv", 1)])
        rstd_t = p.tok("vv")
        mm(PS[2][:, 0:HW], ONE1[:, :], MH_(tt)[0:1, h0:h0 + HW], start=True, stop=False,
           wait=[rstd_t])
        mm(PS[2][:, 0:HW], ONE1[:, :], ML_(tt)[0:1, h0:h0 + HW], start=False, stop=True,
           inc=[("tv", 1)])
        mrep_t = p.tok("tv")
        if full_mode:
            xb_out = XB[:, :].rearrange("q (c t) -> q c t", c=CH)[
                :, :, tt * TW + h0: tt * TW + h0 + HW]
        else:
            xb_out = XB[:, 0:CH * TW].rearrange("q (c t) -> q c t", c=CH)[:, :, h0:h0 + HW]
        xin = XSTG[:, :].rearrange("q (c t) -> q c t", c=CH)
        mrep_b = PS[2][:, 0:HW].unsqueeze(1).to_broadcast([128, CH, HW])
        p.w("vector", "tensor_tensor", (xb_out, xin, mrep_b, OP.subtract),
            wait=[mrep_t], inc=[("vv", 1)])
        psg[2] = p.tok("vv")
        return p.tok("vv")

    def rstd_tm(tt):
        for j in range(4):
            mm(PS[3][:, j:j + 1], RH(tt)[0:1, j * 128:(j + 1) * 128], ONE1[0:1, 0:1],
               start=True, stop=False, wait=(ps_wait(3) if j == 0 else ()))
            mm(PS[3][:, j:j + 1], RL(tt)[0:1, j * 128:(j + 1) * 128], ONE1[0:1, 0:1],
               start=False, stop=True, inc=([("tv", 1)] if j == 3 else ()))
        p.w("vector", "tensor_copy", (RSTM[:, tt * 4:tt * 4 + 4], PS[3][:, 0:4]),
            wait=[p.tok("tv")], inc=[("vv", 1)])
        psg[3] = p.tok("vv")

    class Ring:
        def __init__(self, n, cols, off, pfx):
            self.n, self.cols, self.off, self.pfx = n, cols, off, pfx
            self.i = 0
            self.free = [None] * n

        def load(self, src):
            s = self.i % self.n
            self.i += 1
            dst = WB_BF[:, self.off + s * self.cols: self.off + (s + 1) * self.cols]
            wl = [self.free[s]] if self.free[s] is not None else []
            tok = p.dma("gpsimd", dst, src, wait=wl, sem="%s%d" % (self.pfx, s))
            return dst, tok, s

        def done(self, s, tok):
            self.free[s] = tok

    def xin_ap():
        return XSTG[:, :]

    r_qk = Ring(2, 2048, 0, "rqk")
    r_wv = Ring(16, 512, 4096, "rwv")
    qk_tiles = [r_qk.load(d_wqk.ap()[e, :, :]) for e in range(2)]

    for tt in range(NTT):
        for hh in range(2):
            xt = p.dma("sync", xin_ap(), d_x.ap()[2 * tt + hh, :, :],
                       wait=([st["xstg"]] if st["xstg"] else []), sem="sx")
            phase_A(tt, hh, full_mode=False, wait_x=[xt])
            p.bar()
        rstd_tm(tt)
        if stop_after == "probe3" and tt == 0:
            p.bar()
            wtile0d, _, _ = qk_tiles[0]
            for sl in range(4):
                p.dma("gpsimd", d_dbg.ap()[sl, :, :],
                      XB[:, sl * 2048:(sl + 1) * 2048])
            p.dma("gpsimd", d_dbg.ap()[4, :, :], wtile0d[:, 0:2048])
            p.qfence("sync")
            p.qfence("gpsimd")
            p.bar()
            with nc.Block() as block:
                p.replay(block)
            stack.close()
            return nc
        if stop_after == "probe2" and tt == 0:
            p.bar()
            wtile0c, _, _ = qk_tiles[0]
            for li, ln_ in enumerate([2, 4, 8, 16]):
                for dch in range(ln_):
                    mm(PS[7][:, :], wtile0c[:, dch * 128:(dch + 1) * 128],
                       XB[:, dch * TW:(dch + 1) * TW],
                       start=(dch == 0), stop=(dch == ln_ - 1),
                       wait=(ps_wait(7) if dch == 0 else ()),
                       inc=([("tv", 1)] if dch == ln_ - 1 else ()))
                p.w("vector", "tensor_copy", (KAT[:, 1024:1536], PS[7][:, :]),
                    wait=[p.tok("tv")], inc=[("vv", 1)])
                psg[7] = p.tok("vv")
                p.dma("sync", d_dbg.ap()[4, :, 512 * li:512 * (li + 1)],
                      KAT[:, 1024:1536], wait=[p.tok("vv")])
            p.dma("gpsimd", d_dbg.ap()[0, :, :], XB[:, 0:2048])
            p.qfence("sync")
            p.qfence("gpsimd")
            p.bar()
            with nc.Block() as block:
                p.replay(block)
            stack.close()
            return nc
        if stop_after == "probe0" and tt == 0:
            p.dma("gpsimd", d_dbg.ap()[3, :, 0:1024], XB[:, 0:1024])
            wtile0, wtok0, _ = qk_tiles[0]
            for dch in range(CH):
                mm(PS[6][:, :], wtile0[:, dch * 128:(dch + 1) * 128],
                   XB[:, dch * TW:(dch + 1) * TW],
                   start=(dch == 0), stop=(dch == CH - 1),
                   inc=([("tv", 1)] if dch == CH - 1 else ()))
            p.w("vector", "tensor_mul",
                (QLO().bitcast(F32)[:, 0:512] if False else KAT[:, 1536:2048],
                 PS[6][:, :], RSTD[:, 0:512]), wait=[p.tok("tv")], inc=[("vv", 1)])
            psg[6] = p.tok("vv")
            p.dma("sync", d_dbg.ap()[3, :, 1024:1536], KAT[:, 1536:2048],
                  wait=[p.tok("vv")])
            wtile0b, _, _ = qk_tiles[0]
            p.dma("gpsimd", d_dbg.ap()[3, :, 1536:2048], wtile0b[:, 0:512])
            for li, ln_ in enumerate([2, 4, 8, 16]):
                for dch in range(ln_):
                    mm(PS[7][:, :], wtile0b[:, dch * 128:(dch + 1) * 128],
                       XB[:, dch * TW:(dch + 1) * TW],
                       start=(dch == 0), stop=(dch == ln_ - 1),
                       wait=(ps_wait(7) if dch == 0 else ()),
                       inc=([("tv", 1)] if dch == ln_ - 1 else ()))
                p.w("vector", "tensor_copy", (KAT[:, 1024:1536], PS[7][:, :]),
                    wait=[p.tok("tv")], inc=[("vv", 1)])
                psg[7] = p.tok("vv")
                p.dma("sync", d_dbg.ap()[4, :, 512 * li:512 * (li + 1)],
                      KAT[:, 1024:1536], wait=[p.tok("vv")])
        for e in range(2):
            wtile, wtok, _ = qk_tiles[e]
            for dch in range(CH):
                mm(PS[4][:, :], wtile[:, dch * 128:(dch + 1) * 128],
                   XB[:, dch * TW:(dch + 1) * TW],
                   start=(dch == 0), stop=(dch == CH - 1),
                   wait=([wtok] + ps_wait(4)) if dch == 0 else (),
                   inc=([("tv", 1)] if dch == CH - 1 else ()))
            dst = QA if e == 0 else KAT
            p.w("vector", "tensor_mul",
                (dst[:, tt * TW:(tt + 1) * TW], PS[4][:, :], RSTD[:, tt * TW:(tt + 1) * TW]),
                wait=[p.tok("tv")], inc=[("vv", 1)])
            psg[4] = p.tok("vv")
        for ecg in range(4):
            tiles = [r_wv.load(d_wv.ap()[ecg, dch, :, :]) for dch in range(CH)]
            for j in range(4):
                vb_ = (5, 6, 7)[j % 3]
                tc = tt * 4 + j
                for dch in range(CH):
                    wt, wtk, _ = tiles[dch]
                    mm(PS[vb_][:, :], XB[:, dch * TW + j * 128: dch * TW + (j + 1) * 128], wt,
                       start=(dch == 0), stop=(dch == CH - 1),
                       wait=(([wtk] if j == 0 else []) + (ps_wait(vb_) if dch == 0 else [])),
                       inc=([("tv", 1)] if dch == CH - 1 else ()))
                vs = VSTG[:, (j % 2) * TW:(j % 2 + 1) * TW]
                vw = [p.tok("tv")] + ([st["vstg"][j % 2]] if st["vstg"][j % 2] else [])
                p.w("vector", "tensor_scalar",
                    (vs, PS[vb_][:, :], RSTM[:, tc:tc + 1], None, OP.mult),
                    wait=vw, inc=[("vv", 1)])
                psg[vb_] = p.tok("vv")
                st["vstg"][j % 2] = p.dma(
                    "sync", d_vtm.ap()[tc * 128:(tc + 1) * 128, ecg * 512:(ecg + 1) * 512],
                    vs, wait=[p.tok("vv")], sem="vs%d" % (j % 2))
            lasttok = p.tok("tv")
            for dch in range(CH):
                r_wv.done(tiles[dch][2], lasttok)
        p.bar()
        if stop_after == "probe0" and tt == 0:
            p.w("gpsimd", wait=[p.tok("vs0"), p.tok("vs1")])
            p.dma("gpsimd", d_dbg.ap()[0, :, :], XB[:, 0:2048])
            p.dma("sync", d_dbg.ap()[1, :, 0:512], RSTD[:, 0:512])
            p.dma("sync", d_dbg.ap()[1, :, 512:516], RSTM[:, 0:4])
            p.dma("sync", d_dbg.ap()[2, :, 0:512], QA[:, 0:512])
            p.dma("sync", d_dbg.ap()[2, :, 512:1024], KAT[:, 0:512])
            p.dma("gpsimd", d_dbg.ap()[4, :, 0:1024], d_vtm.ap()[0:128, 0:1024])
            p.qfence("sync")
            p.qfence("gpsimd")
            p.bar()
            with nc.Block() as block:
                p.replay(block)
            stack.close()
            return nc

    p.dma("sync", d_keys.ap()[:, :], KAT[:, :])
    p.qfence("sync")
    p.bar()

    TOKU = SCR_U32[:, 0:2048]
    PKT = SCR_F32[:, 2048:4096]
    PKB = SCR_F32[:, 4096:6144]
    MSK = SCR[:, 24576:26624]
    p.w("gpsimd", "iota", (TOKU, [[1, T]]), {"channel_multiplier": 0})
    p.bar()
    p.w("vector", "tensor_scalar", (MSK, KAT[:, :], 0.0, None, OP.is_le))
    p.w("vector", "tensor_scalar",
        (PKT.bitcast(U32), KAT[:, :].bitcast(U32), 0xFFFFF800, None, OP.bitwise_and))
    p.w("vector", "tensor_tensor", (PKT.bitcast(U32), PKT.bitcast(U32), TOKU, OP.bitwise_or))
    p.w("vector", "tensor_scalar",
        (PKB.bitcast(U32), KAT[:, :].bitcast(U32), 0x80000000, None, OP.bitwise_xor))
    p.w("vector", "tensor_scalar",
        (PKB.bitcast(U32), PKB.bitcast(U32), 0xFFFFF800, None, OP.bitwise_and))
    p.w("vector", "tensor_tensor", (PKB.bitcast(U32), PKB.bitcast(U32), TOKU, OP.bitwise_or))
    p.w("vector", "memset", (TOKU, 0))
    p.w("vector", "copy_predicated", (PKT.bitcast(U32), MSK, TOKU))
    p.w("vector", "tensor_scalar", (MSK, KAT[:, :], 0.0, None, OP.is_ge))
    p.w("vector", "copy_predicated", (PKB.bitcast(U32), MSK, TOKU))
    for i in range(4):
        p.w("vector", "max", kwargs={"out": TV[:, 8 * i:8 * i + 8], "in_": PKT})
        p.w("vector", "drain")
        p.w("vector", "match_replace",
            kwargs={"out": PKT, "in_to_replace": TV[:, 8 * i:8 * i + 8],
                    "in_values": PKT, "imm_value": -1.0})
        p.w("vector", "drain")
    for i in range(4, 8):
        p.w("vector", "max", kwargs={"out": TV[:, 8 * i:8 * i + 8], "in_": PKB})
        p.w("vector", "drain")
        p.w("vector", "match_replace",
            kwargs={"out": PKB, "in_to_replace": TV[:, 8 * i:8 * i + 8],
                    "in_values": PKB, "imm_value": -1.0})
        p.w("vector", "drain")
    p.w("vector", "tensor_scalar",
        (TIU[:, :], TV[:, :].bitcast(U32), 2047, None, OP.bitwise_and))
    p.w("vector", "drain")
    p.w("vector", "tensor_copy", (TIF[:, :], TIU[:, :]))
    p.w("vector", "drain")
    p.w("vector", "scalar_tensor_tensor",
        (IXB[:, 0:64], TIF[:, :], 128.0, PVEC[:, 0:1].to_broadcast([128, 64]),
         OP.mult, OP.add))
    p.w("vector", "drain")
    p.w("vector", "scalar_tensor_tensor",
        (IXB[:, 64:128], TIF[:, :], 1.0, H2K[:, 0:1].to_broadcast([128, 64]),
         OP.mult, OP.add))
    p.w("vector", "drain", inc=[("vv", 1)])
    ix_t = p.tok("vv")
    p.w("tensor", "transpose", (PS[1][:, 0:128], IXB[:, :], IDF[:, :]),
        wait=[ix_t] + ps_wait(1), inc=[("tv", 1)])
    p.w("vector", "tensor_copy", (TBALL[:, :], PS[1][:, 0:128]), wait=[p.tok("tv")],
        inc=[("vv", 1)])
    psg[1] = p.tok("vv")
    tb_t = p.tok("vv")
    p.dma("sync", IXB[0:64, 0:64], TBALL[0:64, 0:128:2], wait=[tb_t],
          allow_slow_non_contiguous=True, sem="ix")
    p.dma("sync", IXB[64:128, 0:64], TBALL[0:64, 1:128:2], wait=[tb_t],
          allow_slow_non_contiguous=True, sem="ix")
    p.dma("sync", IXB[0:64, 64:128], TBALL[64:128, 0:128:2], wait=[tb_t],
          allow_slow_non_contiguous=True, sem="ix")
    tix = p.dma("sync", IXB[64:128, 64:128], TBALL[64:128, 1:128:2], wait=[tb_t],
                allow_slow_non_contiguous=True, sem="ix")
    p.bar()
    p.w("vector", "tensor_copy", (IXVU[:, :], IXB[:, 0:64]), wait=[tix])
    p.w("vector", "tensor_copy", (IXKU[:, :], IXB[:, 64:128]), inc=[("vv", 1)])
    ixu_t = p.tok("vv")
    if stop_after == "probe1":
        p.dma("sync", d_dbg.ap()[0, :, :], QA[:, :])
        p.dma("sync", d_dbg.ap()[4, :, :], KAT[:, :])
    p.w("vector", "tensor_copy", (SCR_BF[:, 12288:14336], QA[:, :]))
    p.w("vector", "scalar_tensor_tensor",
        (SCR_BF[:, 14336:16384], QA[:, :], 1.0, SCR_BF[:, 12288:14336],
         OP.mult, OP.subtract))
    p.w("vector", "tensor_scalar", (PKT, QA[:, :], 0.0, None, OP.is_gt))
    p.w("vector", "tensor_scalar", (PKB, QA[:, :], 0.0, None, OP.is_le), inc=[("vv", 1)])
    stg_t = p.tok("vv")
    p.w("vector", "tensor_copy", (QHI(), SCR_BF[:, 12288:14336]), wait=[stg_t])
    p.w("vector", "tensor_copy", (QLO(), SCR_BF[:, 14336:16384]))
    p.w("vector", "tensor_copy", (MQ(), PKT))
    p.w("vector", "tensor_copy", (MQC(), PKB), inc=[("vv", 1)])
    p.bar()
    for pr in range(64):
        p.w("gpsimd", "indirect_dma_start",
            (VALS[:, pr:pr + 1], None,
             bass.AP(d_keys, 0, [[1, T * 128], [1, 1]]),
             IndirectOffsetOnAxis(IXKU[:, pr:pr + 1], 0)),
            inc=[("gd", 16)])
    for pr in range(64):
        p.w("gpsimd", "indirect_dma_start",
            (VGALL[:, pr * 16:(pr + 1) * 16], None,
             bass.AP(d_vtm, 0, [[16, T * D // 16], [1, 16]]),
             IndirectOffsetOnAxis(IXVU[:, pr:pr + 1], 0)),
            inc=[("gd", 16)])
    p.qfence("gpsimd")
    p.bar()
    p.w("vector", "tensor_copy", (VB[:, :], VALS[:, :]), inc=[("vv", 1)])
    mm(PS[1][:, 0:64], EXB[:, :], VB[:, :], start=True, stop=True,
       wait=[p.tok("vv")] + ps_wait(1), inc=[("tv", 1)])
    p.w("vector", "tensor_tensor", (SCL[:, :], VALS[:, :], PS[1][:, 0:64],
                                    OP.subtract),
        wait=[p.tok("tv")], inc=[("vv", 1)])
    psg[1] = p.tok("vv")
    if stop_after == "probe1":
        p.dma("sync", d_dbg.ap()[1, :, 0:64], TIF[:, :])
        p.dma("sync", d_dbg.ap()[1, :, 64:128], VALS[:, :])
        p.dma("sync", d_dbg.ap()[1, :, 128:129], NEGC[:, :], allow_slow_non_contiguous=True)
        p.dma("sync", d_dbg.ap()[1, :, 129:193], TV[:, :])
        p.dma("gpsimd", d_dbg.ap()[2, :, 0:1024], VGALL[:, 0:1024])
        p.dma("gpsimd", d_dbg.ap()[3, :, 0:1024], d_vtm.ap()[0:128, 0:1024])
        p.qfence("sync")
        p.qfence("gpsimd")
        p.bar()
        with nc.Block() as block:
            p.replay(block)
        stack.close()
        return nc

    r_at = Ring(2, 2048, 4096, "rat")
    r_bt = Ring(8, 512, 8192, "rbt")
    r_wr = Ring(1, 128, 12288, "rwr")
    wr_tile, wr_tok, _ = r_wr.load(d_wr.ap()[:, :])
    HSB = SCR_BF[:, 0:4 * TW]

    def ffn_tt(tt):
        for dch in range(CH):
            mm(PS[0][0:8, :], wr_tile[:, dch * 8:(dch + 1) * 8],
               XB[:, dch * TW:(dch + 1) * TW],
               start=(dch == 0), stop=(dch == CH - 1),
               wait=([wr_tok] + ps_wait(0)) if dch == 0 else (),
               inc=([("tv", 1)] if dch == CH - 1 else ()))
        p.w("vector", "tensor_mul",
            (LGT[0:8, :], PS[0][0:8, :], RSTD[0:8, tt * TW:(tt + 1) * TW]),
            wait=[p.tok("tv")])
        p.w("vector", "tensor_scalar", (LGT[0:8, :], LGT[0:8, :], BRS[0:8, 0:1], None, OP.add),
            inc=[("vv", 1)])
        psg[0] = p.tok("vv")
        lg_t = p.tok("vv")
        for j in range(4):
            p.w("tensor", "transpose", (PS[1][0:128, 0:8], LGT[0:8, j * 128:(j + 1) * 128],
                                        IDF[0:8, 0:8]),
                wait=[lg_t] + ps_wait(1), inc=[("tv", 1)])
            p.w("vector", "tensor_copy", (GTM[:, :], PS[1][0:128, 0:8]), wait=[p.tok("tv")],
                inc=[("vv", 1)])
            psg[1] = p.tok("vv")
            p.w("vector", "reduce_max", (CS1[:, :], GTM[:, :], AX))
            p.w("vector", "tensor_scalar", (CS2[:, :], CS1[:, :], -1.0, None, OP.mult),
                inc=[("vv", 1)])
            p.w("scalar", "activation", (GSC[:, :], GTM[:, :], AF.Exp),
                {"bias": CS2[:, 0:1], "scale": 1.0, "accum_out": CS1[:, 0:1]},
                wait=[p.tok("vv")], inc=[("sv", 1)])
            p.w("vector", "reciprocal", (CS2[:, :], CS1[:, :]), wait=[p.tok("sv")])
            p.w("vector", "reduce_max", (CS1[:, :], GTM[:, :], AX))
            p.w("vector", "tensor_scalar", (GSC[:, :], GTM[:, :], CS1[:, 0:1], None,
                                            OP.is_equal))
            p.w("vector", "tensor_scalar", (GSC[:, :], GSC[:, :], CS2[:, 0:1], 1.0 / 64.0,
                                            OP.mult, OP.mult), inc=[("vv", 1)])
            g_t = p.tok("vv")
            p.w("tensor", "transpose", (PS[1][0:8, 0:128], GSC[:, :], IDF[:, :]),
                wait=[g_t] + ps_wait(1), inc=[("tv", 1)])
            p.w("vector", "tensor_copy", (GFM[0:8, j * 128:(j + 1) * 128], PS[1][0:8, 0:128]),
                wait=[p.tok("tv")], inc=[("vv", 1)])
            psg[1] = p.tok("vv")
        p.w("vector", "tensor_copy", (G8H[0:8, :], GFM[0:8, :]))
        p.w("vector", "scalar_tensor_tensor",
            (G8L[0:8, :], GFM[0:8, :], 1.0, G8H[0:8, :], OP.mult, OP.subtract),
            inc=[("vv", 1)])
        g8_t = p.tok("vv")
        at_tiles = [r_at.load(d_at.ap()[er, :, :]) for er in range(2)]
        for er in range(4):
            hb_ = (2, 5)[er % 2]
            att, atk, aslot = at_tiles[er]
            for dch in range(CH):
                mm(PS[hb_][:, :], att[:, dch * 128:(dch + 1) * 128],
                   XB[:, dch * TW:(dch + 1) * TW],
                   start=(dch == 0), stop=(dch == CH - 1),
                   wait=([atk] + ps_wait(hb_)) if dch == 0 else (),
                   inc=([("tv", 1)] if dch == CH - 1 else ()))
            h_t = p.tok("tv")
            r_at.done(aslot, h_t)
            if er + 2 < 4:
                at_tiles.append(r_at.load(d_at.ap()[er + 2, :, :]))
            else:
                at_tiles.append(None)
            mm(PS[3][:, :], G8A[:, er * 128:(er + 1) * 128], G8H[0:8, :],
               start=True, stop=False, wait=[g8_t] + ps_wait(3))
            mm(PS[3][:, :], G8A[:, er * 128:(er + 1) * 128], G8L[0:8, :],
               start=False, stop=True, inc=[("tv", 1)])
            grep_t = p.tok("tv")
            RGF = VSTG[:, :].bitcast(F32)
            p.w("vector", "tensor_mul",
                (RGF[:, 0:512], PS[3][:, :], RSTD[:, tt * TW:(tt + 1) * TW]),
                wait=[grep_t], inc=[("vv", 1)])
            psg[3] = p.tok("vv")
            p.w("vector", "tensor_mul",
                (HSB[:, er * TW:(er + 1) * TW], PS[hb_][:, :], RGF[:, 0:512]),
                wait=[h_t], inc=[("vv", 1)])
            psg[hb_] = p.tok("vv")
        h_all = p.tok("vv")
        for hh in range(2):
            xt = p.dma("sync", xin_ap(), d_y.ap()[2 * tt + hh, :, :],
                       wait=([st["xstg"]] if st["xstg"] else []), sem="sx")
            p.w("vector", wait=[xt])
            for oc in range(CH):
                ob = (4, 2)[oc % 2]
                btt, btk, bslot = r_bt.load(d_bt.ap()[oc, :, :])
                for er in range(4):
                    mm(PS[ob][:, 0:HW], btt[:, er * 128:(er + 1) * 128],
                       HSB[:, er * TW + hh * HW: er * TW + hh * HW + HW],
                       start=(er == 0), stop=(er == 3),
                       wait=([btk, h_all] + ps_wait(ob)) if er == 0 else (),
                       inc=([("tv", 1)] if er == 3 else ()))
                y_t = p.tok("tv")
                r_bt.done(bslot, y_t)
                xs = XSTG[:, oc * HW:(oc + 1) * HW]
                p.w("vector", "scalar_tensor_tensor",
                    (xs, PS[ob][:, 0:HW], 0.0, xs, OP.bypass, OP.add),
                    wait=[y_t], inc=[("vv", 1)])
                psg[ob] = p.tok("vv")
            st["xstg"] = p.dma("sync", d_y.ap()[2 * tt + hh, :, :], xin_ap(),
                               wait=[p.tok("vv")], sem="sy")
        p.bar()

    r_wo = Ring(2, 2048, 0, "rwo")
    TOPT = SCR_BF[:, 0:CH * TW]
    BOTT = SCR_BF[:, CH * TW:2 * CH * TW]
    QPH_ = XB[0:2, 0:32 * TW]
    QPL_ = XB[0:2, 32 * TW:64 * TW]

    def build_qpair(tt, half, wait=()):
        tks = []
        h0 = 64 * half
        tks.append(p.dma("sync", QPH_[0:1, :].rearrange("q (r t) -> q r t", r=32),
                         QHI()[h0:h0 + 64:2, tt * TW:(tt + 1) * TW], wait=wait, sem="qp"))
        tks.append(p.dma("sync", QPH_[1:2, :].rearrange("q (r t) -> q r t", r=32),
                         QHI()[h0 + 1:h0 + 64:2, tt * TW:(tt + 1) * TW], wait=wait, sem="qp"))
        return tks

    def build_vstack(pr, wait=()):
        sl = pr % 2
        c0 = sl * 224 + 96
        g0 = pr * 16
        p.w("vector", "tensor_copy", (VST[0:32, c0:c0 + 16], VGALL[0:32, g0:g0 + 16]),
            wait=wait)
        p.w("vector", "tensor_copy", (VST[64:96, c0 + 16:c0 + 32], VGALL[64:96, g0:g0 + 16]))
        p.w("vector", "tensor_copy", (VSB[32:64, c0:c0 + 16], VGALL[32:64, g0:g0 + 16]))
        p.w("vector", "tensor_copy", (VSB[96:128, c0 + 16:c0 + 32], VGALL[96:128, g0:g0 + 16]),
            inc=[("vv", 1)])
        return p.tok("vv")

    def attn_a_sparse(tt):
        vs_tok = {}
        pair_end = {}
        for half in range(2):
            qp = build_qpair(tt, half,
                             wait=([pair_end[31]] if half == 1 else ()))
            for tk in qp:
                p.w("tensor", wait=[tk])
            if half == 0:
                vs_tok[0] = build_vstack(0)
                vs_tok[1] = build_vstack(1)
            for k in range(32):
                pr = 32 * half + k
                ch = pr // 4
                sl = pr % 2
                mm(PS[sl][:, :], EPH[:, :], QPH_[0:2, k * TW:(k + 1) * TW],
                   start=True, stop=True, wait=ps_wait(sl), inc=[("tv", 1)])
                arg_t = p.tok("tv")
                pb = PTB[:, (pr % 4) * TW:(pr % 4 + 1) * TW]
                ew = [arg_t] + ([pair_end[pr - 4]] if pr >= 4 else [])
                p.w("scalar", "activation", (pb, PS[sl][:, :], AF.Exp),
                    {"bias": ZRO[:, 0:1], "scale": SCL[:, pr:pr + 1]},
                    wait=ew, inc=[("sv", 1)])
                p.w("vector", "tensor_scalar", (pb, pb, 1e30, None, OP.min),
                    wait=[p.tok("sv")], inc=[("vv", 1)])
                psg[sl] = p.tok("vv")
                exp_t = p.tok("vv")
                mm(PS[2][:, :], SLT[:, 128 - 2 * pr:256 - 2 * pr], pb,
                   start=(pr == 0), stop=(pr == 63),
                   wait=[exp_t] + (ps_wait(2) if pr == 0 else []))
                mm(PS[3][:, :], SLB[:, 128 - 2 * pr:256 - 2 * pr], pb,
                   start=(pr == 0), stop=(pr == 63),
                   wait=(ps_wait(3) if pr == 0 else []),
                   inc=([("tv", 1)] if pr == 63 else ()))
                it, ib = 4 + (ch % 2), 6 + (ch % 2)
                w0 = (pr % 2) * 224 + 96 - (pr % 4) * 32
                mm(PS[it][:, :], VST[:, w0:w0 + 128], pb,
                   start=(pr % 4 == 0), stop=(pr % 4 == 3),
                   wait=[vs_tok[pr]] + (ps_wait(it) if pr % 4 == 0 else []))
                mm(PS[ib][:, :], VSB[:, w0:w0 + 128], pb,
                   start=(pr % 4 == 0), stop=(pr % 4 == 3),
                   wait=(ps_wait(ib) if pr % 4 == 0 else []),
                   inc=[("tv", 1)])
                pair_end[pr] = p.tok("tv")
                if pr % 4 == 3:
                    pv_t = pair_end[pr]
                    p.w("vector", "tensor_copy",
                        (TOPT[:, ch * TW:(ch + 1) * TW], PS[it][:, :]), wait=[pv_t])
                    p.w("vector", "tensor_copy",
                        (BOTT[:, ch * TW:(ch + 1) * TW], PS[ib][:, :]), inc=[("vv", 1)])
                    psg[it] = psg[ib] = p.tok("vv")
                if pr + 2 <= 63:
                    vs_tok[pr + 2] = build_vstack(pr + 2, wait=[pair_end[pr]])
        s_done = p.tok("tv")
        WF = VSTG[:, :].bitcast(F32)
        p.w("vector", "reciprocal", (WF[:, 0:512], PS[2][:, :]), wait=[s_done],
            inc=[("vv", 1)])
        psg[2] = p.tok("vv")
        p.w("vector", "tensor_mul",
            (RSB[:, 0:512], WF[:, 0:512], MQ()[:, tt * TW:(tt + 1) * TW]), inc=[("vv", 1)])
        wtop_t = p.tok("vv")
        p.w("vector", "reciprocal", (WF[:, 0:512], PS[3][:, :]), wait=[wtop_t],
            inc=[("vv", 1)])
        psg[3] = p.tok("vv")
        p.w("vector", "tensor_mul",
            (PTB[:, 0:512], WF[:, 0:512], MQC()[:, tt * TW:(tt + 1) * TW]), inc=[("vv", 1)])
        w_t = p.tok("vv")
        wtr_free = None
        for hh in range(2):
            h0 = hh * HW
            dwait = [w_t] + ([wtr_free] if wtr_free else []) + \
                ([p.tok("sy")] if "sy" in p.cnt else [])
            for c in range(CH):
                p.dma("sync", WTR[0:8, c * HW:(c + 1) * HW],
                      RSB[8 * c:8 * c + 8, h0:h0 + HW], wait=dwait, sem="wt")
                t2 = p.dma("sync", WTR[0:8, 4096 + c * HW: 4096 + (c + 1) * HW],
                           PTB[8 * c:8 * c + 8, h0:h0 + HW], wait=dwait, sem="wt")
            t1 = t2
            for ch in range(CH):
                i = ch % 2
                mm(PS[0 + i][:, 0:HW], REP8[:, :], WTR[0:8, ch * HW:(ch + 1) * HW],
                   start=True, stop=True, wait=[t1, t2] + ps_wait(0 + i))
                mm(PS[2 + i][:, 0:HW], REP8[:, :],
                   WTR[0:8, 4096 + ch * HW: 4096 + (ch + 1) * HW],
                   start=True, stop=True, wait=ps_wait(2 + i), inc=[("tv", 1)])
                rep_t = p.tok("tv")
                tch = TOPT[:, ch * TW + h0: ch * TW + h0 + HW]
                bch = BOTT[:, ch * TW + h0: ch * TW + h0 + HW]
                p.w("vector", "tensor_mul", (tch, tch, PS[0 + i][:, 0:HW]), wait=[rep_t])
                p.w("vector", "tensor_mul", (bch, bch, PS[2 + i][:, 0:HW]))
                p.w("vector", "tensor_add", (tch, tch, bch), inc=[("vv", 1)])
                psg[0 + i] = psg[2 + i] = p.tok("vv")
            wtr_free = p.tok("tv")
        return p.tok("vv")

    for tt in range(NTT):
        attn_a_sparse(tt)
        p.bar()
        if stop_after == "probe5" and tt == 0:
            p.dma("gpsimd", d_dbg.ap()[0, :, :], TOPT[:, 0:2048])
            p.dma("gpsimd", d_dbg.ap()[1, :, :], TOPT[:, 2048:4096])
            p.dma("gpsimd", d_dbg.ap()[2, :, 0:64], VALS[:, :])
            p.dma("gpsimd", d_dbg.ap()[2, :, 64:128], TIF[:, :])
            p.dma("gpsimd", d_dbg.ap()[2, :, 128:192], IXVU[:, :])
            p.dma("gpsimd", d_dbg.ap()[2, :, 192:256], IXKU[:, :])
            p.dma("gpsimd", d_dbg.ap()[3, :, 0:2048], QHI())
            p.dma("gpsimd", d_dbg.ap()[4, :, 0:2048], QLO())
            p.qfence("sync")
            p.qfence("gpsimd")
            p.bar()
            with nc.Block() as block:
                p.replay(block)
            stack.close()
            return nc
        for hh in range(2):
            xt = p.dma("sync", xin_ap(), d_x.ap()[2 * tt + hh, :, :],
                       wait=([st["xstg"]] if st["xstg"] else []), sem="sx")
            p.w("vector", wait=[xt])
            for oc in range(CH):
                ob = (5, 2, 3)[oc % 3]
                wtile, wtok, slot = r_wo.load(d_wo.ap()[oc, :, :])
                for dch in range(CH):
                    mm(PS[ob][:, 0:HW], wtile[:, dch * 128:(dch + 1) * 128],
                       TOPT[:, dch * TW + hh * HW: dch * TW + hh * HW + HW],
                       start=(dch == 0), stop=(dch == CH - 1),
                       wait=([wtok] + ps_wait(ob)) if dch == 0 else (),
                       inc=([("tv", 1)] if dch == CH - 1 else ()))
                wo_t = p.tok("tv")
                r_wo.done(slot, wo_t)
                xs = XSTG[:, oc * HW:(oc + 1) * HW]
                p.w("vector", "scalar_tensor_tensor",
                    (xs, PS[ob][:, 0:HW], 0.0, xs, OP.bypass, OP.add),
                    wait=[wo_t], inc=[("vv", 1)])
                psg[ob] = p.tok("vv")
            p.bar()
            if stop_after == "sub1":
                st["xstg"] = p.dma("sync", d_y.ap()[2 * tt + hh, :, :], xin_ap(),
                                   wait=[p.tok("vv")], sem="sy")
                continue
            phase_A(tt, hh, full_mode=False)
            st["xstg"] = p.dma("sync", d_y.ap()[2 * tt + hh, :, :], xin_ap(),
                               wait=[p.tok("vv")], sem="sy")
            p.bar()
        if stop_after == "sub1":
            continue
        rstd_tm(tt)
        ffn_tt(tt)
        p.bar()

    if stop_after in ("sub1", "sub2"):
        p.qfence("sync")
        p.qfence("gpsimd")
        p.bar()
        with nc.Block() as block:
            p.replay(block)
        stack.close()
        return nc

    for tt in range(NTT):
        for hh in range(2):
            xt = p.dma("sync", xin_ap(), d_y.ap()[2 * tt + hh, :, :],
                       wait=([st["xstg"]] if st["xstg"] else []), sem="sx")
            phase_A(tt, hh, full_mode=True, wait_x=[xt])
            p.bar()
        rstd_tm(tt)
    p.bar()

    if stop_after == "sub3a":
        p.qfence("sync")
        p.qfence("gpsimd")
        p.bar()
        with nc.Block() as block:
            p.replay(block)
        stack.close()
        return nc

    QH2 = SCR_BF[:, 0:4096]
    KH2 = SCR_BF[:, 4096:8192]
    VTM = SCR_BF[:, 8192:12288]
    QSQ = SCR_BF[:, 12288:14336]
    SEXP = SCR_F32[:, 6144:6656]
    SEH = SCR_BF[:, 13312:13824]
    SEL = SCR_BF[:, 13824:14336]
    r_qkv = Ring(4, 2048, 0, "rqv")
    r_vv = Ring(16, 256, 8192, "rvv")

    def xbf(dch, c0, cw):
        return XB[:, dch * T + c0: dch * T + c0 + cw]

    hp_prev = None
    for hp in range(8):
        if hp_prev is not None:
            p.w("vector", wait=[hp_prev])
            p.w("gpsimd", wait=[hp_prev])
        echs = [2 * hp, 2 * hp + 1, 16 + 2 * hp, 16 + 2 * hp + 1]
        tiles = [r_qkv.load(d_wqkv.ap()[e, :, :]) for e in echs]
        vtiles = [r_vv.load(d_wqkvv.ap()[dch, :, 256 * hp:256 * (hp + 1)]) for dch in range(CH)]
        p.dma("gpsimd", QKVBP[0:1, 0:256], d_qkvbrow.ap()[0:1, 256 * hp:256 * (hp + 1)], sem="qb")
        p.dma("gpsimd", QKVBP[0:1, 256:512],
              d_qkvbrow.ap()[0:1, 2048 + 256 * hp:2048 + 256 * (hp + 1)], sem="qb")
        bq_t = p.dma("gpsimd", QKVBP[0:1, 512:768],
                     d_qkvbrow.ap()[0:1, 4096 + 256 * hp:4096 + 256 * (hp + 1)], sem="qb")
        for i, e in enumerate(echs):
            wt, wtk, _ = tiles[i]
            for tt in range(NTT):
                qb_ = (0, 5)[tt % 2]
                for dch in range(CH):
                    mm(PS[qb_][:, :], wt[:, dch * 128:(dch + 1) * 128],
                       xbf(dch, tt * TW, TW), start=(dch == 0), stop=False,
                       wait=([wtk] + ps_wait(qb_)) if dch == 0 else ())
                boff = (i % 2) * 128 + (0 if i < 2 else 256)
                mm(PS[qb_][:, :], QKVBP[0:1, boff:boff + 128], SQVB(tt),
                   start=False, stop=True, wait=[bq_t], inc=[("tv", 1)])
                qk_t = p.tok("tv")
                dst = QH2 if i < 2 else KH2
                dsl = dst[:, (i % 2) * 2048 + tt * TW: (i % 2) * 2048 + (tt + 1) * TW]
                if i < 2:
                    p.w("vector", "tensor_mul",
                        (dsl, PS[qb_][:, :], RSTD[:, tt * TW:(tt + 1) * TW]),
                        wait=[qk_t], inc=[("vv", 1)])
                else:
                    p.w("vector", "scalar_tensor_tensor",
                        (dsl, PS[qb_][:, :], SMS, RSTD[:, tt * TW:(tt + 1) * TW],
                         OP.mult, OP.mult), wait=[qk_t], inc=[("vv", 1)])
                psg[qb_] = p.tok("vv")
        for tc in range(CH):
            tt, j = tc // 4, tc % 4
            vb2 = (1, 6)[tc % 2]
            for dch in range(CH):
                vt, vtk, _ = vtiles[dch]
                mm(PS[vb2][:, 0:256], xbf(dch, tc * 128, 128), vt,
                   start=(dch == 0), stop=False,
                   wait=(([vtk] if tc == 0 else []) + (ps_wait(vb2) if dch == 0 else [])))
            mm(PS[vb2][:, 0:256], SQVB(tt)[0:1, j * 128:(j + 1) * 128],
               QKVBP[0:1, 512:768],
               start=False, stop=True, wait=[bq_t], inc=[("tv", 1)])
            v_t = p.tok("tv")
            p.w("vector", "tensor_scalar",
                (VTM[:, tc * 256:(tc + 1) * 256], PS[vb2][:, 0:256], RSTM[:, tc:tc + 1],
                 None, OP.mult), wait=[v_t], inc=[("vv", 1)])
            psg[vb2] = p.tok("vv")
        p.bar()
        for a in range(2):
            h = 2 * hp + a
            negcb_toks = []
            for tt in range(NTT):
                sb2 = (2, 7)[tt % 2]
                q0 = a * 2048 + tt * TW
                mm(PS[sb2][:, :], QH2[:, q0:q0 + 128],
                   KH2[:, a * 2048:a * 2048 + TW],
                   start=True, stop=True, wait=ps_wait(sb2), inc=[("tv", 1)])
                p.w("vector", "reduce_max", (TV[:, tt:tt + 1], PS[sb2][:, :], AX),
                    wait=[p.tok("tv")], inc=[("vv", 1)])
                psg[sb2] = p.tok("vv")
                p.w("tensor", "transpose", (PS[sb2][0:1, 0:128], TV[:, tt:tt + 1],
                                            IDF[:, :]),
                    wait=[p.tok("vv")] + ps_wait(sb2), inc=[("tv", 1)])
                p.w("vector", "reduce_max",
                    (CELLS[0:1, 4 + tt:5 + tt], PS[sb2][0:1, 0:128], AX),
                    wait=[p.tok("tv")], inc=[("vv", 1)])
                psg[sb2] = p.tok("vv")
                p.w("vector", "tensor_scalar",
                    (CELLS[0:1, 8 + tt:9 + tt].bitcast(BF16)[0:1, 0:1],
                     CELLS[0:1, 4 + tt:5 + tt], -1.0,
                     -30.0, OP.mult, OP.add), inc=[("vv", 1)])
                mm(PS[sb2][:, 0:1], ONE1[:, :],
                   CELLS[0:1, 8 + tt:9 + tt].bitcast(BF16)[0:1, 0:1],
                   start=True, stop=True, wait=[p.tok("vv")] + ps_wait(sb2),
                   inc=[("tv", 1)])
                p.w("vector", "tensor_copy", (NEGCB[:, tt:tt + 1], PS[sb2][:, 0:1]),
                    wait=[p.tok("tv")], inc=[("vv", 1)])
                psg[sb2] = p.tok("vv")
                negcb_toks.append(p.tok("vv"))
            for tt in range(NTT):
                negcb_t = negcb_toks[tt]
                accs = {}
                for kch in range(CH):
                    sb = (2, 3, 4)[kch % 3]
                    sl = kch % 4
                    mm(PS[sb][:, :],
                       KH2[:, a * 2048 + kch * 128: a * 2048 + (kch + 1) * 128],
                       QH2[:, a * 2048 + tt * TW: a * 2048 + (tt + 1) * TW],
                       start=True, stop=True,
                       wait=ps_wait(sb) + ([negcb_t] if kch == 0 else []),
                       inc=[("tv", 1)])
                    arg_t = p.tok("tv")
                    pb = PTB[:, sl * TW:(sl + 1) * TW]
                    p.w("scalar", "activation", (pb, PS[sb][:, :], AF.Exp),
                        {"bias": NEGCB[:, tt:tt + 1], "scale": 1.0},
                        wait=[arg_t] + ([accs[kch - 4]] if kch >= 4 else []),
                        inc=[("sv", 1)])
                    psg[sb] = p.tok("sv")
                    exp_t = p.tok("sv")
                    if kch == 0:
                        p.w("vector", "tensor_copy", (SEXP, pb), wait=[exp_t],
                            inc=[("vv", 1)])
                    else:
                        p.w("vector", "scalar_tensor_tensor",
                            (SEXP, pb, 0.0, SEXP, OP.bypass, OP.add),
                            wait=[exp_t], inc=[("vv", 1)])
                    accs[kch] = p.tok("vv")
                    mm(PS[6][:, :], VTM[:, kch * 256 + a * 128: kch * 256 + (a + 1) * 128],
                       pb, start=(kch == 0), stop=(kch == CH - 1),
                       wait=[exp_t] + (ps_wait(6) if kch == 0 else []),
                       inc=([("tv", 1)] if kch == CH - 1 else ()))
                ctx_t = p.tok("tv")
                p.w("vector", "tensor_copy", (SEH, SEXP),
                    wait=([st["sum_t"]] if st.get("sum_t") else ()))
                p.w("vector", "scalar_tensor_tensor",
                    (SEL, SEXP, 1.0, SEH, OP.mult, OP.subtract), inc=[("vv", 1)])
                mm(PS[5][0:1, :], ONEC[:, :], SEH, start=True, stop=False,
                   wait=[p.tok("vv")] + ps_wait(5))
                mm(PS[5][0:1, :], ONEC[:, :], SEL, start=False, stop=False)
                mm(PS[5][0:1, :], ONE1[0:1, 0:1], TINY[0:1, :],
                   start=False, stop=True, inc=[("tv", 1)])
                sum_t = p.tok("tv")
                st["sum_t"] = sum_t
                p.w("vector", "reciprocal", (SCRW, PS[5][0:1, 0:256]),
                    wait=[ctx_t, sum_t])
                p.w("vector", "reciprocal", (SCRW2, PS[5][0:1, 256:512]), inc=[("vv", 1)])
                psg[5] = p.tok("vv")
                p.w("vector", "tensor_copy", (RHB[0:1, 0:256], SCRW))
                p.w("vector", "tensor_copy", (RHB[0:1, 256:512], SCRW2))
                p.w("vector", "scalar_tensor_tensor",
                    (RLB[0:1, 0:256], SCRW, 1.0, RHB[0:1, 0:256], OP.mult, OP.subtract))
                p.w("vector", "scalar_tensor_tensor",
                    (RLB[0:1, 256:512], SCRW2, 1.0, RHB[0:1, 256:512], OP.mult, OP.subtract),
                    inc=[("vv", 1)])
                mm(PS[7][:, :], ONE1[:, :], RHB, start=True, stop=False,
                   wait=[p.tok("vv")] + ps_wait(7))
                mm(PS[7][:, :], ONE1[:, :], RLB, start=False, stop=True, inc=[("tv", 1)])
                p.w("vector", "tensor_copy", (RSB[:, 0:TW], PS[7][:, :]), wait=[p.tok("tv")],
                    inc=[("vv", 1)])
                psg[7] = p.tok("vv")
                vs = VSTG[:, (tt % 2) * TW:(tt % 2 + 1) * TW]
                vw = [st["vstg"][tt % 2]] if st["vstg"][tt % 2] else []
                p.w("vector", "tensor_mul", (vs, PS[6][:, :], RSB[:, 0:TW]), wait=vw,
                    inc=[("vv", 1)])
                psg[6] = p.tok("vv")
                st["vstg"][tt % 2] = p.dma(
                    "sync", d_ctx.ap()[h, :, tt * TW:(tt + 1) * TW], vs,
                    wait=[p.tok("vv")], sem="vs%d" % (tt % 2))
        qk_last = p.tok("tv")
        hp_prev = qk_last
        for i in range(4):
            r_qkv.done(tiles[i][2], qk_last)
        for dch in range(CH):
            r_vv.done(vtiles[dch][2], qk_last)
        if stop_after == "probe4" and hp == 0:
            p.qfence("sync")
            p.bar()
            for h in range(2):
                p.dma("gpsimd", d_dbg.ap()[h, :, :], d_ctx.ap()[h, :, :])
            p.dma("gpsimd", d_dbg.ap()[2, :, :], QH2[:, 0:2048])
            p.dma("gpsimd", d_dbg.ap()[3, :, :], KH2[:, 0:2048])
            p.dma("gpsimd", d_dbg.ap()[4, :, 0:512], RSB[:, 0:512])
            p.dma("gpsimd", d_dbg.ap()[4, :, 512:513], NEGCB[:, 0:1],
                  allow_slow_non_contiguous=True)
            p.dma("gpsimd", d_dbg.ap()[4, 0:1, 520:536],
                  CELLS[0:1, 0:16], allow_slow_non_contiguous=True)
            p.dma("gpsimd", d_dbg.ap()[4, 0:1, 536:1048], RHB[0:1, 0:512])
            p.dma("gpsimd", d_dbg.ap()[4, :, 1048:2048], VTM[:, 0:1000])
            p.qfence("sync")
            p.qfence("gpsimd")
            p.bar()
            with nc.Block() as block:
                p.replay(block)
            stack.close()
            return nc
        if stop_after and stop_after.startswith("sub3b") and \
                hp == int(stop_after[5:] or 0):
            break
    p.bar()

    if stop_after and stop_after.startswith("sub3b"):
        p.qfence("sync")
        p.qfence("gpsimd")
        p.bar()
        with nc.Block() as block:
            p.replay(block)
        stack.close()
        return nc

    r_wout = Ring(2, 2048, 0, "rwt")
    CTXT = SCR_BF[:, 0:CH * TW]
    ctx_done = [p.tok("vs0"), p.tok("vs1")]
    for tt in range(NTT):
        ctk = []
        for g in range(4):
            ctk.append(p.dma("sync",
                             CTXT.rearrange("q (c t) -> q c t", c=CH)[:, 4 * g:4 * g + 4, :],
                             d_ctx.ap()[4 * g:4 * g + 4, :, tt * TW:(tt + 1) * TW]
                             .rearrange("c q t -> q c t"),
                             wait=(ctx_done if (tt == 0 and g == 0) else ()),
                             sem="ct"))
        for tk in ctk:
            p.w("tensor", wait=[tk])
        if stop_after == "sub3c":
            p.qfence("sync")
            p.qfence("gpsimd")
            p.bar()
            with nc.Block() as block:
                p.replay(block)
            stack.close()
            return nc
        for hh in range(2):
            xt = p.dma("sync", xin_ap(), d_y.ap()[2 * tt + hh, :, :],
                       wait=([st["xstg"]] if st["xstg"] else []), sem="sx")
            p.w("vector", wait=[xt])
            for oc in range(CH):
                if stop_after == "sub3d" and oc == 4:
                    break
                ob = (5, 2, 3)[oc % 3]
                wtile, wtok, slot = r_wout.load(d_wout.ap()[oc, :, :])
                for dch in range(CH):
                    mm(PS[ob][:, 0:HW], wtile[:, dch * 128:(dch + 1) * 128],
                       CTXT[:, dch * TW + hh * HW: dch * TW + hh * HW + HW],
                       start=(dch == 0), stop=(dch == CH - 1),
                       wait=([wtok] + ps_wait(ob)) if dch == 0 else (),
                       inc=([("tv", 1)] if dch == CH - 1 else ()))
                wo_t = p.tok("tv")
                r_wout.done(slot, wo_t)
                xs = XSTG[:, oc * HW:(oc + 1) * HW]
                p.w("vector", "scalar_tensor_tensor",
                    (xs, PS[ob][:, 0:HW], OUTB[:, oc:oc + 1], xs, OP.add, OP.add),
                    wait=[wo_t], inc=[("vv", 1)])
                psg[ob] = p.tok("vv")
            if stop_after == "sub3d":
                p.qfence("sync")
                p.qfence("gpsimd")
                p.bar()
                with nc.Block() as block:
                    p.replay(block)
                stack.close()
                return nc
            p.bar()
            if stop_after == "sub3":
                st["xstg"] = p.dma("sync", d_y.ap()[2 * tt + hh, :, :], xin_ap(),
                                   wait=[p.tok("vv")], sem="sy")
                continue
            phase_A(tt, hh, full_mode=False)
            st["xstg"] = p.dma("sync", d_y.ap()[2 * tt + hh, :, :], xin_ap(),
                               wait=[p.tok("vv")], sem="sy")
            p.bar()
        if stop_after == "sub3":
            continue
        rstd_tm(tt)
        ffn_tt(tt)
        p.bar()

    p.qfence("sync")
    p.qfence("gpsimd")
    p.bar()
    with nc.Block() as block:
        p.replay(block)
    stack.close()
    return nc


def _layout_inputs(inputs):
    f = np.float32
    x = np.asarray(inputs["L0_attn_a_x"], dtype=f)
    Wq = np.asarray(inputs["L0_attn_a_Wq"], dtype=f)
    Wk = np.asarray(inputs["L0_attn_a_Wk"], dtype=f)
    Wv = np.asarray(inputs["L0_attn_a_Wv"], dtype=f)
    Wo = np.asarray(inputs["L0_attn_a_Wo"], dtype=f)
    Wqkv = np.asarray(inputs["L0_attn_b_Wqkv_w"], dtype=f)
    qkvb = np.asarray(inputs["L0_attn_b_Wqkv_b"], dtype=f)
    outw = np.asarray(inputs["L0_attn_b_out_w"], dtype=f)
    outb = np.asarray(inputs["L0_attn_b_out_b"], dtype=f)
    Am = np.asarray(inputs["L0_ffn_A"], dtype=f)
    Bm = np.asarray(inputs["L0_ffn_B"], dtype=f)
    Wr = np.asarray(inputs["L0_ffn_Wr"], dtype=f)
    br = np.asarray(inputs["L0_ffn_br"], dtype=f)

    def lhs_blk(WT, nech):
        E = WT.shape[1]
        return np.ascontiguousarray(
            WT.reshape(CH, 128, nech, E // nech).transpose(2, 1, 0, 3)
            .reshape(nech, 128, CH * (E // nech)))

    wqkT = lhs_blk(np.ascontiguousarray(np.concatenate([Wq.T, Wk.T], axis=1)), 2)
    wvT = np.ascontiguousarray(Wv.T.reshape(CH, 128, 4, 512).transpose(2, 0, 1, 3))
    woT = lhs_blk(np.ascontiguousarray(Wo.T), CH)
    WqkvT = np.ascontiguousarray(Wqkv.T)
    wqkvT = lhs_blk(np.ascontiguousarray(WqkvT[:, 0:4096]), 32)
    wqkvv = np.ascontiguousarray(WqkvT[:, 4096:6144].reshape(CH, 128, 2048))
    woutT = lhs_blk(np.ascontiguousarray(outw.T), CH)
    AT = np.ascontiguousarray(Am.transpose(2, 0, 1).reshape(D, 512))
    atT = lhs_blk(AT, 4)
    BT = np.ascontiguousarray(Bm.transpose(0, 2, 1).reshape(512, D))
    btT = np.ascontiguousarray(
        BT.reshape(4, 128, CH, 128).transpose(2, 1, 0, 3).reshape(CH, 128, 512))
    wrT = np.ascontiguousarray(Wr.T.reshape(CH, 128, 8).transpose(1, 0, 2).reshape(128, CH * 8))
    qkvbrow = np.ascontiguousarray(qkvb.reshape(1, 6144))
    outb2 = np.ascontiguousarray(outb.reshape(16, 128).T)
    br2 = np.ascontiguousarray(br.reshape(8, 1))

    shared = {
        "wqk": wqkT, "wv": wvT, "wo": woT, "wqkv": wqkvT, "wqkvv": wqkvv,
        "wout": woutT, "at": atT, "bt": btT, "wr": wrT,
        "qkvbrow": qkvbrow, "outb2": outb2, "br": br2,
    }
    in_maps = []
    for c in range(8):
        xf = np.ascontiguousarray(x[c].T)
        xblk = np.ascontiguousarray(
            xf.reshape(CH, 128, NTT * 2, HW).transpose(2, 1, 0, 3)
            .reshape(NTT * 2, 128, CH * HW))
        m = dict(shared)
        m["x"] = xblk
        in_maps.append(m)
    return in_maps


_NC_CACHE = {}


def kernel(**inputs):
    in_maps = _layout_inputs(inputs)
    key = "full"
    if key not in _NC_CACHE:
        _NC_CACHE[key] = build_nc()
    nc = _NC_CACHE[key]
    res = run_bass_kernel_spmd(nc, in_maps, core_ids=list(range(8)))
    outs = []
    for c in range(8):
        yblk = res.results[c]["y"].reshape(8, 128, CH, HW)
        xf = yblk.transpose(2, 1, 0, 3).reshape(D, T)
        outs.append(np.ascontiguousarray(xf.T))
    return np.stack(outs, axis=0).astype(np.float32)


# version 19 (final)
# speedup vs baseline: 1.0305x; 1.0059x over previous
import numpy as np
from contextlib import ExitStack

import concourse.bass as bass
import concourse.mybir as mybir
from concourse.bass_utils import run_bass_kernel_spmd
from concourse.bass_core_rust import IndirectOffsetOnAxis

F32 = mybir.dt.float32
BF16 = mybir.dt.bfloat16
U32 = mybir.dt.uint32
U8 = mybir.dt.uint8
OP = mybir.AluOpType
AF = mybir.ActivationFunctionType
AX = mybir.AxisListType.X

D = 2048
T = 2048
CH = 16
NTT = 4
TW = 512
HW = 256
HB = 16
SMS = float(1.0 / np.sqrt(128.0))
EPS = 1e-5
ENGINES = ("sync", "gpsimd", "scalar", "vector", "tensor")


class Prog:
    def __init__(self, nc):
        self.nc = nc
        self.cmd = {e: [] for e in ENGINES}
        self.cnt = {}
        self.sem = {}
        self.nbar = 0
        self.qsems = {}

    def mksem(self, stack, *names):
        self.stack = stack
        for nm in names:
            self.sem[nm] = stack.enter_context(self.nc.semaphore(nm))
            self.cnt[nm] = 0

    def sem_for(self, nm):
        if nm not in self.sem:
            self.sem[nm] = self.stack.enter_context(self.nc.semaphore(nm))
            self.cnt[nm] = 0
        return nm

    def w(self, eng, op=None, args=(), kwargs=None, wait=(), inc=()):
        for nm, n in inc:
            self.cnt[nm] += n
        self.cmd[eng].append((op, tuple(args), dict(kwargs or {}), tuple(wait), tuple(inc)))
        if eng == "vector" and op not in (None, "drain"):
            out_ap = (kwargs or {}).get("out", args[0] if args else None)
            try:
                small = out_ap is not None and out_ap.free_nbytes <= 512
            except Exception:
                small = True
            if small:
                self.cmd[eng].append(("drain", (), {}, (), ()))

    def tok(self, nm):
        return (nm, self.cnt[nm])

    def bar(self):
        self.nbar += 1
        k = self.nbar
        for e in ENGINES:
            self.cnt["b_" + e] += 1
            if e in ("sync",):
                self.cmd[e].append((None, (), {}, (), (("b_" + e, 1),)))
            else:
                self.cmd[e].append(("drain", (), {}, (), (("b_" + e, 1),)))
        for e in ENGINES:
            for o in ENGINES:
                if o != e:
                    self.w(e, wait=[("b_" + o, k)])

    def dma(self, q, out_ap, in_ap, wait=(), sem=None, **kw):
        semn = self.sem_for(sem) if sem else ("gd" if q == "gpsimd" else "qd")
        kwargs = {"out": out_ap, "in_": in_ap}
        kwargs.update(kw)
        self.qsems.setdefault(q, set()).add(semn)
        self.w(q, "dma_start", kwargs=kwargs, wait=wait, inc=[(semn, 16)])
        return self.tok(semn)

    def qfence(self, q):
        base = "gd" if q == "gpsimd" else "qd"
        sems = set(self.qsems.get(q, ())) | {base}
        self.w(q, wait=[self.tok(s) for s in sorted(sems)])

    def replay(self, block):
        sem = self.sem

        def mk(e):
            def run(eng):
                for op, args, kwargs, wait, inc in self.cmd[e]:
                    for nm, v in wait:
                        eng.wait_ge(sem[nm], v)
                    if op is not None:
                        inst = getattr(eng, op)(*args, **kwargs)
                        for nm, n in inc:
                            inst.then_inc(sem[nm], n)
                    else:
                        for nm, n in inc:
                            eng.sem_inc(sem[nm], n)
            return run

        block.sync(mk("sync"))
        block.gpsimd(mk("gpsimd"))
        block.scalar(mk("scalar"))
        block.vector(mk("vector"))
        block.tensor(mk("tensor"))


def build_nc(stop_after=None):
    nc = bass.Bass("TRN2", target_bir_lowering=False)

    d_x = nc.dram_tensor("x", [NTT * 2, 128, CH * HW], F32, kind="ExternalInput")
    d_y = nc.dram_tensor("y", [NTT * 2, 128, CH * HW], F32, kind="ExternalOutput")
    d_wqk = nc.dram_tensor("wqk", [2, 128, CH * 128], F32, kind="ExternalInput")
    d_wv = nc.dram_tensor("wv", [4, CH, 128, 512], F32, kind="ExternalInput")
    d_wo = nc.dram_tensor("wo", [CH, 128, CH * 128], F32, kind="ExternalInput")
    d_wqkv = nc.dram_tensor("wqkv", [32, 128, CH * 128], F32, kind="ExternalInput")
    d_wqkvv = nc.dram_tensor("wqkvv", [CH, 128, 2048], F32, kind="ExternalInput")
    d_wout = nc.dram_tensor("wout", [CH, 128, CH * 128], F32, kind="ExternalInput")
    d_at = nc.dram_tensor("at", [4, 128, CH * 128], F32, kind="ExternalInput")
    d_bt = nc.dram_tensor("bt", [CH, 128, 4 * 128], F32, kind="ExternalInput")
    d_wr = nc.dram_tensor("wr", [128, CH * 8], F32, kind="ExternalInput")
    d_qkvbrow = nc.dram_tensor("qkvbrow", [1, 6144], F32, kind="ExternalInput")
    d_outb2 = nc.dram_tensor("outb2", [128, 16], F32, kind="ExternalInput")
    d_br = nc.dram_tensor("br", [8, 1], F32, kind="ExternalInput")
    d_dbg = nc.dram_tensor("dbg", [5, 128, 2048], F32, kind="ExternalOutput") \
        if stop_after in ("probe1", "probe0", "probe2", "probe3", "probe4",
                          "probe5") else None
    d_vtm = nc.dram_tensor("vtm_s", [T, D], BF16)
    d_keys = nc.dram_tensor("keys_s", [128, T], F32)
    d_ctx = nc.dram_tensor("ctx_s", [HB, 128, T], BF16)

    A = nc.alloc_sbuf_tensor
    XB = A("XB", [128, CH * T], BF16)
    XSTG = A("XSTG", [128, CH * HW], F32)
    SCR = A("SCR", [128, 32 * 1024], U8)
    WBUF = A("WBUF", [128, 25 * 1024], U8)
    P0 = A("P0", [128, 28 * 1024], U8)
    RSTD = A("RSTD", [128, T], F32)
    QA = A("QA", [128, T], F32)
    KAT = A("KAT", [128, T], F32)
    VGALL = A("VGALL", [128, 64 * 16], BF16)
    IDF = A("IDF", [128, 128], F32)
    ONE1 = A("ONE1", [1, 128], BF16)
    ONEC = A("ONEC", [128, 1], BF16)
    EPH = A("EPH", [2, 128], BF16)
    REP8 = A("REP8", [8, 128], BF16)
    G8A = A("G8A", [8, 512], BF16)
    SLT = A("SLT", [128, 256], BF16)
    SLB = A("SLB", [128, 256], BF16)
    VST = A("VST", [128, 448], BF16)
    VSB = A("VSB", [128, 448], BF16)
    TV = A("TV", [128, 64], F32)
    TIU = A("TIU", [128, 64], U32)
    TIF = A("TIF", [128, 64], F32)
    IXB = A("IXB", [128, 128], F32)
    TBALL = A("TBALL", [128, 128], F32)
    IXVU = A("IXVU", [128, 64], U32)
    IXKU = A("IXKU", [128, 64], U32)
    VALS = A("VALS", [128, 64], F32)
    VB = A("VB", [128, 64], BF16)
    SCL = A("SCL", [128, 64], F32)
    EXB = A("EXB", [128, 128], BF16)
    ZRO = A("ZRO", [128, 1], F32)
    TINY = A("TINY", [1, TW], BF16)
    PVEC = A("PVEC", [128, 1], F32)
    H2K = A("H2K", [128, 1], F32)
    NEGC = A("NEGC", [128, 1], F32)
    NEGCB = A("NEGCB", [128, 4], F32)
    CS1 = A("CS1", [128, 1], F32)
    CS2 = A("CS2", [128, 1], F32)
    RSTM = A("RSTM", [128, CH], F32)
    OUTB = A("OUTB", [128, 16], F32)
    GTM = A("GTM", [128, 8], F32)
    GSC = A("GSC", [128, 8], F32)
    PTB = A("PTB", [128, 4 * TW], BF16)
    VSTG = A("VSTG", [128, 2 * TW], BF16)
    RSB = A("RSB", [128, TW], BF16)
    PS = [nc.alloc_psum_tensor("ps%d" % i, [128, 512], F32) for i in range(8)]

    SCR_BF = SCR[:, :].bitcast(BF16)
    SCR_F32 = SCR[:, :].bitcast(F32)
    SCR_U32 = SCR[:, :].bitcast(U32)
    WB_BF = WBUF[:, :].bitcast(BF16)
    P0_F32 = P0[:, :].bitcast(F32)
    P0_BF = P0[:, :].bitcast(BF16)
    QA_BF = QA[:, :].bitcast(BF16)
    KAT_BF = KAT[:, :].bitcast(BF16)
    WTR = XSTG[:, :].bitcast(BF16)
    BRS = P0[:, :].bitcast(F32)[0:8, 7056:7057]
    LGT = SCR_F32[0:8, 1024:1536]
    GFM = SCR_F32[0:8, 1536:2048]
    G8H = SCR_BF[0:8, 4096:4608]
    G8L = SCR_BF[0:8, 4608:5120]

    def MQ():
        return QA_BF[:, 0:2048]

    def MQC():
        return QA_BF[:, 2048:4096]

    def QHI():
        return KAT_BF[:, 0:2048]

    def QLO():
        return KAT_BF[:, 2048:4096]

    def RH(tt):
        return P0_BF[0:1, tt * TW:(tt + 1) * TW]

    def RL(tt):
        return P0_BF[0:1, 2048 + tt * TW: 2048 + (tt + 1) * TW]

    def MH_(tt):
        return P0_BF[0:1, 4096 + tt * TW: 4096 + (tt + 1) * TW]

    def ML_(tt):
        return P0_BF[0:1, 6144 + tt * TW: 6144 + (tt + 1) * TW]

    def SQVB(tt):
        return P0_BF[0:1, 8192 + tt * TW: 8192 + (tt + 1) * TW]

    RHB = P0_BF[0:1, 10240:10752]
    RLB = P0_BF[0:1, 10752:11264]
    QKVBP = P0_BF[0:1, 13344:14112]
    SCRA = P0_F32[0:1, 5632:5888]
    SCRB = P0_F32[0:1, 5888:6144]
    SCRW = P0_F32[0:1, 6144:6400]
    SCRW2 = P0_F32[0:1, 6400:6656]
    CELLS = P0_F32[0:1, 6656:6672]
    EPSC = P0_F32[0:1, 6672:6673]

    p = Prog(nc)
    stack = ExitStack()
    p.mksem(stack, "qd", "gd", "tv", "vv", "sv",
             *["b_" + e for e in ENGINES])
    psg = [None] * 8
    st = {"xstg": None, "vstg": [None, None]}

    def ps_wait(i):
        return [psg[i]] if psg[i] else []

    def mm(out_ap, lhsT, rhs, start, stop, wait=(), inc=()):
        p.w("tensor", "matmul", (out_ap, lhsT, rhs), {"start": start, "stop": stop},
            wait=wait, inc=inc)

    IOTA = TBALL[:, :].bitcast(U32)
    IOTA5 = VGALL[:, :].bitcast(U32)[:, 0:512]
    p.w("gpsimd", "iota", (IOTA[:, 0:1], [[0, 1]]), {"channel_multiplier": 1})
    p.bar()
    p.w("vector", "tensor_copy", (PVEC[:, :], IOTA[:, 0:1]))
    p.bar()
    p.w("vector", "tensor_scalar", (H2K[:, :], PVEC[:, :], 2048.0, None, OP.mult))
    p.w("vector", "memset", (EPSC, float(EPS)))
    p.w("vector", "memset", (ONE1[:, :], 1.0))
    p.w("vector", "memset", (ONEC[:, :], 1.0))
    p.w("gpsimd", "iota", (IOTA[:, 0:128], [[1, 128]]), {"channel_multiplier": 0})
    p.bar()
    p.w("vector", "tensor_copy", (QA[:, 0:128], IOTA[:, 0:128]))
    p.bar()
    p.w("vector", "tensor_scalar", (IDF[:, :], QA[:, 0:128], PVEC[:, 0:1], None,
                                    OP.is_equal))
    p.w("gpsimd", "iota", (IOTA[:, 0:128], [[1, 2], [0, 64]]), {"channel_multiplier": 0})
    p.bar()
    p.w("vector", "tensor_copy", (QA[:, 0:128], IOTA[:, 0:128]))
    p.bar()
    p.w("vector", "tensor_scalar", (EPH[:, :], QA[0:2, 0:128], PVEC[0:2, 0:1], None,
                                    OP.is_equal))
    p.w("gpsimd", "iota", (IOTA[:, 0:128], [[1, 8], [0, 16]]), {"channel_multiplier": 0})
    p.bar()
    p.w("vector", "tensor_copy", (QA[:, 0:128], IOTA[:, 0:128]))
    p.bar()
    p.w("vector", "tensor_scalar", (REP8[:, :], QA[0:8, 0:128], PVEC[0:8, 0:1], None,
                                    OP.is_equal))
    p.w("gpsimd", "iota", (IOTA5[:, 0:512], [[1, 8], [0, 64]]), {"channel_multiplier": 0})
    p.bar()
    p.w("vector", "tensor_copy", (QA[:, 128:640], IOTA5[:, 0:512]))
    p.bar()
    p.w("vector", "tensor_scalar", (G8A[:, :], QA[0:8, 128:640], PVEC[0:8, 0:1], None,
                                    OP.is_equal))
    p.w("gpsimd", "iota", (IOTA[:, 0:128], [[32, 4], [0, 32]]), {"channel_multiplier": 0})
    p.bar()
    p.w("vector", "tensor_copy", (QA[:, 0:128], IOTA[:, 0:128]))
    p.bar()
    p.w("vector", "tensor_scalar", (EXB[:, :], QA[:, 0:128], PVEC[:, 0:1], None,
                                    OP.is_equal))
    p.w("vector", "memset", (ZRO[:, :], 0.0))
    p.w("vector", "memset", (TINY[:, :], 1e-20))
    p.w("vector", "memset", (SLT[:, :], 0.0))
    p.w("vector", "memset", (SLT[0:32, 128:129], 1.0))
    p.w("vector", "memset", (SLT[64:96, 129:130], 1.0))
    p.w("vector", "memset", (SLB[:, :], 0.0))
    p.w("vector", "memset", (SLB[32:64, 128:129], 1.0))
    p.w("vector", "memset", (SLB[96:128, 129:130], 1.0))
    p.w("vector", "memset", (VST[:, :], 0.0))
    p.w("vector", "memset", (VSB[:, :], 0.0))
    p.dma("sync", OUTB[:, :], d_outb2.ap()[:, :])
    p.dma("sync", BRS[:, :], d_br.ap()[:, :])
    p.qfence("sync")
    p.bar()

    XC = SCR_BF[:, 8192:8192 + CH * HW]
    SQH = SCR_BF[:, 12288:12288 + CH * HW]

    def phase_A(tt, hh, full_mode, wait_x=()):
        h0 = hh * HW
        p.w("scalar", "activation", (SQH, XSTG[:, :], AF.Square), wait=wait_x,
            inc=[("sv", 1)])
        sq_t = p.tok("sv")
        p.w("vector", "tensor_copy", (XC, XSTG[:, :]), wait=wait_x, inc=[("vv", 1)])
        cast_t = p.tok("vv")
        for dch in range(CH):
            mm(PS[0][0:1, 0:HW], ONEC[:, :], XC[:, dch * HW:(dch + 1) * HW],
               start=(dch == 0), stop=(dch == CH - 1),
               wait=([cast_t] + ps_wait(0)) if dch == 0 else ())
        for dch in range(CH):
            mm(PS[1][0:1, 0:HW], ONEC[:, :], SQH[:, dch * HW:(dch + 1) * HW],
               start=(dch == 0), stop=(dch == CH - 1),
               wait=([sq_t] + ps_wait(1)) if dch == 0 else (),
               inc=[("tv", 1)] if dch == CH - 1 else ())
        mm_t = p.tok("tv")
        p.w("vector", "tensor_scalar", (SCRA, PS[0][0:1, 0:HW], 1.0 / D, None, OP.mult),
            wait=[mm_t])
        p.w("vector", "tensor_tensor", (SCRB, SCRA, SCRA, OP.mult))
        p.w("vector", "scalar_tensor_tensor",
            (SCRB, PS[1][0:1, 0:HW], 1.0 / D, SCRB, OP.mult, OP.subtract), inc=[("vv", 1)])
        psg[0] = psg[1] = p.tok("vv")
        p.w("scalar", "activation", (SCRB, SCRB, AF.Sqrt), {"bias": EPSC, "scale": 1.0},
            wait=[p.tok("vv")], inc=[("sv", 1)])
        p.w("vector", "tensor_copy", (SQVB(tt)[0:1, h0:h0 + HW], SCRB), wait=[p.tok("sv")])
        p.w("vector", "reciprocal", (SCRB, SCRB))
        p.w("vector", "tensor_copy", (RH(tt)[0:1, h0:h0 + HW], SCRB))
        p.w("vector", "scalar_tensor_tensor",
            (RL(tt)[0:1, h0:h0 + HW], SCRB, 1.0, RH(tt)[0:1, h0:h0 + HW],
             OP.mult, OP.subtract))
        p.w("vector", "tensor_copy", (MH_(tt)[0:1, h0:h0 + HW], SCRA))
        p.w("vector", "scalar_tensor_tensor",
            (ML_(tt)[0:1, h0:h0 + HW], SCRA, 1.0, MH_(tt)[0:1, h0:h0 + HW],
             OP.mult, OP.subtract), inc=[("vv", 1)])
        rows_t = p.tok("vv")
        mm(PS[2][:, 0:HW], ONE1[:, :], RH(tt)[0:1, h0:h0 + HW], start=True, stop=False,
           wait=[rows_t] + ps_wait(2))
        mm(PS[2][:, 0:HW], ONE1[:, :], RL(tt)[0:1, h0:h0 + HW], start=False, stop=True,
           inc=[("tv", 1)])
        p.w("vector", "tensor_copy",
            (RSTD[:, tt * TW + h0: tt * TW + h0 + HW], PS[2][:, 0:HW]),
            wait=[p.tok("tv")], inc=[("vv", 1)])
        rstd_t = p.tok("vv")
        mm(PS[2][:, 0:HW], ONE1[:, :], MH_(tt)[0:1, h0:h0 + HW], start=True, stop=False,
           wait=[rstd_t])
        mm(PS[2][:, 0:HW], ONE1[:, :], ML_(tt)[0:1, h0:h0 + HW], start=False, stop=True,
           inc=[("tv", 1)])
        mrep_t = p.tok("tv")
        if full_mode:
            xb_out = XB[:, :].rearrange("q (c t) -> q c t", c=CH)[
                :, :, tt * TW + h0: tt * TW + h0 + HW]
        else:
            xb_out = XB[:, 0:CH * TW].rearrange("q (c t) -> q c t", c=CH)[:, :, h0:h0 + HW]
        xin = XSTG[:, :].rearrange("q (c t) -> q c t", c=CH)
        mrep_b = PS[2][:, 0:HW].unsqueeze(1).to_broadcast([128, CH, HW])
        p.w("vector", "tensor_tensor", (xb_out, xin, mrep_b, OP.subtract),
            wait=[mrep_t], inc=[("vv", 1)])
        psg[2] = p.tok("vv")
        return p.tok("vv")

    def rstd_tm(tt):
        for j in range(4):
            mm(PS[3][:, j:j + 1], RH(tt)[0:1, j * 128:(j + 1) * 128], ONE1[0:1, 0:1],
               start=True, stop=False, wait=(ps_wait(3) if j == 0 else ()))
            mm(PS[3][:, j:j + 1], RL(tt)[0:1, j * 128:(j + 1) * 128], ONE1[0:1, 0:1],
               start=False, stop=True, inc=([("tv", 1)] if j == 3 else ()))
        p.w("vector", "tensor_copy", (RSTM[:, tt * 4:tt * 4 + 4], PS[3][:, 0:4]),
            wait=[p.tok("tv")], inc=[("vv", 1)])
        psg[3] = p.tok("vv")

    class Ring:
        def __init__(self, n, cols, off, pfx):
            self.n, self.cols, self.off, self.pfx = n, cols, off, pfx
            self.i = 0
            self.free = [None] * n

        def load(self, src):
            s = self.i % self.n
            self.i += 1
            dst = WB_BF[:, self.off + s * self.cols: self.off + (s + 1) * self.cols]
            wl = [self.free[s]] if self.free[s] is not None else []
            tok = p.dma("gpsimd", dst, src, wait=wl, sem="%s%d" % (self.pfx, s))
            return dst, tok, s

        def done(self, s, tok):
            self.free[s] = tok

    def xin_ap():
        return XSTG[:, :]

    r_qk = Ring(2, 2048, 0, "rqk")
    r_wv = Ring(16, 512, 4096, "rwv")
    qk_tiles = [r_qk.load(d_wqk.ap()[e, :, :]) for e in range(2)]

    for tt in range(NTT):
        for hh in range(2):
            xt = p.dma("sync", xin_ap(), d_x.ap()[2 * tt + hh, :, :],
                       wait=([st["xstg"]] if st["xstg"] else []), sem="sx")
            phase_A(tt, hh, full_mode=False, wait_x=[xt])
            p.bar()
        rstd_tm(tt)
        if stop_after == "probe3" and tt == 0:
            p.bar()
            wtile0d, _, _ = qk_tiles[0]
            for sl in range(4):
                p.dma("gpsimd", d_dbg.ap()[sl, :, :],
                      XB[:, sl * 2048:(sl + 1) * 2048])
            p.dma("gpsimd", d_dbg.ap()[4, :, :], wtile0d[:, 0:2048])
            p.qfence("sync")
            p.qfence("gpsimd")
            p.bar()
            with nc.Block() as block:
                p.replay(block)
            stack.close()
            return nc
        if stop_after == "probe2" and tt == 0:
            p.bar()
            wtile0c, _, _ = qk_tiles[0]
            for li, ln_ in enumerate([2, 4, 8, 16]):
                for dch in range(ln_):
                    mm(PS[7][:, :], wtile0c[:, dch * 128:(dch + 1) * 128],
                       XB[:, dch * TW:(dch + 1) * TW],
                       start=(dch == 0), stop=(dch == ln_ - 1),
                       wait=(ps_wait(7) if dch == 0 else ()),
                       inc=([("tv", 1)] if dch == ln_ - 1 else ()))
                p.w("vector", "tensor_copy", (KAT[:, 1024:1536], PS[7][:, :]),
                    wait=[p.tok("tv")], inc=[("vv", 1)])
                psg[7] = p.tok("vv")
                p.dma("sync", d_dbg.ap()[4, :, 512 * li:512 * (li + 1)],
                      KAT[:, 1024:1536], wait=[p.tok("vv")])
            p.dma("gpsimd", d_dbg.ap()[0, :, :], XB[:, 0:2048])
            p.qfence("sync")
            p.qfence("gpsimd")
            p.bar()
            with nc.Block() as block:
                p.replay(block)
            stack.close()
            return nc
        if stop_after == "probe0" and tt == 0:
            p.dma("gpsimd", d_dbg.ap()[3, :, 0:1024], XB[:, 0:1024])
            wtile0, wtok0, _ = qk_tiles[0]
            for dch in range(CH):
                mm(PS[6][:, :], wtile0[:, dch * 128:(dch + 1) * 128],
                   XB[:, dch * TW:(dch + 1) * TW],
                   start=(dch == 0), stop=(dch == CH - 1),
                   inc=([("tv", 1)] if dch == CH - 1 else ()))
            p.w("vector", "tensor_mul",
                (QLO().bitcast(F32)[:, 0:512] if False else KAT[:, 1536:2048],
                 PS[6][:, :], RSTD[:, 0:512]), wait=[p.tok("tv")], inc=[("vv", 1)])
            psg[6] = p.tok("vv")
            p.dma("sync", d_dbg.ap()[3, :, 1024:1536], KAT[:, 1536:2048],
                  wait=[p.tok("vv")])
            wtile0b, _, _ = qk_tiles[0]
            p.dma("gpsimd", d_dbg.ap()[3, :, 1536:2048], wtile0b[:, 0:512])
            for li, ln_ in enumerate([2, 4, 8, 16]):
                for dch in range(ln_):
                    mm(PS[7][:, :], wtile0b[:, dch * 128:(dch + 1) * 128],
                       XB[:, dch * TW:(dch + 1) * TW],
                       start=(dch == 0), stop=(dch == ln_ - 1),
                       wait=(ps_wait(7) if dch == 0 else ()),
                       inc=([("tv", 1)] if dch == ln_ - 1 else ()))
                p.w("vector", "tensor_copy", (KAT[:, 1024:1536], PS[7][:, :]),
                    wait=[p.tok("tv")], inc=[("vv", 1)])
                psg[7] = p.tok("vv")
                p.dma("sync", d_dbg.ap()[4, :, 512 * li:512 * (li + 1)],
                      KAT[:, 1024:1536], wait=[p.tok("vv")])
        for e in range(2):
            wtile, wtok, _ = qk_tiles[e]
            for dch in range(CH):
                mm(PS[4][:, :], wtile[:, dch * 128:(dch + 1) * 128],
                   XB[:, dch * TW:(dch + 1) * TW],
                   start=(dch == 0), stop=(dch == CH - 1),
                   wait=([wtok] + ps_wait(4)) if dch == 0 else (),
                   inc=([("tv", 1)] if dch == CH - 1 else ()))
            dst = QA if e == 0 else KAT
            p.w("vector", "tensor_mul",
                (dst[:, tt * TW:(tt + 1) * TW], PS[4][:, :], RSTD[:, tt * TW:(tt + 1) * TW]),
                wait=[p.tok("tv")], inc=[("vv", 1)])
            psg[4] = p.tok("vv")
        for ecg in range(4):
            tiles = [r_wv.load(d_wv.ap()[ecg, dch, :, :]) for dch in range(CH)]
            for j in range(4):
                vb_ = (5, 6, 7)[j % 3]
                tc = tt * 4 + j
                for dch in range(CH):
                    wt, wtk, _ = tiles[dch]
                    mm(PS[vb_][:, :], XB[:, dch * TW + j * 128: dch * TW + (j + 1) * 128], wt,
                       start=(dch == 0), stop=(dch == CH - 1),
                       wait=(([wtk] if j == 0 else []) + (ps_wait(vb_) if dch == 0 else [])),
                       inc=([("tv", 1)] if dch == CH - 1 else ()))
                vs = VSTG[:, (j % 2) * TW:(j % 2 + 1) * TW]
                vw = [p.tok("tv")] + ([st["vstg"][j % 2]] if st["vstg"][j % 2] else [])
                p.w("vector", "tensor_scalar",
                    (vs, PS[vb_][:, :], RSTM[:, tc:tc + 1], None, OP.mult),
                    wait=vw, inc=[("vv", 1)])
                psg[vb_] = p.tok("vv")
                st["vstg"][j % 2] = p.dma(
                    "sync", d_vtm.ap()[tc * 128:(tc + 1) * 128, ecg * 512:(ecg + 1) * 512],
                    vs, wait=[p.tok("vv")], sem="vs%d" % (j % 2))
            lasttok = p.tok("tv")
            for dch in range(CH):
                r_wv.done(tiles[dch][2], lasttok)
        p.bar()
        if stop_after == "probe0" and tt == 0:
            p.w("gpsimd", wait=[p.tok("vs0"), p.tok("vs1")])
            p.dma("gpsimd", d_dbg.ap()[0, :, :], XB[:, 0:2048])
            p.dma("sync", d_dbg.ap()[1, :, 0:512], RSTD[:, 0:512])
            p.dma("sync", d_dbg.ap()[1, :, 512:516], RSTM[:, 0:4])
            p.dma("sync", d_dbg.ap()[2, :, 0:512], QA[:, 0:512])
            p.dma("sync", d_dbg.ap()[2, :, 512:1024], KAT[:, 0:512])
            p.dma("gpsimd", d_dbg.ap()[4, :, 0:1024], d_vtm.ap()[0:128, 0:1024])
            p.qfence("sync")
            p.qfence("gpsimd")
            p.bar()
            with nc.Block() as block:
                p.replay(block)
            stack.close()
            return nc

    p.dma("sync", d_keys.ap()[:, :], KAT[:, :])
    p.qfence("sync")
    p.bar()

    TOKU = SCR_U32[:, 0:2048]
    PKT = SCR_F32[:, 2048:4096]
    PKB = SCR_F32[:, 4096:6144]
    MSK = SCR[:, 24576:26624]
    p.w("gpsimd", "iota", (TOKU, [[1, T]]), {"channel_multiplier": 0})
    p.bar()
    p.w("vector", "tensor_scalar", (MSK, KAT[:, :], 0.0, None, OP.is_le))
    p.w("vector", "tensor_scalar",
        (PKT.bitcast(U32), KAT[:, :].bitcast(U32), 0xFFFFF800, None, OP.bitwise_and))
    p.w("vector", "tensor_tensor", (PKT.bitcast(U32), PKT.bitcast(U32), TOKU, OP.bitwise_or))
    p.w("vector", "tensor_scalar",
        (PKB.bitcast(U32), KAT[:, :].bitcast(U32), 0x80000000, None, OP.bitwise_xor))
    p.w("vector", "tensor_scalar",
        (PKB.bitcast(U32), PKB.bitcast(U32), 0xFFFFF800, None, OP.bitwise_and))
    p.w("vector", "tensor_tensor", (PKB.bitcast(U32), PKB.bitcast(U32), TOKU, OP.bitwise_or))
    p.w("vector", "memset", (TOKU, 0))
    p.w("vector", "copy_predicated", (PKT.bitcast(U32), MSK, TOKU))
    p.w("vector", "tensor_scalar", (MSK, KAT[:, :], 0.0, None, OP.is_ge))
    p.w("vector", "copy_predicated", (PKB.bitcast(U32), MSK, TOKU))
    for i in range(4):
        p.w("vector", "max", kwargs={"out": TV[:, 8 * i:8 * i + 8], "in_": PKT})
        p.w("vector", "drain")
        p.w("vector", "match_replace",
            kwargs={"out": PKT, "in_to_replace": TV[:, 8 * i:8 * i + 8],
                    "in_values": PKT, "imm_value": -1.0})
        p.w("vector", "drain")
    for i in range(4, 8):
        p.w("vector", "max", kwargs={"out": TV[:, 8 * i:8 * i + 8], "in_": PKB})
        p.w("vector", "drain")
        p.w("vector", "match_replace",
            kwargs={"out": PKB, "in_to_replace": TV[:, 8 * i:8 * i + 8],
                    "in_values": PKB, "imm_value": -1.0})
        p.w("vector", "drain")
    p.w("vector", "tensor_scalar",
        (TIU[:, :], TV[:, :].bitcast(U32), 2047, None, OP.bitwise_and))
    p.w("vector", "drain")
    p.w("vector", "tensor_copy", (TIF[:, :], TIU[:, :]))
    p.w("vector", "drain")
    p.w("vector", "scalar_tensor_tensor",
        (IXB[:, 0:64], TIF[:, :], 128.0, PVEC[:, 0:1].to_broadcast([128, 64]),
         OP.mult, OP.add))
    p.w("vector", "drain")
    p.w("vector", "scalar_tensor_tensor",
        (IXB[:, 64:128], TIF[:, :], 1.0, H2K[:, 0:1].to_broadcast([128, 64]),
         OP.mult, OP.add))
    p.w("vector", "drain", inc=[("vv", 1)])
    ix_t = p.tok("vv")
    p.w("tensor", "transpose", (PS[1][:, 0:128], IXB[:, :], IDF[:, :]),
        wait=[ix_t] + ps_wait(1), inc=[("tv", 1)])
    p.w("vector", "tensor_copy", (TBALL[:, :], PS[1][:, 0:128]), wait=[p.tok("tv")],
        inc=[("vv", 1)])
    psg[1] = p.tok("vv")
    tb_t = p.tok("vv")
    p.dma("sync", IXB[0:64, 0:64], TBALL[0:64, 0:128:2], wait=[tb_t],
          allow_slow_non_contiguous=True, sem="ix")
    p.dma("sync", IXB[64:128, 0:64], TBALL[0:64, 1:128:2], wait=[tb_t],
          allow_slow_non_contiguous=True, sem="ix")
    p.dma("sync", IXB[0:64, 64:128], TBALL[64:128, 0:128:2], wait=[tb_t],
          allow_slow_non_contiguous=True, sem="ix")
    tix = p.dma("sync", IXB[64:128, 64:128], TBALL[64:128, 1:128:2], wait=[tb_t],
                allow_slow_non_contiguous=True, sem="ix")
    p.bar()
    p.w("vector", "tensor_copy", (IXVU[:, :], IXB[:, 0:64]), wait=[tix])
    p.w("vector", "tensor_copy", (IXKU[:, :], IXB[:, 64:128]), inc=[("vv", 1)])
    ixu_t = p.tok("vv")
    if stop_after == "probe1":
        p.dma("sync", d_dbg.ap()[0, :, :], QA[:, :])
        p.dma("sync", d_dbg.ap()[4, :, :], KAT[:, :])
    p.w("vector", "tensor_copy", (SCR_BF[:, 12288:14336], QA[:, :]))
    p.w("vector", "scalar_tensor_tensor",
        (SCR_BF[:, 14336:16384], QA[:, :], 1.0, SCR_BF[:, 12288:14336],
         OP.mult, OP.subtract))
    p.w("vector", "tensor_scalar", (PKT, QA[:, :], 0.0, None, OP.is_gt))
    p.w("vector", "tensor_scalar", (PKB, QA[:, :], 0.0, None, OP.is_le), inc=[("vv", 1)])
    stg_t = p.tok("vv")
    p.w("vector", "tensor_copy", (QHI(), SCR_BF[:, 12288:14336]), wait=[stg_t])
    p.w("vector", "tensor_copy", (QLO(), SCR_BF[:, 14336:16384]))
    p.w("vector", "tensor_copy", (MQ(), PKT))
    p.w("vector", "tensor_copy", (MQC(), PKB), inc=[("vv", 1)])
    p.bar()
    for pr in range(64):
        p.w("gpsimd", "indirect_dma_start",
            (VALS[:, pr:pr + 1], None,
             bass.AP(d_keys, 0, [[1, T * 128], [1, 1]]),
             IndirectOffsetOnAxis(IXKU[:, pr:pr + 1], 0)),
            inc=[("gd", 16)])
    for pr in range(64):
        p.w("gpsimd", "indirect_dma_start",
            (VGALL[:, pr * 16:(pr + 1) * 16], None,
             bass.AP(d_vtm, 0, [[16, T * D // 16], [1, 16]]),
             IndirectOffsetOnAxis(IXVU[:, pr:pr + 1], 0)),
            inc=[("gd", 16)])
    p.qfence("gpsimd")
    p.bar()
    p.w("vector", "tensor_copy", (VB[:, :], VALS[:, :]), inc=[("vv", 1)])
    mm(PS[1][:, 0:64], EXB[:, :], VB[:, :], start=True, stop=True,
       wait=[p.tok("vv")] + ps_wait(1), inc=[("tv", 1)])
    p.w("vector", "tensor_tensor", (SCL[:, :], VALS[:, :], PS[1][:, 0:64],
                                    OP.subtract),
        wait=[p.tok("tv")], inc=[("vv", 1)])
    psg[1] = p.tok("vv")
    if stop_after == "probe1":
        p.dma("sync", d_dbg.ap()[1, :, 0:64], TIF[:, :])
        p.dma("sync", d_dbg.ap()[1, :, 64:128], VALS[:, :])
        p.dma("sync", d_dbg.ap()[1, :, 128:129], NEGC[:, :], allow_slow_non_contiguous=True)
        p.dma("sync", d_dbg.ap()[1, :, 129:193], TV[:, :])
        p.dma("gpsimd", d_dbg.ap()[2, :, 0:1024], VGALL[:, 0:1024])
        p.dma("gpsimd", d_dbg.ap()[3, :, 0:1024], d_vtm.ap()[0:128, 0:1024])
        p.qfence("sync")
        p.qfence("gpsimd")
        p.bar()
        with nc.Block() as block:
            p.replay(block)
        stack.close()
        return nc

    r_at = Ring(2, 2048, 4096, "rat")
    r_bt = Ring(8, 512, 8192, "rbt")
    r_wr = Ring(1, 128, 12288, "rwr")
    wr_tile, wr_tok, _ = r_wr.load(d_wr.ap()[:, :])
    HSB = SCR_BF[:, 0:4 * TW]

    def ffn_tt(tt):
        for dch in range(CH):
            mm(PS[0][0:8, :], wr_tile[:, dch * 8:(dch + 1) * 8],
               XB[:, dch * TW:(dch + 1) * TW],
               start=(dch == 0), stop=(dch == CH - 1),
               wait=([wr_tok] + ps_wait(0)) if dch == 0 else (),
               inc=([("tv", 1)] if dch == CH - 1 else ()))
        p.w("vector", "tensor_mul",
            (LGT[0:8, :], PS[0][0:8, :], RSTD[0:8, tt * TW:(tt + 1) * TW]),
            wait=[p.tok("tv")])
        p.w("vector", "tensor_scalar", (LGT[0:8, :], LGT[0:8, :], BRS[0:8, 0:1], None, OP.add),
            inc=[("vv", 1)])
        psg[0] = p.tok("vv")
        lg_t = p.tok("vv")
        for j in range(4):
            p.w("tensor", "transpose", (PS[1][0:128, 0:8], LGT[0:8, j * 128:(j + 1) * 128],
                                        IDF[0:8, 0:8]),
                wait=[lg_t] + ps_wait(1), inc=[("tv", 1)])
            p.w("vector", "tensor_copy", (GTM[:, :], PS[1][0:128, 0:8]), wait=[p.tok("tv")],
                inc=[("vv", 1)])
            psg[1] = p.tok("vv")
            p.w("vector", "reduce_max", (CS1[:, :], GTM[:, :], AX))
            p.w("vector", "tensor_scalar", (CS2[:, :], CS1[:, :], -1.0, None, OP.mult),
                inc=[("vv", 1)])
            p.w("scalar", "activation", (GSC[:, :], GTM[:, :], AF.Exp),
                {"bias": CS2[:, 0:1], "scale": 1.0, "accum_out": CS1[:, 0:1]},
                wait=[p.tok("vv")], inc=[("sv", 1)])
            p.w("vector", "reciprocal", (CS2[:, :], CS1[:, :]), wait=[p.tok("sv")])
            p.w("vector", "reduce_max", (CS1[:, :], GTM[:, :], AX))
            p.w("vector", "tensor_scalar", (GSC[:, :], GTM[:, :], CS1[:, 0:1], None,
                                            OP.is_equal))
            p.w("vector", "tensor_scalar", (GSC[:, :], GSC[:, :], CS2[:, 0:1], 1.0 / 64.0,
                                            OP.mult, OP.mult), inc=[("vv", 1)])
            g_t = p.tok("vv")
            p.w("tensor", "transpose", (PS[1][0:8, 0:128], GSC[:, :], IDF[:, :]),
                wait=[g_t] + ps_wait(1), inc=[("tv", 1)])
            p.w("vector", "tensor_copy", (GFM[0:8, j * 128:(j + 1) * 128], PS[1][0:8, 0:128]),
                wait=[p.tok("tv")], inc=[("vv", 1)])
            psg[1] = p.tok("vv")
        p.w("vector", "tensor_copy", (G8H[0:8, :], GFM[0:8, :]))
        p.w("vector", "scalar_tensor_tensor",
            (G8L[0:8, :], GFM[0:8, :], 1.0, G8H[0:8, :], OP.mult, OP.subtract),
            inc=[("vv", 1)])
        g8_t = p.tok("vv")
        at_tiles = [r_at.load(d_at.ap()[er, :, :]) for er in range(2)]
        for er in range(4):
            hb_ = (2, 5)[er % 2]
            att, atk, aslot = at_tiles[er]
            for dch in range(CH):
                mm(PS[hb_][:, :], att[:, dch * 128:(dch + 1) * 128],
                   XB[:, dch * TW:(dch + 1) * TW],
                   start=(dch == 0), stop=(dch == CH - 1),
                   wait=([atk] + ps_wait(hb_)) if dch == 0 else (),
                   inc=([("tv", 1)] if dch == CH - 1 else ()))
            h_t = p.tok("tv")
            r_at.done(aslot, h_t)
            if er + 2 < 4:
                at_tiles.append(r_at.load(d_at.ap()[er + 2, :, :]))
            else:
                at_tiles.append(None)
            mm(PS[3][:, :], G8A[:, er * 128:(er + 1) * 128], G8H[0:8, :],
               start=True, stop=False, wait=[g8_t] + ps_wait(3))
            mm(PS[3][:, :], G8A[:, er * 128:(er + 1) * 128], G8L[0:8, :],
               start=False, stop=True, inc=[("tv", 1)])
            grep_t = p.tok("tv")
            RGF = VSTG[:, :].bitcast(F32)
            p.w("vector", "tensor_mul",
                (RGF[:, 0:512], PS[3][:, :], RSTD[:, tt * TW:(tt + 1) * TW]),
                wait=[grep_t], inc=[("vv", 1)])
            psg[3] = p.tok("vv")
            p.w("vector", "tensor_mul",
                (HSB[:, er * TW:(er + 1) * TW], PS[hb_][:, :], RGF[:, 0:512]),
                wait=[h_t], inc=[("vv", 1)])
            psg[hb_] = p.tok("vv")
        h_all = p.tok("vv")
        for hh in range(2):
            xt = p.dma("sync", xin_ap(), d_y.ap()[2 * tt + hh, :, :],
                       wait=([st["xstg"]] if st["xstg"] else []), sem="sx")
            p.w("vector", wait=[xt])
            for oc in range(CH):
                ob = (4, 6, 7)[oc % 3]
                btt, btk, bslot = r_bt.load(d_bt.ap()[oc, :, :])
                for er in range(4):
                    mm(PS[ob][:, 0:HW], btt[:, er * 128:(er + 1) * 128],
                       HSB[:, er * TW + hh * HW: er * TW + hh * HW + HW],
                       start=(er == 0), stop=(er == 3),
                       wait=([btk, h_all] + ps_wait(ob)) if er == 0 else (),
                       inc=([("tv", 1)] if er == 3 else ()))
                y_t = p.tok("tv")
                r_bt.done(bslot, y_t)
                xs = XSTG[:, oc * HW:(oc + 1) * HW]
                p.w("vector", "scalar_tensor_tensor",
                    (xs, PS[ob][:, 0:HW], 0.0, xs, OP.bypass, OP.add),
                    wait=[y_t], inc=[("vv", 1)])
                psg[ob] = p.tok("vv")
            st["xstg"] = p.dma("sync", d_y.ap()[2 * tt + hh, :, :], xin_ap(),
                               wait=[p.tok("vv")], sem="sy")
        p.bar()

    r_wo = Ring(2, 2048, 0, "rwo")
    TOPT = SCR_BF[:, 0:CH * TW]
    BOTT = SCR_BF[:, CH * TW:2 * CH * TW]
    QPH_ = XB[0:2, 0:32 * TW]
    QPL_ = XB[0:2, 32 * TW:64 * TW]

    def build_qpair(tt, half, wait=()):
        tks = []
        h0 = 64 * half
        tks.append(p.dma("sync", QPH_[0:1, :].rearrange("q (r t) -> q r t", r=32),
                         QHI()[h0:h0 + 64:2, tt * TW:(tt + 1) * TW], wait=wait, sem="qp"))
        tks.append(p.dma("sync", QPH_[1:2, :].rearrange("q (r t) -> q r t", r=32),
                         QHI()[h0 + 1:h0 + 64:2, tt * TW:(tt + 1) * TW], wait=wait, sem="qp"))
        return tks

    def build_vstack(pr, wait=()):
        sl = pr % 2
        c0 = sl * 224 + 96
        g0 = pr * 16
        p.w("vector", "tensor_copy", (VST[0:32, c0:c0 + 16], VGALL[0:32, g0:g0 + 16]),
            wait=wait)
        p.w("vector", "tensor_copy", (VST[64:96, c0 + 16:c0 + 32], VGALL[64:96, g0:g0 + 16]))
        p.w("vector", "tensor_copy", (VSB[32:64, c0:c0 + 16], VGALL[32:64, g0:g0 + 16]))
        p.w("vector", "tensor_copy", (VSB[96:128, c0 + 16:c0 + 32], VGALL[96:128, g0:g0 + 16]),
            inc=[("vv", 1)])
        return p.tok("vv")

    def attn_a_sparse(tt):
        vs_tok = {}
        pair_end = {}
        for half in range(2):
            qp = build_qpair(tt, half,
                             wait=([pair_end[31]] if half == 1 else ()))
            for tk in qp:
                p.w("tensor", wait=[tk])
            if half == 0:
                vs_tok[0] = build_vstack(0)
                vs_tok[1] = build_vstack(1)
            for k in range(32):
                pr = 32 * half + k
                ch = pr // 4
                sl = pr % 2
                mm(PS[sl][:, :], EPH[:, :], QPH_[0:2, k * TW:(k + 1) * TW],
                   start=True, stop=True, wait=ps_wait(sl), inc=[("tv", 1)])
                arg_t = p.tok("tv")
                pb = PTB[:, (pr % 4) * TW:(pr % 4 + 1) * TW]
                ew = [arg_t] + ([pair_end[pr - 4]] if pr >= 4 else [])
                p.w("scalar", "activation", (pb, PS[sl][:, :], AF.Exp),
                    {"bias": ZRO[:, 0:1], "scale": SCL[:, pr:pr + 1]},
                    wait=ew, inc=[("sv", 1)])
                p.w("vector", "tensor_scalar", (pb, pb, 1e30, None, OP.min),
                    wait=[p.tok("sv")], inc=[("vv", 1)])
                psg[sl] = p.tok("vv")
                exp_t = p.tok("vv")
                mm(PS[2][:, :], SLT[:, 128 - 2 * pr:256 - 2 * pr], pb,
                   start=(pr == 0), stop=(pr == 63),
                   wait=[exp_t] + (ps_wait(2) if pr == 0 else []))
                mm(PS[3][:, :], SLB[:, 128 - 2 * pr:256 - 2 * pr], pb,
                   start=(pr == 0), stop=(pr == 63),
                   wait=(ps_wait(3) if pr == 0 else []),
                   inc=([("tv", 1)] if pr == 63 else ()))
                it, ib = 4 + (ch % 2), 6 + (ch % 2)
                w0 = (pr % 2) * 224 + 96 - (pr % 4) * 32
                mm(PS[it][:, :], VST[:, w0:w0 + 128], pb,
                   start=(pr % 4 == 0), stop=(pr % 4 == 3),
                   wait=[vs_tok[pr]] + (ps_wait(it) if pr % 4 == 0 else []))
                mm(PS[ib][:, :], VSB[:, w0:w0 + 128], pb,
                   start=(pr % 4 == 0), stop=(pr % 4 == 3),
                   wait=(ps_wait(ib) if pr % 4 == 0 else []),
                   inc=[("tv", 1)])
                pair_end[pr] = p.tok("tv")
                if pr % 4 == 3:
                    pv_t = pair_end[pr]
                    p.w("vector", "tensor_copy",
                        (TOPT[:, ch * TW:(ch + 1) * TW], PS[it][:, :]), wait=[pv_t])
                    p.w("vector", "tensor_copy",
                        (BOTT[:, ch * TW:(ch + 1) * TW], PS[ib][:, :]), inc=[("vv", 1)])
                    psg[it] = psg[ib] = p.tok("vv")
                if pr + 2 <= 63:
                    vs_tok[pr + 2] = build_vstack(pr + 2, wait=[pair_end[pr]])
        s_done = p.tok("tv")
        WF = VSTG[:, :].bitcast(F32)
        p.w("vector", "reciprocal", (WF[:, 0:512], PS[2][:, :]), wait=[s_done],
            inc=[("vv", 1)])
        psg[2] = p.tok("vv")
        p.w("vector", "tensor_mul",
            (RSB[:, 0:512], WF[:, 0:512], MQ()[:, tt * TW:(tt + 1) * TW]), inc=[("vv", 1)])
        wtop_t = p.tok("vv")
        p.w("vector", "reciprocal", (WF[:, 0:512], PS[3][:, :]), wait=[wtop_t],
            inc=[("vv", 1)])
        psg[3] = p.tok("vv")
        p.w("vector", "tensor_mul",
            (PTB[:, 0:512], WF[:, 0:512], MQC()[:, tt * TW:(tt + 1) * TW]), inc=[("vv", 1)])
        w_t = p.tok("vv")
        wtr_free = None
        for hh in range(2):
            h0 = hh * HW
            dwait = [w_t] + ([wtr_free] if wtr_free else []) + \
                ([p.tok("sy")] if "sy" in p.cnt else [])
            for c in range(CH):
                p.dma("sync", WTR[0:8, c * HW:(c + 1) * HW],
                      RSB[8 * c:8 * c + 8, h0:h0 + HW], wait=dwait, sem="wt")
                t2 = p.dma("sync", WTR[0:8, 4096 + c * HW: 4096 + (c + 1) * HW],
                           PTB[8 * c:8 * c + 8, h0:h0 + HW], wait=dwait, sem="wt")
            t1 = t2
            for ch in range(CH):
                i = ch % 2
                mm(PS[0 + i][:, 0:HW], REP8[:, :], WTR[0:8, ch * HW:(ch + 1) * HW],
                   start=True, stop=True, wait=[t1, t2] + ps_wait(0 + i))
                mm(PS[2 + i][:, 0:HW], REP8[:, :],
                   WTR[0:8, 4096 + ch * HW: 4096 + (ch + 1) * HW],
                   start=True, stop=True, wait=ps_wait(2 + i), inc=[("tv", 1)])
                rep_t = p.tok("tv")
                tch = TOPT[:, ch * TW + h0: ch * TW + h0 + HW]
                bch = BOTT[:, ch * TW + h0: ch * TW + h0 + HW]
                p.w("vector", "tensor_mul", (tch, tch, PS[0 + i][:, 0:HW]), wait=[rep_t])
                p.w("vector", "tensor_mul", (bch, bch, PS[2 + i][:, 0:HW]))
                p.w("vector", "tensor_add", (tch, tch, bch), inc=[("vv", 1)])
                psg[0 + i] = psg[2 + i] = p.tok("vv")
            wtr_free = p.tok("tv")
        return p.tok("vv")

    for tt in range(NTT):
        attn_a_sparse(tt)
        p.bar()
        if stop_after == "probe5" and tt == 0:
            p.dma("gpsimd", d_dbg.ap()[0, :, :], TOPT[:, 0:2048])
            p.dma("gpsimd", d_dbg.ap()[1, :, :], TOPT[:, 2048:4096])
            p.dma("gpsimd", d_dbg.ap()[2, :, 0:64], VALS[:, :])
            p.dma("gpsimd", d_dbg.ap()[2, :, 64:128], TIF[:, :])
            p.dma("gpsimd", d_dbg.ap()[2, :, 128:192], IXVU[:, :])
            p.dma("gpsimd", d_dbg.ap()[2, :, 192:256], IXKU[:, :])
            p.dma("gpsimd", d_dbg.ap()[3, :, 0:2048], QHI())
            p.dma("gpsimd", d_dbg.ap()[4, :, 0:2048], QLO())
            p.qfence("sync")
            p.qfence("gpsimd")
            p.bar()
            with nc.Block() as block:
                p.replay(block)
            stack.close()
            return nc
        for hh in range(2):
            xt = p.dma("sync", xin_ap(), d_x.ap()[2 * tt + hh, :, :],
                       wait=([st["xstg"]] if st["xstg"] else []), sem="sx")
            p.w("vector", wait=[xt])
            for oc in range(CH):
                ob = (5, 2, 3)[oc % 3]
                wtile, wtok, slot = r_wo.load(d_wo.ap()[oc, :, :])
                for dch in range(CH):
                    mm(PS[ob][:, 0:HW], wtile[:, dch * 128:(dch + 1) * 128],
                       TOPT[:, dch * TW + hh * HW: dch * TW + hh * HW + HW],
                       start=(dch == 0), stop=(dch == CH - 1),
                       wait=([wtok] + ps_wait(ob)) if dch == 0 else (),
                       inc=([("tv", 1)] if dch == CH - 1 else ()))
                wo_t = p.tok("tv")
                r_wo.done(slot, wo_t)
                xs = XSTG[:, oc * HW:(oc + 1) * HW]
                p.w("vector", "scalar_tensor_tensor",
                    (xs, PS[ob][:, 0:HW], 0.0, xs, OP.bypass, OP.add),
                    wait=[wo_t], inc=[("vv", 1)])
                psg[ob] = p.tok("vv")
            p.bar()
            if stop_after == "sub1":
                st["xstg"] = p.dma("sync", d_y.ap()[2 * tt + hh, :, :], xin_ap(),
                                   wait=[p.tok("vv")], sem="sy")
                continue
            phase_A(tt, hh, full_mode=False)
            st["xstg"] = p.dma("sync", d_y.ap()[2 * tt + hh, :, :], xin_ap(),
                               wait=[p.tok("vv")], sem="sy")
            p.bar()
        if stop_after == "sub1":
            continue
        rstd_tm(tt)
        ffn_tt(tt)
        p.bar()

    if stop_after in ("sub1", "sub2"):
        p.qfence("sync")
        p.qfence("gpsimd")
        p.bar()
        with nc.Block() as block:
            p.replay(block)
        stack.close()
        return nc

    for tt in range(NTT):
        for hh in range(2):
            xt = p.dma("sync", xin_ap(), d_y.ap()[2 * tt + hh, :, :],
                       wait=([st["xstg"]] if st["xstg"] else []), sem="sx")
            phase_A(tt, hh, full_mode=True, wait_x=[xt])
            p.bar()
        rstd_tm(tt)
    p.bar()

    if stop_after == "sub3a":
        p.qfence("sync")
        p.qfence("gpsimd")
        p.bar()
        with nc.Block() as block:
            p.replay(block)
        stack.close()
        return nc

    QH2 = SCR_BF[:, 0:4096]
    KH2 = SCR_BF[:, 4096:8192]
    VTM = SCR_BF[:, 8192:12288]
    QSQ = SCR_BF[:, 12288:14336]
    SEXP = SCR_F32[:, 6144:6656]
    SEH = SCR_BF[:, 13312:13824]
    SEL = SCR_BF[:, 13824:14336]
    r_qkv = Ring(4, 2048, 0, "rqv")
    r_vv = Ring(16, 256, 8192, "rvv")

    def xbf(dch, c0, cw):
        return XB[:, dch * T + c0: dch * T + c0 + cw]

    hp_prev = None
    for hp in range(8):
        if hp_prev is not None:
            p.w("vector", wait=[hp_prev])
            p.w("gpsimd", wait=[hp_prev])
        echs = [2 * hp, 2 * hp + 1, 16 + 2 * hp, 16 + 2 * hp + 1]
        tiles = [r_qkv.load(d_wqkv.ap()[e, :, :]) for e in echs]
        vtiles = [r_vv.load(d_wqkvv.ap()[dch, :, 256 * hp:256 * (hp + 1)]) for dch in range(CH)]
        p.dma("gpsimd", QKVBP[0:1, 0:256], d_qkvbrow.ap()[0:1, 256 * hp:256 * (hp + 1)], sem="qb")
        p.dma("gpsimd", QKVBP[0:1, 256:512],
              d_qkvbrow.ap()[0:1, 2048 + 256 * hp:2048 + 256 * (hp + 1)], sem="qb")
        bq_t = p.dma("gpsimd", QKVBP[0:1, 512:768],
                     d_qkvbrow.ap()[0:1, 4096 + 256 * hp:4096 + 256 * (hp + 1)], sem="qb")
        for i, e in enumerate(echs):
            wt, wtk, _ = tiles[i]
            for tt in range(NTT):
                qb_ = (0, 5)[tt % 2]
                for dch in range(CH):
                    mm(PS[qb_][:, :], wt[:, dch * 128:(dch + 1) * 128],
                       xbf(dch, tt * TW, TW), start=(dch == 0), stop=False,
                       wait=([wtk] + ps_wait(qb_)) if dch == 0 else ())
                boff = (i % 2) * 128 + (0 if i < 2 else 256)
                mm(PS[qb_][:, :], QKVBP[0:1, boff:boff + 128], SQVB(tt),
                   start=False, stop=True, wait=[bq_t], inc=[("tv", 1)])
                qk_t = p.tok("tv")
                dst = QH2 if i < 2 else KH2
                dsl = dst[:, (i % 2) * 2048 + tt * TW: (i % 2) * 2048 + (tt + 1) * TW]
                if i < 2:
                    p.w("vector", "tensor_mul",
                        (dsl, PS[qb_][:, :], RSTD[:, tt * TW:(tt + 1) * TW]),
                        wait=[qk_t], inc=[("vv", 1)])
                else:
                    p.w("vector", "scalar_tensor_tensor",
                        (dsl, PS[qb_][:, :], SMS, RSTD[:, tt * TW:(tt + 1) * TW],
                         OP.mult, OP.mult), wait=[qk_t], inc=[("vv", 1)])
                psg[qb_] = p.tok("vv")
        for tc in range(CH):
            tt, j = tc // 4, tc % 4
            vb2 = (1, 6)[tc % 2]
            for dch in range(CH):
                vt, vtk, _ = vtiles[dch]
                mm(PS[vb2][:, 0:256], xbf(dch, tc * 128, 128), vt,
                   start=(dch == 0), stop=False,
                   wait=(([vtk] if tc == 0 else []) + (ps_wait(vb2) if dch == 0 else [])))
            mm(PS[vb2][:, 0:256], SQVB(tt)[0:1, j * 128:(j + 1) * 128],
               QKVBP[0:1, 512:768],
               start=False, stop=True, wait=[bq_t], inc=[("tv", 1)])
            v_t = p.tok("tv")
            p.w("vector", "tensor_scalar",
                (VTM[:, tc * 256:(tc + 1) * 256], PS[vb2][:, 0:256], RSTM[:, tc:tc + 1],
                 None, OP.mult), wait=[v_t], inc=[("vv", 1)])
            psg[vb2] = p.tok("vv")
        p.bar()
        for a in range(2):
            h = 2 * hp + a
            negcb_toks = []
            for tt in range(NTT):
                sb2 = (2, 7)[tt % 2]
                q0 = a * 2048 + tt * TW
                mm(PS[sb2][:, :], QH2[:, q0:q0 + 128],
                   KH2[:, a * 2048:a * 2048 + TW],
                   start=True, stop=True, wait=ps_wait(sb2), inc=[("tv", 1)])
                p.w("vector", "reduce_max", (TV[:, tt:tt + 1], PS[sb2][:, :], AX),
                    wait=[p.tok("tv")], inc=[("vv", 1)])
                psg[sb2] = p.tok("vv")
                p.w("tensor", "transpose", (PS[sb2][0:1, 0:128], TV[:, tt:tt + 1],
                                            IDF[:, :]),
                    wait=[p.tok("vv")] + ps_wait(sb2), inc=[("tv", 1)])
                p.w("vector", "reduce_max",
                    (CELLS[0:1, 4 + tt:5 + tt], PS[sb2][0:1, 0:128], AX),
                    wait=[p.tok("tv")], inc=[("vv", 1)])
                psg[sb2] = p.tok("vv")
                p.w("vector", "tensor_scalar",
                    (CELLS[0:1, 8 + tt:9 + tt].bitcast(BF16)[0:1, 0:1],
                     CELLS[0:1, 4 + tt:5 + tt], -1.0,
                     -30.0, OP.mult, OP.add), inc=[("vv", 1)])
                mm(PS[sb2][:, 0:1], ONE1[:, :],
                   CELLS[0:1, 8 + tt:9 + tt].bitcast(BF16)[0:1, 0:1],
                   start=True, stop=True, wait=[p.tok("vv")] + ps_wait(sb2),
                   inc=[("tv", 1)])
                p.w("vector", "tensor_copy", (NEGCB[:, tt:tt + 1], PS[sb2][:, 0:1]),
                    wait=[p.tok("tv")], inc=[("vv", 1)])
                psg[sb2] = p.tok("vv")
                negcb_toks.append(p.tok("vv"))
            for tt in range(NTT):
                negcb_t = negcb_toks[tt]
                accs = {}
                for kch in range(CH):
                    sb = (2, 3, 4)[kch % 3]
                    sl = kch % 4
                    mm(PS[sb][:, :],
                       KH2[:, a * 2048 + kch * 128: a * 2048 + (kch + 1) * 128],
                       QH2[:, a * 2048 + tt * TW: a * 2048 + (tt + 1) * TW],
                       start=True, stop=True,
                       wait=ps_wait(sb) + ([negcb_t] if kch == 0 else []),
                       inc=[("tv", 1)])
                    arg_t = p.tok("tv")
                    pb = PTB[:, sl * TW:(sl + 1) * TW]
                    p.w("scalar", "activation", (pb, PS[sb][:, :], AF.Exp),
                        {"bias": NEGCB[:, tt:tt + 1], "scale": 1.0},
                        wait=[arg_t] + ([accs[kch - 4]] if kch >= 4 else []),
                        inc=[("sv", 1)])
                    psg[sb] = p.tok("sv")
                    exp_t = p.tok("sv")
                    if kch == 0:
                        p.w("vector", "tensor_copy", (SEXP, pb), wait=[exp_t],
                            inc=[("vv", 1)])
                    else:
                        p.w("vector", "scalar_tensor_tensor",
                            (SEXP, pb, 0.0, SEXP, OP.bypass, OP.add),
                            wait=[exp_t], inc=[("vv", 1)])
                    accs[kch] = p.tok("vv")
                    mm(PS[6][:, :], VTM[:, kch * 256 + a * 128: kch * 256 + (a + 1) * 128],
                       pb, start=(kch == 0), stop=(kch == CH - 1),
                       wait=[exp_t] + (ps_wait(6) if kch == 0 else []),
                       inc=([("tv", 1)] if kch == CH - 1 else ()))
                ctx_t = p.tok("tv")
                p.w("vector", "tensor_copy", (SEH, SEXP),
                    wait=([st["sum_t"]] if st.get("sum_t") else ()))
                p.w("vector", "scalar_tensor_tensor",
                    (SEL, SEXP, 1.0, SEH, OP.mult, OP.subtract), inc=[("vv", 1)])
                mm(PS[5][0:1, :], ONEC[:, :], SEH, start=True, stop=False,
                   wait=[p.tok("vv")] + ps_wait(5))
                mm(PS[5][0:1, :], ONEC[:, :], SEL, start=False, stop=False)
                mm(PS[5][0:1, :], ONE1[0:1, 0:1], TINY[0:1, :],
                   start=False, stop=True, inc=[("tv", 1)])
                sum_t = p.tok("tv")
                st["sum_t"] = sum_t
                p.w("vector", "reciprocal", (SCRW, PS[5][0:1, 0:256]),
                    wait=[ctx_t, sum_t])
                p.w("vector", "reciprocal", (SCRW2, PS[5][0:1, 256:512]), inc=[("vv", 1)])
                psg[5] = p.tok("vv")
                p.w("vector", "tensor_copy", (RHB[0:1, 0:256], SCRW))
                p.w("vector", "tensor_copy", (RHB[0:1, 256:512], SCRW2))
                p.w("vector", "scalar_tensor_tensor",
                    (RLB[0:1, 0:256], SCRW, 1.0, RHB[0:1, 0:256], OP.mult, OP.subtract))
                p.w("vector", "scalar_tensor_tensor",
                    (RLB[0:1, 256:512], SCRW2, 1.0, RHB[0:1, 256:512], OP.mult, OP.subtract),
                    inc=[("vv", 1)])
                mm(PS[7][:, :], ONE1[:, :], RHB, start=True, stop=False,
                   wait=[p.tok("vv")] + ps_wait(7))
                mm(PS[7][:, :], ONE1[:, :], RLB, start=False, stop=True, inc=[("tv", 1)])
                p.w("vector", "tensor_copy", (RSB[:, 0:TW], PS[7][:, :]), wait=[p.tok("tv")],
                    inc=[("vv", 1)])
                psg[7] = p.tok("vv")
                vs = VSTG[:, (tt % 2) * TW:(tt % 2 + 1) * TW]
                vw = [st["vstg"][tt % 2]] if st["vstg"][tt % 2] else []
                p.w("vector", "tensor_mul", (vs, PS[6][:, :], RSB[:, 0:TW]), wait=vw,
                    inc=[("vv", 1)])
                psg[6] = p.tok("vv")
                st["vstg"][tt % 2] = p.dma(
                    "sync", d_ctx.ap()[h, :, tt * TW:(tt + 1) * TW], vs,
                    wait=[p.tok("vv")], sem="vs%d" % (tt % 2))
        qk_last = p.tok("tv")
        hp_prev = qk_last
        for i in range(4):
            r_qkv.done(tiles[i][2], qk_last)
        for dch in range(CH):
            r_vv.done(vtiles[dch][2], qk_last)
        if stop_after == "probe4" and hp == 0:
            p.qfence("sync")
            p.bar()
            for h in range(2):
                p.dma("gpsimd", d_dbg.ap()[h, :, :], d_ctx.ap()[h, :, :])
            p.dma("gpsimd", d_dbg.ap()[2, :, :], QH2[:, 0:2048])
            p.dma("gpsimd", d_dbg.ap()[3, :, :], KH2[:, 0:2048])
            p.dma("gpsimd", d_dbg.ap()[4, :, 0:512], RSB[:, 0:512])
            p.dma("gpsimd", d_dbg.ap()[4, :, 512:513], NEGCB[:, 0:1],
                  allow_slow_non_contiguous=True)
            p.dma("gpsimd", d_dbg.ap()[4, 0:1, 520:536],
                  CELLS[0:1, 0:16], allow_slow_non_contiguous=True)
            p.dma("gpsimd", d_dbg.ap()[4, 0:1, 536:1048], RHB[0:1, 0:512])
            p.dma("gpsimd", d_dbg.ap()[4, :, 1048:2048], VTM[:, 0:1000])
            p.qfence("sync")
            p.qfence("gpsimd")
            p.bar()
            with nc.Block() as block:
                p.replay(block)
            stack.close()
            return nc
        if stop_after and stop_after.startswith("sub3b") and \
                hp == int(stop_after[5:] or 0):
            break
    p.bar()

    if stop_after and stop_after.startswith("sub3b"):
        p.qfence("sync")
        p.qfence("gpsimd")
        p.bar()
        with nc.Block() as block:
            p.replay(block)
        stack.close()
        return nc

    r_wout = Ring(2, 2048, 0, "rwt")
    CTXT = SCR_BF[:, 0:CH * TW]
    ctx_done = [p.tok("vs0"), p.tok("vs1")]
    for tt in range(NTT):
        ctk = []
        for g in range(4):
            ctk.append(p.dma("sync",
                             CTXT.rearrange("q (c t) -> q c t", c=CH)[:, 4 * g:4 * g + 4, :],
                             d_ctx.ap()[4 * g:4 * g + 4, :, tt * TW:(tt + 1) * TW]
                             .rearrange("c q t -> q c t"),
                             wait=(ctx_done if (tt == 0 and g == 0) else ()),
                             sem="ct"))
        for tk in ctk:
            p.w("tensor", wait=[tk])
        if stop_after == "sub3c":
            p.qfence("sync")
            p.qfence("gpsimd")
            p.bar()
            with nc.Block() as block:
                p.replay(block)
            stack.close()
            return nc
        for hh in range(2):
            xt = p.dma("sync", xin_ap(), d_y.ap()[2 * tt + hh, :, :],
                       wait=([st["xstg"]] if st["xstg"] else []), sem="sx")
            p.w("vector", wait=[xt])
            for oc in range(CH):
                if stop_after == "sub3d" and oc == 4:
                    break
                ob = (5, 2, 3)[oc % 3]
                wtile, wtok, slot = r_wout.load(d_wout.ap()[oc, :, :])
                for dch in range(CH):
                    mm(PS[ob][:, 0:HW], wtile[:, dch * 128:(dch + 1) * 128],
                       CTXT[:, dch * TW + hh * HW: dch * TW + hh * HW + HW],
                       start=(dch == 0), stop=(dch == CH - 1),
                       wait=([wtok] + ps_wait(ob)) if dch == 0 else (),
                       inc=([("tv", 1)] if dch == CH - 1 else ()))
                wo_t = p.tok("tv")
                r_wout.done(slot, wo_t)
                xs = XSTG[:, oc * HW:(oc + 1) * HW]
                p.w("vector", "scalar_tensor_tensor",
                    (xs, PS[ob][:, 0:HW], OUTB[:, oc:oc + 1], xs, OP.add, OP.add),
                    wait=[wo_t], inc=[("vv", 1)])
                psg[ob] = p.tok("vv")
            if stop_after == "sub3d":
                p.qfence("sync")
                p.qfence("gpsimd")
                p.bar()
                with nc.Block() as block:
                    p.replay(block)
                stack.close()
                return nc
            p.bar()
            if stop_after == "sub3":
                st["xstg"] = p.dma("sync", d_y.ap()[2 * tt + hh, :, :], xin_ap(),
                                   wait=[p.tok("vv")], sem="sy")
                continue
            phase_A(tt, hh, full_mode=False)
            st["xstg"] = p.dma("sync", d_y.ap()[2 * tt + hh, :, :], xin_ap(),
                               wait=[p.tok("vv")], sem="sy")
            p.bar()
        if stop_after == "sub3":
            continue
        rstd_tm(tt)
        ffn_tt(tt)
        p.bar()

    p.qfence("sync")
    p.qfence("gpsimd")
    p.bar()
    with nc.Block() as block:
        p.replay(block)
    stack.close()
    return nc


def _layout_inputs(inputs):
    f = np.float32
    x = np.asarray(inputs["L0_attn_a_x"], dtype=f)
    Wq = np.asarray(inputs["L0_attn_a_Wq"], dtype=f)
    Wk = np.asarray(inputs["L0_attn_a_Wk"], dtype=f)
    Wv = np.asarray(inputs["L0_attn_a_Wv"], dtype=f)
    Wo = np.asarray(inputs["L0_attn_a_Wo"], dtype=f)
    Wqkv = np.asarray(inputs["L0_attn_b_Wqkv_w"], dtype=f)
    qkvb = np.asarray(inputs["L0_attn_b_Wqkv_b"], dtype=f)
    outw = np.asarray(inputs["L0_attn_b_out_w"], dtype=f)
    outb = np.asarray(inputs["L0_attn_b_out_b"], dtype=f)
    Am = np.asarray(inputs["L0_ffn_A"], dtype=f)
    Bm = np.asarray(inputs["L0_ffn_B"], dtype=f)
    Wr = np.asarray(inputs["L0_ffn_Wr"], dtype=f)
    br = np.asarray(inputs["L0_ffn_br"], dtype=f)

    def lhs_blk(WT, nech):
        E = WT.shape[1]
        return np.ascontiguousarray(
            WT.reshape(CH, 128, nech, E // nech).transpose(2, 1, 0, 3)
            .reshape(nech, 128, CH * (E // nech)))

    wqkT = lhs_blk(np.ascontiguousarray(np.concatenate([Wq.T, Wk.T], axis=1)), 2)
    wvT = np.ascontiguousarray(Wv.T.reshape(CH, 128, 4, 512).transpose(2, 0, 1, 3))
    woT = lhs_blk(np.ascontiguousarray(Wo.T), CH)
    WqkvT = np.ascontiguousarray(Wqkv.T)
    wqkvT = lhs_blk(np.ascontiguousarray(WqkvT[:, 0:4096]), 32)
    wqkvv = np.ascontiguousarray(WqkvT[:, 4096:6144].reshape(CH, 128, 2048))
    woutT = lhs_blk(np.ascontiguousarray(outw.T), CH)
    AT = np.ascontiguousarray(Am.transpose(2, 0, 1).reshape(D, 512))
    atT = lhs_blk(AT, 4)
    BT = np.ascontiguousarray(Bm.transpose(0, 2, 1).reshape(512, D))
    btT = np.ascontiguousarray(
        BT.reshape(4, 128, CH, 128).transpose(2, 1, 0, 3).reshape(CH, 128, 512))
    wrT = np.ascontiguousarray(Wr.T.reshape(CH, 128, 8).transpose(1, 0, 2).reshape(128, CH * 8))
    qkvbrow = np.ascontiguousarray(qkvb.reshape(1, 6144))
    outb2 = np.ascontiguousarray(outb.reshape(16, 128).T)
    br2 = np.ascontiguousarray(br.reshape(8, 1))

    shared = {
        "wqk": wqkT, "wv": wvT, "wo": woT, "wqkv": wqkvT, "wqkvv": wqkvv,
        "wout": woutT, "at": atT, "bt": btT, "wr": wrT,
        "qkvbrow": qkvbrow, "outb2": outb2, "br": br2,
    }
    in_maps = []
    for c in range(8):
        xf = np.ascontiguousarray(x[c].T)
        xblk = np.ascontiguousarray(
            xf.reshape(CH, 128, NTT * 2, HW).transpose(2, 1, 0, 3)
            .reshape(NTT * 2, 128, CH * HW))
        m = dict(shared)
        m["x"] = xblk
        in_maps.append(m)
    return in_maps


_NC_CACHE = {}


def kernel(**inputs):
    in_maps = _layout_inputs(inputs)
    key = "full"
    if key not in _NC_CACHE:
        _NC_CACHE[key] = build_nc()
    nc = _NC_CACHE[key]
    res = run_bass_kernel_spmd(nc, in_maps, core_ids=list(range(8)))
    outs = []
    for c in range(8):
        yblk = res.results[c]["y"].reshape(8, 128, CH, HW)
        xf = yblk.transpose(2, 1, 0, 3).reshape(D, T)
        outs.append(np.ascontiguousarray(xf.T))
    return np.stack(outs, axis=0).astype(np.float32)


# version 20
# speedup vs baseline: 1.0325x; 1.0019x over previous
import numpy as np
from contextlib import ExitStack

import concourse.bass as bass
import concourse.mybir as mybir
from concourse.bass_utils import run_bass_kernel_spmd
from concourse.bass_core_rust import IndirectOffsetOnAxis

F32 = mybir.dt.float32
BF16 = mybir.dt.bfloat16
U32 = mybir.dt.uint32
U8 = mybir.dt.uint8
OP = mybir.AluOpType
AF = mybir.ActivationFunctionType
AX = mybir.AxisListType.X

D = 2048
T = 2048
CH = 16
NTT = 4
TW = 512
HW = 256
HB = 16
SMS = float(1.0 / np.sqrt(128.0))
EPS = 1e-5
ENGINES = ("sync", "gpsimd", "scalar", "vector", "tensor")


class Prog:
    def __init__(self, nc):
        self.nc = nc
        self.cmd = {e: [] for e in ENGINES}
        self.cnt = {}
        self.sem = {}
        self.nbar = 0
        self.qsems = {}

    def mksem(self, stack, *names):
        self.stack = stack
        for nm in names:
            self.sem[nm] = stack.enter_context(self.nc.semaphore(nm))
            self.cnt[nm] = 0

    def sem_for(self, nm):
        if nm not in self.sem:
            self.sem[nm] = self.stack.enter_context(self.nc.semaphore(nm))
            self.cnt[nm] = 0
        return nm

    def w(self, eng, op=None, args=(), kwargs=None, wait=(), inc=()):
        for nm, n in inc:
            self.cnt[nm] += n
        self.cmd[eng].append((op, tuple(args), dict(kwargs or {}), tuple(wait), tuple(inc)))
        if eng == "vector" and op not in (None, "drain"):
            out_ap = (kwargs or {}).get("out", args[0] if args else None)
            try:
                small = out_ap is not None and out_ap.free_nbytes <= 512
            except Exception:
                small = True
            if small:
                self.cmd[eng].append(("drain", (), {}, (), ()))

    def tok(self, nm):
        return (nm, self.cnt[nm])

    def bar(self):
        self.nbar += 1
        k = self.nbar
        for e in ENGINES:
            self.cnt["b_" + e] += 1
            if e in ("sync",):
                self.cmd[e].append((None, (), {}, (), (("b_" + e, 1),)))
            else:
                self.cmd[e].append(("drain", (), {}, (), (("b_" + e, 1),)))
        for e in ENGINES:
            for o in ENGINES:
                if o != e:
                    self.w(e, wait=[("b_" + o, k)])

    def dma(self, q, out_ap, in_ap, wait=(), sem=None, **kw):
        semn = self.sem_for(sem) if sem else ("gd" if q == "gpsimd" else "qd")
        kwargs = {"out": out_ap, "in_": in_ap}
        kwargs.update(kw)
        self.qsems.setdefault(q, set()).add(semn)
        self.w(q, "dma_start", kwargs=kwargs, wait=wait, inc=[(semn, 16)])
        return self.tok(semn)

    def qfence(self, q):
        base = "gd" if q == "gpsimd" else "qd"
        sems = set(self.qsems.get(q, ())) | {base}
        self.w(q, wait=[self.tok(s) for s in sorted(sems)])

    def replay(self, block):
        sem = self.sem

        def mk(e):
            def run(eng):
                for op, args, kwargs, wait, inc in self.cmd[e]:
                    for nm, v in wait:
                        eng.wait_ge(sem[nm], v)
                    if op is not None:
                        inst = getattr(eng, op)(*args, **kwargs)
                        for nm, n in inc:
                            inst.then_inc(sem[nm], n)
                    else:
                        for nm, n in inc:
                            eng.sem_inc(sem[nm], n)
            return run

        block.sync(mk("sync"))
        block.gpsimd(mk("gpsimd"))
        block.scalar(mk("scalar"))
        block.vector(mk("vector"))
        block.tensor(mk("tensor"))


def build_nc(stop_after=None):
    nc = bass.Bass("TRN2", target_bir_lowering=False)

    d_x = nc.dram_tensor("x", [NTT * 2, 128, CH * HW], F32, kind="ExternalInput")
    d_y = nc.dram_tensor("y", [NTT * 2, 128, CH * HW], F32, kind="ExternalOutput")
    d_wqk = nc.dram_tensor("wqk", [2, 128, CH * 128], F32, kind="ExternalInput")
    d_wv = nc.dram_tensor("wv", [4, CH, 128, 512], F32, kind="ExternalInput")
    d_wo = nc.dram_tensor("wo", [CH, 128, CH * 128], F32, kind="ExternalInput")
    d_wqkv = nc.dram_tensor("wqkv", [32, 128, CH * 128], F32, kind="ExternalInput")
    d_wqkvv = nc.dram_tensor("wqkvv", [CH, 128, 2048], F32, kind="ExternalInput")
    d_wout = nc.dram_tensor("wout", [CH, 128, CH * 128], F32, kind="ExternalInput")
    d_at = nc.dram_tensor("at", [4, 128, CH * 128], F32, kind="ExternalInput")
    d_bt = nc.dram_tensor("bt", [CH, 128, 4 * 128], F32, kind="ExternalInput")
    d_wr = nc.dram_tensor("wr", [128, CH * 8], F32, kind="ExternalInput")
    d_qkvbrow = nc.dram_tensor("qkvbrow", [1, 6144], F32, kind="ExternalInput")
    d_outb2 = nc.dram_tensor("outb2", [128, 16], F32, kind="ExternalInput")
    d_br = nc.dram_tensor("br", [8, 1], F32, kind="ExternalInput")
    d_dbg = nc.dram_tensor("dbg", [5, 128, 2048], F32, kind="ExternalOutput") \
        if stop_after in ("probe1", "probe0", "probe2", "probe3", "probe4",
                          "probe5") else None
    d_vtm = nc.dram_tensor("vtm_s", [T, D], BF16)
    d_keys = nc.dram_tensor("keys_s", [128, T], F32)
    d_ctx = nc.dram_tensor("ctx_s", [HB, 128, T], BF16)

    A = nc.alloc_sbuf_tensor
    XB = A("XB", [128, CH * T], BF16)
    XSTG = A("XSTG", [128, CH * HW], F32)
    SCR = A("SCR", [128, 32 * 1024], U8)
    WBUF = A("WBUF", [128, 25 * 1024], U8)
    P0 = A("P0", [128, 28 * 1024], U8)
    RSTD = A("RSTD", [128, T], F32)
    QA = A("QA", [128, T], F32)
    KAT = A("KAT", [128, T], F32)
    VGALL = A("VGALL", [128, 64 * 16], BF16)
    IDF = A("IDF", [128, 128], F32)
    ONE1 = A("ONE1", [1, 128], BF16)
    ONEC = A("ONEC", [128, 1], BF16)
    EPH = A("EPH", [2, 128], BF16)
    REP8 = A("REP8", [8, 128], BF16)
    G8A = A("G8A", [8, 512], BF16)
    SLT = A("SLT", [128, 256], BF16)
    SLB = A("SLB", [128, 256], BF16)
    VST = A("VST", [128, 448], BF16)
    VSB = A("VSB", [128, 448], BF16)
    TV = A("TV", [128, 64], F32)
    TIU = A("TIU", [128, 64], U32)
    TIF = A("TIF", [128, 64], F32)
    IXB = A("IXB", [128, 128], F32)
    TBALL = A("TBALL", [128, 128], F32)
    IXVU = A("IXVU", [128, 64], U32)
    IXKU = A("IXKU", [128, 64], U32)
    VALS = A("VALS", [128, 64], F32)
    VB = A("VB", [128, 64], BF16)
    SCL = A("SCL", [128, 64], F32)
    EXB = A("EXB", [128, 128], BF16)
    ZRO = A("ZRO", [128, 1], F32)
    TINY = A("TINY", [1, TW], BF16)
    PVEC = A("PVEC", [128, 1], F32)
    H2K = A("H2K", [128, 1], F32)
    NEGC = A("NEGC", [128, 1], F32)
    NEGCB = A("NEGCB", [128, 4], F32)
    CS1 = A("CS1", [128, 1], F32)
    CS2 = A("CS2", [128, 1], F32)
    RSTM = A("RSTM", [128, CH], F32)
    OUTB = A("OUTB", [128, 16], F32)
    GTM = A("GTM", [128, 8], F32)
    GSC = A("GSC", [128, 8], F32)
    PTB = A("PTB", [128, 4 * TW], BF16)
    VSTG = A("VSTG", [128, 2 * TW], BF16)
    RSB = A("RSB", [128, TW], BF16)
    PS = [nc.alloc_psum_tensor("ps%d" % i, [128, 512], F32) for i in range(8)]

    SCR_BF = SCR[:, :].bitcast(BF16)
    SCR_F32 = SCR[:, :].bitcast(F32)
    SCR_U32 = SCR[:, :].bitcast(U32)
    WB_BF = WBUF[:, :].bitcast(BF16)
    P0_F32 = P0[:, :].bitcast(F32)
    P0_BF = P0[:, :].bitcast(BF16)
    QA_BF = QA[:, :].bitcast(BF16)
    KAT_BF = KAT[:, :].bitcast(BF16)
    WTR = XSTG[:, :].bitcast(BF16)
    BRS = P0[:, :].bitcast(F32)[0:8, 7056:7057]
    LGT = SCR_F32[0:8, 1024:1536]
    GFM = SCR_F32[0:8, 1536:2048]
    G8H = SCR_BF[0:8, 4096:4608]
    G8L = SCR_BF[0:8, 4608:5120]

    def MQ():
        return QA_BF[:, 0:2048]

    def MQC():
        return QA_BF[:, 2048:4096]

    def QHI():
        return KAT_BF[:, 0:2048]

    def QLO():
        return KAT_BF[:, 2048:4096]

    def RH(tt):
        return P0_BF[0:1, tt * TW:(tt + 1) * TW]

    def RL(tt):
        return P0_BF[0:1, 2048 + tt * TW: 2048 + (tt + 1) * TW]

    def MH_(tt):
        return P0_BF[0:1, 4096 + tt * TW: 4096 + (tt + 1) * TW]

    def ML_(tt):
        return P0_BF[0:1, 6144 + tt * TW: 6144 + (tt + 1) * TW]

    def SQVB(tt):
        return P0_BF[0:1, 8192 + tt * TW: 8192 + (tt + 1) * TW]

    RHB = P0_BF[0:1, 10240:10752]
    RLB = P0_BF[0:1, 10752:11264]
    QKVBP = P0_BF[0:1, 13344:14112]
    SCRA = P0_F32[0:1, 5632:5888]
    SCRB = P0_F32[0:1, 5888:6144]
    SCRW = P0_F32[0:1, 6144:6400]
    SCRW2 = P0_F32[0:1, 6400:6656]
    CELLS = P0_F32[0:1, 6656:6672]
    EPSC = P0_F32[0:1, 6672:6673]

    p = Prog(nc)
    stack = ExitStack()
    p.mksem(stack, "qd", "gd", "tv", "vv", "sv",
             *["b_" + e for e in ENGINES])
    psg = [None] * 8
    st = {"xstg": None, "vstg": [None, None]}

    def ps_wait(i):
        return [psg[i]] if psg[i] else []

    def mm(out_ap, lhsT, rhs, start, stop, wait=(), inc=()):
        p.w("tensor", "matmul", (out_ap, lhsT, rhs), {"start": start, "stop": stop},
            wait=wait, inc=inc)

    IOTA = TBALL[:, :].bitcast(U32)
    IOTA5 = VGALL[:, :].bitcast(U32)[:, 0:512]
    p.w("gpsimd", "iota", (IOTA[:, 0:1], [[0, 1]]), {"channel_multiplier": 1})
    p.bar()
    p.w("vector", "tensor_copy", (PVEC[:, :], IOTA[:, 0:1]))
    p.bar()
    p.w("vector", "tensor_scalar", (H2K[:, :], PVEC[:, :], 2048.0, None, OP.mult))
    p.w("vector", "memset", (EPSC, float(EPS)))
    p.w("vector", "memset", (ONE1[:, :], 1.0))
    p.w("vector", "memset", (ONEC[:, :], 1.0))
    p.w("gpsimd", "iota", (IOTA[:, 0:128], [[1, 128]]), {"channel_multiplier": 0})
    p.bar()
    p.w("vector", "tensor_copy", (QA[:, 0:128], IOTA[:, 0:128]))
    p.bar()
    p.w("vector", "tensor_scalar", (IDF[:, :], QA[:, 0:128], PVEC[:, 0:1], None,
                                    OP.is_equal))
    p.w("gpsimd", "iota", (IOTA[:, 0:128], [[1, 2], [0, 64]]), {"channel_multiplier": 0})
    p.bar()
    p.w("vector", "tensor_copy", (QA[:, 0:128], IOTA[:, 0:128]))
    p.bar()
    p.w("vector", "tensor_scalar", (EPH[:, :], QA[0:2, 0:128], PVEC[0:2, 0:1], None,
                                    OP.is_equal))
    p.w("gpsimd", "iota", (IOTA[:, 0:128], [[1, 8], [0, 16]]), {"channel_multiplier": 0})
    p.bar()
    p.w("vector", "tensor_copy", (QA[:, 0:128], IOTA[:, 0:128]))
    p.bar()
    p.w("vector", "tensor_scalar", (REP8[:, :], QA[0:8, 0:128], PVEC[0:8, 0:1], None,
                                    OP.is_equal))
    p.w("gpsimd", "iota", (IOTA5[:, 0:512], [[1, 8], [0, 64]]), {"channel_multiplier": 0})
    p.bar()
    p.w("vector", "tensor_copy", (QA[:, 128:640], IOTA5[:, 0:512]))
    p.bar()
    p.w("vector", "tensor_scalar", (G8A[:, :], QA[0:8, 128:640], PVEC[0:8, 0:1], None,
                                    OP.is_equal))
    p.w("gpsimd", "iota", (IOTA[:, 0:128], [[32, 4], [0, 32]]), {"channel_multiplier": 0})
    p.bar()
    p.w("vector", "tensor_copy", (QA[:, 0:128], IOTA[:, 0:128]))
    p.bar()
    p.w("vector", "tensor_scalar", (EXB[:, :], QA[:, 0:128], PVEC[:, 0:1], None,
                                    OP.is_equal))
    p.w("vector", "memset", (ZRO[:, :], 0.0))
    p.w("vector", "memset", (TINY[:, :], 1e-20))
    p.w("vector", "memset", (SLT[:, :], 0.0))
    p.w("vector", "memset", (SLT[0:32, 128:129], 1.0))
    p.w("vector", "memset", (SLT[64:96, 129:130], 1.0))
    p.w("vector", "memset", (SLB[:, :], 0.0))
    p.w("vector", "memset", (SLB[32:64, 128:129], 1.0))
    p.w("vector", "memset", (SLB[96:128, 129:130], 1.0))
    p.w("vector", "memset", (VST[:, :], 0.0))
    p.w("vector", "memset", (VSB[:, :], 0.0))
    p.dma("sync", OUTB[:, :], d_outb2.ap()[:, :])
    p.dma("sync", BRS[:, :], d_br.ap()[:, :])
    p.qfence("sync")
    p.bar()

    XC = SCR_BF[:, 8192:8192 + CH * HW]
    SQH = SCR_BF[:, 12288:12288 + CH * HW]

    def phase_A(tt, hh, full_mode, wait_x=()):
        h0 = hh * HW
        p.w("scalar", "activation", (SQH, XSTG[:, :], AF.Square), wait=wait_x,
            inc=[("sv", 1)])
        sq_t = p.tok("sv")
        p.w("vector", "tensor_copy", (XC, XSTG[:, :]), wait=wait_x, inc=[("vv", 1)])
        cast_t = p.tok("vv")
        for dch in range(CH):
            mm(PS[0][0:1, 0:HW], ONEC[:, :], XC[:, dch * HW:(dch + 1) * HW],
               start=(dch == 0), stop=(dch == CH - 1),
               wait=([cast_t] + ps_wait(0)) if dch == 0 else ())
        for dch in range(CH):
            mm(PS[1][0:1, 0:HW], ONEC[:, :], SQH[:, dch * HW:(dch + 1) * HW],
               start=(dch == 0), stop=(dch == CH - 1),
               wait=([sq_t] + ps_wait(1)) if dch == 0 else (),
               inc=[("tv", 1)] if dch == CH - 1 else ())
        mm_t = p.tok("tv")
        p.w("vector", "tensor_scalar", (SCRA, PS[0][0:1, 0:HW], 1.0 / D, None, OP.mult),
            wait=[mm_t])
        p.w("vector", "tensor_tensor", (SCRB, SCRA, SCRA, OP.mult))
        p.w("vector", "scalar_tensor_tensor",
            (SCRB, PS[1][0:1, 0:HW], 1.0 / D, SCRB, OP.mult, OP.subtract), inc=[("vv", 1)])
        psg[0] = psg[1] = p.tok("vv")
        p.w("scalar", "activation", (SCRB, SCRB, AF.Sqrt), {"bias": EPSC, "scale": 1.0},
            wait=[p.tok("vv")], inc=[("sv", 1)])
        p.w("vector", "tensor_copy", (SQVB(tt)[0:1, h0:h0 + HW], SCRB), wait=[p.tok("sv")])
        p.w("vector", "reciprocal", (SCRB, SCRB))
        p.w("vector", "tensor_copy", (RH(tt)[0:1, h0:h0 + HW], SCRB))
        p.w("vector", "scalar_tensor_tensor",
            (RL(tt)[0:1, h0:h0 + HW], SCRB, 1.0, RH(tt)[0:1, h0:h0 + HW],
             OP.mult, OP.subtract))
        p.w("vector", "tensor_copy", (MH_(tt)[0:1, h0:h0 + HW], SCRA))
        p.w("vector", "scalar_tensor_tensor",
            (ML_(tt)[0:1, h0:h0 + HW], SCRA, 1.0, MH_(tt)[0:1, h0:h0 + HW],
             OP.mult, OP.subtract), inc=[("vv", 1)])
        rows_t = p.tok("vv")
        mm(PS[2][:, 0:HW], ONE1[:, :], RH(tt)[0:1, h0:h0 + HW], start=True, stop=False,
           wait=[rows_t] + ps_wait(2))
        mm(PS[2][:, 0:HW], ONE1[:, :], RL(tt)[0:1, h0:h0 + HW], start=False, stop=True,
           inc=[("tv", 1)])
        p.w("vector", "tensor_copy",
            (RSTD[:, tt * TW + h0: tt * TW + h0 + HW], PS[2][:, 0:HW]),
            wait=[p.tok("tv")], inc=[("vv", 1)])
        rstd_t = p.tok("vv")
        mm(PS[2][:, 0:HW], ONE1[:, :], MH_(tt)[0:1, h0:h0 + HW], start=True, stop=False,
           wait=[rstd_t])
        mm(PS[2][:, 0:HW], ONE1[:, :], ML_(tt)[0:1, h0:h0 + HW], start=False, stop=True,
           inc=[("tv", 1)])
        mrep_t = p.tok("tv")
        if full_mode:
            xb_out = XB[:, :].rearrange("q (c t) -> q c t", c=CH)[
                :, :, tt * TW + h0: tt * TW + h0 + HW]
        else:
            xb_out = XB[:, 0:CH * TW].rearrange("q (c t) -> q c t", c=CH)[:, :, h0:h0 + HW]
        xin = XSTG[:, :].rearrange("q (c t) -> q c t", c=CH)
        mrep_b = PS[2][:, 0:HW].unsqueeze(1).to_broadcast([128, CH, HW])
        p.w("vector", "tensor_tensor", (xb_out, xin, mrep_b, OP.subtract),
            wait=[mrep_t], inc=[("vv", 1)])
        psg[2] = p.tok("vv")
        return p.tok("vv")

    def rstd_tm(tt):
        for j in range(4):
            mm(PS[3][:, j:j + 1], RH(tt)[0:1, j * 128:(j + 1) * 128], ONE1[0:1, 0:1],
               start=True, stop=False, wait=(ps_wait(3) if j == 0 else ()))
            mm(PS[3][:, j:j + 1], RL(tt)[0:1, j * 128:(j + 1) * 128], ONE1[0:1, 0:1],
               start=False, stop=True, inc=([("tv", 1)] if j == 3 else ()))
        p.w("vector", "tensor_copy", (RSTM[:, tt * 4:tt * 4 + 4], PS[3][:, 0:4]),
            wait=[p.tok("tv")], inc=[("vv", 1)])
        psg[3] = p.tok("vv")

    class Ring:
        def __init__(self, n, cols, off, pfx):
            self.n, self.cols, self.off, self.pfx = n, cols, off, pfx
            self.i = 0
            self.free = [None] * n

        def load(self, src):
            s = self.i % self.n
            self.i += 1
            dst = WB_BF[:, self.off + s * self.cols: self.off + (s + 1) * self.cols]
            wl = [self.free[s]] if self.free[s] is not None else []
            tok = p.dma("gpsimd", dst, src, wait=wl, sem="%s%d" % (self.pfx, s))
            return dst, tok, s

        def done(self, s, tok):
            self.free[s] = tok

    def xin_ap():
        return XSTG[:, :]

    r_qk = Ring(2, 2048, 0, "rqk")
    r_wv = Ring(16, 512, 4096, "rwv")
    qk_tiles = [r_qk.load(d_wqk.ap()[e, :, :]) for e in range(2)]

    for tt in range(NTT):
        for hh in range(2):
            xt = p.dma("sync", xin_ap(), d_x.ap()[2 * tt + hh, :, :],
                       wait=([st["xstg"]] if st["xstg"] else []), sem="sx")
            phase_A(tt, hh, full_mode=False, wait_x=[xt])
            p.bar()
        rstd_tm(tt)
        if stop_after == "probe3" and tt == 0:
            p.bar()
            wtile0d, _, _ = qk_tiles[0]
            for sl in range(4):
                p.dma("gpsimd", d_dbg.ap()[sl, :, :],
                      XB[:, sl * 2048:(sl + 1) * 2048])
            p.dma("gpsimd", d_dbg.ap()[4, :, :], wtile0d[:, 0:2048])
            p.qfence("sync")
            p.qfence("gpsimd")
            p.bar()
            with nc.Block() as block:
                p.replay(block)
            stack.close()
            return nc
        if stop_after == "probe2" and tt == 0:
            p.bar()
            wtile0c, _, _ = qk_tiles[0]
            for li, ln_ in enumerate([2, 4, 8, 16]):
                for dch in range(ln_):
                    mm(PS[7][:, :], wtile0c[:, dch * 128:(dch + 1) * 128],
                       XB[:, dch * TW:(dch + 1) * TW],
                       start=(dch == 0), stop=(dch == ln_ - 1),
                       wait=(ps_wait(7) if dch == 0 else ()),
                       inc=([("tv", 1)] if dch == ln_ - 1 else ()))
                p.w("vector", "tensor_copy", (KAT[:, 1024:1536], PS[7][:, :]),
                    wait=[p.tok("tv")], inc=[("vv", 1)])
                psg[7] = p.tok("vv")
                p.dma("sync", d_dbg.ap()[4, :, 512 * li:512 * (li + 1)],
                      KAT[:, 1024:1536], wait=[p.tok("vv")])
            p.dma("gpsimd", d_dbg.ap()[0, :, :], XB[:, 0:2048])
            p.qfence("sync")
            p.qfence("gpsimd")
            p.bar()
            with nc.Block() as block:
                p.replay(block)
            stack.close()
            return nc
        if stop_after == "probe0" and tt == 0:
            p.dma("gpsimd", d_dbg.ap()[3, :, 0:1024], XB[:, 0:1024])
            wtile0, wtok0, _ = qk_tiles[0]
            for dch in range(CH):
                mm(PS[6][:, :], wtile0[:, dch * 128:(dch + 1) * 128],
                   XB[:, dch * TW:(dch + 1) * TW],
                   start=(dch == 0), stop=(dch == CH - 1),
                   inc=([("tv", 1)] if dch == CH - 1 else ()))
            p.w("vector", "tensor_mul",
                (QLO().bitcast(F32)[:, 0:512] if False else KAT[:, 1536:2048],
                 PS[6][:, :], RSTD[:, 0:512]), wait=[p.tok("tv")], inc=[("vv", 1)])
            psg[6] = p.tok("vv")
            p.dma("sync", d_dbg.ap()[3, :, 1024:1536], KAT[:, 1536:2048],
                  wait=[p.tok("vv")])
            wtile0b, _, _ = qk_tiles[0]
            p.dma("gpsimd", d_dbg.ap()[3, :, 1536:2048], wtile0b[:, 0:512])
            for li, ln_ in enumerate([2, 4, 8, 16]):
                for dch in range(ln_):
                    mm(PS[7][:, :], wtile0b[:, dch * 128:(dch + 1) * 128],
                       XB[:, dch * TW:(dch + 1) * TW],
                       start=(dch == 0), stop=(dch == ln_ - 1),
                       wait=(ps_wait(7) if dch == 0 else ()),
                       inc=([("tv", 1)] if dch == ln_ - 1 else ()))
                p.w("vector", "tensor_copy", (KAT[:, 1024:1536], PS[7][:, :]),
                    wait=[p.tok("tv")], inc=[("vv", 1)])
                psg[7] = p.tok("vv")
                p.dma("sync", d_dbg.ap()[4, :, 512 * li:512 * (li + 1)],
                      KAT[:, 1024:1536], wait=[p.tok("vv")])
        for e in range(2):
            wtile, wtok, _ = qk_tiles[e]
            for dch in range(CH):
                mm(PS[4][:, :], wtile[:, dch * 128:(dch + 1) * 128],
                   XB[:, dch * TW:(dch + 1) * TW],
                   start=(dch == 0), stop=(dch == CH - 1),
                   wait=([wtok] + ps_wait(4)) if dch == 0 else (),
                   inc=([("tv", 1)] if dch == CH - 1 else ()))
            dst = QA if e == 0 else KAT
            p.w("vector", "tensor_mul",
                (dst[:, tt * TW:(tt + 1) * TW], PS[4][:, :], RSTD[:, tt * TW:(tt + 1) * TW]),
                wait=[p.tok("tv")], inc=[("vv", 1)])
            psg[4] = p.tok("vv")
        for ecg in range(4):
            tiles = [r_wv.load(d_wv.ap()[ecg, dch, :, :]) for dch in range(CH)]
            for j in range(4):
                vb_ = (5, 6, 7)[j % 3]
                tc = tt * 4 + j
                for dch in range(CH):
                    wt, wtk, _ = tiles[dch]
                    mm(PS[vb_][:, :], XB[:, dch * TW + j * 128: dch * TW + (j + 1) * 128], wt,
                       start=(dch == 0), stop=(dch == CH - 1),
                       wait=(([wtk] if j == 0 else []) + (ps_wait(vb_) if dch == 0 else [])),
                       inc=([("tv", 1)] if dch == CH - 1 else ()))
                vs = VSTG[:, (j % 2) * TW:(j % 2 + 1) * TW]
                vw = [p.tok("tv")] + ([st["vstg"][j % 2]] if st["vstg"][j % 2] else [])
                p.w("vector", "tensor_scalar",
                    (vs, PS[vb_][:, :], RSTM[:, tc:tc + 1], None, OP.mult),
                    wait=vw, inc=[("vv", 1)])
                psg[vb_] = p.tok("vv")
                st["vstg"][j % 2] = p.dma(
                    "sync", d_vtm.ap()[tc * 128:(tc + 1) * 128, ecg * 512:(ecg + 1) * 512],
                    vs, wait=[p.tok("vv")], sem="vs%d" % (j % 2))
            lasttok = p.tok("tv")
            for dch in range(CH):
                r_wv.done(tiles[dch][2], lasttok)
        p.bar()
        if stop_after == "probe0" and tt == 0:
            p.w("gpsimd", wait=[p.tok("vs0"), p.tok("vs1")])
            p.dma("gpsimd", d_dbg.ap()[0, :, :], XB[:, 0:2048])
            p.dma("sync", d_dbg.ap()[1, :, 0:512], RSTD[:, 0:512])
            p.dma("sync", d_dbg.ap()[1, :, 512:516], RSTM[:, 0:4])
            p.dma("sync", d_dbg.ap()[2, :, 0:512], QA[:, 0:512])
            p.dma("sync", d_dbg.ap()[2, :, 512:1024], KAT[:, 0:512])
            p.dma("gpsimd", d_dbg.ap()[4, :, 0:1024], d_vtm.ap()[0:128, 0:1024])
            p.qfence("sync")
            p.qfence("gpsimd")
            p.bar()
            with nc.Block() as block:
                p.replay(block)
            stack.close()
            return nc

    p.dma("sync", d_keys.ap()[:, :], KAT[:, :])
    p.qfence("sync")
    p.bar()

    TOKU = SCR_U32[:, 0:2048]
    PKT = SCR_F32[:, 2048:4096]
    PKB = SCR_F32[:, 4096:6144]
    MSK = SCR[:, 24576:26624]
    p.w("gpsimd", "iota", (TOKU, [[1, T]]), {"channel_multiplier": 0})
    p.bar()
    p.w("vector", "tensor_scalar", (MSK, KAT[:, :], 0.0, None, OP.is_le))
    p.w("vector", "tensor_scalar",
        (PKT.bitcast(U32), KAT[:, :].bitcast(U32), 0xFFFFF800, None, OP.bitwise_and))
    p.w("vector", "tensor_tensor", (PKT.bitcast(U32), PKT.bitcast(U32), TOKU, OP.bitwise_or))
    p.w("vector", "tensor_scalar",
        (PKB.bitcast(U32), KAT[:, :].bitcast(U32), 0x80000000, None, OP.bitwise_xor))
    p.w("vector", "tensor_scalar",
        (PKB.bitcast(U32), PKB.bitcast(U32), 0xFFFFF800, None, OP.bitwise_and))
    p.w("vector", "tensor_tensor", (PKB.bitcast(U32), PKB.bitcast(U32), TOKU, OP.bitwise_or))
    p.w("vector", "memset", (TOKU, 0))
    p.w("vector", "copy_predicated", (PKT.bitcast(U32), MSK, TOKU))
    p.w("vector", "tensor_scalar", (MSK, KAT[:, :], 0.0, None, OP.is_ge))
    p.w("vector", "copy_predicated", (PKB.bitcast(U32), MSK, TOKU))
    for i in range(4):
        p.w("vector", "max", kwargs={"out": TV[:, 8 * i:8 * i + 8], "in_": PKT})
        p.w("vector", "drain")
        p.w("vector", "match_replace",
            kwargs={"out": PKT, "in_to_replace": TV[:, 8 * i:8 * i + 8],
                    "in_values": PKT, "imm_value": -1.0})
        p.w("vector", "drain")
    for i in range(4, 8):
        p.w("vector", "max", kwargs={"out": TV[:, 8 * i:8 * i + 8], "in_": PKB})
        p.w("vector", "drain")
        p.w("vector", "match_replace",
            kwargs={"out": PKB, "in_to_replace": TV[:, 8 * i:8 * i + 8],
                    "in_values": PKB, "imm_value": -1.0})
        p.w("vector", "drain")
    p.w("vector", "tensor_scalar",
        (TIU[:, :], TV[:, :].bitcast(U32), 2047, None, OP.bitwise_and))
    p.w("vector", "drain")
    p.w("vector", "tensor_copy", (TIF[:, :], TIU[:, :]))
    p.w("vector", "drain")
    p.w("vector", "scalar_tensor_tensor",
        (IXB[:, 0:64], TIF[:, :], 128.0, PVEC[:, 0:1].to_broadcast([128, 64]),
         OP.mult, OP.add))
    p.w("vector", "drain")
    p.w("vector", "scalar_tensor_tensor",
        (IXB[:, 64:128], TIF[:, :], 1.0, H2K[:, 0:1].to_broadcast([128, 64]),
         OP.mult, OP.add))
    p.w("vector", "drain", inc=[("vv", 1)])
    ix_t = p.tok("vv")
    p.w("tensor", "transpose", (PS[1][:, 0:128], IXB[:, :], IDF[:, :]),
        wait=[ix_t] + ps_wait(1), inc=[("tv", 1)])
    p.w("vector", "tensor_copy", (TBALL[:, :], PS[1][:, 0:128]), wait=[p.tok("tv")],
        inc=[("vv", 1)])
    psg[1] = p.tok("vv")
    tb_t = p.tok("vv")
    p.dma("sync", IXB[0:64, 0:64], TBALL[0:64, 0:128:2], wait=[tb_t],
          allow_slow_non_contiguous=True, sem="ix")
    p.dma("sync", IXB[64:128, 0:64], TBALL[0:64, 1:128:2], wait=[tb_t],
          allow_slow_non_contiguous=True, sem="ix")
    p.dma("sync", IXB[0:64, 64:128], TBALL[64:128, 0:128:2], wait=[tb_t],
          allow_slow_non_contiguous=True, sem="ix")
    tix = p.dma("sync", IXB[64:128, 64:128], TBALL[64:128, 1:128:2], wait=[tb_t],
                allow_slow_non_contiguous=True, sem="ix")
    p.bar()
    p.w("vector", "tensor_copy", (IXVU[:, :], IXB[:, 0:64]), wait=[tix])
    p.w("vector", "tensor_copy", (IXKU[:, :], IXB[:, 64:128]), inc=[("vv", 1)])
    ixu_t = p.tok("vv")
    if stop_after == "probe1":
        p.dma("sync", d_dbg.ap()[0, :, :], QA[:, :])
        p.dma("sync", d_dbg.ap()[4, :, :], KAT[:, :])
    p.w("vector", "tensor_copy", (SCR_BF[:, 12288:14336], QA[:, :]))
    p.w("vector", "scalar_tensor_tensor",
        (SCR_BF[:, 14336:16384], QA[:, :], 1.0, SCR_BF[:, 12288:14336],
         OP.mult, OP.subtract))
    p.w("vector", "tensor_scalar", (PKT, QA[:, :], 0.0, None, OP.is_gt))
    p.w("vector", "tensor_scalar", (PKB, QA[:, :], 0.0, None, OP.is_le), inc=[("vv", 1)])
    stg_t = p.tok("vv")
    p.w("vector", "tensor_copy", (QHI(), SCR_BF[:, 12288:14336]), wait=[stg_t])
    p.w("vector", "tensor_copy", (QLO(), SCR_BF[:, 14336:16384]))
    p.w("vector", "tensor_copy", (MQ(), PKT))
    p.w("vector", "tensor_copy", (MQC(), PKB), inc=[("vv", 1)])
    p.bar()
    for pr in range(64):
        p.w("gpsimd", "indirect_dma_start",
            (VALS[:, pr:pr + 1], None,
             bass.AP(d_keys, 0, [[1, T * 128], [1, 1]]),
             IndirectOffsetOnAxis(IXKU[:, pr:pr + 1], 0)),
            inc=[("gd", 16)])
    for pr in range(64):
        p.w("gpsimd", "indirect_dma_start",
            (VGALL[:, pr * 16:(pr + 1) * 16], None,
             bass.AP(d_vtm, 0, [[16, T * D // 16], [1, 16]]),
             IndirectOffsetOnAxis(IXVU[:, pr:pr + 1], 0)),
            inc=[("gd", 16)])
    p.qfence("gpsimd")
    p.bar()
    p.w("vector", "tensor_copy", (VB[:, :], VALS[:, :]), inc=[("vv", 1)])
    mm(PS[1][:, 0:64], EXB[:, :], VB[:, :], start=True, stop=True,
       wait=[p.tok("vv")] + ps_wait(1), inc=[("tv", 1)])
    p.w("vector", "tensor_tensor", (SCL[:, :], VALS[:, :], PS[1][:, 0:64],
                                    OP.subtract),
        wait=[p.tok("tv")], inc=[("vv", 1)])
    psg[1] = p.tok("vv")
    if stop_after == "probe1":
        p.dma("sync", d_dbg.ap()[1, :, 0:64], TIF[:, :])
        p.dma("sync", d_dbg.ap()[1, :, 64:128], VALS[:, :])
        p.dma("sync", d_dbg.ap()[1, :, 128:129], NEGC[:, :], allow_slow_non_contiguous=True)
        p.dma("sync", d_dbg.ap()[1, :, 129:193], TV[:, :])
        p.dma("gpsimd", d_dbg.ap()[2, :, 0:1024], VGALL[:, 0:1024])
        p.dma("gpsimd", d_dbg.ap()[3, :, 0:1024], d_vtm.ap()[0:128, 0:1024])
        p.qfence("sync")
        p.qfence("gpsimd")
        p.bar()
        with nc.Block() as block:
            p.replay(block)
        stack.close()
        return nc

    r_at = Ring(2, 2048, 4096, "rat")
    r_bt = Ring(8, 512, 8192, "rbt")
    r_wr = Ring(1, 128, 12288, "rwr")
    wr_tile, wr_tok, _ = r_wr.load(d_wr.ap()[:, :])
    HSB = SCR_BF[:, 0:4 * TW]

    def ffn_tt(tt):
        for dch in range(CH):
            mm(PS[0][0:8, :], wr_tile[:, dch * 8:(dch + 1) * 8],
               XB[:, dch * TW:(dch + 1) * TW],
               start=(dch == 0), stop=(dch == CH - 1),
               wait=([wr_tok] + ps_wait(0)) if dch == 0 else (),
               inc=([("tv", 1)] if dch == CH - 1 else ()))
        p.w("vector", "tensor_mul",
            (LGT[0:8, :], PS[0][0:8, :], RSTD[0:8, tt * TW:(tt + 1) * TW]),
            wait=[p.tok("tv")])
        p.w("vector", "tensor_scalar", (LGT[0:8, :], LGT[0:8, :], BRS[0:8, 0:1], None, OP.add),
            inc=[("vv", 1)])
        psg[0] = p.tok("vv")
        lg_t = p.tok("vv")
        for j in range(4):
            p.w("tensor", "transpose", (PS[1][0:128, 0:8], LGT[0:8, j * 128:(j + 1) * 128],
                                        IDF[0:8, 0:8]),
                wait=[lg_t] + ps_wait(1), inc=[("tv", 1)])
            p.w("vector", "tensor_copy", (GTM[:, :], PS[1][0:128, 0:8]), wait=[p.tok("tv")],
                inc=[("vv", 1)])
            psg[1] = p.tok("vv")
            p.w("vector", "reduce_max", (CS1[:, :], GTM[:, :], AX))
            p.w("vector", "tensor_scalar", (CS2[:, :], CS1[:, :], -1.0, None, OP.mult),
                inc=[("vv", 1)])
            p.w("scalar", "activation", (GSC[:, :], GTM[:, :], AF.Exp),
                {"bias": CS2[:, 0:1], "scale": 1.0, "accum_out": CS1[:, 0:1]},
                wait=[p.tok("vv")], inc=[("sv", 1)])
            p.w("vector", "reciprocal", (CS2[:, :], CS1[:, :]), wait=[p.tok("sv")])
            p.w("vector", "reduce_max", (CS1[:, :], GTM[:, :], AX))
            p.w("vector", "tensor_scalar", (GSC[:, :], GTM[:, :], CS1[:, 0:1], None,
                                            OP.is_equal))
            p.w("vector", "tensor_scalar", (GSC[:, :], GSC[:, :], CS2[:, 0:1], 1.0 / 64.0,
                                            OP.mult, OP.mult), inc=[("vv", 1)])
            g_t = p.tok("vv")
            p.w("tensor", "transpose", (PS[1][0:8, 0:128], GSC[:, :], IDF[:, :]),
                wait=[g_t] + ps_wait(1), inc=[("tv", 1)])
            p.w("vector", "tensor_copy", (GFM[0:8, j * 128:(j + 1) * 128], PS[1][0:8, 0:128]),
                wait=[p.tok("tv")], inc=[("vv", 1)])
            psg[1] = p.tok("vv")
        p.w("vector", "tensor_copy", (G8H[0:8, :], GFM[0:8, :]))
        p.w("vector", "scalar_tensor_tensor",
            (G8L[0:8, :], GFM[0:8, :], 1.0, G8H[0:8, :], OP.mult, OP.subtract),
            inc=[("vv", 1)])
        g8_t = p.tok("vv")
        at_tiles = [r_at.load(d_at.ap()[er, :, :]) for er in range(2)]
        for er in range(4):
            hb_ = (2, 5)[er % 2]
            att, atk, aslot = at_tiles[er]
            for dch in range(CH):
                mm(PS[hb_][:, :], att[:, dch * 128:(dch + 1) * 128],
                   XB[:, dch * TW:(dch + 1) * TW],
                   start=(dch == 0), stop=(dch == CH - 1),
                   wait=([atk] + ps_wait(hb_)) if dch == 0 else (),
                   inc=([("tv", 1)] if dch == CH - 1 else ()))
            h_t = p.tok("tv")
            r_at.done(aslot, h_t)
            if er + 2 < 4:
                at_tiles.append(r_at.load(d_at.ap()[er + 2, :, :]))
            else:
                at_tiles.append(None)
            mm(PS[3][:, :], G8A[:, er * 128:(er + 1) * 128], G8H[0:8, :],
               start=True, stop=False, wait=[g8_t] + ps_wait(3))
            mm(PS[3][:, :], G8A[:, er * 128:(er + 1) * 128], G8L[0:8, :],
               start=False, stop=True, inc=[("tv", 1)])
            grep_t = p.tok("tv")
            RGF = VSTG[:, :].bitcast(F32)
            p.w("vector", "tensor_mul",
                (RGF[:, 0:512], PS[3][:, :], RSTD[:, tt * TW:(tt + 1) * TW]),
                wait=[grep_t], inc=[("vv", 1)])
            psg[3] = p.tok("vv")
            p.w("vector", "tensor_mul",
                (HSB[:, er * TW:(er + 1) * TW], PS[hb_][:, :], RGF[:, 0:512]),
                wait=[h_t], inc=[("vv", 1)])
            psg[hb_] = p.tok("vv")
        h_all = p.tok("vv")
        for hh in range(2):
            xt = p.dma("sync", xin_ap(), d_y.ap()[2 * tt + hh, :, :],
                       wait=([st["xstg"]] if st["xstg"] else []), sem="sx")
            p.w("vector", wait=[xt])
            for oc in range(CH):
                ob = (4, 6, 7)[oc % 3]
                btt, btk, bslot = r_bt.load(d_bt.ap()[oc, :, :])
                for er in range(4):
                    mm(PS[ob][:, 0:HW], btt[:, er * 128:(er + 1) * 128],
                       HSB[:, er * TW + hh * HW: er * TW + hh * HW + HW],
                       start=(er == 0), stop=(er == 3),
                       wait=([btk, h_all] + ps_wait(ob)) if er == 0 else (),
                       inc=([("tv", 1)] if er == 3 else ()))
                y_t = p.tok("tv")
                r_bt.done(bslot, y_t)
                xs = XSTG[:, oc * HW:(oc + 1) * HW]
                p.w("vector", "scalar_tensor_tensor",
                    (xs, PS[ob][:, 0:HW], 0.0, xs, OP.bypass, OP.add),
                    wait=[y_t], inc=[("vv", 1)])
                psg[ob] = p.tok("vv")
            st["xstg"] = p.dma("sync", d_y.ap()[2 * tt + hh, :, :], xin_ap(),
                               wait=[p.tok("vv")], sem="sy")
        p.bar()

    r_wo = Ring(2, 2048, 0, "rwo")
    TOPT = SCR_BF[:, 0:CH * TW]
    BOTT = SCR_BF[:, CH * TW:2 * CH * TW]
    QPH_ = XB[0:2, 0:32 * TW]
    QPL_ = XB[0:2, 32 * TW:64 * TW]

    def build_qpair(tt, half, wait=()):
        tks = []
        h0 = 64 * half
        tks.append(p.dma("sync", QPH_[0:1, :].rearrange("q (r t) -> q r t", r=32),
                         QHI()[h0:h0 + 64:2, tt * TW:(tt + 1) * TW], wait=wait, sem="qp"))
        tks.append(p.dma("sync", QPH_[1:2, :].rearrange("q (r t) -> q r t", r=32),
                         QHI()[h0 + 1:h0 + 64:2, tt * TW:(tt + 1) * TW], wait=wait, sem="qp"))
        return tks

    def build_vstack(pr, wait=()):
        sl = pr % 2
        c0 = sl * 224 + 96
        g0 = pr * 16
        p.w("vector", "tensor_copy", (VST[0:32, c0:c0 + 16], VGALL[0:32, g0:g0 + 16]),
            wait=wait)
        p.w("vector", "tensor_copy", (VST[64:96, c0 + 16:c0 + 32], VGALL[64:96, g0:g0 + 16]))
        p.w("vector", "tensor_copy", (VSB[32:64, c0:c0 + 16], VGALL[32:64, g0:g0 + 16]))
        p.w("vector", "tensor_copy", (VSB[96:128, c0 + 16:c0 + 32], VGALL[96:128, g0:g0 + 16]),
            inc=[("vv", 1)])
        return p.tok("vv")

    def attn_a_sparse(tt):
        vs_tok = {}
        pair_end = {}
        for half in range(2):
            qp = build_qpair(tt, half,
                             wait=([pair_end[31]] if half == 1 else ()))
            for tk in qp:
                p.w("tensor", wait=[tk])
            if half == 0:
                vs_tok[0] = build_vstack(0)
                vs_tok[1] = build_vstack(1)
            for k in range(32):
                pr = 32 * half + k
                ch = pr // 4
                sl = pr % 2
                mm(PS[sl][:, :], EPH[:, :], QPH_[0:2, k * TW:(k + 1) * TW],
                   start=True, stop=True, wait=ps_wait(sl), inc=[("tv", 1)])
                arg_t = p.tok("tv")
                pb = PTB[:, (pr % 4) * TW:(pr % 4 + 1) * TW]
                ew = [arg_t] + ([pair_end[pr - 4]] if pr >= 4 else [])
                p.w("scalar", "activation", (pb, PS[sl][:, :], AF.Exp),
                    {"bias": ZRO[:, 0:1], "scale": SCL[:, pr:pr + 1]},
                    wait=ew, inc=[("sv", 1)])
                p.w("vector", "tensor_scalar", (pb, pb, 1e30, None, OP.min),
                    wait=[p.tok("sv")], inc=[("vv", 1)])
                psg[sl] = p.tok("vv")
                exp_t = p.tok("vv")
                mm(PS[2][:, :], SLT[:, 128 - 2 * pr:256 - 2 * pr], pb,
                   start=(pr == 0), stop=(pr == 63),
                   wait=[exp_t] + (ps_wait(2) if pr == 0 else []))
                mm(PS[3][:, :], SLB[:, 128 - 2 * pr:256 - 2 * pr], pb,
                   start=(pr == 0), stop=(pr == 63),
                   wait=(ps_wait(3) if pr == 0 else []),
                   inc=([("tv", 1)] if pr == 63 else ()))
                it, ib = 4 + (ch % 2), 6 + (ch % 2)
                w0 = (pr % 2) * 224 + 96 - (pr % 4) * 32
                mm(PS[it][:, :], VST[:, w0:w0 + 128], pb,
                   start=(pr % 4 == 0), stop=(pr % 4 == 3),
                   wait=[vs_tok[pr]] + (ps_wait(it) if pr % 4 == 0 else []))
                mm(PS[ib][:, :], VSB[:, w0:w0 + 128], pb,
                   start=(pr % 4 == 0), stop=(pr % 4 == 3),
                   wait=(ps_wait(ib) if pr % 4 == 0 else []),
                   inc=[("tv", 1)])
                pair_end[pr] = p.tok("tv")
                if pr % 4 == 3:
                    pv_t = pair_end[pr]
                    p.w("vector", "tensor_copy",
                        (TOPT[:, ch * TW:(ch + 1) * TW], PS[it][:, :]), wait=[pv_t])
                    p.w("vector", "tensor_copy",
                        (BOTT[:, ch * TW:(ch + 1) * TW], PS[ib][:, :]), inc=[("vv", 1)])
                    psg[it] = psg[ib] = p.tok("vv")
                if pr + 2 <= 63:
                    vs_tok[pr + 2] = build_vstack(pr + 2, wait=[pair_end[pr]])
        s_done = p.tok("tv")
        WF = VSTG[:, :].bitcast(F32)
        p.w("vector", "reciprocal", (WF[:, 0:512], PS[2][:, :]), wait=[s_done],
            inc=[("vv", 1)])
        psg[2] = p.tok("vv")
        p.w("vector", "tensor_mul",
            (RSB[:, 0:512], WF[:, 0:512], MQ()[:, tt * TW:(tt + 1) * TW]), inc=[("vv", 1)])
        wtop_t = p.tok("vv")
        p.w("vector", "reciprocal", (WF[:, 0:512], PS[3][:, :]), wait=[wtop_t],
            inc=[("vv", 1)])
        psg[3] = p.tok("vv")
        p.w("vector", "tensor_mul",
            (PTB[:, 0:512], WF[:, 0:512], MQC()[:, tt * TW:(tt + 1) * TW]), inc=[("vv", 1)])
        w_t = p.tok("vv")
        wtr_free = None
        for hh in range(2):
            h0 = hh * HW
            dwait = [w_t] + ([wtr_free] if wtr_free else []) + \
                ([p.tok("sy")] if "sy" in p.cnt else [])
            for c in range(CH):
                p.dma("sync", WTR[0:8, c * HW:(c + 1) * HW],
                      RSB[8 * c:8 * c + 8, h0:h0 + HW], wait=dwait, sem="wt")
                t2 = p.dma("sync", WTR[0:8, 4096 + c * HW: 4096 + (c + 1) * HW],
                           PTB[8 * c:8 * c + 8, h0:h0 + HW], wait=dwait, sem="wt")
            t1 = t2
            for ch in range(CH):
                i = ch % 2
                mm(PS[0 + i][:, 0:HW], REP8[:, :], WTR[0:8, ch * HW:(ch + 1) * HW],
                   start=True, stop=True, wait=[t1, t2] + ps_wait(0 + i))
                mm(PS[2 + i][:, 0:HW], REP8[:, :],
                   WTR[0:8, 4096 + ch * HW: 4096 + (ch + 1) * HW],
                   start=True, stop=True, wait=ps_wait(2 + i), inc=[("tv", 1)])
                rep_t = p.tok("tv")
                tch = TOPT[:, ch * TW + h0: ch * TW + h0 + HW]
                bch = BOTT[:, ch * TW + h0: ch * TW + h0 + HW]
                p.w("vector", "tensor_mul", (tch, tch, PS[0 + i][:, 0:HW]), wait=[rep_t])
                p.w("vector", "tensor_mul", (bch, bch, PS[2 + i][:, 0:HW]))
                p.w("vector", "tensor_add", (tch, tch, bch), inc=[("vv", 1)])
                psg[0 + i] = psg[2 + i] = p.tok("vv")
            wtr_free = p.tok("tv")
        return p.tok("vv")

    for tt in range(NTT):
        attn_a_sparse(tt)
        p.bar()
        if stop_after == "probe5" and tt == 0:
            p.dma("gpsimd", d_dbg.ap()[0, :, :], TOPT[:, 0:2048])
            p.dma("gpsimd", d_dbg.ap()[1, :, :], TOPT[:, 2048:4096])
            p.dma("gpsimd", d_dbg.ap()[2, :, 0:64], VALS[:, :])
            p.dma("gpsimd", d_dbg.ap()[2, :, 64:128], TIF[:, :])
            p.dma("gpsimd", d_dbg.ap()[2, :, 128:192], IXVU[:, :])
            p.dma("gpsimd", d_dbg.ap()[2, :, 192:256], IXKU[:, :])
            p.dma("gpsimd", d_dbg.ap()[3, :, 0:2048], QHI())
            p.dma("gpsimd", d_dbg.ap()[4, :, 0:2048], QLO())
            p.qfence("sync")
            p.qfence("gpsimd")
            p.bar()
            with nc.Block() as block:
                p.replay(block)
            stack.close()
            return nc
        for hh in range(2):
            xt = p.dma("sync", xin_ap(), d_x.ap()[2 * tt + hh, :, :],
                       wait=([st["xstg"]] if st["xstg"] else []), sem="sx")
            p.w("vector", wait=[xt])
            for oc in range(CH):
                ob = (5, 2, 3, 6)[oc % 4]
                wtile, wtok, slot = r_wo.load(d_wo.ap()[oc, :, :])
                for dch in range(CH):
                    mm(PS[ob][:, 0:HW], wtile[:, dch * 128:(dch + 1) * 128],
                       TOPT[:, dch * TW + hh * HW: dch * TW + hh * HW + HW],
                       start=(dch == 0), stop=(dch == CH - 1),
                       wait=([wtok] + ps_wait(ob)) if dch == 0 else (),
                       inc=([("tv", 1)] if dch == CH - 1 else ()))
                wo_t = p.tok("tv")
                r_wo.done(slot, wo_t)
                xs = XSTG[:, oc * HW:(oc + 1) * HW]
                p.w("vector", "scalar_tensor_tensor",
                    (xs, PS[ob][:, 0:HW], 0.0, xs, OP.bypass, OP.add),
                    wait=[wo_t], inc=[("vv", 1)])
                psg[ob] = p.tok("vv")
            p.bar()
            if stop_after == "sub1":
                st["xstg"] = p.dma("sync", d_y.ap()[2 * tt + hh, :, :], xin_ap(),
                                   wait=[p.tok("vv")], sem="sy")
                continue
            phase_A(tt, hh, full_mode=False)
            st["xstg"] = p.dma("sync", d_y.ap()[2 * tt + hh, :, :], xin_ap(),
                               wait=[p.tok("vv")], sem="sy")
            p.bar()
        if stop_after == "sub1":
            continue
        rstd_tm(tt)
        ffn_tt(tt)
        p.bar()

    if stop_after in ("sub1", "sub2"):
        p.qfence("sync")
        p.qfence("gpsimd")
        p.bar()
        with nc.Block() as block:
            p.replay(block)
        stack.close()
        return nc

    for tt in range(NTT):
        for hh in range(2):
            xt = p.dma("sync", xin_ap(), d_y.ap()[2 * tt + hh, :, :],
                       wait=([st["xstg"]] if st["xstg"] else []), sem="sx")
            phase_A(tt, hh, full_mode=True, wait_x=[xt])
            p.bar()
        rstd_tm(tt)
    p.bar()

    if stop_after == "sub3a":
        p.qfence("sync")
        p.qfence("gpsimd")
        p.bar()
        with nc.Block() as block:
            p.replay(block)
        stack.close()
        return nc

    QH2 = SCR_BF[:, 0:4096]
    KH2 = SCR_BF[:, 4096:8192]
    VTM = SCR_BF[:, 8192:12288]
    QSQ = SCR_BF[:, 12288:14336]
    SEXP = SCR_F32[:, 6144:6656]
    SEH = SCR_BF[:, 13312:13824]
    SEL = SCR_BF[:, 13824:14336]
    r_qkv = Ring(4, 2048, 0, "rqv")
    r_vv = Ring(16, 256, 8192, "rvv")

    def xbf(dch, c0, cw):
        return XB[:, dch * T + c0: dch * T + c0 + cw]

    hp_prev = None
    for hp in range(8):
        if hp_prev is not None:
            p.w("vector", wait=[hp_prev])
            p.w("gpsimd", wait=[hp_prev])
        echs = [2 * hp, 2 * hp + 1, 16 + 2 * hp, 16 + 2 * hp + 1]
        tiles = [r_qkv.load(d_wqkv.ap()[e, :, :]) for e in echs]
        vtiles = [r_vv.load(d_wqkvv.ap()[dch, :, 256 * hp:256 * (hp + 1)]) for dch in range(CH)]
        p.dma("gpsimd", QKVBP[0:1, 0:256], d_qkvbrow.ap()[0:1, 256 * hp:256 * (hp + 1)], sem="qb")
        p.dma("gpsimd", QKVBP[0:1, 256:512],
              d_qkvbrow.ap()[0:1, 2048 + 256 * hp:2048 + 256 * (hp + 1)], sem="qb")
        bq_t = p.dma("gpsimd", QKVBP[0:1, 512:768],
                     d_qkvbrow.ap()[0:1, 4096 + 256 * hp:4096 + 256 * (hp + 1)], sem="qb")
        for i, e in enumerate(echs):
            wt, wtk, _ = tiles[i]
            for tt in range(NTT):
                qb_ = (0, 5)[tt % 2]
                for dch in range(CH):
                    mm(PS[qb_][:, :], wt[:, dch * 128:(dch + 1) * 128],
                       xbf(dch, tt * TW, TW), start=(dch == 0), stop=False,
                       wait=([wtk] + ps_wait(qb_)) if dch == 0 else ())
                boff = (i % 2) * 128 + (0 if i < 2 else 256)
                mm(PS[qb_][:, :], QKVBP[0:1, boff:boff + 128], SQVB(tt),
                   start=False, stop=True, wait=[bq_t], inc=[("tv", 1)])
                qk_t = p.tok("tv")
                dst = QH2 if i < 2 else KH2
                dsl = dst[:, (i % 2) * 2048 + tt * TW: (i % 2) * 2048 + (tt + 1) * TW]
                if i < 2:
                    p.w("vector", "tensor_mul",
                        (dsl, PS[qb_][:, :], RSTD[:, tt * TW:(tt + 1) * TW]),
                        wait=[qk_t], inc=[("vv", 1)])
                else:
                    p.w("vector", "scalar_tensor_tensor",
                        (dsl, PS[qb_][:, :], SMS, RSTD[:, tt * TW:(tt + 1) * TW],
                         OP.mult, OP.mult), wait=[qk_t], inc=[("vv", 1)])
                psg[qb_] = p.tok("vv")
        for tc in range(CH):
            tt, j = tc // 4, tc % 4
            vb2 = (1, 6)[tc % 2]
            for dch in range(CH):
                vt, vtk, _ = vtiles[dch]
                mm(PS[vb2][:, 0:256], xbf(dch, tc * 128, 128), vt,
                   start=(dch == 0), stop=False,
                   wait=(([vtk] if tc == 0 else []) + (ps_wait(vb2) if dch == 0 else [])))
            mm(PS[vb2][:, 0:256], SQVB(tt)[0:1, j * 128:(j + 1) * 128],
               QKVBP[0:1, 512:768],
               start=False, stop=True, wait=[bq_t], inc=[("tv", 1)])
            v_t = p.tok("tv")
            p.w("vector", "tensor_scalar",
                (VTM[:, tc * 256:(tc + 1) * 256], PS[vb2][:, 0:256], RSTM[:, tc:tc + 1],
                 None, OP.mult), wait=[v_t], inc=[("vv", 1)])
            psg[vb2] = p.tok("vv")
        p.bar()
        for a in range(2):
            h = 2 * hp + a
            negcb_toks = []
            for tt in range(NTT):
                sb2 = (2, 7)[tt % 2]
                q0 = a * 2048 + tt * TW
                mm(PS[sb2][:, :], QH2[:, q0:q0 + 128],
                   KH2[:, a * 2048:a * 2048 + TW],
                   start=True, stop=True, wait=ps_wait(sb2), inc=[("tv", 1)])
                p.w("vector", "reduce_max", (TV[:, tt:tt + 1], PS[sb2][:, :], AX),
                    wait=[p.tok("tv")], inc=[("vv", 1)])
                psg[sb2] = p.tok("vv")
                p.w("tensor", "transpose", (PS[sb2][0:1, 0:128], TV[:, tt:tt + 1],
                                            IDF[:, :]),
                    wait=[p.tok("vv")] + ps_wait(sb2), inc=[("tv", 1)])
                p.w("vector", "reduce_max",
                    (CELLS[0:1, 4 + tt:5 + tt], PS[sb2][0:1, 0:128], AX),
                    wait=[p.tok("tv")], inc=[("vv", 1)])
                psg[sb2] = p.tok("vv")
                p.w("vector", "tensor_scalar",
                    (CELLS[0:1, 8 + tt:9 + tt].bitcast(BF16)[0:1, 0:1],
                     CELLS[0:1, 4 + tt:5 + tt], -1.0,
                     -30.0, OP.mult, OP.add), inc=[("vv", 1)])
                mm(PS[sb2][:, 0:1], ONE1[:, :],
                   CELLS[0:1, 8 + tt:9 + tt].bitcast(BF16)[0:1, 0:1],
                   start=True, stop=True, wait=[p.tok("vv")] + ps_wait(sb2),
                   inc=[("tv", 1)])
                p.w("vector", "tensor_copy", (NEGCB[:, tt:tt + 1], PS[sb2][:, 0:1]),
                    wait=[p.tok("tv")], inc=[("vv", 1)])
                psg[sb2] = p.tok("vv")
                negcb_toks.append(p.tok("vv"))
            for tt in range(NTT):
                negcb_t = negcb_toks[tt]
                accs = {}
                for kch in range(CH):
                    sb = (2, 3, 4)[kch % 3]
                    sl = kch % 4
                    mm(PS[sb][:, :],
                       KH2[:, a * 2048 + kch * 128: a * 2048 + (kch + 1) * 128],
                       QH2[:, a * 2048 + tt * TW: a * 2048 + (tt + 1) * TW],
                       start=True, stop=True,
                       wait=ps_wait(sb) + ([negcb_t] if kch == 0 else []),
                       inc=[("tv", 1)])
                    arg_t = p.tok("tv")
                    pb = PTB[:, sl * TW:(sl + 1) * TW]
                    p.w("scalar", "activation", (pb, PS[sb][:, :], AF.Exp),
                        {"bias": NEGCB[:, tt:tt + 1], "scale": 1.0},
                        wait=[arg_t] + ([accs[kch - 4]] if kch >= 4 else []),
                        inc=[("sv", 1)])
                    psg[sb] = p.tok("sv")
                    exp_t = p.tok("sv")
                    if kch == 0:
                        p.w("vector", "tensor_copy", (SEXP, pb), wait=[exp_t],
                            inc=[("vv", 1)])
                    else:
                        p.w("vector", "scalar_tensor_tensor",
                            (SEXP, pb, 0.0, SEXP, OP.bypass, OP.add),
                            wait=[exp_t], inc=[("vv", 1)])
                    accs[kch] = p.tok("vv")
                    mm(PS[6][:, :], VTM[:, kch * 256 + a * 128: kch * 256 + (a + 1) * 128],
                       pb, start=(kch == 0), stop=(kch == CH - 1),
                       wait=[exp_t] + (ps_wait(6) if kch == 0 else []),
                       inc=([("tv", 1)] if kch == CH - 1 else ()))
                ctx_t = p.tok("tv")
                p.w("vector", "tensor_copy", (SEH, SEXP),
                    wait=([st["sum_t"]] if st.get("sum_t") else ()))
                p.w("vector", "scalar_tensor_tensor",
                    (SEL, SEXP, 1.0, SEH, OP.mult, OP.subtract), inc=[("vv", 1)])
                mm(PS[5][0:1, :], ONEC[:, :], SEH, start=True, stop=False,
                   wait=[p.tok("vv")] + ps_wait(5))
                mm(PS[5][0:1, :], ONEC[:, :], SEL, start=False, stop=False)
                mm(PS[5][0:1, :], ONE1[0:1, 0:1], TINY[0:1, :],
                   start=False, stop=True, inc=[("tv", 1)])
                sum_t = p.tok("tv")
                st["sum_t"] = sum_t
                p.w("vector", "reciprocal", (SCRW, PS[5][0:1, 0:256]),
                    wait=[ctx_t, sum_t])
                p.w("vector", "reciprocal", (SCRW2, PS[5][0:1, 256:512]), inc=[("vv", 1)])
                psg[5] = p.tok("vv")
                p.w("vector", "tensor_copy", (RHB[0:1, 0:256], SCRW))
                p.w("vector", "tensor_copy", (RHB[0:1, 256:512], SCRW2))
                p.w("vector", "scalar_tensor_tensor",
                    (RLB[0:1, 0:256], SCRW, 1.0, RHB[0:1, 0:256], OP.mult, OP.subtract))
                p.w("vector", "scalar_tensor_tensor",
                    (RLB[0:1, 256:512], SCRW2, 1.0, RHB[0:1, 256:512], OP.mult, OP.subtract),
                    inc=[("vv", 1)])
                mm(PS[7][:, :], ONE1[:, :], RHB, start=True, stop=False,
                   wait=[p.tok("vv")] + ps_wait(7))
                mm(PS[7][:, :], ONE1[:, :], RLB, start=False, stop=True, inc=[("tv", 1)])
                p.w("vector", "tensor_copy", (RSB[:, 0:TW], PS[7][:, :]), wait=[p.tok("tv")],
                    inc=[("vv", 1)])
                psg[7] = p.tok("vv")
                vs = VSTG[:, (tt % 2) * TW:(tt % 2 + 1) * TW]
                vw = [st["vstg"][tt % 2]] if st["vstg"][tt % 2] else []
                p.w("vector", "tensor_mul", (vs, PS[6][:, :], RSB[:, 0:TW]), wait=vw,
                    inc=[("vv", 1)])
                psg[6] = p.tok("vv")
                st["vstg"][tt % 2] = p.dma(
                    "sync", d_ctx.ap()[h, :, tt * TW:(tt + 1) * TW], vs,
                    wait=[p.tok("vv")], sem="vs%d" % (tt % 2))
        qk_last = p.tok("tv")
        hp_prev = qk_last
        for i in range(4):
            r_qkv.done(tiles[i][2], qk_last)
        for dch in range(CH):
            r_vv.done(vtiles[dch][2], qk_last)
        if stop_after == "probe4" and hp == 0:
            p.qfence("sync")
            p.bar()
            for h in range(2):
                p.dma("gpsimd", d_dbg.ap()[h, :, :], d_ctx.ap()[h, :, :])
            p.dma("gpsimd", d_dbg.ap()[2, :, :], QH2[:, 0:2048])
            p.dma("gpsimd", d_dbg.ap()[3, :, :], KH2[:, 0:2048])
            p.dma("gpsimd", d_dbg.ap()[4, :, 0:512], RSB[:, 0:512])
            p.dma("gpsimd", d_dbg.ap()[4, :, 512:513], NEGCB[:, 0:1],
                  allow_slow_non_contiguous=True)
            p.dma("gpsimd", d_dbg.ap()[4, 0:1, 520:536],
                  CELLS[0:1, 0:16], allow_slow_non_contiguous=True)
            p.dma("gpsimd", d_dbg.ap()[4, 0:1, 536:1048], RHB[0:1, 0:512])
            p.dma("gpsimd", d_dbg.ap()[4, :, 1048:2048], VTM[:, 0:1000])
            p.qfence("sync")
            p.qfence("gpsimd")
            p.bar()
            with nc.Block() as block:
                p.replay(block)
            stack.close()
            return nc
        if stop_after and stop_after.startswith("sub3b") and \
                hp == int(stop_after[5:] or 0):
            break
    p.bar()

    if stop_after and stop_after.startswith("sub3b"):
        p.qfence("sync")
        p.qfence("gpsimd")
        p.bar()
        with nc.Block() as block:
            p.replay(block)
        stack.close()
        return nc

    r_wout = Ring(2, 2048, 0, "rwt")
    CTXT = SCR_BF[:, 0:CH * TW]
    ctx_done = [p.tok("vs0"), p.tok("vs1")]
    for tt in range(NTT):
        ctk = []
        for g in range(4):
            ctk.append(p.dma("sync",
                             CTXT.rearrange("q (c t) -> q c t", c=CH)[:, 4 * g:4 * g + 4, :],
                             d_ctx.ap()[4 * g:4 * g + 4, :, tt * TW:(tt + 1) * TW]
                             .rearrange("c q t -> q c t"),
                             wait=(ctx_done if (tt == 0 and g == 0) else ()),
                             sem="ct"))
        for tk in ctk:
            p.w("tensor", wait=[tk])
        if stop_after == "sub3c":
            p.qfence("sync")
            p.qfence("gpsimd")
            p.bar()
            with nc.Block() as block:
                p.replay(block)
            stack.close()
            return nc
        for hh in range(2):
            xt = p.dma("sync", xin_ap(), d_y.ap()[2 * tt + hh, :, :],
                       wait=([st["xstg"]] if st["xstg"] else []), sem="sx")
            p.w("vector", wait=[xt])
            for oc in range(CH):
                if stop_after == "sub3d" and oc == 4:
                    break
                ob = (5, 2, 3, 6)[oc % 4]
                wtile, wtok, slot = r_wout.load(d_wout.ap()[oc, :, :])
                for dch in range(CH):
                    mm(PS[ob][:, 0:HW], wtile[:, dch * 128:(dch + 1) * 128],
                       CTXT[:, dch * TW + hh * HW: dch * TW + hh * HW + HW],
                       start=(dch == 0), stop=(dch == CH - 1),
                       wait=([wtok] + ps_wait(ob)) if dch == 0 else (),
                       inc=([("tv", 1)] if dch == CH - 1 else ()))
                wo_t = p.tok("tv")
                r_wout.done(slot, wo_t)
                xs = XSTG[:, oc * HW:(oc + 1) * HW]
                p.w("vector", "scalar_tensor_tensor",
                    (xs, PS[ob][:, 0:HW], OUTB[:, oc:oc + 1], xs, OP.add, OP.add),
                    wait=[wo_t], inc=[("vv", 1)])
                psg[ob] = p.tok("vv")
            if stop_after == "sub3d":
                p.qfence("sync")
                p.qfence("gpsimd")
                p.bar()
                with nc.Block() as block:
                    p.replay(block)
                stack.close()
                return nc
            p.bar()
            if stop_after == "sub3":
                st["xstg"] = p.dma("sync", d_y.ap()[2 * tt + hh, :, :], xin_ap(),
                                   wait=[p.tok("vv")], sem="sy")
                continue
            phase_A(tt, hh, full_mode=False)
            st["xstg"] = p.dma("sync", d_y.ap()[2 * tt + hh, :, :], xin_ap(),
                               wait=[p.tok("vv")], sem="sy")
            p.bar()
        if stop_after == "sub3":
            continue
        rstd_tm(tt)
        ffn_tt(tt)
        p.bar()

    p.qfence("sync")
    p.qfence("gpsimd")
    p.bar()
    with nc.Block() as block:
        p.replay(block)
    stack.close()
    return nc


def _layout_inputs(inputs):
    f = np.float32
    x = np.asarray(inputs["L0_attn_a_x"], dtype=f)
    Wq = np.asarray(inputs["L0_attn_a_Wq"], dtype=f)
    Wk = np.asarray(inputs["L0_attn_a_Wk"], dtype=f)
    Wv = np.asarray(inputs["L0_attn_a_Wv"], dtype=f)
    Wo = np.asarray(inputs["L0_attn_a_Wo"], dtype=f)
    Wqkv = np.asarray(inputs["L0_attn_b_Wqkv_w"], dtype=f)
    qkvb = np.asarray(inputs["L0_attn_b_Wqkv_b"], dtype=f)
    outw = np.asarray(inputs["L0_attn_b_out_w"], dtype=f)
    outb = np.asarray(inputs["L0_attn_b_out_b"], dtype=f)
    Am = np.asarray(inputs["L0_ffn_A"], dtype=f)
    Bm = np.asarray(inputs["L0_ffn_B"], dtype=f)
    Wr = np.asarray(inputs["L0_ffn_Wr"], dtype=f)
    br = np.asarray(inputs["L0_ffn_br"], dtype=f)

    def lhs_blk(WT, nech):
        E = WT.shape[1]
        return np.ascontiguousarray(
            WT.reshape(CH, 128, nech, E // nech).transpose(2, 1, 0, 3)
            .reshape(nech, 128, CH * (E // nech)))

    wqkT = lhs_blk(np.ascontiguousarray(np.concatenate([Wq.T, Wk.T], axis=1)), 2)
    wvT = np.ascontiguousarray(Wv.T.reshape(CH, 128, 4, 512).transpose(2, 0, 1, 3))
    woT = lhs_blk(np.ascontiguousarray(Wo.T), CH)
    WqkvT = np.ascontiguousarray(Wqkv.T)
    wqkvT = lhs_blk(np.ascontiguousarray(WqkvT[:, 0:4096]), 32)
    wqkvv = np.ascontiguousarray(WqkvT[:, 4096:6144].reshape(CH, 128, 2048))
    woutT = lhs_blk(np.ascontiguousarray(outw.T), CH)
    AT = np.ascontiguousarray(Am.transpose(2, 0, 1).reshape(D, 512))
    atT = lhs_blk(AT, 4)
    BT = np.ascontiguousarray(Bm.transpose(0, 2, 1).reshape(512, D))
    btT = np.ascontiguousarray(
        BT.reshape(4, 128, CH, 128).transpose(2, 1, 0, 3).reshape(CH, 128, 512))
    wrT = np.ascontiguousarray(Wr.T.reshape(CH, 128, 8).transpose(1, 0, 2).reshape(128, CH * 8))
    qkvbrow = np.ascontiguousarray(qkvb.reshape(1, 6144))
    outb2 = np.ascontiguousarray(outb.reshape(16, 128).T)
    br2 = np.ascontiguousarray(br.reshape(8, 1))

    shared = {
        "wqk": wqkT, "wv": wvT, "wo": woT, "wqkv": wqkvT, "wqkvv": wqkvv,
        "wout": woutT, "at": atT, "bt": btT, "wr": wrT,
        "qkvbrow": qkvbrow, "outb2": outb2, "br": br2,
    }
    in_maps = []
    for c in range(8):
        xf = np.ascontiguousarray(x[c].T)
        xblk = np.ascontiguousarray(
            xf.reshape(CH, 128, NTT * 2, HW).transpose(2, 1, 0, 3)
            .reshape(NTT * 2, 128, CH * HW))
        m = dict(shared)
        m["x"] = xblk
        in_maps.append(m)
    return in_maps


_NC_CACHE = {}


def kernel(**inputs):
    in_maps = _layout_inputs(inputs)
    key = "full"
    if key not in _NC_CACHE:
        _NC_CACHE[key] = build_nc()
    nc = _NC_CACHE[key]
    res = run_bass_kernel_spmd(nc, in_maps, core_ids=list(range(8)))
    outs = []
    for c in range(8):
        yblk = res.results[c]["y"].reshape(8, 128, CH, HW)
        xf = yblk.transpose(2, 1, 0, 3).reshape(D, T)
        outs.append(np.ascontiguousarray(xf.T))
    return np.stack(outs, axis=0).astype(np.float32)
